# version 18
# speedup vs baseline: 1.1073x; 1.0105x over previous
import contextlib
import numpy as np
import ml_dtypes

import concourse.bass as bass
import concourse.mybir as mybir
from concourse.bass_utils import run_bass_kernel_spmd

F32 = mybir.dt.float32
BF16 = mybir.dt.bfloat16
AF = mybir.ActivationFunctionType
ALU = mybir.AluOpType
AX = mybir.AxisListType
BF = ml_dtypes.bfloat16

D = 1024
NBLK = 8
NH = 16
HD = 64
L0L = 4
FF = 4096
FBLK = 32
KEXP = 8
RR = 64
MH = 2048
MHB = 16
EPS_LN0 = 1e-12
EPS_MOD = 1e-5


class Buf:
    __slots__ = ("name", "writer", "readers", "fenced")

    def __init__(self, name="", nofence=False):
        self.name = name
        self.writer = None
        self.readers = []
        self.fenced = nofence


SAME_ENGINE_WAIT = {"pe": False, "act": True, "dve": True, "pool": True, "sp": True}


class Prog:
    ENGINES = ("sp", "pe", "act", "dve", "pool")

    def __init__(self, nc, n_dma_sems=24):
        self.nc = nc
        self.ops = {e: [] for e in self.ENGINES}
        self.ctr = {e: 0 for e in self.ENGINES}
        self.hw = {e: {} for e in self.ENGINES}
        self.dma_pools = {
            "sp": [f"dma_s{i}" for i in range(10)],
            "pool": [f"dma_p{i}" for i in range(10)],
            "act": [f"dma_a{i}" for i in range(4)],
        }
        self.dma_sems = [s for v in self.dma_pools.values() for s in v]
        self.dma_base = {s: 0 for s in self.dma_sems}
        self.dma_rr = {q: 0 for q in self.dma_pools}
        self.sem_handles = {}
        self.out_groups = []
        self.fence = {}
        self._es = contextlib.ExitStack()

    def scope(self):
        return contextlib.ExitStack()

    def close_scope(self, es):
        es.close()
        f = dict(self.fence)
        for e in self.ENGINES:
            if self.ctr[e] > 0:
                f["s_" + e] = max(f.get("s_" + e, 0), self.ctr[e])
        for s, v in self.dma_base.items():
            if v > 0:
                f[s] = max(f.get(s, 0), v)
        self.fence = f

    def sbuf(self, name, shape, dtype, es=None):
        return (es or self._es).enter_context(self.nc.sbuf_tensor(name, shape, dtype))

    def psum(self, name, shape, dtype=F32, es=None):
        return (es or self._es).enter_context(self.nc.psum_tensor(name, shape, dtype))

    def _want(self, waits, engine, sem, val):
        if sem == "s_" + engine and not SAME_ENGINE_WAIT[engine]:
            return
        if self.hw[engine].get(sem, 0) >= val:
            return
        if waits.get(sem, 0) < val:
            waits[sem] = val

    def _dep_waits(self, engine, reads, writes):
        waits = {}
        for b in reads:
            if b.writer:
                self._want(waits, engine, b.writer[0], b.writer[1])
        for b in writes:
            if b.writer is None and not b.fenced:
                b.fenced = True
                for s, v in self.fence.items():
                    self._want(waits, engine, s, v)
            if b.writer:
                self._want(waits, engine, b.writer[0], b.writer[1])
            for r in b.readers:
                self._want(waits, engine, r[0], r[1])
        return waits

    def _commit(self, token, reads, writes):
        for b in reads:
            b.readers.append(token)
        for b in writes:
            b.writer = token
            b.readers = []

    def op(self, engine, fn, reads=(), writes=(), name=""):
        waits = self._dep_waits(engine, reads, writes)
        self.ctr[engine] += 1
        val = self.ctr[engine]
        sem = "s_" + engine
        self.ops[engine].append((waits, fn, sem, name))
        for s, v in waits.items():
            self.hw[engine][s] = max(self.hw[engine].get(s, 0), v)
        if not SAME_ENGINE_WAIT[engine]:
            self.hw[engine][sem] = val
        self._commit((sem, val), reads, writes)
        return (sem, val)

    def dma(self, queue, pairs, reads=(), writes=(), name="", is_output=False):
        pool = self.dma_pools[queue]
        sem = pool[self.dma_rr[queue]]
        self.dma_rr[queue] = (self.dma_rr[queue] + 1) % len(pool)
        base = self.dma_base[sem]
        waits = self._dep_waits(queue, reads, writes)
        if base > 0:
            self._want(waits, queue, sem, base)
        fin = base + 16 * len(pairs)
        self.dma_base[sem] = fin

        def fn(eng, pairs=pairs, sem_name=sem):
            last = None
            for (o, i, kw) in pairs:
                last = eng.dma_start(o, i, **kw).then_inc(self.sem_handles[sem_name], 16)
            return last

        self.ops[queue].append((waits, fn, None, name))
        for s, v in waits.items():
            self.hw[queue][s] = max(self.hw[queue].get(s, 0), v)
        self._commit((sem, fin), reads, writes)
        if is_output:
            self.out_groups.append((sem, fin))
        return (sem, fin)

    def finish_and_build(self):
        waits = {}
        for s, v in self.out_groups:
            waits[s] = max(waits.get(s, 0), v)
        for s, v in self.dma_base.items():
            if v > 0 and self.hw["sp"].get(s, 0) < v:
                waits[s] = max(waits.get(s, 0), v)
        self.ops["sp"].append((waits, lambda eng: eng.nop(), None, "final"))
        nc = self.nc
        for s in ["s_" + e for e in self.ENGINES] + self.dma_sems:
            self.sem_handles[s] = self._es.enter_context(nc.semaphore(s))
        attr = {"sp": "sync", "pe": "tensor", "act": "scalar",
                "dve": "vector", "pool": "gpsimd"}
        self.inst_map = []
        with nc.Block() as blk:
            for key in self.ENGINES:
                def body(eng, key=key):
                    for waits_, fn, incsem, name_ in self.ops[key]:
                        w0 = nc.n_instructions()
                        for s, v in waits_.items():
                            eng.wait_ge(self.sem_handles[s], v)
                        s0 = nc.n_instructions()
                        last = fn(eng)
                        if incsem is not None:
                            last.then_inc(self.sem_handles[incsem], 1)
                        self.inst_map.append((w0, s0, nc.n_instructions(), key, name_))
                getattr(blk, attr[key])(body)
        self._es.close()


class DT:

    def __init__(self, P, name, nblk, width, dtype, es=None):
        self.t = P.sbuf(name, [128, nblk * width], dtype, es=es)
        self.nblk = nblk
        self.width = width
        self.bufs = [Buf(f"{name}.{i}") for i in range(nblk)]

    def ap(self, blk, c0=0, c1=None):
        if c1 is None:
            c1 = self.width
        return self.t[:, blk * self.width + c0: blk * self.width + c1]


def mmg(P, items, reads, writes, name="", skip_check=False):
    def fn(eng):
        last = None
        for (o, l, r, st, sp) in items:
            last = eng.matmul(o, l, r, start=st, stop=sp, skip_group_check=skip_check)
        return last
    return P.op("pe", fn, reads=reads, writes=writes, name=name)


def accg(n):
    return [(i == 0, i == n - 1) for i in range(n)]


class Tile:

    def __init__(self, P, name, shape, dtype, es=None, psum=False):
        self.t = (P.psum(name, shape, dtype, es=es) if psum
                  else P.sbuf(name, shape, dtype, es=es))
        self.buf = Buf(name)

    def __getitem__(self, sl):
        return self.t[sl]



def emit_ln(P, C, xy, out, outbf, g_col, b_col, eps, tag, chunk_cb=None,
            interleave=None):
    nt = xy.width
    CKS = min(256, nt)
    CKA = min(512, nt)
    nck_ = nt // CKA
    cbf = C["cbf"]
    rows, rows_buf = C["rows"], C["rows_buf"]
    rbf, rbf_bufs = C["rows_bf"], C["rows_bf_bufs"]
    ps_st, ps_st_buf = C["ps_st"], C["ps_st_buf"]
    ps_a, ps_a_buf = C["ps_bca"], C["ps_bca_buf"]
    ps_c, ps_c_buf = C["ps_bcc"], C["ps_bcc_buf"]
    ones_col, ones_row, const_buf = C["ones_col"], C["ones_row"], C["const_buf"]
    inv_d = 1.0 / D

    def stats(ck):
        c0 = ck * CKA
        for sub in range(CKA // CKS):
            s0 = c0 + sub * CKS
            r = sub * CKS
            P.op("act", lambda e, s0=s0: e.copy(
                cbf.t[:].rearrange("p (a c) -> p a c", c=CKS),
                xy.t[:].rearrange("p (a c) -> p a c", c=xy.width)[:, :, s0:s0 + CKS]),
                reads=xy.bufs, writes=cbf.bufs, name=f"{tag}.cast")
            mmg(P, [(ps_st[0:1, 0:CKS], ones_col[:, 0:1], cbf.ap(b), st, sp)
                    for b, (st, sp) in zip(range(NBLK), accg(NBLK))],
                reads=[cbf.bufs[b] for b in range(NBLK)] + [const_buf],
                writes=[ps_st_buf], name=f"{tag}.s1")
            P.op("dve", lambda e: e.tensor_mul(cbf.t[:], cbf.t[:], cbf.t[:]),
                 reads=cbf.bufs, writes=cbf.bufs, name=f"{tag}.sq")
            P.op("act", lambda e, r=r: e.copy(rows[0:1, r:r + CKS], ps_st[0:1, 0:CKS]),
                 reads=[ps_st_buf], writes=[rows_buf], name=f"{tag}.s1ev")
            mmg(P, [(ps_a[0:1, 0:CKS], ones_col[:, 0:1], cbf.ap(b), st, sp)
                    for b, (st, sp) in zip(range(NBLK), accg(NBLK))],
                reads=[cbf.bufs[b] for b in range(NBLK)] + [const_buf],
                writes=[ps_a_buf], name=f"{tag}.s2")
            P.op("act", lambda e, r=r: e.copy(rows[0:1, CKA + r:CKA + r + CKS],
                                              ps_a[0:1, 0:CKS]),
                 reads=[ps_a_buf], writes=[rows_buf], name=f"{tag}.s2ev")

        r0 = rows[0:1, 0:CKA]
        r1 = rows[0:1, CKA:2 * CKA]
        msq = rows[0:1, 2 * CKA:3 * CKA]
        P.op("dve", lambda e: e.tensor_scalar_mul(r0, r0, inv_d),
             reads=[rows_buf], writes=[rows_buf], name=f"{tag}.m")
        P.op("dve", lambda e: e.tensor_scalar_mul(r1, r1, inv_d),
             reads=[rows_buf], writes=[rows_buf], name=f"{tag}.e2")
        P.op("dve", lambda e: e.tensor_mul(msq, r0, r0),
             reads=[rows_buf], writes=[rows_buf], name=f"{tag}.msq")
        P.op("dve", lambda e: e.tensor_sub(r1, r1, msq),
             reads=[rows_buf], writes=[rows_buf], name=f"{tag}.var")
        P.op("dve", lambda e: e.tensor_scalar_add(r1, r1, float(eps)),
             reads=[rows_buf], writes=[rows_buf], name=f"{tag}.veps")
        P.op("act", lambda e: e.activation(r1, r1, AF.Sqrt, bias=0.0, scale=1.0),
             reads=[rows_buf], writes=[rows_buf], name=f"{tag}.sd")
        P.op("dve", lambda e: e.reciprocal(r1, r1),
             reads=[rows_buf], writes=[rows_buf], name=f"{tag}.a")
        P.op("dve", lambda e: e.scalar_tensor_tensor(r0, r0, -1.0, r1, ALU.mult, ALU.mult),
             reads=[rows_buf], writes=[rows_buf], name=f"{tag}.c")
        sl = (ck % 2) * 2 * CKA
        P.op("dve", lambda e, sl=sl: e.tensor_copy(rbf[0:1, sl:sl + CKA], r1),
             reads=[rows_buf], writes=[rbf_bufs[ck % 2]], name=f"{tag}.abf")
        P.op("dve", lambda e, sl=sl: e.tensor_copy(rbf[0:1, sl + CKA:sl + 2 * CKA], r0),
             reads=[rows_buf], writes=[rbf_bufs[ck % 2]], name=f"{tag}.cbf")

    def apply(ck):
        c0 = ck * CKA
        sl = (ck % 2) * 2 * CKA
        mmg(P, [(ps_a[:, 0:CKA], ones_row[0:1, 0:128], rbf[0:1, sl:sl + CKA],
                 True, True)],
            reads=[rbf_bufs[ck % 2], const_buf], writes=[ps_a_buf], name=f"{tag}.bca")
        mmg(P, [(ps_c[:, 0:CKA], ones_row[0:1, 0:128], rbf[0:1, sl + CKA:sl + 2 * CKA],
                 True, True)],
            reads=[rbf_bufs[ck % 2], const_buf], writes=[ps_c_buf], name=f"{tag}.bcc")
        def o3(t, w):
            return t[:].rearrange("p (a c) -> p a c", c=w)[:, :, c0:c0 + CKA]
        bca_b = ps_a[:, 0:CKA].rearrange("p (a c) -> p a c", a=1).broadcast_to(
            [128, NBLK, CKA])
        bcc_b = ps_c[:, 0:CKA].rearrange("p (a c) -> p a c", a=1).broadcast_to(
            [128, NBLK, CKA])
        P.op("dve", lambda e: e.tensor_mul(
            o3(out.t, out.width), o3(xy.t, xy.width), bca_b),
            reads=list(xy.bufs) + [ps_a_buf], writes=list(out.bufs),
            name=f"{tag}.ap1")
        P.op("dve", lambda e: e.tensor_add(
            o3(out.t, out.width), o3(out.t, out.width), bcc_b),
            reads=list(out.bufs) + [ps_c_buf], writes=list(out.bufs),
            name=f"{tag}.ap2")
        if g_col is not None:
            for blk in range(NBLK):
                gap, gbufs = g_col(blk)
                bap, bbufs = b_col(blk)
                P.op("dve", lambda e, b=blk, c0=c0, gap=gap, bap=bap: e.tensor_scalar(
                    out.ap(b, c0, c0 + CKA), out.ap(b, c0, c0 + CKA),
                    gap, bap, ALU.mult, ALU.add),
                    reads=[out.bufs[blk]] + gbufs + bbufs, writes=[out.bufs[blk]],
                    name=f"{tag}.ap3")
        if outbf is not None:
            P.op("dve", lambda e: e.tensor_copy(
                o3(outbf.t, outbf.width), o3(out.t, out.width)),
                reads=list(out.bufs), writes=list(outbf.bufs), name=f"{tag}.bf")

    if interleave is not None:
        for ck in range(nck_):
            interleave(ck)
            stats(ck)
            if ck >= 1:
                apply(ck - 1)
            if ck >= 2 and chunk_cb is not None:
                chunk_cb(ck - 2)
        apply(nck_ - 1)
        if chunk_cb is not None:
            if nck_ >= 2:
                chunk_cb(nck_ - 2)
            chunk_cb(nck_ - 1)
        return
    stats(0)
    if nck_ > 1:
        stats(1)
    apply(0)
    for ck in range(1, nck_):
        if ck + 1 < nck_:
            stats(ck + 1)
        apply(ck)
        if chunk_cb is not None:
            chunk_cb(ck - 1)
    if chunk_cb is not None:
        chunk_cb(nck_ - 1)


def make_ln_common(P, nt, es, G, pfx=""):
    C = {}
    C["cbf"] = DT(P, pfx + "ln_cbf", NBLK, min(256, nt), BF16, es=es)
    C["rows"] = P.sbuf(pfx + "ln_rows", [1, 3 * min(512, nt)], F32, es=es)
    C["rows_buf"] = Buf("ln_rows")
    C["rows_bf"] = P.sbuf(pfx + "ln_rows_bf", [1, 4 * min(512, nt)], BF16, es=es)
    C["rows_bf_bufs"] = [Buf("ln_rows_bf0"), Buf("ln_rows_bf1")]
    C["ps_st"] = P.psum(pfx + "ps_st", [1, 512], F32, es=es)
    C["ps_st_buf"] = Buf("ps_st")
    C["ps_bca"] = P.psum(pfx + "ps_bca", [128, 512], F32, es=es)
    C["ps_bca_buf"] = Buf("ps_bca")
    C["ps_bcc"] = P.psum(pfx + "ps_bcc", [128, 512], F32, es=es)
    C["ps_bcc_buf"] = Buf("ps_bcc")
    C["ones_col"] = G["ones_col"]
    C["ones_row"] = G["ones_row"]
    C["const_buf"] = G["const_buf"]
    return C



def emit_globals(P, G):
    ones_col = P.sbuf("g_ones_col", [128, 4], BF16)
    ones_row = P.sbuf("g_ones_row", [33, 512], BF16)
    ones_col_f = P.sbuf("g_ones_col_f", [128, 1], F32)
    ones_row_f = P.sbuf("g_ones_row_f", [1, 128], F32)
    eqm = P.sbuf("g_eqm", [128, 128], BF16)
    eqf = P.sbuf("g_eqf", [128, 128], F32)
    const_buf = Buf("g_const")
    P.op("pool", lambda e: e.memset(ones_col[:], 1.0), writes=[const_buf], name="g.oc")
    P.op("pool", lambda e: e.memset(ones_row[:], 1.0), writes=[const_buf], name="g.or")
    P.op("pool", lambda e: e.memset(ones_col_f[:], 1.0), writes=[const_buf], name="g.ocf")
    P.op("pool", lambda e: e.memset(ones_row_f[:], 1.0), writes=[const_buf], name="g.orf")
    P.op("pool", lambda e: e.iota(eqf[:], [[1, 128]], base=0, channel_multiplier=-1, allow_small_or_imprecise_dtypes=True),
         writes=[const_buf], name="g.iota")
    P.op("dve", lambda e: e.tensor_single_scalar(eqm[:], eqf[:], 0.0, ALU.is_equal),
         reads=[const_buf], writes=[const_buf], name="g.eqm")
    G["ones_col"] = ones_col
    G["ones_row"] = ones_row
    G["ones_col_f"] = ones_col_f
    G["ones_row_f"] = ones_row_f
    G["eqm"] = eqm
    G["const_buf"] = const_buf


class ParamPack:

    def __init__(self, P, name, specs, queue="sp", es=None):
        self.off = {}
        tot = 0
        for k, ap, w in specs:
            self.off[k] = (tot, w)
            tot += w
        self.t = P.sbuf(name, [128, tot], F32, es=es)
        self.buf = Buf(name)
        pairs = []
        for k, ap, w in specs:
            o, _ = self.off[k]
            pairs.append((self.t[:, o:o + w], ap, {}))
        P.dma(queue, pairs, writes=[self.buf], name=f"{name}.load")

    def col(self, key, i):
        o, w = self.off[key]
        assert i < w
        return self.t[:, o + i:o + i + 1]

    def ap(self, key):
        o, w = self.off[key]
        return self.t[:, o:o + w]



def emit_l0_attn(P, G, W, x_in_ap, x_out_ap, nt, upto=None, xio=None, prestage_cb=None):
    xio = xio or {}
    XIN = (lambda b: [xio["in"][b]]) if xio.get("in") else (lambda b: [])
    XOUT = (lambda b: [xio["out"][b]]) if xio.get("out") else (lambda b: [])
    FINAL = xio.get("final", True)
    """L0_attn block (+ module postLN). x_in_ap(b, blk)/x_out_ap(b, blk) -> dram APs."""
    nb = G["nb"]
    CK = min(512, nt)
    nck = nt // CK
    nsub = nt // 128
    es = P.scope()

    hx = DT(P, "a_hx", NBLK, nt, F32, es=es)
    ctmp = Tile(P, "a_ctmp", [128, CK], F32, es=es)
    ps1 = Tile(P, "a_ps1", [128, 512], F32, es=es, psum=True)
    ps2 = Tile(P, "a_ps2", [128, 512], F32, es=es, psum=True)
    ps3 = Tile(P, "a_ps3", [128, 512], F32, es=es, psum=True)
    ps4 = Tile(P, "a_ps4", [128, 512], F32, es=es, psum=True)
    C = make_ln_common(P, nt, es, G, "a_")

    PP = ParamPack(P, "a_pp", [
        ("w1c", W["conv1_w_cols"], 160),
        ("w2c", W["conv2_w_cols"], 160),
        ("b1c", W["conv1_b_cols"], 32),
        ("b2c", W["conv2_b_cols"], 8),
        ("ln0g", W["ln0_g_cols"], 8),
        ("ln0b", W["ln0_b_cols"], 8),
        ("bq", W["bq_cols"], 32),
        ("bk", W["bk_cols"], 32),
        ("lnAg", W["lnA_g_cols"], 32),
        ("lnAb", W["lnA_b_cols"], 32),
        ("b1f", W["ffn_b1_cols"], 32),
        ("b2f", W["ffn_b2_cols"], 8),
        ("ln2g", W["ln2_g_cols"], 8),
        ("ln2b", W["ln2_b_cols"], 8),
    ], es=es)
    bvr = P.sbuf("a_bvr", [1, D], BF16, es=es)
    bvr_buf = Buf("a_bvr")
    S_sw = P.sbuf("a_Ssw", [128, 128], BF16, es=es)
    S_buf = Buf("a_Ssw")
    P.dma("sp", [(S_sw[:], W["S_swap"], {})], writes=[S_buf], name="a.S")

    wkb = P.nc.dram_tensor("a_wkb", [L0L, 128, NBLK, D], BF16, kind="Internal").ap()
    wvb = P.nc.dram_tensor("a_wvb", [L0L, 128, NBLK, D], BF16, kind="Internal").ap()
    wqb = P.nc.dram_tensor("a_wqb", [L0L, 128, NBLK, D], BF16, kind="Internal").ap()
    w1b = P.nc.dram_tensor("a_w1b", [FBLK, 128, NBLK * 128], BF16, kind="Internal").ap()
    w2b = P.nc.dram_tensor("a_w2b", [NBLK, 128, FBLK * 128], BF16, kind="Internal").ap()
    wkb_bufs = [Buf(f"a_wkb{l}", nofence=True) for l in range(L0L)]
    wvb_bufs = [Buf(f"a_wvb{l}", nofence=True) for l in range(L0L)]
    wqb_bufs = [Buf(f"a_wqb{l}", nofence=True) for l in range(L0L)]
    w1b_bufs = [Buf(f"a_w1b{i}", nofence=True) for i in range(4)]
    w2b_bufs = [Buf(f"a_w2b{i}", nofence=True) for i in range(4)]
    for l in range(L0L):
        P.dma("pool", [(wkb[l], W["Wk_r"][l], {})], writes=[wkb_bufs[l]],
              name=f"a.ck{l}")
        P.dma("pool", [(wvb[l], W["Wv_r"][l], {})], writes=[wvb_bufs[l]],
              name=f"a.cv{l}")
        P.dma("pool", [(wqb[l], W["Wq_r"][l], {})], writes=[wqb_bufs[l]],
              name=f"a.cq{l}")
        h = l
        nq1 = FBLK // 4
        P.dma("pool", [(w1b[h * nq1 + j].rearrange("p (a m) -> p a m", m=128),
                        W["W1_r"][:, :, (h * nq1 + j) * 128:(h * nq1 + j + 1) * 128],
                        {}) for j in range(nq1)],
              writes=[w1b_bufs[h]], name=f"a.c1w{h}")
        P.dma("pool", [(w2b[h * 2 + j].rearrange("p (f m) -> p f m", m=128)
                        [:, fh * 16:(fh + 1) * 16],
                        W["W2_r"][:, fh * 16:(fh + 1) * 16,
                                  (h * 2 + j) * 128:(h * 2 + j + 1) * 128], {})
                       for j in range(2) for fh in range(2)],
              writes=[w2b_bufs[h]], name=f"a.c2w{h}")
    bvb = P.nc.dram_tensor("a_bvb", [1, L0L * D], BF16, kind="Internal").ap()
    bvb_buf = Buf("a_bvb", nofence=True)
    P.dma("pool", [(bvb, W["bv_rows"], {})], writes=[bvb_buf], name="a.cbv")
    if prestage_cb is not None:
        prestage_cb()

    for b in range(nb):
        esc = P.scope()
        xpad = P.sbuf(f"a_xpad{b}", [128, NBLK * (nt + 4)], BF16, es=esc)
        xpad_bufs = [Buf(f"a_xpad.{i}") for i in range(NBLK)]
        cg = P.sbuf(f"a_cg{b}", [128, 4 * (nt + 4)], BF16, es=esc)
        cg_bufs = [Buf(f"a_cg.{i}") for i in range(4)]
        diag1 = P.sbuf(f"a_diag1{b}", [128, 20 * 128], BF16, es=esc)
        diag1_buf = Buf("a_diag1")
        diag2 = P.sbuf(f"a_diag2{b}", [128, 20 * 128], BF16, es=esc)
        diag2_buf = Buf("a_diag2")

        def zero_borders(e):
            for blk in range(NBLK):
                o = blk * (nt + 4)
                e.memset(xpad[:, o:o + 2], 0.0)
                e.memset(xpad[:, o + nt + 2:o + nt + 4], 0.0)
            for m in range(4):
                o = m * (nt + 4)
                e.memset(cg[:, o:o + 2], 0.0)
                last = e.memset(cg[:, o + nt + 2:o + nt + 4], 0.0)
            return last
        P.op("pool", zero_borders, writes=xpad_bufs + cg_bufs, name=f"a.zb{b}")

        for blk in range(NBLK):
            P.dma("sp" if blk % 2 == 0 else "act",
                  [(hx.ap(blk), x_in_ap(b, blk), {})],
                  reads=XIN(b), writes=[hx.bufs[blk]], name=f"a.x{b}.{blk}")
            P.op("pool", lambda e, blk=blk: e.tensor_copy(
                xpad[:, blk * (nt + 4) + 2: blk * (nt + 4) + 2 + nt], hx.ap(blk)),
                reads=[hx.bufs[blk]], writes=[xpad_bufs[blk]], name=f"a.xpc{b}.{blk}")

        for blk in range(NBLK):
            def build_diag(e, blk=blk, diag=diag1, key="w1c"):
                for i in range(20):
                    last = e.tensor_scalar_mul(
                        diag[:, i * 128:(i + 1) * 128], G["eqm"][:],
                        PP.col(key, blk * 20 + i))
                return last
            P.op("dve", build_diag, reads=[G["const_buf"], PP.buf],
                 writes=[diag1_buf], name=f"a.d1.{b}.{blk}")
            def build_diag2(e, blk=blk):
                for i in range(20):
                    last = e.tensor_scalar_mul(
                        diag2[:, i * 128:(i + 1) * 128], G["eqm"][:],
                        PP.col("w2c", blk * 20 + i))
                return last
            P.op("dve", build_diag2, reads=[G["const_buf"], PP.buf],
                 writes=[diag2_buf], name=f"a.d2.{b}.{blk}")

            for m in range(4):
                for ck in range(nck):
                    c0 = ck * CK
                    ps = ps1 if (ck % 2 == 0) else ps2
                    mmg(P, [(ps[:, 0:CK], diag1[:, (m * 5 + j) * 128:(m * 5 + j + 1) * 128],
                             xpad[:, blk * (nt + 4) + j + c0: blk * (nt + 4) + j + c0 + CK],
                             st, sp) for j, (st, sp) in zip(range(5), accg(5))],
                        reads=[diag1_buf, xpad_bufs[blk]], writes=[ps.buf],
                        name=f"a.c1.{b}.{blk}.{m}.{ck}")
                    P.op("act", lambda e, m=m, c0=c0, ps=ps, blk=blk: e.activation(
                        cg[:, m * (nt + 4) + 2 + c0: m * (nt + 4) + 2 + c0 + CK],
                        ps[:, 0:CK], AF.Gelu, bias=PP.col("b1c", blk * 4 + m), scale=1.0),
                        reads=[ps.buf, PP.buf], writes=[cg_bufs[m]],
                        name=f"a.c1e.{b}.{blk}.{m}.{ck}")
            for ck in range(nck):
                c0 = ck * CK
                psc = ps3 if ck % 2 == 0 else ps4
                items = []
                flags = accg(20)
                for m in range(4):
                    for j in range(5):
                        i = m * 5 + j
                        st, sp = flags[i]
                        items.append((psc[:, 0:CK],
                                      diag2[:, i * 128:(i + 1) * 128],
                                      cg[:, m * (nt + 4) + j + c0: m * (nt + 4) + j + c0 + CK],
                                      st, sp))
                mmg(P, items, reads=[diag2_buf] + cg_bufs, writes=[psc.buf],
                    name=f"a.c2.{b}.{blk}.{ck}")
                P.op("act", lambda e, c0=c0, blk=blk, psc=psc: e.copy(
                    ctmp[:, 0:CK], psc[:, 0:CK]),
                    reads=[psc.buf], writes=[ctmp.buf], name=f"a.c2e.{b}.{blk}.{ck}")
                P.op("dve", lambda e, c0=c0, blk=blk: e.scalar_tensor_tensor(
                    hx.ap(blk, c0, c0 + CK), ctmp[:, 0:CK], PP.col("b2c", blk),
                    hx.ap(blk, c0, c0 + CK), ALU.add, ALU.add),
                    reads=[ctmp.buf, PP.buf, hx.bufs[blk]], writes=[hx.bufs[blk]],
                    name=f"a.c2a.{b}.{blk}.{ck}")
        P.close_scope(esc)

        esm = P.scope()
        hbf = DT(P, f"a_hbf{b}", NBLK, nt, BF16, es=esm)
        esi = P.scope()
        wA = P.sbuf(f"a_wA{b}", [128, NBLK * D], BF16, es=esi)
        wA_buf = Buf("a_wA")
        wB = P.sbuf(f"a_wB{b}", [128, NBLK * D], BF16, es=esi)
        wB_buf = Buf("a_wB")
        cosck = P.sbuf(f"a_cos{b}", [128, NBLK * CK], BF16, es=esi)
        cos_buf = Buf("a_cos")
        sinck = P.sbuf(f"a_sin{b}", [128, NBLK * CK], BF16, es=esi)
        sin_buf = Buf("a_sin")
        wQ = P.sbuf(f"a_wQ{b}", [128, NBLK * D], BF16, es=esi)
        wQ_buf = Buf("a_wQ")
        ksb = DT(P, f"a_k{b}", NBLK, CK, BF16, es=esi)
        qsb = ksb
        tmp1 = Tile(P, f"a_tmp1{b}", [128, CK], BF16, es=esi)
        krt = Tile(P, f"a_krt{b}", [128, D], BF16, es=esi)
        vtk = P.sbuf(f"a_vtk{b}", [128, (CK // 128) * D], BF16, es=esi)
        vtk_bufs = [Buf(f"a_vtk.{i}") for i in range(CK // 128)]
        kvsb = Tile(P, f"a_kv{b}", [128, 512], BF16, es=esi)
        ps_kv = Tile(P, f"a_pskv{b}", [128, 512], F32, es=esi, psum=True)
        kmacc = Tile(P, f"a_kmacc{b}", [128, NBLK * (nck + 2)], F32, es=esi)
        kmbf = Tile(P, f"a_kmbf{b}", [128, NBLK], BF16, es=esi)
        zbf = Tile(P, f"a_zbf{b}", [16, CK], BF16, es=esi)
        kmsel = Tile(P, f"a_kmsel{b}", [128, 128], BF16, es=esi)
        zsel = Tile(P, f"a_zsel{b}", [16, 1024], BF16, es=esi)
        zbc = P.sbuf(f"a_zbc{b}", [128, CK], BF16, es=esi)
        zbc_bufs = [Buf("a_zbc.0")]

        P.dma("sp", [(zsel[:, :], W["Z_sel"], {})], writes=[zsel.buf], name=f"a.zsel{b}")
        def rope_inplace(P, dst, blk, psr, tag):
            mmg(P, [(psr[:, 0:CK], S_sw[:], dst.ap(blk), True, True)],
                reads=[S_buf, dst.bufs[blk]], writes=[psr.buf], name=f"{tag}.xr")
            P.op("dve", lambda e, blk=blk, psr=psr: e.tensor_mul(
                tmp1[:, 0:CK], psr[:, 0:CK], sinck[:, blk * CK:(blk + 1) * CK]),
                reads=[psr.buf, sin_buf], writes=[tmp1.buf], name=f"{tag}.s")
            P.op("dve", lambda e, blk=blk: e.tensor_mul(
                dst.ap(blk), dst.ap(blk), cosck[:, blk * CK:(blk + 1) * CK]),
                reads=[dst.bufs[blk], cos_buf], writes=[dst.bufs[blk]], name=f"{tag}.c")
            P.op("dve", lambda e, blk=blk: e.tensor_add(
                dst.ap(blk), dst.ap(blk), tmp1[:, 0:CK]),
                reads=[dst.bufs[blk], tmp1.buf], writes=[dst.bufs[blk]], name=f"{tag}.a")

        def emit_layer_loads(l):
            P.dma("sp", [(wA[:].rearrange("p (a m) -> p a m", m=D), wkb[l], {})],
                  reads=[wkb_bufs[l]], writes=[wA_buf], name=f"a.wk{b}{l}")
            P.dma("act", [(wB[:].rearrange("p (a m) -> p a m", m=D), wvb[l], {})],
                  reads=[wvb_bufs[l]], writes=[wB_buf], name=f"a.wv{b}{l}")
            P.dma("sp", [(wQ[:].rearrange("p (a m) -> p a m", m=D), wqb[l], {})],
                  reads=[wqb_bufs[l]], writes=[wQ_buf], name=f"a.wq{b}{l}")
            P.dma("sp", [(bvr[0:1, :], bvb[0:1, l * D:(l + 1) * D], {})],
                  reads=[bvb_buf], writes=[bvr_buf], name=f"a.bv{b}{l}")

        def k_ck(l, ck):
            c0 = ck * CK
            P.dma("sp", [(cosck[:], W["cos_t"][ck], {})],
                  writes=[cos_buf], name=f"a.cos.{b}.{l}.k{ck}")
            P.dma("act", [(sinck[:], W["sin_t"][ck], {})],
                  writes=[sin_buf], name=f"a.sin.{b}.{l}.k{ck}")
            for blk in range(NBLK):
                psk = (ps1, ps4, ps2)[blk % 3]
                mmg(P, [(psk[:, 0:CK], wA[:, kb * D + blk * 128:kb * D + blk * 128 + 128],
                         hbf.ap(kb, c0, c0 + CK), st, sp)
                        for kb, (st, sp) in zip(range(NBLK), accg(NBLK))],
                    reads=[wA_buf] + [hbf.bufs[kb] for kb in range(NBLK)],
                    writes=[psk.buf], name=f"a.kmm.{b}.{l}.{ck}.{blk}")
                P.op("act", lambda e, blk=blk, l=l, ck=ck, psk=psk: e.activation(
                    ksb.ap(blk), psk[:, 0:CK], AF.Gelu,
                    bias=PP.col("bk", l * 8 + blk), scale=1.0,
                    accum_out=kmacc[:, blk * (nck + 2) + ck: blk * (nck + 2) + ck + 1]),
                    reads=[psk.buf, PP.buf], writes=[ksb.bufs[blk], kmacc.buf],
                    name=f"a.kev.{b}.{l}.{ck}.{blk}")
                if blk == NBLK - 1:
                    P.op("dve", lambda e: e.tensor_scalar_add(
                        ksb.t[:], ksb.t[:], 0.21),
                        reads=ksb.bufs, writes=ksb.bufs,
                        name=f"a.k21.{b}.{l}.{ck}")
            vpairs = [(s, half) for s in range(CK // 128) for half in range(2)]
            for blk in range(NBLK):
                if blk < len(vpairs):
                    s, half = vpairs[blk]
                    t0 = s * 128
                    items = [(ps3[:, 0:512],
                              hbf.ap(kb, c0 + t0, c0 + t0 + 128),
                              wB[:, kb * D + half * 512:kb * D + half * 512 + 512],
                              st, False) for kb, (st, _) in zip(range(NBLK), accg(NBLK))]
                    items.append((ps3[:, 0:512], G["ones_row"][0:1, 0:128],
                                  bvr[0:1, half * 512:half * 512 + 512],
                                  False, True))
                    mmg(P, items,
                        reads=[wB_buf, bvr_buf, G["const_buf"]] +
                              [hbf.bufs[kb] for kb in range(NBLK)],
                        writes=[ps3.buf], name=f"a.vmm.{b}.{l}.{ck}.{s}.{half}")
                    P.op("act", lambda e, s=s, half=half: e.copy(
                        vtk[:, (s * 2 + half) * 512:(s * 2 + half) * 512 + 512],
                        ps3[:, 0:512]),
                        reads=[ps3.buf], writes=[vtk_bufs[s]],
                        name=f"a.vev.{b}.{l}.{ck}.{s}.{half}")
                psr = ps1 if blk % 2 == 0 else ps4
                rope_inplace(P, ksb, blk, psr, f"a.kr.{b}.{l}.{ck}.{blk}")
            for s in range(CK // 128):
                t0 = s * 128
                glob_s = ck * (CK // 128) + s
                for half in range(2):
                    pst = ps1 if half == 0 else ps4
                    mmg(P, [(pst[:, q * 128:(q + 1) * 128],
                             ksb.ap(half * 4 + q, t0, t0 + 128), G["eqm"][:],
                             q == 0, q == 3) for q in range(4)],
                        reads=[ksb.bufs[half * 4 + q] for q in range(4)] + [G["const_buf"]],
                        writes=[pst.buf], name=f"a.ktp.{b}.{l}.{glob_s}.{half}")
                    P.op("act", lambda e, half=half, pst=pst: e.copy(
                        krt[:, half * 512:(half + 1) * 512], pst[:, 0:512]),
                        reads=[pst.buf], writes=[krt.buf],
                        name=f"a.ktpe.{b}.{l}.{glob_s}.{half}")
                items = []
                for h in range(NH):
                    items.append((kv_psum_ap(ps_kv, h),
                                  krt[:, h * HD:(h + 1) * HD],
                                  vtk[:, s * D + h * HD:s * D + (h + 1) * HD],
                                  glob_s == 0 and h < 2,
                                  glob_s == nsub - 1 and h >= NH - 2))
                mmg(P, items, reads=[krt.buf, vtk_bufs[s]], writes=[ps_kv.buf],
                    name=f"a.kv.{b}.{l}.{glob_s}", skip_check=True)

        def k_fin(l):
            P.op("act", lambda e: e.activation(kvsb[:, :], ps_kv[:, :],
                                               AF.Identity, bias=0.0, scale=1.0 / nt),
                 reads=[ps_kv.buf], writes=[kvsb.buf], name=f"a.kvev.{b}.{l}")

            def km_red(e):
                for blk in range(NBLK):
                    o = blk * (nck + 2)
                    last = e.reduce_sum(kmacc[:, o + nck:o + nck + 1],
                                        kmacc[:, o:o + nck], axis=AX.X)
                return last
            P.op("dve", km_red, reads=[kmacc.buf], writes=[kmacc.buf],
                 name=f"a.kmr.{b}.{l}")

            def km_fin(e):
                for blk in range(NBLK):
                    o = blk * (nck + 2)
                    last = e.tensor_scalar(kmacc[:, o + nck + 1:o + nck + 2],
                                           kmacc[:, o + nck:o + nck + 1],
                                           1.0 / nt, 0.21, ALU.mult, ALU.add)
                return last
            P.op("dve", km_fin, reads=[kmacc.buf], writes=[kmacc.buf],
                 name=f"a.kmf.{b}.{l}")

            def km_cast(e):
                for blk in range(NBLK):
                    o = blk * (nck + 2)
                    last = e.tensor_copy(kmbf[:, blk:blk + 1],
                                         kmacc[:, o + nck + 1:o + nck + 2])
                return last
            P.op("dve", km_cast, reads=[kmacc.buf], writes=[kmbf.buf],
                 name=f"a.kmc.{b}.{l}")
            P.op("pool", lambda e: e.memset(kmsel[:, :], 0.0),
                 reads=[], writes=[kmsel.buf], name=f"a.kms0.{b}.{l}")

            def kmsel_fill(e):
                for blk in range(NBLK):
                    o = blk * 16
                    e.tensor_copy(kmsel[0:64, o + 2 * blk:o + 2 * blk + 1],
                                  kmbf[0:64, blk:blk + 1])
                    last = e.tensor_copy(kmsel[64:128, o + 2 * blk + 1:o + 2 * blk + 2],
                                         kmbf[64:128, blk:blk + 1])
                return last
            P.op("dve", kmsel_fill, reads=[kmbf.buf, kmsel.buf], writes=[kmsel.buf],
                 name=f"a.kms.{b}.{l}")


        def q_ck(l, ck):
            c0 = ck * CK
            P.dma("sp", [(cosck[:], W["cos_t"][ck], {})],
                  writes=[cos_buf], name=f"a.cos.{b}.{l}.q{ck}")
            P.dma("act", [(sinck[:], W["sin_t"][ck], {})],
                  writes=[sin_buf], name=f"a.sin.{b}.{l}.q{ck}")
            for blk in range(NBLK):
                psk = (ps1, ps4, ps2)[blk % 3]
                mmg(P, [(psk[:, 0:CK], wQ[:, kb * D + blk * 128:kb * D + blk * 128 + 128],
                         hbf.ap(kb, c0, c0 + CK), st, sp)
                        for kb, (st, sp) in zip(range(NBLK), accg(NBLK))],
                    reads=[wQ_buf] + [hbf.bufs[kb] for kb in range(NBLK)],
                    writes=[psk.buf], name=f"a.qmm.{b}.{l}.{ck}.{blk}")
                P.op("act", lambda e, blk=blk, l=l, psk=psk: e.activation(
                    qsb.ap(blk), psk[:, 0:CK], AF.Gelu,
                    bias=PP.col("bq", l * 8 + blk), scale=1.0),
                    reads=[psk.buf, PP.buf], writes=[qsb.bufs[blk]],
                    name=f"a.qev.{b}.{l}.{ck}.{blk}")
            P.op("dve", lambda e: e.tensor_scalar_add(qsb.t[:], qsb.t[:], 0.21),
                 reads=qsb.bufs, writes=qsb.bufs, name=f"a.q21.{b}.{l}.{ck}")
            items = [(ps3[0:16, 0:CK], kmsel[:, blk * 16:(blk + 1) * 16],
                      qsb.ap(blk), st, sp)
                     for blk, (st, sp) in zip(range(NBLK), accg(NBLK))]
            mmg(P, items,
                reads=[kmsel.buf] + [qsb.bufs[blk] for blk in range(NBLK)],
                writes=[ps3.buf], name=f"a.zdot.{b}.{l}.{ck}")
            P.op("dve", lambda e, c0=c0: e.tensor_scalar_add(
                ps3[0:16, 0:CK], ps3[0:16, 0:CK], 1e-6),
                reads=[ps3.buf], writes=[ps3.buf], name=f"a.zn.{b}.{l}.{ck}")
            def zrec(e, c0=c0):
                with P.nc.allow_low_precision(reason="z cast to bf16 as before"):
                    return e.reciprocal(zbf[0:16, 0:CK], ps3[0:16, 0:CK])
            P.op("dve", zrec, reads=[ps3.buf], writes=[zbf.buf],
                 name=f"a.zr.{b}.{l}.{ck}")
            for blk in range(NBLK):
                psr = ps1 if blk % 2 == 0 else ps4
                rope_inplace(P, qsb, blk, psr, f"a.qr.{b}.{l}.{ck}.{blk}")
            for blk in range(NBLK):
                psz = ps2 if blk % 2 == 0 else ps3
                mmg(P, [(psz[:, 0:CK], zsel[:, blk * 128:(blk + 1) * 128],
                         zbf[0:16, 0:CK], True, True)],
                    reads=[zsel.buf, zbf.buf], writes=[psz.buf],
                    name=f"a.zbc.{b}.{l}.{ck}.{blk}")
                P.op("act", lambda e, psz=psz: e.copy(
                    zbc[:, 0:CK], psz[:, 0:CK]),
                    reads=[psz.buf], writes=[zbc_bufs[0]],
                    name=f"a.zbce.{b}.{l}.{ck}.{blk}")
                P.op("dve", lambda e, blk=blk: e.tensor_mul(
                    qsb.ap(blk), qsb.ap(blk), zbc[:, 0:CK]),
                    reads=[qsb.bufs[blk], zbc_bufs[0]], writes=[qsb.bufs[blk]],
                    name=f"a.qz.{b}.{l}.{ck}.{blk}")
            for blk in range(NBLK):
                pso = ps2 if blk % 2 == 0 else ps3
                items = []
                for hh in range(2):
                    h = blk * 2 + hh
                    items.append((pso[hh * 64:hh * 64 + 64, 0:CK],
                                  kvsb[64 * (h % 2):64 * (h % 2) + 64,
                                       64 * (h // 2):64 * (h // 2) + 64],
                                  qsb.t[64 * (h % 2):64 * (h % 2) + 64,
                                        (h // 2) * CK + 0:(h // 2) * CK + CK],
                                  True, True))
                mmg(P, items, reads=[kvsb.buf, qsb.bufs[blk]],
                    writes=[pso.buf], name=f"a.o.{b}.{l}.{ck}.{blk}")
                P.op("dve", lambda e, blk=blk, c0=c0, pso=pso: e.tensor_add(
                    hx.ap(blk, c0, c0 + CK), hx.ap(blk, c0, c0 + CK), pso[:, 0:CK]),
                    reads=[hx.bufs[blk], pso.buf], writes=[hx.bufs[blk]],
                    name=f"a.oa.{b}.{l}.{ck}.{blk}")

        def lnA_cols(i):
            if i is None:
                return (lambda blk: (PP.col("ln0g", blk), [PP.buf]),
                        lambda blk: (PP.col("ln0b", blk), [PP.buf]),
                        f"a.ln0.{b}")
            return (lambda blk, i=i: (PP.col("lnAg", i * 8 + blk), [PP.buf]),
                    lambda blk, i=i: (PP.col("lnAb", i * 8 + blk), [PP.buf]),
                    f"a.lnA.{b}.{i}")

        def dump_out():
            P.dma("sp", [(x_out_ap(b, blk), hx.ap(blk), {}) for blk in range(NBLK)],
                  reads=[hx.bufs[blk] for blk in range(NBLK)],
                  writes=XOUT(b), name=f"a.out{b}", is_output=FINAL)

        broke = False
        emit_layer_loads(0)
        gc, bc, tg = lnA_cols(None)
        if upto == "ln0":
            emit_ln(P, C, hx, hx, hbf, gc, bc, EPS_LN0, tg)
            dump_out()
            broke = True
        else:
            emit_ln(P, C, hx, hx, hbf, gc, bc, EPS_LN0, tg,
                    chunk_cb=lambda ck: k_ck(0, ck))
            k_fin(0)
        for l in range(L0L):
            if broke:
                break
            gc, bc, tg = lnA_cols(l)
            if upto is not None:
                for ck in range(nck):
                    q_ck(l, ck)
                if upto == f"lnA{l}":
                    emit_ln(P, C, hx, hx, hbf, gc, bc, EPS_LN0, tg)
                    dump_out()
                    broke = True
                    break
                if l < L0L - 1:
                    emit_layer_loads(l + 1)
                    emit_ln(P, C, hx, hx, hbf, gc, bc, EPS_LN0, tg,
                            chunk_cb=lambda ck, l=l: k_ck(l + 1, ck))
                    k_fin(l + 1)
            elif l < L0L - 1:
                emit_layer_loads(l + 1)
                emit_ln(P, C, hx, hx, hbf, gc, bc, EPS_LN0, tg,
                        chunk_cb=lambda ck, l=l: k_ck(l + 1, ck),
                        interleave=lambda ck, l=l: q_ck(l, ck))
                k_fin(l + 1)
            else:
                for ck in range(nck):
                    q_ck(l, ck)
        if broke:
            P.close_scope(esi)
            P.close_scope(esm)
            continue
        P.close_scope(esi)
        esf = P.scope()
        w1t = P.sbuf(f"a_w1t{b}", [128, 4 * 1024], BF16, es=esf)
        w1t_bufs = [Buf(f"a_w1t.{i}") for i in range(4)]
        w2t = P.sbuf(f"a_w2t{b}", [128, 2 * FBLK * 128], BF16, es=esf)
        w2t_bufs = [Buf("a_w2t.0"), Buf("a_w2t.1")]
        h1bf = P.sbuf(f"a_h1{b}", [128, FBLK * CK], BF16, es=esf)
        h1_bufs = [Buf(f"a_h1.{i}") for i in range(FBLK)]
        xres = P.sbuf(f"a_xres{b}", [128, 2 * CK], F32, es=esf)
        xres_bufs = [Buf("a_xres.0"), Buf("a_xres.1")]

        def ffn_ck(ck):
            c0 = ck * CK
            for mb in range(FBLK):
                slot = mb % 4
                P.dma(("sp", "act", "pool")[mb % 3],
                      [(w1t[:, slot * 1024:(slot + 1) * 1024], w1b[mb], {})],
                      reads=[w1b_bufs[mb // (FBLK // 4)]],
                      writes=[w1t_bufs[slot]], name=f"a.w1.{b}.{ck}.{mb}")
                ps = (ps1, ps2, ps3)[mb % 3]
                mmg(P, [(ps[:, 0:CK],
                         w1t[:, slot * 1024 + kb * 128: slot * 1024 + kb * 128 + 128],
                         hbf.ap(kb, c0, c0 + CK), st, sp)
                        for kb, (st, sp) in zip(range(NBLK), accg(NBLK))],
                    reads=[w1t_bufs[slot]] + [hbf.bufs[kb] for kb in range(NBLK)],
                    writes=[ps.buf], name=f"a.f1.{b}.{ck}.{mb}")
                P.op("act", lambda e, mb=mb, ps=ps: e.activation(
                    h1bf[:, mb * CK:(mb + 1) * CK], ps[:, 0:CK], AF.Gelu,
                    bias=PP.col("b1f", mb), scale=1.0),
                    reads=[ps.buf, PP.buf], writes=[h1_bufs[mb]],
                    name=f"a.f1e.{b}.{ck}.{mb}")
            for blk in range(NBLK):
                slot = blk % 2
                psf = ps3 if slot == 0 else ps4
                P.dma("sp" if blk % 2 == 0 else "act",
                      [(w2t[:, slot * FBLK * 128:(slot + 1) * FBLK * 128], w2b[blk], {})],
                      reads=[w2b_bufs[blk // 2]],
                      writes=[w2t_bufs[slot]], name=f"a.w2.{b}.{ck}.{blk}")
                mmg(P, [(psf[:, 0:CK],
                         w2t[:, slot * FBLK * 128 + hb * 128: slot * FBLK * 128 + hb * 128 + 128],
                         h1bf[:, hb * CK:(hb + 1) * CK], st, sp)
                        for hb, (st, sp) in zip(range(FBLK), accg(FBLK))],
                    reads=[w2t_bufs[slot]] + h1_bufs, writes=[psf.buf],
                    name=f"a.f2.{b}.{ck}.{blk}")
                P.op("act", lambda e, blk=blk, psf=psf: e.copy(ctmp[:, 0:CK], psf[:, 0:CK]),
                     reads=[psf.buf], writes=[ctmp.buf], name=f"a.f2e.{b}.{ck}.{blk}")
                P.op("dve", lambda e, blk=blk, c0=c0: e.scalar_tensor_tensor(
                    hx.ap(blk, c0, c0 + CK), ctmp[:, 0:CK], PP.col("b2f", blk),
                    hx.ap(blk, c0, c0 + CK), ALU.add, ALU.add),
                    reads=[ctmp.buf, PP.buf, hx.bufs[blk]], writes=[hx.bufs[blk]],
                    name=f"a.f2a.{b}.{ck}.{blk}")

        gc, bc, tg = lnA_cols(3)
        emit_ln(P, C, hx, hx, hbf, gc, bc, EPS_LN0, tg, chunk_cb=ffn_ck)

        def xadd_ck(ck):
            c0 = ck * CK
            for blk in range(NBLK):
                slot = blk % 2
                P.dma("sp" if blk % 2 == 0 else "act",
                      [(xres[:, slot * CK:slot * CK + CK],
                        x_in_ap(b, blk)[:, c0:c0 + CK], {})],
                      reads=XIN(b), writes=[xres_bufs[slot]], name=f"a.xr.{b}.{blk}.{ck}")
                P.op("dve", lambda e, blk=blk, c0=c0, slot=slot: e.tensor_add(
                    hx.ap(blk, c0, c0 + CK), hx.ap(blk, c0, c0 + CK),
                    xres[:, slot * CK:slot * CK + CK]),
                    reads=[hx.bufs[blk], xres_bufs[slot]], writes=[hx.bufs[blk]],
                    name=f"a.xra.{b}.{blk}.{ck}")

        emit_ln(P, C, hx, hx, None,
                lambda blk: (PP.col("ln2g", blk), [PP.buf]),
                lambda blk: (PP.col("ln2b", blk), [PP.buf]),
                EPS_LN0, f"a.ln2.{b}")
        if upto == "ln2":
            P.dma("sp", [(x_out_ap(b, blk), hx.ap(blk), {}) for blk in range(NBLK)],
                  reads=[hx.bufs[blk] for blk in range(NBLK)],
                  writes=XOUT(b), name=f"a.out{b}", is_output=FINAL)
            P.close_scope(esf)
            P.close_scope(esm)
            continue

        out_buf_chain = XOUT(b) if XOUT(b) else [Buf(f"a_outc{b}")]

        def out_ck(ck):
            c0 = ck * CK
            P.dma("sp", [(x_out_ap(b, blk)[:, c0:c0 + CK],
                          hx.ap(blk, c0, c0 + CK), {}) for blk in range(NBLK)],
                  reads=[hx.bufs[blk] for blk in range(NBLK)],
                  writes=out_buf_chain, name=f"a.out{b}.{ck}", is_output=FINAL)

        emit_ln(P, C, hx, hx, None, None, None, EPS_MOD, f"a.pln.{b}",
                chunk_cb=out_ck, interleave=xadd_ck)
        P.close_scope(esf)
        P.close_scope(esm)

    P.close_scope(es)


def kv_psum_ap(ps, h):
    r = 64 * (h % 2)
    c = 64 * (h // 2)
    return ps[r:r + 64, c:c + 64]



def _cols(v):
    v = np.asarray(v, np.float32).reshape(-1, 128)
    return np.ascontiguousarray(v.T)


def declare_l0a_weights(nc, nt):
    def t(n, s, dt=F32):
        return nc.dram_tensor(n, s, dt, kind="ExternalInput").ap()
    W = {}
    W["conv1_w_cols"] = t("l0a_w1c", [128, 160])
    W["conv2_w_cols"] = t("l0a_w2c", [128, 160])
    W["conv1_b_cols"] = t("l0a_b1c", [128, 32])
    W["conv2_b_cols"] = t("l0a_b2c", [128, 8])
    W["ln0_g_cols"] = t("l0a_ln0g", [128, 8])
    W["ln0_b_cols"] = t("l0a_ln0b", [128, 8])
    W["Wq_r"] = t("l0a_wq", [4, 128, 8, D])
    W["Wk_r"] = t("l0a_wk", [4, 128, 8, D])
    W["Wv_r"] = t("l0a_wv", [4, 128, 8, D])
    W["bq_cols"] = t("l0a_bq", [128, 32])
    W["bk_cols"] = t("l0a_bk", [128, 32])
    W["bv_rows"] = t("l0a_bv", [1, 4 * D])
    W["lnA_g_cols"] = t("l0a_lnAg", [128, 32])
    W["lnA_b_cols"] = t("l0a_lnAb", [128, 32])
    W["W1_r"] = t("l0a_w1", [128, 8, FF])
    W["ffn_b1_cols"] = t("l0a_fb1", [128, 32])
    W["W2_r"] = t("l0a_w2", [128, FBLK, D])
    W["ffn_b2_cols"] = t("l0a_fb2", [128, 8])
    W["ln2_g_cols"] = t("l0a_ln2g", [128, 8])
    W["ln2_b_cols"] = t("l0a_ln2b", [128, 8])
    ckl = min(512, nt)
    W["cos_t"] = t("l0a_cos", [nt // ckl, 128, 8 * ckl], BF16)
    W["sin_t"] = t("l0a_sin", [nt // ckl, 128, 8 * ckl], BF16)
    W["S_swap"] = t("l0a_S", [128, 128], BF16)
    W["Z_sel"] = t("l0a_zsel", [16, 1024], BF16)
    return W


def host_l0a(inp, nt):
    H = {}
    w1v = np.asarray(inp["L0_attn_conv1_w"], np.float32)[:, 0, :]
    H["l0a_w1c"] = np.ascontiguousarray(
        w1v.reshape(8, 128, 4, 5).transpose(1, 0, 2, 3).reshape(128, 160))
    w2v = np.asarray(inp["L0_attn_conv2_w"], np.float32)
    H["l0a_w2c"] = np.ascontiguousarray(
        w2v.reshape(8, 128, 4, 5).transpose(1, 0, 2, 3).reshape(128, 160))
    b1 = np.asarray(inp["L0_attn_conv1_b"], np.float32)
    H["l0a_b1c"] = np.ascontiguousarray(
        b1.reshape(8, 128, 4).transpose(1, 0, 2).reshape(128, 32))
    H["l0a_b2c"] = _cols(inp["L0_attn_conv2_b"])
    H["l0a_ln0g"] = _cols(inp["L0_attn_ln0_g"])
    H["l0a_ln0b"] = _cols(inp["L0_attn_ln0_b"])
    for nm, key in (("wq", "L0_attn_Wq"), ("wk", "L0_attn_Wk"), ("wv", "L0_attn_Wv")):
        w = np.asarray(inp[key], np.float32)
        H["l0a_" + nm] = np.ascontiguousarray(
            w.reshape(4, 8, 128, D).transpose(0, 2, 1, 3))
    for nm, key in (("bq", "L0_attn_bq"), ("bk", "L0_attn_bk")):
        bv = np.asarray(inp[key], np.float32)
        H["l0a_" + nm] = np.ascontiguousarray(
            bv.reshape(4, 8, 128).transpose(2, 0, 1).reshape(128, 32))
    H["l0a_bv"] = np.asarray(inp["L0_attn_bv"], np.float32).reshape(1, 4 * D).copy()
    for nm, key in (("lnAg", "L0_attn_lnA_g"), ("lnAb", "L0_attn_lnA_b")):
        v = np.asarray(inp[key], np.float32)
        H["l0a_" + nm] = np.ascontiguousarray(
            v.reshape(4, 8, 128).transpose(2, 0, 1).reshape(128, 32))
    w1 = np.asarray(inp["L0_attn_W1"], np.float32)
    H["l0a_w1"] = np.ascontiguousarray(w1.reshape(8, 128, FF).transpose(1, 0, 2))
    H["l0a_fb1"] = np.ascontiguousarray(
        np.asarray(inp["L0_attn_b1"], np.float32).reshape(32, 128).T)
    w2 = np.asarray(inp["L0_attn_W2"], np.float32)
    H["l0a_w2"] = np.ascontiguousarray(w2.reshape(FBLK, 128, D).transpose(1, 0, 2))
    H["l0a_fb2"] = _cols(inp["L0_attn_b2"])
    H["l0a_ln2g"] = _cols(inp["L0_attn_ln2_g"])
    H["l0a_ln2b"] = _cols(inp["L0_attn_ln2_b"])
    inv = (1e4 ** (-np.arange(0, D, 2, dtype=np.float64) / D))
    freqs = np.arange(nt, dtype=np.float64)[:, None] * inv[None, :]
    cosf = np.repeat(np.cos(freqs).T, 2, axis=0)
    sinf = np.repeat(np.sin(freqs).T, 2, axis=0)
    sinf[0::2] *= -1.0
    ckl = min(512, nt)
    H["l0a_cos"] = np.ascontiguousarray(
        cosf.reshape(8, 128, nt // ckl, ckl).transpose(2, 1, 0, 3)
        .reshape(nt // ckl, 128, 8 * ckl)).astype(BF)
    H["l0a_sin"] = np.ascontiguousarray(
        sinf.reshape(8, 128, nt // ckl, ckl).transpose(2, 1, 0, 3)
        .reshape(nt // ckl, 128, 8 * ckl)).astype(BF)
    S = np.zeros((128, 128), np.float32)
    for i in range(64):
        S[2 * i, 2 * i + 1] = 1.0
        S[2 * i + 1, 2 * i] = 1.0
    H["l0a_S"] = S.astype(BF)
    zs = np.zeros((16, 1024), np.float32)
    for blk in range(8):
        zs[2 * blk, blk * 128: blk * 128 + 64] = 1.0
        zs[2 * blk + 1, blk * 128 + 64: blk * 128 + 128] = 1.0
    H["l0a_zsel"] = zs.astype(BF)
    return H



def declare_l1a_weights(nc, nt):
    def t(n, s, dt=F32):
        return nc.dram_tensor(n, s, dt, kind="ExternalInput").ap()
    W = {}
    W["WqT"] = t("l1a_wq", [128, 8, D])
    W["WkT"] = t("l1a_wk", [128, 8, D])
    W["WvT"] = t("l1a_wv", [128, 8, D])
    W["WoT"] = t("l1a_wo", [128, 8, D])
    W["bq_cols"] = t("l1a_bq", [128, 8])
    W["bk_cols"] = t("l1a_bk", [128, 8])
    W["bv_row"] = t("l1a_bv", [1, D])
    W["bo_cols"] = t("l1a_bo", [128, 8])
    W["tril"] = t("l1a_tril", [128, 128], BF16)
    return W


def host_l1a(inp, nt):
    H = {}
    for nm, key in (("wq", "L1_attn_Wq"), ("wk", "L1_attn_Wk"),
                    ("wv", "L1_attn_Wv"), ("wo", "L1_attn_Wo")):
        w = np.ascontiguousarray(np.asarray(inp[key], np.float32).T)
        H["l1a_" + nm] = np.ascontiguousarray(w.reshape(8, 128, D).transpose(1, 0, 2))
    H["l1a_bq"] = _cols(inp["L1_attn_bq"])
    H["l1a_bk"] = _cols(inp["L1_attn_bk"])
    H["l1a_bv"] = np.asarray(inp["L1_attn_bv"], np.float32).reshape(1, D).copy()
    H["l1a_bo"] = _cols(inp["L1_attn_bo"])
    m = np.zeros((128, 128), np.float32)
    for p in range(128):
        m[p, : p + 1] = 1.0
    H["l1a_tril"] = m.astype(BF)
    return H


def emit_l1_attn(P, G, W, x_in_ap, x_out_ap, nt, scratch, xio=None):
    xio = xio or {}
    XIN = (lambda b: [xio["in"][b]]) if xio.get("in") else (lambda b: [])
    XOUT = (lambda b: [xio["out"][b]]) if xio.get("out") else (lambda b: [])
    FINAL = xio.get("final", True)
    """scratch: tks [nb, 8, 128, nt] bf16 (k d-major), tvs [nb, nt//128, 128, D] bf16."""
    nb = G["nb"]
    CK = min(512, nt)
    nck = nt // CK
    nq = nt // 128
    GQ = 2
    es = P.scope()

    hx = DT(P, "t_hx", NBLK, nt, F32, es=es)
    ctmp = Tile(P, "t_ctmp", [128, 256], F32, es=es)
    ps1 = Tile(P, "t_ps1", [128, 512], F32, es=es, psum=True)
    ps_pv = Tile(P, "t_pspv", [128, 2048], F32, es=es, psum=True)
    C = make_ln_common(P, nt, es, G, "t_")
    PP = ParamPack(P, "t_pp", [
        ("bq", W["bq_cols"], 8),
        ("bk", W["bk_cols"], 8),
        ("bo", W["bo_cols"], 8),
    ], es=es)
    bvr = P.sbuf("t_bvr", [1, D], BF16, es=es)
    bvr_buf = Buf("t_bvr")
    P.dma("pool", [(bvr[0:1, :], W["bv_row"], {})], writes=[bvr_buf], name="t.bvr")
    tril = P.sbuf("t_tril", [128, 128], BF16, es=es)
    tril_buf = Buf("t_tril")
    P.dma("sp", [(tril[:], W["tril"], {})], writes=[tril_buf], name="t.mask")

    wbuf = P.sbuf("t_w", [128, NBLK * D], BF16, es=es)
    w_buf = Buf("t_w")
    wo = P.sbuf("t_wo", [128, NBLK * D], BF16, es=es)
    wo_buf = Buf("t_wo")
    xck = P.sbuf("t_xck", [128, 2 * NBLK * CK], BF16, es=es)
    xck_bufs = [Buf("t_xck.0"), Buf("t_xck.1")]
    kch = P.sbuf("t_kch", [128, 2 * NBLK * CK], BF16, es=es)
    kch_bufs = [Buf("t_kch.0"), Buf("t_kch.1")]
    kstg = kch
    kstg_bufs = [kch_bufs[0]] * NBLK
    vch = P.sbuf("t_vch", [128, 2 * D], BF16, es=es)
    vch_bufs = [Buf("t_vch.0"), Buf("t_vch.1")]
    qg = DT(P, "t_qg", NBLK, GQ * 128, BF16, es=es)
    sab = P.sbuf("t_sab", [128, GQ * nt], F32, es=es)
    sab_bufs = [Buf(f"t_sab.{i}") for i in range(GQ)]
    ebf = Tile(P, "t_ebf", [128, nt], BF16, es=es)
    wn = P.sbuf("t_wn", [128, GQ * nt], BF16, es=es)
    wn_bufs = [Buf(f"t_wn.{i}") for i in range(GQ)]
    wT = P.sbuf("t_wT", [128, (nt // 128) * GQ * 128], BF16, es=es)
    wT_buf = Buf("t_wT")
    ao = DT(P, "t_ao", NBLK, GQ * 128, BF16, es=es)
    cols = Tile(P, "t_cols", [128, 8], F32, es=es)
    xres = P.sbuf("t_xres", [128, 2 * GQ * 128], F32, es=es)
    xres_bufs = [Buf("t_xres.0"), Buf("t_xres.1")]
    tks_bufs = [Buf(f"t_tks.{i}", nofence=True) for i in range(nb)]
    tvs_bufs = [Buf(f"t_tvs.{i}", nofence=True) for i in range(nb)]

    for b in range(nb):
        P.dma("pool", [(wbuf[:].rearrange("p (a m) -> p a m", m=D), W["WkT"], {})],
              writes=[w_buf], name=f"t.wk{b}")
        for ck in range(nck):
            c0 = ck * CK
            slot = ck % 2
            P.dma("pool", [(xck[:, slot * NBLK * CK:(slot + 1) * NBLK * CK].rearrange(
                                "p (a c) -> p a c", c=CK),
                            x_in_ap(b, None)[:, :, c0:c0 + CK].rearrange(
                                "a p c -> p a c"), {})],
                  reads=XIN(b), writes=[xck_bufs[slot]], name=f"t.xk{b}.{ck}")
            for blk in range(NBLK):
                mmg(P, [(ps1[:, 0:CK],
                         wbuf[:, kb * D + blk * 128:kb * D + blk * 128 + 128],
                         xck[:, slot * NBLK * CK + kb * CK: slot * NBLK * CK + kb * CK + CK],
                         st, sp) for kb, (st, sp) in zip(range(NBLK), accg(NBLK))],
                    reads=[w_buf, xck_bufs[slot]], writes=[ps1.buf],
                    name=f"t.kmm{b}.{ck}.{blk}")
                P.op("act", lambda e, blk=blk: e.activation(
                    kstg[:, blk * CK:(blk + 1) * CK], ps1[:, 0:CK], AF.Identity,
                    bias=PP.col("bk", blk), scale=1.0),
                    reads=[ps1.buf, PP.buf], writes=[kstg_bufs[blk]],
                    name=f"t.kev{b}.{ck}.{blk}")
            P.dma("sp", [(scratch["tks"][b, blk][:, c0:c0 + CK],
                          kstg[:, blk * CK:(blk + 1) * CK], {}) for blk in range(NBLK)],
                  reads=kstg_bufs, writes=[tks_bufs[b]], name=f"t.kst{b}.{ck}")
        P.dma("pool", [(wbuf[:].rearrange("p (a m) -> p a m", m=D), W["WvT"], {})],
              writes=[w_buf], name=f"t.wv{b}")
        for ck in range(nck):
            c0 = ck * CK
            slot = ck % 2
            P.dma("pool", [(xck[:, slot * NBLK * CK:(slot + 1) * NBLK * CK].rearrange(
                                "p (a c) -> p a c", c=CK),
                            x_in_ap(b, None)[:, :, c0:c0 + CK].rearrange(
                                "a p c -> p a c"), {})],
                  reads=XIN(b), writes=[xck_bufs[slot]], name=f"t.xv{b}.{ck}")
            for s in range(CK // 128):
                sub = ck * (CK // 128) + s
                t0 = s * 128
                vslot = sub % 2
                for half in range(2):
                    items = [(ps1[:, 0:512],
                              xck[:, slot * NBLK * CK + kb * CK + t0:
                                  slot * NBLK * CK + kb * CK + t0 + 128],
                              wbuf[:, kb * D + half * 512:kb * D + half * 512 + 512],
                              st, False) for kb, (st, _) in zip(range(NBLK), accg(NBLK))]
                    items.append((ps1[:, 0:512], G["ones_row"][0:1, 0:128],
                                  bvr[0:1, half * 512:half * 512 + 512], False, True))
                    mmg(P, items, reads=[w_buf, bvr_buf, G["const_buf"], xck_bufs[slot]],
                        writes=[ps1.buf], name=f"t.vmm{b}.{sub}.{half}")
                    P.op("act", lambda e, vslot=vslot, half=half: e.copy(
                        vch[:, vslot * D + half * 512: vslot * D + half * 512 + 512],
                        ps1[:, 0:512]),
                        reads=[ps1.buf], writes=[vch_bufs[vslot]],
                        name=f"t.vev{b}.{sub}.{half}")
                P.dma("sp", [(scratch["tvs"][b, sub], vch[:, vslot * D:(vslot + 1) * D], {})],
                      reads=[vch_bufs[vslot]], writes=[tvs_bufs[b]], name=f"t.vst{b}.{sub}")
        P.dma("pool", [(wbuf[:].rearrange("p (a m) -> p a m", m=D), W["WqT"], {})],
              writes=[w_buf], name=f"t.wq{b}")
        P.dma("pool", [(wo[:].rearrange("p (a m) -> p a m", m=D), W["WoT"], {})],
              writes=[wo_buf], name=f"t.wo{b}")

        for gq in range(nq // GQ):
            q0 = gq * GQ * 128
            slot = gq % 2
            P.dma("pool", [(xck[:, slot * NBLK * CK: slot * NBLK * CK + NBLK * GQ * 128]
                            .rearrange("p (a c) -> p a c", c=GQ * 128),
                            x_in_ap(b, None)[:, :, q0:q0 + GQ * 128].rearrange(
                                "a p c -> p a c"), {})],
                  reads=XIN(b), writes=[xck_bufs[slot]], name=f"t.xq{b}.{gq}")
            for blk in range(NBLK):
                mmg(P, [(ps1[:, 0:GQ * 128],
                         wbuf[:, kb * D + blk * 128:kb * D + blk * 128 + 128],
                         xck[:, slot * NBLK * CK + kb * GQ * 128:
                             slot * NBLK * CK + kb * GQ * 128 + GQ * 128],
                         st, sp) for kb, (st, sp) in zip(range(NBLK), accg(NBLK))],
                    reads=[w_buf, xck_bufs[slot]], writes=[ps1.buf],
                    name=f"t.qmm{b}.{gq}.{blk}")
                P.op("act", lambda e, blk=blk: e.activation(
                    qg.ap(blk), ps1[:, 0:GQ * 128], AF.Identity,
                    bias=PP.col("bq", blk), scale=1.0),
                    reads=[ps1.buf, PP.buf], writes=[qg.bufs[blk]],
                    name=f"t.qev{b}.{gq}.{blk}")
            nvalid = q0 + GQ * 128
            nkc_s = (nvalid + CK - 1) // CK
            for kc in range(nkc_s):
                k0 = kc * CK
                kslot = kc % 2
                P.dma("sp", [(kch[:, kslot * NBLK * CK + blk * CK:
                                  kslot * NBLK * CK + (blk + 1) * CK],
                              scratch["tks"][b, blk][:, k0:k0 + CK], {})
                             for blk in range(NBLK)],
                      reads=[tks_bufs[b]], writes=[kch_bufs[kslot]],
                      name=f"t.kld{b}.{gq}.{kc}")
                for qi in range(GQ):
                    mmg(P, [(ps1[:, 0:CK],
                             qg.ap(kb, qi * 128, qi * 128 + 128),
                             kch[:, kslot * NBLK * CK + kb * CK:
                                 kslot * NBLK * CK + kb * CK + CK], st, sp)
                            for kb, (st, sp) in zip(range(NBLK), accg(NBLK))],
                        reads=[qg.bufs[kb] for kb in range(NBLK)] + [kch_bufs[kslot]],
                        writes=[ps1.buf], name=f"t.smm{b}.{gq}.{qi}.{kc}")
                    P.op("act", lambda e, qi=qi, k0=k0: e.activation(
                        sab[:, qi * nt + k0: qi * nt + k0 + CK], ps1[:, 0:CK], AF.Abs,
                        bias=0.0, scale=1.0 / 32.0),
                        reads=[ps1.buf], writes=[sab_bufs[qi]],
                        name=f"t.sev{b}.{gq}.{qi}.{kc}")
            for qi in range(GQ):
                qtile = gq * GQ + qi
                vq = (qtile + 1) * 128
                P.op("dve", lambda e, qi=qi, vq=vq: e.reduce_max(
                    cols[:, 0:1], sab[:, qi * nt: qi * nt + vq], axis=AX.X, negate=True),
                    reads=[sab_bufs[qi]], writes=[cols.buf], name=f"t.mx{b}.{qtile}")
                P.op("act", lambda e, qi=qi, vq=vq: e.activation(
                    ebf[:, 0:vq], sab[:, qi * nt: qi * nt + vq],
                    AF.Exp, bias=cols[:, 0:1], scale=1.0),
                    reads=[sab_bufs[qi], cols.buf], writes=[ebf.buf],
                    name=f"t.exp{b}.{qtile}")
                if qtile > 0:
                    P.op("dve", lambda e, qi=qi, vq=vq: e.tensor_copy(
                        wn[:, qi * nt: qi * nt + vq - 128], ebf[:, 0:vq - 128]),
                        reads=[ebf.buf], writes=[wn_bufs[qi]], name=f"t.cpe{b}.{qtile}")
                P.op("dve", lambda e, qi=qi, vq=vq: e.tensor_mul(
                    wn[:, qi * nt + vq - 128: qi * nt + vq],
                    ebf[:, vq - 128:vq], tril[:, :]),
                    reads=[ebf.buf, tril_buf], writes=[wn_bufs[qi]],
                    name=f"t.msk{b}.{qtile}")
                if qi == 0 and GQ == 2:
                    P.op("dve", lambda e, qi=qi, vq=vq: e.memset(
                        wn[:, qi * nt + vq: qi * nt + vq + 128], 0.0),
                        reads=[], writes=[wn_bufs[qi]], name=f"t.z{b}.{qtile}")
                P.op("dve", lambda e, qi=qi, vq=vq: e.reduce_sum(
                    cols[:, 1:2], wn[:, qi * nt: qi * nt + vq], axis=AX.X),
                    reads=[wn_bufs[qi]], writes=[cols.buf], name=f"t.sum{b}.{qtile}")
                P.op("dve", lambda e: e.reciprocal(cols[:, 2:3], cols[:, 1:2]),
                     reads=[cols.buf], writes=[cols.buf], name=f"t.rc{b}.{qtile}")
                P.op("dve", lambda e, qi=qi, vq=vq: e.tensor_scalar_mul(
                    wn[:, qi * nt: qi * nt + vq], wn[:, qi * nt: qi * nt + vq],
                    cols[:, 2:3]),
                    reads=[wn_bufs[qi], cols.buf], writes=[wn_bufs[qi]],
                    name=f"t.wsc{b}.{qtile}")
            nkc128 = nvalid // 128
            for kc in range(nkc128):
                mmg(P, [(ps1[:, qi * 128:(qi + 1) * 128],
                         wn[:, qi * nt + kc * 128: qi * nt + kc * 128 + 128],
                         G["eqm"][:], qi == 0, qi == GQ - 1) for qi in range(GQ)],
                    reads=[wn_bufs[qi] for qi in range(GQ)] + [G["const_buf"]],
                    writes=[ps1.buf], name=f"t.tp{b}.{gq}.{kc}")
                P.op("act", lambda e, kc=kc: e.copy(
                    wT[:, kc * GQ * 128:(kc + 1) * GQ * 128], ps1[:, 0:GQ * 128]),
                    reads=[ps1.buf], writes=[wT_buf], name=f"t.tpe{b}.{gq}.{kc}")
            for kc in range(nkc128):
                vslot = kc % 2
                P.dma("sp", [(vch[:, vslot * D:(vslot + 1) * D],
                              scratch["tvs"][b, kc], {})],
                      reads=[tvs_bufs[b]], writes=[vch_bufs[vslot]],
                      name=f"t.vld{b}.{gq}.{kc}")
                items = []
                for vd in range(NBLK):
                    items.append((ps_pv[:, vd * 256: vd * 256 + GQ * 128],
                                  vch[:, vslot * D + vd * 128: vslot * D + vd * 128 + 128],
                                  wT[:, kc * GQ * 128:(kc + 1) * GQ * 128],
                                  kc == 0 and vd % 2 == 0,
                                  kc == nkc128 - 1 and vd % 2 == 1))
                mmg(P, items, reads=[vch_bufs[vslot], wT_buf], writes=[ps_pv.buf],
                    name=f"t.pv{b}.{gq}.{kc}")
            for vd in range(NBLK):
                P.op("dve", lambda e, vd=vd: e.tensor_copy(
                    ao.ap(vd), ps_pv[:, vd * 256: vd * 256 + GQ * 128]),
                    reads=[ps_pv.buf], writes=[ao.bufs[vd]], name=f"t.pve{b}.{gq}.{vd}")
            for blk in range(NBLK):
                mmg(P, [(ps1[:, 0:GQ * 128],
                         wo[:, kb * D + blk * 128:kb * D + blk * 128 + 128],
                         ao.ap(kb), st, sp)
                        for kb, (st, sp) in zip(range(NBLK), accg(NBLK))],
                    reads=[wo_buf] + [ao.bufs[kb] for kb in range(NBLK)],
                    writes=[ps1.buf], name=f"t.omm{b}.{gq}.{blk}")
                P.op("act", lambda e, blk=blk: e.activation(
                    ctmp[:, 0:GQ * 128], ps1[:, 0:GQ * 128], AF.Identity,
                    bias=PP.col("bo", blk), scale=1.0),
                    reads=[ps1.buf, PP.buf], writes=[ctmp.buf],
                    name=f"t.oev{b}.{gq}.{blk}")
                xslot = blk % 2
                P.dma("sp", [(xres[:, xslot * GQ * 128:(xslot + 1) * GQ * 128],
                              x_in_ap(b, blk)[:, q0:q0 + GQ * 128], {})],
                      reads=XIN(b), writes=[xres_bufs[xslot]], name=f"t.xr{b}.{gq}.{blk}")
                P.op("dve", lambda e, blk=blk, q0=q0, xslot=xslot: e.tensor_add(
                    hx.ap(blk, q0, q0 + GQ * 128), ctmp[:, 0:GQ * 128],
                    xres[:, xslot * GQ * 128:(xslot + 1) * GQ * 128]),
                    reads=[ctmp.buf, xres_bufs[xslot]], writes=[hx.bufs[blk]],
                    name=f"t.oadd{b}.{gq}.{blk}")
        emit_ln(P, C, hx, hx, None, None, None, EPS_MOD, f"t.pln.{b}")
        P.dma("sp", [(x_out_ap(b, blk), hx.ap(blk), {}) for blk in range(NBLK)],
              reads=[hx.bufs[blk] for blk in range(NBLK)],
              writes=XOUT(b), name=f"t.out{b}", is_output=FINAL)

    P.close_scope(es)



def declare_l0f_weights(nc, nt):
    def t(n, s, dt=F32):
        return nc.dram_tensor(n, s, dt, kind="ExternalInput").ap()
    W = {}
    W["WQT"] = t("l0f_wq", [128, 8, D])
    W["WKT"] = t("l0f_wk", [128, 8, D])
    W["WOT"] = t("l0f_wo", [128, 8, D])
    W["WinvT"] = t("l0f_winv", [128, 8, D])
    W["Vf"] = t("l0f_vf", [128, 8, 512])
    W["Wf"] = t("l0f_wf", [128, 8, 512])
    W["UTf"] = t("l0f_uf", [4, 128, D])
    W["bf"] = t("l0f_bf", [8, D])
    W["Vi"] = t("l0f_vi", [128, 8, 512])
    W["Wie"] = t("l0f_wie", [128, 8, 512])
    W["UTi"] = t("l0f_ui", [4, 128, D])
    W["bi"] = t("l0f_bi", [8, D])
    W["rw1T"] = t("l0f_rw1", [128, 8, D])
    W["rb1_cols"] = t("l0f_rb1", [128, 8])
    W["rw2T"] = t("l0f_rw2", [128, 8, 8])
    W["rb2"] = t("l0f_rb2", [1, 8])
    W["ebias"] = t("l0f_eb", [1, 8])
    W["alpha"] = t("l0f_alpha", [1, 1])
    W["nv"] = t("l0f_nv", [128, nt])
    W["S_all"] = t("l0f_sall", [8, 512], BF16)
    return W


def _kr_layout(v):
    v = np.asarray(v, np.float32)
    return np.ascontiguousarray(v.reshape(KEXP, 8, 128, RR).transpose(2, 1, 0, 3)
                                .reshape(128, 8, KEXP * RR))


def host_l0f(inp, nt):
    H = {}
    for nm, key in (("wq", "L0_ffn_W_Q"), ("wk", "L0_ffn_W_K"),
                    ("wo", "L0_ffn_W_O"), ("winv", "L0_ffn_W_inv")):
        w = np.ascontiguousarray(np.asarray(inp[key], np.float32).T)
        H["l0f_" + nm] = np.ascontiguousarray(w.reshape(8, 128, D).transpose(1, 0, 2))
    H["l0f_vf"] = _kr_layout(inp["L0_ffn_V_fwd"])
    H["l0f_wf"] = _kr_layout(inp["L0_ffn_W_fwd"])
    H["l0f_vi"] = _kr_layout(inp["L0_ffn_V_inv"])
    H["l0f_wie"] = _kr_layout(inp["L0_ffn_W_inv_exp"])
    for nm, key in (("uf", "L0_ffn_U_fwd"), ("ui", "L0_ffn_U_inv")):
        u = np.asarray(inp[key], np.float32)
        H["l0f_" + nm] = np.ascontiguousarray(
            u.transpose(0, 2, 1).reshape(4, 128, D))
    H["l0f_bf"] = np.asarray(inp["L0_ffn_b_fwd"], np.float32).copy()
    H["l0f_bi"] = np.asarray(inp["L0_ffn_b_inv"], np.float32).copy()
    rw1 = np.ascontiguousarray(np.asarray(inp["L0_ffn_router_w1"], np.float32).T)
    H["l0f_rw1"] = np.ascontiguousarray(rw1.reshape(8, 128, D).transpose(1, 0, 2))
    H["l0f_rb1"] = _cols(inp["L0_ffn_router_b1"])
    rw2 = np.ascontiguousarray(np.asarray(inp["L0_ffn_router_w2"], np.float32).T)
    H["l0f_rw2"] = np.ascontiguousarray(rw2.reshape(8, 128, 8).transpose(1, 0, 2))
    H["l0f_rb2"] = np.asarray(inp["L0_ffn_router_b2"], np.float32).reshape(1, 8).copy()
    H["l0f_eb"] = np.asarray(inp["L0_ffn_expert_bias"], np.float32).reshape(1, 8).copy()
    H["l0f_alpha"] = np.asarray(inp["L0_ffn_alpha_bi"], np.float32).reshape(1, 1).copy()
    H["l0f_nv"] = np.broadcast_to(
        1.0 / np.arange(1, nt + 1, dtype=np.float32), (128, nt)).copy()
    sall = np.zeros((8, 512), np.float32)
    for mp in range(4):
        for p in range(128):
            sall[2 * mp + p // 64, mp * 128 + p] = 1.0
    H["l0f_sall"] = sall.astype(BF)
    return H


def emit_l0_ffn(P, G, W, x_in_ap, x_out_ap, nt, scratch, xio=None):
    xio = xio or {}
    XIN = (lambda b: [xio["in"][b]]) if xio.get("in") else (lambda b: [])
    XOUT = (lambda b: [xio["out"][b]]) if xio.get("out") else (lambda b: [])
    FINAL = xio.get("final", True)
    """scratch: dict with dram APs qs, ks, fs: [nb, 8, 128, nt] bf16,
    pcs: [128, 4*D] bf16 (precombined inv weights: VVi | WWi | RW1i x2)."""
    nb = G["nb"]
    CK = min(512, nt)
    nck = nt // CK
    es = P.scope()

    ps1 = Tile(P, "f_ps1", [128, 512], F32, es=es, psum=True)
    ps2 = Tile(P, "f_ps2", [128, 512], F32, es=es, psum=True)
    ps3 = Tile(P, "f_ps3", [128, 512], F32, es=es, psum=True)
    ps_o = Tile(P, "f_pso", [128, 512], F32, es=es, psum=True)
    C = make_ln_common(P, nt, es, G, "f_")
    PP = ParamPack(P, "f_pp", [("rb1", W["rb1_cols"], 8)], es=es)

    nv = P.sbuf("f_nv", [128, 512], F32, es=es)
    nv_buf = Buf("f_nv")
    sall = P.sbuf("f_sall", [8, 512], BF16, es=es)
    sall_buf = Buf("f_sall")
    P.dma("sp", [(sall[:], W["S_all"], {})], writes=[sall_buf], name="f.sall")
    rw2sb = P.sbuf("f_rw2", [128, 64], BF16, es=es)
    rw2_buf = Buf("f_rw2")
    P.dma("pool", [(rw2sb[:].rearrange("p (a e) -> p a e", e=8), W["rw2T"], {})],
          writes=[rw2_buf], name="f.rw2")
    srow = Tile(P, "f_srow", [1, 24], F32, es=es)
    cebbf = Tile(P, "f_ceb", [1, 8], BF16, es=es)
    P.dma("sp", [(srow[0:1, 0:8], W["rb2"], {}), (srow[0:1, 8:16], W["ebias"], {}),
                 (srow[0:1, 16:17], W["alpha"], {})],
          writes=[srow.buf], name="f.rows")
    P.op("dve", lambda e: e.tensor_add(srow[0:1, 0:8], srow[0:1, 0:8], srow[0:1, 8:16]),
         reads=[srow.buf], writes=[srow.buf], name="f.ceb")
    P.op("dve", lambda e: e.tensor_copy(cebbf[0:1, :], srow[0:1, 0:8]),
         reads=[srow.buf], writes=[cebbf.buf], name="f.cebc")
    acol = Tile(P, "f_acol", [128, 1], F32, es=es)
    mmg(P, [(ps1[:, 0:1], G["ones_row_f"][0:1, :], srow[0:1, 16:17], True, True)],
        reads=[srow.buf, G["const_buf"]], writes=[ps1.buf], name="f.abc")
    P.op("act", lambda e: e.copy(acol[:, 0:1], ps1[:, 0:1]),
         reads=[ps1.buf], writes=[acol.buf], name="f.abe")

    vX = P.sbuf("f_vX", [128, 8 * 512], BF16, es=es)
    vX_buf = Buf("f_vX")
    wX = P.sbuf("f_wX", [128, 8 * 512], BF16, es=es)
    wX_buf = Buf("f_wX")
    uX = P.sbuf("f_uX", [128, 4 * D], BF16, es=es)
    uX_buf = Buf("f_uX")
    bX = P.sbuf("f_bX", [8, D], BF16, es=es)
    bX_buf = Buf("f_bX")
    rX = P.sbuf("f_rX", [128, NBLK * D], BF16, es=es)
    rX_buf = Buf("f_rX")
    woX = P.sbuf("f_woX", [128, NBLK * D], BF16, es=es)
    wo_buf = Buf("f_woX")
    qs_bufs = [Buf(f"f_qs.{i}", nofence=True) for i in range(nb)]
    ks_bufs = [Buf(f"f_ks.{i}", nofence=True) for i in range(nb)]
    fs_bufs = [Buf(f"f_fs.{i}", nofence=True) for i in range(nb)]
    pcs_buf = Buf("f_pcs", nofence=True)
    pcs2_buf = Buf("f_pcs2", nofence=True)

    es0 = P.scope()
    winv = P.sbuf("f_winv", [128, NBLK * D], BF16, es=es0)
    winv_buf = Buf("f_winv")
    P.dma("pool", [(winv[:].rearrange("p (a m) -> p a m", m=D), W["WinvT"], {})],
          writes=[winv_buf], name="f.winv")
    for dst_off, srcW in ((0, "Wie"), (1, "Vi")):
        P.dma("pool", [(vX[:].rearrange("p (a c) -> p a c", c=512), W[srcW], {})],
              writes=[vX_buf], name=f"f.p0l{dst_off}")
        for dblk in range(NBLK):
            for k in range(KEXP):
                mmg(P, [(ps1[:, 0:64],
                         winv[:, kb * D + dblk * 128: kb * D + dblk * 128 + 128],
                         vX[:, kb * 512 + k * 64: kb * 512 + k * 64 + 64], st, sp)
                        for kb, (st, sp) in zip(range(NBLK), accg(NBLK))],
                    reads=[winv_buf, vX_buf], writes=[ps1.buf],
                    name=f"f.p0m{dst_off}.{dblk}.{k}")
                P.op("act", lambda e, dblk=dblk, k=k: e.copy(
                    wX[:, dblk * 512 + k * 64: dblk * 512 + k * 64 + 64], ps1[:, 0:64]),
                    reads=[ps1.buf], writes=[wX_buf], name=f"f.p0e{dst_off}.{dblk}.{k}")
        P.dma("sp", [(scratch["pcs"][:, dst_off * 4096:(dst_off + 1) * 4096], wX[:], {})],
              reads=[wX_buf], writes=[pcs_buf], name=f"f.p0s{dst_off}")
    rwi = P.sbuf("f_rwi", [128, NBLK * D], BF16, es=es0)
    rwi_buf = Buf("f_rwi")
    P.dma("pool", [(rX[:].rearrange("p (a m) -> p a m", m=D), W["rw1T"], {})],
          writes=[rX_buf], name="f.p0r")
    for dblk in range(NBLK):
        for hc in range(2):
            mmg(P, [(ps1[:, 0:512],
                     winv[:, kb * D + dblk * 128: kb * D + dblk * 128 + 128],
                     rX[:, kb * D + hc * 512: kb * D + hc * 512 + 512], st, sp)
                    for kb, (st, sp) in zip(range(NBLK), accg(NBLK))],
                reads=[winv_buf, rX_buf], writes=[ps1.buf], name=f"f.p0rm.{dblk}.{hc}")
            P.op("act", lambda e, dblk=dblk, hc=hc: e.copy(
                rwi[:, dblk * 1024 + hc * 512: dblk * 1024 + hc * 512 + 512],
                ps1[:, 0:512]),
                reads=[ps1.buf], writes=[rwi_buf], name=f"f.p0re.{dblk}.{hc}")
    P.dma("sp", [(scratch["pcs2"][:, :], rwi[:], {})], reads=[rwi_buf],
          writes=[pcs2_buf], name="f.p0rs")
    P.close_scope(es0)

    qck = P.sbuf("f_qck", [128, NBLK * CK], BF16, es=es)
    qck_bufs = [Buf(f"f_qck.{i}") for i in range(NBLK)]
    kck = P.sbuf("f_kck", [128, NBLK * CK], BF16, es=es)
    kck_bufs = [Buf(f"f_kck.{i}") for i in range(NBLK)]
    esqk = P.scope()
    xck = P.sbuf("f_xck", [128, 2 * NBLK * CK], BF16, es=esqk)
    xck_bufs = [Buf("f_xck.0"), Buf("f_xck.1")]
    for tgt, wkey, snm in (("q", "WQT", "qs"), ("k", "WKT", "ks")):
        P.dma("pool", [(rX[:].rearrange("p (a m) -> p a m", m=D), W[wkey], {})],
              writes=[rX_buf], name=f"f.w{tgt}")
        for b in range(nb):
            for ck in range(nck):
                c0 = ck * CK
                slot = ck % 2
                P.dma("pool", [(xck[:, slot * NBLK * CK:(slot + 1) * NBLK * CK]
                                .rearrange("p (a c) -> p a c", c=CK),
                                x_in_ap(b, None)[:, :, c0:c0 + CK].rearrange(
                                    "a p c -> p a c"), {})],
                      reads=XIN(b), writes=[xck_bufs[slot]], name=f"f.x{tgt}{b}.{ck}")
                for blk in range(NBLK):
                    mmg(P, [(ps1[:, 0:CK],
                             rX[:, kb * D + blk * 128:kb * D + blk * 128 + 128],
                             xck[:, slot * NBLK * CK + kb * CK:
                                 slot * NBLK * CK + kb * CK + CK], st, sp)
                            for kb, (st, sp) in zip(range(NBLK), accg(NBLK))],
                        reads=[rX_buf, xck_bufs[slot]], writes=[ps1.buf],
                        name=f"f.{tgt}mm{b}.{ck}.{blk}")
                    P.op("act", lambda e, blk=blk: e.copy(
                        qck[:, blk * CK:(blk + 1) * CK], ps1[:, 0:CK]),
                        reads=[ps1.buf], writes=[qck_bufs[blk]],
                        name=f"f.{tgt}ev{b}.{ck}.{blk}")
                P.dma("sp", [(scratch[snm][b, blk][:, c0:c0 + CK],
                              qck[:, blk * CK:(blk + 1) * CK], {})
                             for blk in range(NBLK)],
                      reads=qck_bufs,
                      writes=[(qs_bufs if tgt == "q" else ks_bufs)[b]],
                      name=f"f.{tgt}st{b}.{ck}")

    P.close_scope(esqk)
    hx = DT(P, "f_hx", NBLK, nt, F32, es=es)
    hrbf = P.sbuf("f_hr", [128, NBLK * CK], BF16, es=es)
    hr_bufs = [Buf(f"f_hr.{i}") for i in range(NBLK)]
    lgt = Tile(P, "f_lgt", [128, 8], F32, es=es)
    prb = Tile(P, "f_prb", [128, 8], F32, es=es)
    pcol = Tile(P, "f_pcol", [128, 4], F32, es=es)
    wbf = Tile(P, "f_wbf", [128, 8], BF16, es=es)
    wTb = Tile(P, "f_wT", [8, CK], BF16, es=es)
    xv = P.sbuf("f_xv", [128, 4 * CK], BF16, es=es)
    xv_bufs = [Buf(f"f_xv.{i}") for i in range(4)]
    cum = P.sbuf("f_cum", [128, 4 * CK], BF16, es=es)
    cum_bufs = [Buf(f"f_cum.{i}") for i in range(4)]
    pw = P.sbuf("f_pw", [128, 4 * CK], BF16, es=es)
    pw_bufs = [Buf(f"f_pw.{i}") for i in range(4)]
    carry = Tile(P, "f_carry", [128, 4], F32, es=es)
    fck = P.sbuf("f_fck", [128, NBLK * CK], BF16, es=es)
    fck_bufs = [Buf(f"f_fck.{i}") for i in range(NBLK)]
    cbk = fck
    cbk_bufs = fck_bufs
    tmpb = Tile(P, "f_tmpb", [128, CK], BF16, es=es)
    ctmp = Tile(P, "f_ctmp", [128, CK], F32, es=es)
    xres = P.sbuf("f_xres", [128, 2 * CK], F32, es=es)
    xres_bufs = [Buf("f_xres.0"), Buf("f_xres.1")]

    P.dma("pool", [(woX[:].rearrange("p (a m) -> p a m", m=D), W["WOT"], {})],
          writes=[wo_buf], name="f.wo")

    for br in range(2):
        if br == 0:
            P.dma("pool", [(vX[:].rearrange("p (a c) -> p a c", c=512), W["Vf"], {})],
                  writes=[vX_buf], name="f.vf")
            P.dma("pool", [(wX[:].rearrange("p (a c) -> p a c", c=512), W["Wf"], {})],
                  writes=[wX_buf], name="f.wf")
            P.dma("pool", [(uX[:].rearrange("p (a m) -> p a m", m=D),
                            W["UTf"].rearrange("a p m -> p a m"), {})],
                  writes=[uX_buf], name="f.uf")
            P.dma("pool", [(bX[:], W["bf"], {})], writes=[bX_buf], name="f.bf")
            P.dma("pool", [(rX[:].rearrange("p (a m) -> p a m", m=D), W["rw1T"], {})],
                  writes=[rX_buf], name="f.r1f")
        else:
            P.dma("sp", [(vX[:], scratch["pcs"][:, 0:4096], {})],
                  reads=[pcs_buf], writes=[vX_buf], name="f.vi")
            P.dma("sp", [(wX[:], scratch["pcs"][:, 4096:8192], {})],
                  reads=[pcs_buf], writes=[wX_buf], name="f.wi")
            P.dma("pool", [(uX[:].rearrange("p (a m) -> p a m", m=D),
                            W["UTi"].rearrange("a p m -> p a m"), {})],
                  writes=[uX_buf], name="f.ui")
            P.dma("pool", [(bX[:], W["bi"], {})], writes=[bX_buf], name="f.bi")
            P.dma("sp", [(rX[:], scratch["pcs2"][:, :], {})],
                  reads=[pcs2_buf], writes=[rX_buf], name="f.r1i")
        for b in range(nb):
            P.op("pool", lambda e: e.memset(carry[:, :], 0.0),
                 writes=[carry.buf], name=f"f.cz{br}.{b}")
            for ck in range(nck):
                c0 = ck * CK
                P.dma("sp", [(qck[:, blk * CK:(blk + 1) * CK],
                              scratch["qs"][b, blk][:, c0:c0 + CK], {})
                             for blk in range(NBLK)],
                      reads=[qs_bufs[b]], writes=qck_bufs, name=f"f.lq{br}.{b}.{ck}")
                P.dma("sp", [(kck[:, blk * CK:(blk + 1) * CK],
                              scratch["ks"][b, blk][:, c0:c0 + CK], {})
                             for blk in range(NBLK)],
                      reads=[ks_bufs[b]], writes=kck_bufs, name=f"f.lk{br}.{b}.{ck}")
                if br == 1:
                    P.dma("sp", [(fck[:, blk * CK:(blk + 1) * CK],
                                  scratch["fs"][b, blk][:, c0:c0 + CK], {})
                                 for blk in range(NBLK)],
                          reads=[fs_bufs[b]], writes=fck_bufs, name=f"f.lf{b}.{ck}")
                for blk in range(NBLK):
                    mmg(P, [(ps1[:, 0:CK],
                             rX[:, kb * D + blk * 128:kb * D + blk * 128 + 128],
                             qck[:, kb * CK:(kb + 1) * CK], st, sp)
                            for kb, (st, sp) in zip(range(NBLK), accg(NBLK))],
                        reads=[rX_buf] + qck_bufs, writes=[ps1.buf],
                        name=f"f.rm{br}.{b}.{ck}.{blk}")
                    P.op("act", lambda e, blk=blk: e.activation(
                        hrbf[:, blk * CK:(blk + 1) * CK], ps1[:, 0:CK], AF.Gelu,
                        bias=PP.col("rb1", blk), scale=1.0),
                        reads=[ps1.buf, PP.buf], writes=[hr_bufs[blk]],
                        name=f"f.re{br}.{b}.{ck}.{blk}")
                for tch in range(CK // 128):
                    t0 = tch * 128
                    items = [(ps_o[:, 0:8],
                              hrbf[:, kb * CK + t0: kb * CK + t0 + 128],
                              rw2sb[:, kb * 8:(kb + 1) * 8], st, False)
                             for kb, (st, _) in zip(range(NBLK), accg(NBLK))]
                    items.append((ps_o[:, 0:8], G["ones_row"][0:1, 0:128],
                                  cebbf[0:1, :], False, True))
                    mmg(P, items, reads=hr_bufs + [rw2_buf, cebbf.buf, G["const_buf"]],
                        writes=[ps_o.buf], name=f"f.lg{br}.{b}.{ck}.{tch}")
                    P.op("dve", lambda e: e.tensor_copy(lgt[:, :], ps_o[:, 0:8]),
                         reads=[ps_o.buf], writes=[lgt.buf], name=f"f.lge{br}.{b}.{ck}.{tch}")
                    P.op("dve", lambda e: e.reduce_max(pcol[:, 0:1], lgt[:, :],
                                                       axis=AX.X, negate=True),
                         reads=[lgt.buf], writes=[pcol.buf], name=f"f.mx{br}.{b}.{ck}.{tch}")
                    P.op("act", lambda e: e.activation(prb[:, :], lgt[:, :], AF.Exp,
                                                       bias=pcol[:, 0:1], scale=1.0,
                                                       accum_out=pcol[:, 1:2]),
                         reads=[lgt.buf, pcol.buf], writes=[prb.buf, pcol.buf],
                         name=f"f.ex{br}.{b}.{ck}.{tch}")
                    P.op("dve", lambda e: e.reciprocal(pcol[:, 2:3], pcol[:, 1:2]),
                         reads=[pcol.buf], writes=[pcol.buf], name=f"f.rc{br}.{b}.{ck}.{tch}")
                    P.op("dve", lambda e: e.tensor_scalar_mul(wbf[:, :], prb[:, :],
                                                              pcol[:, 2:3]),
                         reads=[prb.buf, pcol.buf], writes=[wbf.buf],
                         name=f"f.wn{br}.{b}.{ck}.{tch}")
                    mmg(P, [(ps3[0:8, t0:t0 + 128], wbf[:, :], G["eqm"][:],
                             tch == 0, tch == CK // 128 - 1)],
                        reads=[wbf.buf, G["const_buf"]], writes=[ps3.buf],
                        name=f"f.wt{br}.{b}.{ck}.{tch}")
                P.op("act", lambda e: e.copy(wTb[0:8, 0:CK], ps3[0:8, 0:CK]),
                     reads=[ps3.buf], writes=[wTb.buf], name=f"f.wte{br}.{b}.{ck}")
                for mp in range(4):
                    mmg(P, [(ps2[:, 0:CK],
                             vX[:, kb * 512 + mp * 128: kb * 512 + mp * 128 + 128],
                             qck[:, kb * CK:(kb + 1) * CK], st, sp)
                            for kb, (st, sp) in zip(range(NBLK), accg(NBLK))],
                        reads=[vX_buf] + qck_bufs, writes=[ps2.buf],
                        name=f"f.xv{br}.{b}.{ck}.{mp}")
                    P.op("act", lambda e, mp=mp: e.copy(
                        xv[:, mp * CK:(mp + 1) * CK], ps2[:, 0:CK]),
                        reads=[ps2.buf], writes=[xv_bufs[mp]],
                        name=f"f.xve{br}.{b}.{ck}.{mp}")
                    mmg(P, [(ps2[:, 0:CK],
                             wX[:, kb * 512 + mp * 128: kb * 512 + mp * 128 + 128],
                             kck[:, kb * CK:(kb + 1) * CK], st, sp)
                            for kb, (st, sp) in zip(range(NBLK), accg(NBLK))],
                        reads=[wX_buf] + kck_bufs, writes=[ps2.buf],
                        name=f"f.yw{br}.{b}.{ck}.{mp}")
                    P.op("dve", lambda e, mp=mp: e.tensor_tensor_scan(
                        cum[:, mp * CK:(mp + 1) * CK],
                        G["ones_col_f"][:, 0:1].broadcast_to([128, CK]),
                        ps2[:, 0:CK], carry[:, mp:mp + 1], ALU.mult, ALU.add),
                        reads=[ps2.buf, carry.buf, G["const_buf"]],
                        writes=[cum_bufs[mp]], name=f"f.sc{br}.{b}.{ck}.{mp}")
                    P.op("dve", lambda e, mp=mp: e.tensor_copy(
                        carry[:, mp:mp + 1], cum[:, mp * CK + CK - 1: mp * CK + CK]),
                        reads=[cum_bufs[mp]], writes=[carry.buf],
                        name=f"f.cu{br}.{b}.{ck}.{mp}")
                    mmg(P, [(ps3[:, 0:CK], sall[:, mp * 128:(mp + 1) * 128],
                             wTb[0:8, 0:CK], True, True)],
                        reads=[sall_buf, wTb.buf], writes=[ps3.buf],
                        name=f"f.wb{br}.{b}.{ck}.{mp}")
                    P.op("dve", lambda e, mp=mp: e.tensor_mul(
                        tmpb[:, 0:CK], xv[:, mp * CK:(mp + 1) * CK],
                        cum[:, mp * CK:(mp + 1) * CK]),
                        reads=[xv_bufs[mp], cum_bufs[mp]], writes=[tmpb.buf],
                        name=f"f.pr{br}.{b}.{ck}.{mp}")
                    P.op("dve", lambda e, mp=mp: e.tensor_mul(
                        pw[:, mp * CK:(mp + 1) * CK], tmpb[:, 0:CK], ps3[:, 0:CK]),
                        reads=[tmpb.buf, ps3.buf], writes=[pw_bufs[mp]],
                        name=f"f.pw{br}.{b}.{ck}.{mp}")
                for dblk in range(NBLK):
                    items = [(ps_o[:, 0:CK],
                              uX[:, mp * D + dblk * 128: mp * D + dblk * 128 + 128],
                              pw[:, mp * CK:(mp + 1) * CK], st, False)
                             for mp, (st, _) in zip(range(4), accg(4))]
                    items.append((ps_o[:, 0:CK], bX[0:8, dblk * 128:dblk * 128 + 128],
                                  wTb[0:8, 0:CK], False, True))
                    mmg(P, items, reads=pw_bufs + [uX_buf, bX_buf, wTb.buf],
                        writes=[ps_o.buf], name=f"f.om{br}.{b}.{ck}.{dblk}")
                    if br == 0:
                        P.op("act", lambda e, dblk=dblk: e.copy(
                            fck[:, dblk * CK:(dblk + 1) * CK], ps_o[:, 0:CK]),
                            reads=[ps_o.buf], writes=[fck_bufs[dblk]],
                            name=f"f.oe{br}.{b}.{ck}.{dblk}")
                    else:
                        P.op("dve", lambda e, dblk=dblk: e.scalar_tensor_tensor(
                            cbk[:, dblk * CK:(dblk + 1) * CK], ps_o[:, 0:CK],
                            acol[:, 0:1], fck[:, dblk * CK:(dblk + 1) * CK],
                            ALU.mult, ALU.add),
                            reads=[ps_o.buf, acol.buf, fck_bufs[dblk]],
                            writes=[cbk_bufs[dblk]], name=f"f.cb{br}.{b}.{ck}.{dblk}")
                if br == 0:
                    P.dma("sp", [(scratch["fs"][b, blk][:, c0:c0 + CK],
                                  fck[:, blk * CK:(blk + 1) * CK], {})
                                 for blk in range(NBLK)],
                          reads=fck_bufs, writes=[fs_bufs[b]], name=f"f.fst{b}.{ck}")
                if br == 1:
                    P.dma("sp", [(nv[:, 0:CK], W["nv"][:, c0:c0 + CK], {})],
                          writes=[nv_buf], name=f"f.nvl{b}.{ck}")
                    for blk in range(NBLK):
                        mmg(P, [(ps1[:, 0:CK],
                                 woX[:, kb * D + blk * 128:kb * D + blk * 128 + 128],
                                 cbk[:, kb * CK:(kb + 1) * CK], st, sp)
                                for kb, (st, sp) in zip(range(NBLK), accg(NBLK))],
                            reads=[wo_buf] + cbk_bufs, writes=[ps1.buf],
                            name=f"f.wom{b}.{ck}.{blk}")
                        P.op("dve", lambda e, blk=blk: e.tensor_mul(
                            ctmp[:, 0:CK], ps1[:, 0:CK], nv[:, 0:CK]),
                            reads=[ps1.buf, nv_buf], writes=[ctmp.buf],
                            name=f"f.nv{b}.{ck}.{blk}")
                        xslot = blk % 2
                        P.dma("sp", [(xres[:, xslot * CK:(xslot + 1) * CK],
                                      x_in_ap(b, blk)[:, c0:c0 + CK], {})],
                              reads=XIN(b), writes=[xres_bufs[xslot]],
                              name=f"f.xr{b}.{ck}.{blk}")
                        P.op("dve", lambda e, blk=blk, c0=c0, xslot=xslot: e.tensor_add(
                            hx.ap(blk, c0, c0 + CK), ctmp[:, 0:CK],
                            xres[:, xslot * CK:(xslot + 1) * CK]),
                            reads=[ctmp.buf, xres_bufs[xslot]], writes=[hx.bufs[blk]],
                            name=f"f.hx{b}.{ck}.{blk}")
            if br == 1:
                emit_ln(P, C, hx, hx, None, None, None, EPS_MOD, f"f.pln.{b}")
                P.dma("sp", [(x_out_ap(b, blk), hx.ap(blk), {}) for blk in range(NBLK)],
                      reads=[hx.bufs[blk] for blk in range(NBLK)],
                      writes=XOUT(b), name=f"f.out{b}", is_output=FINAL)

    P.close_scope(es)



def declare_l1f_weights(nc, nt):
    def t(n, s, dt=F32):
        return nc.dram_tensor(n, s, dt, kind="ExternalInput").ap()
    W = {}
    W["gwT"] = t("l1f_gw", [128, 8, 8])
    W["gpT"] = t("l1f_gp", [KEXP, 128, 8, MH])
    W["upT"] = t("l1f_up", [KEXP, 128, 8, MH])
    W["dpT"] = t("l1f_dp", [KEXP, 128, MHB, D])
    W["TRI"] = t("l1f_tri", [128, 128], BF16)
    W["SEL"] = t("l1f_sel", [16, 2])
    W["ESEL"] = t("l1f_esel", [8, 1024], BF16)
    return W


def host_l1f(inp, nt):
    H = {}
    gw = np.ascontiguousarray(np.asarray(inp["L1_ffn_gate_w"], np.float32).T)
    H["l1f_gw"] = np.ascontiguousarray(gw.reshape(8, 128, 8).transpose(1, 0, 2))
    gp = np.asarray(inp["L1_ffn_gate_proj_w"], np.float32)
    H["l1f_gp"] = np.ascontiguousarray(
        gp.transpose(0, 2, 1).reshape(KEXP, 8, 128, MH).transpose(0, 2, 1, 3))
    up = np.asarray(inp["L1_ffn_up_proj_w"], np.float32)
    H["l1f_up"] = np.ascontiguousarray(
        up.transpose(0, 2, 1).reshape(KEXP, 8, 128, MH).transpose(0, 2, 1, 3))
    dp = np.asarray(inp["L1_ffn_down_proj_w"], np.float32)
    H["l1f_dp"] = np.ascontiguousarray(
        dp.transpose(0, 2, 1).reshape(KEXP, MHB, 128, D).transpose(0, 2, 1, 3))
    tri = np.triu(np.ones((128, 128), np.float32))
    H["l1f_tri"] = tri.astype(BF)
    esel = np.zeros((8, 1024), np.float32)
    for ei in range(8):
        esel[ei, ei * 128:(ei + 1) * 128] = 1.0
    H["l1f_esel"] = esel.astype(BF)
    return H


def emit_l1f_prestage(P, W):
    gpb = P.nc.dram_tensor("m_gpb", [KEXP, 2, 128, NBLK * (MH // 2)], BF16,
                           kind="Internal").ap()
    upb = P.nc.dram_tensor("m_upb", [KEXP, 2, 128, NBLK * (MH // 2)], BF16,
                           kind="Internal").ap()
    dpb = P.nc.dram_tensor("m_dpb", [KEXP, 128, MHB, D], BF16, kind="Internal").ap()
    wbf_bufs = {"gp": [Buf(f"m_gpb{i}", nofence=True) for i in range(KEXP)],
                "up": [Buf(f"m_upb{i}", nofence=True) for i in range(KEXP)],
                "dp": [Buf(f"m_dpb{i}", nofence=True) for i in range(KEXP)]}
    for e_ in range(KEXP):
        P.dma("pool", [(gpb[e_, half].rearrange("p (a m) -> p a m", m=MH // 2),
                        W["gpT"][e_][:, :, half * (MH // 2):(half + 1) * (MH // 2)],
                        {}) for half in range(2)],
              writes=[wbf_bufs["gp"][e_]], name=f"m.cgp{e_}")
        P.dma("pool", [(upb[e_, half].rearrange("p (a m) -> p a m", m=MH // 2),
                        W["upT"][e_][:, :, half * (MH // 2):(half + 1) * (MH // 2)],
                        {}) for half in range(2)],
              writes=[wbf_bufs["up"][e_]], name=f"m.cup{e_}")
        P.dma("pool", [(dpb[e_], W["dpT"][e_], {})], writes=[wbf_bufs["dp"][e_]],
              name=f"m.cdp{e_}")
    return {"gpb": gpb, "upb": upb, "dpb": dpb, "bufs": wbf_bufs}


def emit_l1_ffn(P, G, W, x_in_ap, x_out_ap, nt, capacity, do_collective, cc, xio=None,
                prestage=None):
    xio = xio or {}
    XIN = (lambda b: [xio["in"][b]]) if xio.get("in") else (lambda b: [])
    XOUT = (lambda b: [xio["out"][b]]) if xio.get("out") else (lambda b: [])
    FINAL = xio.get("final", True)
    """cc: dict with Internal dram APs cc_in (2,8), cc_out (16,8) (if do_collective)."""
    nb = G["nb"]
    CK = min(512, nt)
    nck = nt // CK
    ntch = nt // 128
    HHB = MHB // 2
    es = P.scope()

    ps_g = Tile(P, "m_psg", [128, 512], F32, es=es, psum=True)
    ps_g2 = Tile(P, "m_psg2", [128, 512], F32, es=es, psum=True)
    ps_a = Tile(P, "m_psa", [128, 512], F32, es=es, psum=True)
    ps_u = Tile(P, "m_psu", [128, 512], F32, es=es, psum=True)
    ps_y = Tile(P, "m_psy", [128, 512], F32, es=es, psum=True)
    C = make_ln_common(P, nt, es, G, "m_")

    gwsb = P.sbuf("m_gw", [128, 64], BF16, es=es)
    gw_buf = Buf("m_gw")
    P.dma("pool", [(gwsb[:].rearrange("p (a e) -> p a e", e=8), W["gwT"], {})],
          writes=[gw_buf], name="m.gw")
    tri = P.sbuf("m_tri", [128, 128], BF16, es=es)
    tri_buf = Buf("m_tri")
    P.dma("sp", [(tri[:], W["TRI"], {})], writes=[tri_buf], name="m.tri")
    selsb = P.sbuf("m_sel", [16, 2], F32, es=es)
    sel_buf = Buf("m_sel")
    P.dma("sp", [(selsb[:], W["SEL"], {})], writes=[sel_buf], name="m.sel")
    esel = P.sbuf("m_esel", [8, 1024], BF16, es=es)
    esel_buf = Buf("m_esel")
    P.dma("sp", [(esel[:], W["ESEL"], {})], writes=[esel_buf], name="m.esel")

    ohsb = P.sbuf("m_oh", [128, nb * 2 * ntch * 8], BF16, es=es)
    oh_buf = Buf("m_oh")
    ohfb = P.sbuf("m_ohf", [128, nb * 2 * ntch * 8], F32, es=es)
    ohf_buf = Buf("m_ohf")
    wcol = P.sbuf("m_wc", [128, nb * 2 * ntch], F32, es=es)
    wc_buf = Buf("m_wc")
    mrc = P.sbuf("m_mr", [128, nb * 2 * ntch], F32, es=es)
    mr_buf = Buf("m_mr")
    prev = Tile(P, "m_prev", [1, 16], F32, es=es)
    offs = Tile(P, "m_offs", [1, 16], F32, es=es)
    ccsb = Tile(P, "m_cc", [16, 8], F32, es=es)
    lgt = Tile(P, "m_lgt", [128, 8], F32, es=es)
    prb = Tile(P, "m_prb", [128, 8], F32, es=es)
    p2 = Tile(P, "m_p2", [128, 8], F32, es=es)
    pcol = Tile(P, "m_pcol", [128, 8], F32, es=es)
    t8 = Tile(P, "m_t8", [128, 8], F32, es=es)
    xall = P.sbuf("m_xall", [128, NBLK * nt], BF16, es=es)
    xall_bufs = [Buf(f"m_xall.{i}") for i in range(NBLK)]

    P.op("pool", lambda e: e.memset(prev[:, :], 0.0), writes=[prev.buf], name="m.pz")

    if prestage is None:
        prestage = emit_l1f_prestage(P, W)
    gpb, upb, dpb = prestage["gpb"], prestage["upb"], prestage["dpb"]
    wbf_bufs = prestage["bufs"]

    def oh_ap(b, k, tch, t=None):
        o = (((b * 2 + k) * ntch) + tch) * 8
        return o

    for b in range(nb):
        P.dma("pool", [(xall[:, kb * nt:(kb + 1) * nt], x_in_ap(b, kb), {})
                       for kb in range(NBLK)],
              reads=XIN(b), writes=xall_bufs, name=f"m.x{b}")
        for tch in range(ntch):
            t0 = tch * 128
            mmg(P, [(ps_g[:, 0:8],
                     xall[:, kb * nt + t0: kb * nt + t0 + 128],
                     gwsb[:, kb * 8:(kb + 1) * 8], st, sp)
                    for kb, (st, sp) in zip(range(NBLK), accg(NBLK))],
                reads=[gw_buf] + xall_bufs, writes=[ps_g.buf],
                name=f"m.lg{b}.{tch}")
            P.op("dve", lambda e: e.tensor_copy(lgt[:, :], ps_g[:, 0:8]),
                 reads=[ps_g.buf], writes=[lgt.buf], name=f"m.lge{b}.{tch}")
            P.op("dve", lambda e: e.reduce_max(pcol[:, 0:1], lgt[:, :],
                                               axis=AX.X, negate=True),
                 reads=[lgt.buf], writes=[pcol.buf], name=f"m.mx{b}.{tch}")
            P.op("act", lambda e: e.activation(prb[:, :], lgt[:, :], AF.Exp,
                                               bias=pcol[:, 0:1], scale=1.0,
                                               accum_out=pcol[:, 1:2]),
                 reads=[lgt.buf, pcol.buf], writes=[prb.buf, pcol.buf],
                 name=f"m.ex{b}.{tch}")
            P.op("dve", lambda e: e.reciprocal(pcol[:, 2:3], pcol[:, 1:2]),
                 reads=[pcol.buf], writes=[pcol.buf], name=f"m.rc{b}.{tch}")
            P.op("dve", lambda e: e.tensor_scalar_mul(prb[:, :], prb[:, :], pcol[:, 2:3]),
                 reads=[prb.buf, pcol.buf], writes=[prb.buf], name=f"m.pn{b}.{tch}")
            P.op("dve", lambda e: e.reduce_max(pcol[:, 3:4], prb[:, :], axis=AX.X),
                 reads=[prb.buf], writes=[pcol.buf], name=f"m.v1{b}.{tch}")
            o = oh_ap(b, 0, tch)
            P.op("dve", lambda e, o=o: e.tensor_single_scalar(
                ohfb[:, o:o + 8], prb[:, :], pcol[:, 3:4], ALU.is_equal),
                reads=[prb.buf, pcol.buf], writes=[ohf_buf], name=f"m.oh1{b}.{tch}")
            P.op("dve", lambda e, o=o: e.tensor_copy(ohsb[:, o:o + 8], ohfb[:, o:o + 8]),
                 reads=[ohf_buf], writes=[oh_buf], name=f"m.oh1c{b}.{tch}")
            P.op("dve", lambda e, o=o: e.scalar_tensor_tensor(
                p2[:, :], ohfb[:, o:o + 8], -1e30, prb[:, :], ALU.mult, ALU.add),
                reads=[ohf_buf, prb.buf], writes=[p2.buf], name=f"m.p2{b}.{tch}")
            P.op("dve", lambda e: e.reduce_max(pcol[:, 4:5], p2[:, :], axis=AX.X),
                 reads=[p2.buf], writes=[pcol.buf], name=f"m.v2{b}.{tch}")
            o2 = oh_ap(b, 1, tch)
            P.op("dve", lambda e, o2=o2: e.tensor_single_scalar(
                ohfb[:, o2:o2 + 8], p2[:, :], pcol[:, 4:5], ALU.is_equal),
                reads=[p2.buf, pcol.buf], writes=[ohf_buf], name=f"m.oh2{b}.{tch}")
            P.op("dve", lambda e, o2=o2: e.tensor_copy(ohsb[:, o2:o2 + 8], ohfb[:, o2:o2 + 8]),
                 reads=[ohf_buf], writes=[oh_buf], name=f"m.oh2c{b}.{tch}")
            P.op("dve", lambda e: e.tensor_add(pcol[:, 5:6], pcol[:, 3:4], pcol[:, 4:5]),
                 reads=[pcol.buf], writes=[pcol.buf], name=f"m.s12{b}.{tch}")
            P.op("dve", lambda e: e.tensor_scalar_add(pcol[:, 5:6], pcol[:, 5:6], 1e-9),
                 reads=[pcol.buf], writes=[pcol.buf], name=f"m.se{b}.{tch}")
            P.op("dve", lambda e: e.reciprocal(pcol[:, 6:7], pcol[:, 5:6]),
                 reads=[pcol.buf], writes=[pcol.buf], name=f"m.sr{b}.{tch}")
            wo1 = (b * 2 + 0) * ntch + tch
            wo2 = (b * 2 + 1) * ntch + tch
            P.op("dve", lambda e, wo1=wo1: e.tensor_mul(
                wcol[:, wo1:wo1 + 1], pcol[:, 3:4], pcol[:, 6:7]),
                reads=[pcol.buf], writes=[wc_buf], name=f"m.w1{b}.{tch}")
            P.op("dve", lambda e, wo2=wo2: e.tensor_mul(
                wcol[:, wo2:wo2 + 1], pcol[:, 4:5], pcol[:, 6:7]),
                reads=[pcol.buf], writes=[wc_buf], name=f"m.w2{b}.{tch}")
            items = []
            for k, o_ in ((0, o), (1, o2)):
                items.append((ps_g[:, 16 + k * 8:16 + k * 8 + 8], tri[:],
                              ohsb[:, o_:o_ + 8], k == 0, False))
            for k, o_ in ((0, o), (1, o2)):
                items.append((ps_g[0:1, 32 + k * 8:32 + k * 8 + 8],
                              G["ones_col"][:, 0:1], ohsb[:, o_:o_ + 8], False, False))

            def prev_items(eng, items=items):
                last = None
                for (oap, l, r, st, sp) in items:
                    last = eng.matmul(oap, l, r, start=st, stop=sp,
                                      skip_group_check=True)
                for k in range(2):
                    last = eng.matmul(ps_g[:, 16 + k * 8:16 + k * 8 + 8],
                                      G["ones_row_f"][0:1, :],
                                      prev[0:1, k * 8:(k + 1) * 8],
                                      start=False, stop=(k == 1),
                                      skip_group_check=True)
                return last
            P.op("pe", prev_items,
                 reads=[tri_buf, oh_buf, prev.buf, G["const_buf"]],
                 writes=[ps_g.buf], name=f"m.rk{b}.{tch}")
            P.op("dve", lambda e: e.tensor_add(prev[0:1, :], prev[0:1, :],
                                               ps_g[0:1, 32:48]),
                 reads=[prev.buf, ps_g.buf], writes=[prev.buf], name=f"m.pu{b}.{tch}")
            for k, o_ in ((0, o), (1, o2)):
                wo = (b * 2 + k) * ntch + tch
                P.op("dve", lambda e, k=k, o_=o_: e.tensor_mul(
                    t8[:, :], ps_g[:, 16 + k * 8:16 + k * 8 + 8], ohfb[:, o_:o_ + 8]),
                    reads=[ps_g.buf, ohf_buf], writes=[t8.buf], name=f"m.mr1{b}.{tch}.{k}")
                P.op("dve", lambda e, wo=wo: e.reduce_sum(
                    mrc[:, wo:wo + 1], t8[:, :], axis=AX.X),
                    reads=[t8.buf], writes=[mr_buf], name=f"m.mr2{b}.{tch}.{k}")

    if do_collective:
        ccin_buf = Buf("m_ccin")
        ccbuf = Buf("m_ccdma")
        zpad = Tile(P, "m_zpad", [2, 128], F32, es=es)
        P.op("pool", lambda e: e.memset(zpad[:, :], 0.0), writes=[zpad.buf], name="m.zp")
        P.dma("sp", [(cc["cc_in"], zpad[:, :], {})],
              reads=[zpad.buf], writes=[ccin_buf], name="m.ccz")
        P.dma("sp", [(cc["cc_in"][0:1, 0:8], prev[0:1, 0:8], {}),
                     (cc["cc_in"][1:2, 0:8], prev[0:1, 8:16], {})],
              reads=[prev.buf], writes=[ccin_buf], name="m.ccout")

        def collfn(e):
            return e.collective_compute(
                "AllGather", ALU.bypass, replica_groups=[list(range(8))],
                ins=[cc["cc_in"].opt()], outs=[cc["cc_out"].opt()])
        P.op("pool", collfn, reads=[ccin_buf], writes=[ccbuf], name="m.cc")
        P.dma("sp", [(ccsb[:, :], cc["cc_out"][:, 0:8], {})], reads=[ccbuf],
              writes=[ccsb.buf], name="m.ccin")
        mmg(P, [(ps_g[0:1, k * 8:(k + 1) * 8], selsb[:, k:k + 1], ccsb[:, :],
                 k == 0, k == 1) for k in range(2)],
            reads=[sel_buf, ccsb.buf], writes=[ps_g.buf], name="m.off")
        P.op("act", lambda e: e.copy(offs[0:1, :], ps_g[0:1, 0:16]),
             reads=[ps_g.buf], writes=[offs.buf], name="m.offe")
    else:
        P.op("pool", lambda e: e.memset(offs[:, :], 0.0), writes=[offs.buf], name="m.off0")

    hx = DT(P, "m_hx", NBLK, nt, F32, es=es)
    cmb = Tile(P, "m_cmb", [128, 8], F32, es=es)
    cmbbf = Tile(P, "m_cmbbf", [128, 8], BF16, es=es)
    combT = Tile(P, "m_combT", [8, nt], BF16, es=es)
    gph = P.sbuf("m_gph", [128, NBLK * MH // 2], BF16, es=es)
    gph_buf = Buf("m_gph")
    uph = P.sbuf("m_uph", [128, NBLK * MH // 2], BF16, es=es)
    uph_buf = Buf("m_uph")
    dph = P.sbuf("m_dph", [128, HHB * D], BF16, es=es)
    dph_buf = Buf("m_dph")
    cck = P.sbuf("m_cck", [128, HHB * CK], BF16, es=es)
    cck_bufs = [Buf(f"m_cck.{i}") for i in range(HHB)]
    xek = P.sbuf("m_xek", [128, NBLK * CK], BF16, es=es)
    xek_bufs = [Buf(f"m_xek.{i}") for i in range(NBLK)]
    cbs = Tile(P, "m_cbs", [128, CK], BF16, es=es)
    gsb = P.sbuf("m_gsb", [128, 2 * CK], BF16, es=es)
    gsb_bufs = [Buf("m_gsb.0"), Buf("m_gsb.1")]

    for b in range(nb):
        for tch in range(ntch):
            t0 = tch * 128
            o = oh_ap(b, 0, tch)
            o2 = oh_ap(b, 1, tch)
            for k, o_ in ((0, o), (1, o2)):
                wo = (b * 2 + k) * ntch + tch
                mmg(P, [(ps_a[:, 0:8], G["ones_row_f"][0:1, :],
                         offs[0:1, k * 8:(k + 1) * 8], True, True)],
                    reads=[offs.buf, G["const_buf"]], writes=[ps_a.buf],
                    name=f"m.ob{b}.{tch}.{k}")
                P.op("dve", lambda e, o_=o_: e.tensor_mul(
                    t8[:, :], ps_a[:, 0:8], ohfb[:, o_:o_ + 8]),
                    reads=[ps_a.buf, ohf_buf], writes=[t8.buf], name=f"m.os{b}.{tch}.{k}")
                P.op("dve", lambda e: e.reduce_sum(pcol[:, 0:1], t8[:, :], axis=AX.X),
                     reads=[t8.buf], writes=[pcol.buf], name=f"m.or{b}.{tch}.{k}")
                P.op("dve", lambda e, wo=wo: e.tensor_add(
                    pcol[:, 1:2], pcol[:, 0:1], mrc[:, wo:wo + 1]),
                    reads=[pcol.buf, mr_buf], writes=[pcol.buf], name=f"m.rt{b}.{tch}.{k}")
                P.op("dve", lambda e: e.tensor_single_scalar(
                    pcol[:, 2:3], pcol[:, 1:2], capacity + 0.5, ALU.is_lt),
                    reads=[pcol.buf], writes=[pcol.buf], name=f"m.kp{b}.{tch}.{k}")
                P.op("dve", lambda e, wo=wo, k=k: e.tensor_mul(
                    pcol[:, 3 + k:4 + k], pcol[:, 2:3], wcol[:, wo:wo + 1]),
                    reads=[pcol.buf, wc_buf], writes=[pcol.buf], name=f"m.wk{b}.{tch}.{k}")
            P.op("dve", lambda e, o=o: e.tensor_scalar_mul(
                cmb[:, :], ohfb[:, o:o + 8], pcol[:, 3:4]),
                reads=[ohf_buf, pcol.buf], writes=[cmb.buf], name=f"m.cb1{b}.{tch}")
            P.op("dve", lambda e, o2=o2: e.scalar_tensor_tensor(
                cmb[:, :], ohfb[:, o2:o2 + 8], pcol[:, 4:5], cmb[:, :],
                ALU.mult, ALU.add),
                reads=[ohf_buf, pcol.buf, cmb.buf], writes=[cmb.buf],
                name=f"m.cb2{b}.{tch}")
            P.op("dve", lambda e: e.tensor_copy(cmbbf[:, :], cmb[:, :]),
                 reads=[cmb.buf], writes=[cmbbf.buf], name=f"m.cbc{b}.{tch}")
            mmg(P, [(ps_y[0:8, (t0 % 512):(t0 % 512) + 128], cmbbf[:, :], G["eqm"][:],
                     t0 % 512 == 0, t0 % 512 == 384 or tch == ntch - 1)],
                reads=[cmbbf.buf, G["const_buf"]], writes=[ps_y.buf],
                name=f"m.ct{b}.{tch}")
            if t0 % 512 == 384 or tch == ntch - 1:
                c0 = (t0 // 512) * 512
                P.op("act", lambda e, c0=c0, t0=t0: e.copy(
                    combT[0:8, c0:c0 + (t0 % 512) + 128], ps_y[0:8, 0:(t0 % 512) + 128]),
                    reads=[ps_y.buf], writes=[combT.buf], name=f"m.cte{b}.{tch}")
        P.dma("sp", [(hx.ap(blk), x_in_ap(b, blk), {}) for blk in range(NBLK)],
              reads=XIN(b), writes=[hx.bufs[i] for i in range(NBLK)], name=f"m.hx{b}")
        P.dma("pool", [(xall[:, kb * nt:(kb + 1) * nt], x_in_ap(b, kb), {})
                       for kb in range(NBLK)],
              reads=XIN(b), writes=xall_bufs, name=f"m.xc{b}")
        for e_ in range(KEXP):
            for half in range(2):
                h0 = half * HHB
                P.dma("pool", [(gph[:], gpb[e_, half], {})],
                      reads=[wbf_bufs["gp"][e_]], writes=[gph_buf], name=f"m.gp{b}.{e_}.{half}")
                P.dma("sp", [(uph[:], upb[e_, half], {})],
                      reads=[wbf_bufs["up"][e_]], writes=[uph_buf], name=f"m.up{b}.{e_}.{half}")
                P.dma("act", [(dph[:].rearrange("p (a m) -> p a m", m=D),
                               dpb[e_][:, h0:h0 + HHB, :], {})],
                      reads=[wbf_bufs["dp"][e_]], writes=[dph_buf], name=f"m.dp{b}.{e_}.{half}")
                for ck in range(nck):
                    c0 = ck * CK
                    mmg(P, [(ps_a[:, 0:CK], esel[:, e_ * 128:(e_ + 1) * 128],
                             combT[0:8, c0:c0 + CK], True, True)],
                        reads=[combT.buf, esel_buf], writes=[ps_a.buf],
                        name=f"m.cbb{b}.{e_}.{half}.{ck}")
                    P.op("act", lambda e: e.copy(cbs[:, 0:CK], ps_a[:, 0:CK]),
                         reads=[ps_a.buf], writes=[cbs.buf],
                         name=f"m.cbe{b}.{e_}.{half}.{ck}")

                    def mk_xe(e, c0=c0):
                        for kb in range(NBLK):
                            last = e.tensor_mul(
                                xek[:, kb * CK:(kb + 1) * CK],
                                xall[:, kb * nt + c0: kb * nt + c0 + CK], cbs[:, 0:CK])
                        return last
                    P.op("dve", mk_xe, reads=xall_bufs + [cbs.buf],
                         writes=xek_bufs, name=f"m.xe{b}.{e_}.{half}.{ck}")
                    for hb in range(HHB):
                        psg = ps_g if hb % 2 == 0 else ps_g2
                        mmg(P, [(psg[:, 0:CK],
                                 gph[:, kb * (MH // 2) + hb * 128:
                                     kb * (MH // 2) + hb * 128 + 128],
                                 xall[:, kb * nt + c0: kb * nt + c0 + CK], st, sp)
                                for kb, (st, sp) in zip(range(NBLK), accg(NBLK))],
                            reads=[gph_buf] + xall_bufs, writes=[psg.buf],
                            name=f"m.g{b}.{e_}.{half}.{ck}.{hb}")
                        gslot = hb % 2
                        P.op("act", lambda e, psg=psg, gslot=gslot: e.activation(
                            gsb[:, gslot * CK:(gslot + 1) * CK], psg[:, 0:CK],
                            AF.Silu, bias=0.0, scale=1.0),
                            reads=[psg.buf], writes=[gsb_bufs[gslot]],
                            name=f"m.si{b}.{e_}.{half}.{ck}.{hb}")
                        mmg(P, [(ps_u[:, 0:CK],
                                 uph[:, kb * (MH // 2) + hb * 128:
                                     kb * (MH // 2) + hb * 128 + 128],
                                 xek[:, kb * CK:(kb + 1) * CK], st, sp)
                                for kb, (st, sp) in zip(range(NBLK), accg(NBLK))],
                            reads=[uph_buf] + xek_bufs, writes=[ps_u.buf],
                            name=f"m.u{b}.{e_}.{half}.{ck}.{hb}")
                        P.op("dve", lambda e, hb=hb, gslot=gslot: e.tensor_mul(
                            cck[:, hb * CK:(hb + 1) * CK],
                            gsb[:, gslot * CK:(gslot + 1) * CK], ps_u[:, 0:CK]),
                            reads=[gsb_bufs[gslot], ps_u.buf], writes=[cck_bufs[hb]],
                            name=f"m.c{b}.{e_}.{half}.{ck}.{hb}")
                    for dblk in range(NBLK):
                        psy = ps_y if dblk % 2 == 0 else ps_a
                        mmg(P, [(psy[:, 0:CK],
                                 dph[:, hb * D + dblk * 128: hb * D + dblk * 128 + 128],
                                 cck[:, hb * CK:(hb + 1) * CK], st, sp)
                                for hb, (st, sp) in zip(range(HHB), accg(HHB))],
                            reads=[dph_buf] + cck_bufs, writes=[psy.buf],
                            name=f"m.y{b}.{e_}.{half}.{ck}.{dblk}")
                        P.op("dve", lambda e, dblk=dblk, c0=c0, psy=psy: e.tensor_add(
                            hx.ap(dblk, c0, c0 + CK), hx.ap(dblk, c0, c0 + CK),
                            psy[:, 0:CK]),
                            reads=[hx.bufs[dblk], psy.buf], writes=[hx.bufs[dblk]],
                            name=f"m.ya{b}.{e_}.{half}.{ck}.{dblk}")
        emit_ln(P, C, hx, hx, None, None, None, EPS_MOD, f"m.pln.{b}")
        P.dma("sp", [(x_out_ap(b, blk), hx.ap(blk), {}) for blk in range(NBLK)],
              reads=[hx.bufs[blk] for blk in range(NBLK)],
              writes=XOUT(b), name=f"m.out{b}", is_output=FINAL)

    P.close_scope(es)



NB_CORE = 4
NT_FULL = 2048
NCORES = 8
CAPACITY = int(1.25 * 32 * 2048 / KEXP)

_prog_cache = {}


def _build_stage(stage):
    if stage in _prog_cache:
        return _prog_cache[stage]
    nt, nb = NT_FULL, NB_CORE
    nc = bass.Bass("TRN2", target_bir_lowering=False)
    x_in = nc.dram_tensor("x_in", [nb, 8, 128, nt], F32, kind="ExternalInput").ap()
    x_out = nc.dram_tensor("x_out", [nb, 8, 128, nt], F32, kind="ExternalOutput").ap()
    P = Prog(nc)
    G = {"nb": nb}
    emit_globals(P, G)
    in_ap = lambda b, blk: x_in[b] if blk is None else x_in[b, blk]
    out_ap = lambda b, blk: x_out[b, blk]
    if stage == "l0a":
        W = declare_l0a_weights(nc, nt)
        emit_l0_attn(P, G, W, in_ap, out_ap, nt)
    elif stage == "l0f":
        W = declare_l0f_weights(nc, nt)
        scratch = {
            "qs": nc.dram_tensor("f_qs", [nb, 8, 128, nt], BF16, kind="Internal").ap(),
            "ks": nc.dram_tensor("f_ks", [nb, 8, 128, nt], BF16, kind="Internal").ap(),
            "fs": nc.dram_tensor("f_fs", [nb, 8, 128, nt], BF16, kind="Internal").ap(),
            "pcs": nc.dram_tensor("f_pcs", [128, 8192], BF16, kind="Internal").ap(),
            "pcs2": nc.dram_tensor("f_pcs2", [128, 8192], BF16, kind="Internal").ap(),
        }
        emit_l0_ffn(P, G, W, in_ap, out_ap, nt, scratch)
    elif stage == "l1a":
        W = declare_l1a_weights(nc, nt)
        tsc = {
            "tks": nc.dram_tensor("t_tks", [nb, 8, 128, nt], BF16, kind="Internal").ap(),
            "tvs": nc.dram_tensor("t_tvs", [nb, nt // 128, 128, D], BF16,
                                  kind="Internal").ap(),
        }
        emit_l1_attn(P, G, W, in_ap, out_ap, nt, tsc)
    elif stage == "l1f":
        W = declare_l1f_weights(nc, nt)
        cc = {
            "cc_in": nc.dram_tensor("m_ccin", [2, 128], F32, kind="Internal").ap(),
            "cc_out": nc.dram_tensor("m_ccout", [16, 128], F32, kind="Internal").ap(),
        }
        emit_l1_ffn(P, G, W, in_ap, out_ap, nt, CAPACITY, True, cc)
    P.finish_and_build()
    nc.finalize()
    names = {"x_in"} | {ap.tensor.name for ap in W.values()}
    _prog_cache[stage] = (nc, names)
    return _prog_cache[stage]


def _sel_core(c):
    sel = np.zeros((16, 2), np.float32)
    for r in range(16):
        if r // 2 < c:
            sel[r, r % 2] = 1.0
    return sel


def kernel_unfused(**inputs):
    nt, nb = NT_FULL, NB_CORE
    inputs = {k: np.asarray(v) for k, v in inputs.items()}
    Hw = {}
    Hw.update(host_l0a(inputs, nt))
    Hw.update(host_l0f(inputs, nt))
    Hw.update(host_l1a(inputs, nt))
    Hw.update(host_l1f(inputs, nt))
    x = np.asarray(inputs["L0_attn_x"], np.float32)
    xd = np.ascontiguousarray(x.transpose(0, 2, 1)).reshape(NCORES, nb, 8, 128, nt)

    cur = [xd[c] for c in range(NCORES)]
    for stage in ["l0a", "l0f", "l1a", "l1f"]:
        nc, names = _build_stage(stage)
        in_maps = []
        for c in range(NCORES):
            m = {k: v for k, v in Hw.items() if k in names}
            m["x_in"] = cur[c]
            if "l1f_sel" in names:
                m["l1f_sel"] = _sel_core(c)
            in_maps.append(m)
        res = run_bass_kernel_spmd(nc, in_maps, core_ids=list(range(NCORES)))
        cur = [np.asarray(res.results[c]["x_out"]) for c in range(NCORES)]

    y = np.stack(cur).reshape(NCORES * nb, 1024, nt)
    return np.ascontiguousarray(y.transpose(0, 2, 1)).astype(np.float32)



def _build_fused():
    if "fused" in _prog_cache:
        return _prog_cache["fused"]
    nt, nb = NT_FULL, NB_CORE
    nc = bass.Bass("TRN2", target_bir_lowering=False)
    x_in = nc.dram_tensor("x_in", [nb, 8, 128, nt], F32, kind="ExternalInput").ap()
    x_out = nc.dram_tensor("x_out", [nb, 8, 128, nt], F32, kind="ExternalOutput").ap()
    xs1 = nc.dram_tensor("xs1", [nb, 8, 128, nt], F32, kind="Internal").ap()
    xs2 = nc.dram_tensor("xs2", [nb, 8, 128, nt], F32, kind="Internal").ap()
    xs3 = nc.dram_tensor("xs3", [nb, 8, 128, nt], F32, kind="Internal").ap()

    Wa = declare_l0a_weights(nc, nt)
    Wf = declare_l0f_weights(nc, nt)
    Wt = declare_l1a_weights(nc, nt)
    Wm = declare_l1f_weights(nc, nt)
    fscr = {
        "qs": nc.dram_tensor("f_qs", [nb, 8, 128, nt], BF16, kind="Internal").ap(),
        "ks": nc.dram_tensor("f_ks", [nb, 8, 128, nt], BF16, kind="Internal").ap(),
        "fs": nc.dram_tensor("f_fs", [nb, 8, 128, nt], BF16, kind="Internal").ap(),
        "pcs": nc.dram_tensor("f_pcs", [128, 8192], BF16, kind="Internal").ap(),
        "pcs2": nc.dram_tensor("f_pcs2", [128, 8192], BF16, kind="Internal").ap(),
    }
    tscr = {
        "tks": nc.dram_tensor("t_tks", [nb, 8, 128, nt], BF16, kind="Internal").ap(),
        "tvs": nc.dram_tensor("t_tvs", [nb, nt // 128, 128, D], BF16,
                              kind="Internal").ap(),
    }
    cc = {
        "cc_in": nc.dram_tensor("m_ccin", [2, 128], F32, kind="Internal").ap(),
        "cc_out": nc.dram_tensor("m_ccout", [16, 128], F32, kind="Internal").ap(),
    }

    P = Prog(nc)
    G = {"nb": nb}
    emit_globals(P, G)
    b1 = [Buf(f"xs1.{i}") for i in range(nb)]
    b2 = [Buf(f"xs2.{i}") for i in range(nb)]
    b3 = [Buf(f"xs3.{i}") for i in range(nb)]

    def mk(ap):
        return lambda b, blk: ap[b] if blk is None else ap[b, blk]

    ps_holder = {}
    emit_l0_attn(P, G, Wa, mk(x_in), mk(xs1), nt,
                 xio={"in": None, "out": b1, "final": False},
                 prestage_cb=lambda: ps_holder.update(emit_l1f_prestage(P, Wm)))
    emit_l0_ffn(P, G, Wf, mk(xs1), mk(xs2), nt, fscr,
                xio={"in": b1, "out": b2, "final": False})
    emit_l1_attn(P, G, Wt, mk(xs2), mk(xs3), nt, tscr,
                 xio={"in": b2, "out": b3, "final": False})
    emit_l1_ffn(P, G, Wm, mk(xs3), mk(x_out), nt, CAPACITY, True, cc,
                xio={"in": b3, "out": None, "final": True}, prestage=ps_holder)
    P.finish_and_build()
    nc.finalize()
    names = {"x_in"}
    for W in (Wa, Wf, Wt, Wm):
        names |= {ap.tensor.name for ap in W.values()}
    _prog_cache["fused"] = (nc, names)
    _prog_cache["fused_P"] = P
    return _prog_cache["fused"]


def kernel(**inputs):
    nt, nb = NT_FULL, NB_CORE
    inputs = {k: np.asarray(v) for k, v in inputs.items()}
    Hw = {}
    Hw.update(host_l0a(inputs, nt))
    Hw.update(host_l0f(inputs, nt))
    Hw.update(host_l1a(inputs, nt))
    Hw.update(host_l1f(inputs, nt))
    x = np.asarray(inputs["L0_attn_x"], np.float32)
    xd = np.ascontiguousarray(x.transpose(0, 2, 1)).reshape(NCORES, nb, 8, 128, nt)
    nc, names = _build_fused()
    in_maps = []
    for c in range(NCORES):
        m = {k: v for k, v in Hw.items() if k in names}
        m["x_in"] = xd[c]
        m["l1f_sel"] = _sel_core(c)
        in_maps.append(m)
    res = run_bass_kernel_spmd(nc, in_maps, core_ids=list(range(NCORES)))
    cur = [np.asarray(res.results[c]["x_out"]) for c in range(NCORES)]
    y = np.stack(cur).reshape(NCORES * nb, 1024, nt)
    return np.ascontiguousarray(y.transpose(0, 2, 1)).astype(np.float32)


# version 19
# speedup vs baseline: 1.1170x; 1.0088x over previous
import contextlib
import numpy as np
import ml_dtypes

import concourse.bass as bass
import concourse.mybir as mybir
from concourse.bass_utils import run_bass_kernel_spmd

F32 = mybir.dt.float32
BF16 = mybir.dt.bfloat16
AF = mybir.ActivationFunctionType
ALU = mybir.AluOpType
AX = mybir.AxisListType
BF = ml_dtypes.bfloat16

D = 1024
NBLK = 8
NH = 16
HD = 64
L0L = 4
FF = 4096
FBLK = 32
KEXP = 8
RR = 64
MH = 2048
MHB = 16
EPS_LN0 = 1e-12
EPS_MOD = 1e-5


class Buf:
    __slots__ = ("name", "writer", "readers", "fenced")

    def __init__(self, name="", nofence=False):
        self.name = name
        self.writer = None
        self.readers = []
        self.fenced = nofence


SAME_ENGINE_WAIT = {"pe": False, "act": True, "dve": True, "pool": True, "sp": True}


class Prog:
    ENGINES = ("sp", "pe", "act", "dve", "pool")

    def __init__(self, nc, n_dma_sems=24):
        self.nc = nc
        self.ops = {e: [] for e in self.ENGINES}
        self.ctr = {e: 0 for e in self.ENGINES}
        self.hw = {e: {} for e in self.ENGINES}
        self.dma_pools = {
            "sp": [f"dma_s{i}" for i in range(10)],
            "pool": [f"dma_p{i}" for i in range(10)],
            "act": [f"dma_a{i}" for i in range(4)],
        }
        self.dma_sems = [s for v in self.dma_pools.values() for s in v]
        self.dma_base = {s: 0 for s in self.dma_sems}
        self.dma_rr = {q: 0 for q in self.dma_pools}
        self.sem_handles = {}
        self.out_groups = []
        self.fence = {}
        self._es = contextlib.ExitStack()

    def scope(self):
        return contextlib.ExitStack()

    def close_scope(self, es):
        es.close()
        f = dict(self.fence)
        for e in self.ENGINES:
            if self.ctr[e] > 0:
                f["s_" + e] = max(f.get("s_" + e, 0), self.ctr[e])
        for s, v in self.dma_base.items():
            if v > 0:
                f[s] = max(f.get(s, 0), v)
        self.fence = f

    def sbuf(self, name, shape, dtype, es=None):
        return (es or self._es).enter_context(self.nc.sbuf_tensor(name, shape, dtype))

    def psum(self, name, shape, dtype=F32, es=None):
        return (es or self._es).enter_context(self.nc.psum_tensor(name, shape, dtype))

    def _want(self, waits, engine, sem, val):
        if sem == "s_" + engine and not SAME_ENGINE_WAIT[engine]:
            return
        if self.hw[engine].get(sem, 0) >= val:
            return
        if waits.get(sem, 0) < val:
            waits[sem] = val

    def _dep_waits(self, engine, reads, writes):
        waits = {}
        for b in reads:
            if b.writer:
                self._want(waits, engine, b.writer[0], b.writer[1])
        for b in writes:
            if b.writer is None and not b.fenced:
                b.fenced = True
                for s, v in self.fence.items():
                    self._want(waits, engine, s, v)
            if b.writer:
                self._want(waits, engine, b.writer[0], b.writer[1])
            for r in b.readers:
                self._want(waits, engine, r[0], r[1])
        return waits

    def _commit(self, token, reads, writes):
        for b in reads:
            b.readers.append(token)
        for b in writes:
            b.writer = token
            b.readers = []

    def op(self, engine, fn, reads=(), writes=(), name=""):
        waits = self._dep_waits(engine, reads, writes)
        self.ctr[engine] += 1
        val = self.ctr[engine]
        sem = "s_" + engine
        self.ops[engine].append((waits, fn, sem, name))
        for s, v in waits.items():
            self.hw[engine][s] = max(self.hw[engine].get(s, 0), v)
        if not SAME_ENGINE_WAIT[engine]:
            self.hw[engine][sem] = val
        self._commit((sem, val), reads, writes)
        return (sem, val)

    def dma(self, queue, pairs, reads=(), writes=(), name="", is_output=False):
        pool = self.dma_pools[queue]
        sem = pool[self.dma_rr[queue]]
        self.dma_rr[queue] = (self.dma_rr[queue] + 1) % len(pool)
        base = self.dma_base[sem]
        waits = self._dep_waits(queue, reads, writes)
        if base > 0:
            self._want(waits, queue, sem, base)
        fin = base + 16 * len(pairs)
        self.dma_base[sem] = fin

        def fn(eng, pairs=pairs, sem_name=sem):
            last = None
            for (o, i, kw) in pairs:
                last = eng.dma_start(o, i, **kw).then_inc(self.sem_handles[sem_name], 16)
            return last

        self.ops[queue].append((waits, fn, None, name))
        for s, v in waits.items():
            self.hw[queue][s] = max(self.hw[queue].get(s, 0), v)
        self._commit((sem, fin), reads, writes)
        if is_output:
            self.out_groups.append((sem, fin))
        return (sem, fin)

    def finish_and_build(self):
        waits = {}
        for s, v in self.out_groups:
            waits[s] = max(waits.get(s, 0), v)
        for s, v in self.dma_base.items():
            if v > 0 and self.hw["sp"].get(s, 0) < v:
                waits[s] = max(waits.get(s, 0), v)
        self.ops["sp"].append((waits, lambda eng: eng.nop(), None, "final"))
        nc = self.nc
        for s in ["s_" + e for e in self.ENGINES] + self.dma_sems:
            self.sem_handles[s] = self._es.enter_context(nc.semaphore(s))
        attr = {"sp": "sync", "pe": "tensor", "act": "scalar",
                "dve": "vector", "pool": "gpsimd"}
        self.inst_map = []
        with nc.Block() as blk:
            for key in self.ENGINES:
                def body(eng, key=key):
                    for waits_, fn, incsem, name_ in self.ops[key]:
                        w0 = nc.n_instructions()
                        for s, v in waits_.items():
                            eng.wait_ge(self.sem_handles[s], v)
                        s0 = nc.n_instructions()
                        last = fn(eng)
                        if incsem is not None:
                            last.then_inc(self.sem_handles[incsem], 1)
                        self.inst_map.append((w0, s0, nc.n_instructions(), key, name_))
                getattr(blk, attr[key])(body)
        self._es.close()


class DT:

    def __init__(self, P, name, nblk, width, dtype, es=None):
        self.t = P.sbuf(name, [128, nblk * width], dtype, es=es)
        self.nblk = nblk
        self.width = width
        self.bufs = [Buf(f"{name}.{i}") for i in range(nblk)]

    def ap(self, blk, c0=0, c1=None):
        if c1 is None:
            c1 = self.width
        return self.t[:, blk * self.width + c0: blk * self.width + c1]


def mmg(P, items, reads, writes, name="", skip_check=False):
    def fn(eng):
        last = None
        for (o, l, r, st, sp) in items:
            last = eng.matmul(o, l, r, start=st, stop=sp, skip_group_check=skip_check)
        return last
    return P.op("pe", fn, reads=reads, writes=writes, name=name)


def accg(n):
    return [(i == 0, i == n - 1) for i in range(n)]


class Tile:

    def __init__(self, P, name, shape, dtype, es=None, psum=False):
        self.t = (P.psum(name, shape, dtype, es=es) if psum
                  else P.sbuf(name, shape, dtype, es=es))
        self.buf = Buf(name)

    def __getitem__(self, sl):
        return self.t[sl]



def emit_ln(P, C, xy, out, outbf, g_col, b_col, eps, tag, chunk_cb=None,
            interleave=None):
    nt = xy.width
    CKS = min(256, nt)
    CKA = min(512, nt)
    nck_ = nt // CKA
    cbf = C["cbf"]
    rows, rows_buf = C["rows"], C["rows_buf"]
    rbf, rbf_bufs = C["rows_bf"], C["rows_bf_bufs"]
    ps_st, ps_st_buf = C["ps_st"], C["ps_st_buf"]
    ps_a, ps_a_buf = C["ps_bca"], C["ps_bca_buf"]
    ps_c, ps_c_buf = C["ps_bcc"], C["ps_bcc_buf"]
    ones_col, ones_row, const_buf = C["ones_col"], C["ones_row"], C["const_buf"]
    inv_d = 1.0 / D

    def stats(ck):
        c0 = ck * CKA
        for sub in range(CKA // CKS):
            s0 = c0 + sub * CKS
            r = sub * CKS
            P.op("act", lambda e, s0=s0: e.copy(
                cbf.t[:].rearrange("p (a c) -> p a c", c=CKS),
                xy.t[:].rearrange("p (a c) -> p a c", c=xy.width)[:, :, s0:s0 + CKS]),
                reads=xy.bufs, writes=cbf.bufs, name=f"{tag}.cast")
            mmg(P, [(ps_st[0:1, 0:CKS], ones_col[:, 0:1], cbf.ap(b), st, sp)
                    for b, (st, sp) in zip(range(NBLK), accg(NBLK))],
                reads=[cbf.bufs[b] for b in range(NBLK)] + [const_buf],
                writes=[ps_st_buf], name=f"{tag}.s1")
            P.op("dve", lambda e: e.tensor_mul(cbf.t[:], cbf.t[:], cbf.t[:]),
                 reads=cbf.bufs, writes=cbf.bufs, name=f"{tag}.sq")
            P.op("act", lambda e, r=r: e.copy(rows[0:1, r:r + CKS], ps_st[0:1, 0:CKS]),
                 reads=[ps_st_buf], writes=[rows_buf], name=f"{tag}.s1ev")
            mmg(P, [(ps_a[0:1, 0:CKS], ones_col[:, 0:1], cbf.ap(b), st, sp)
                    for b, (st, sp) in zip(range(NBLK), accg(NBLK))],
                reads=[cbf.bufs[b] for b in range(NBLK)] + [const_buf],
                writes=[ps_a_buf], name=f"{tag}.s2")
            P.op("act", lambda e, r=r: e.copy(rows[0:1, CKA + r:CKA + r + CKS],
                                              ps_a[0:1, 0:CKS]),
                 reads=[ps_a_buf], writes=[rows_buf], name=f"{tag}.s2ev")

        r0 = rows[0:1, 0:CKA]
        r1 = rows[0:1, CKA:2 * CKA]
        msq = rows[0:1, 2 * CKA:3 * CKA]
        P.op("dve", lambda e: e.tensor_scalar_mul(r0, r0, inv_d),
             reads=[rows_buf], writes=[rows_buf], name=f"{tag}.m")
        P.op("dve", lambda e: e.tensor_scalar_mul(r1, r1, inv_d),
             reads=[rows_buf], writes=[rows_buf], name=f"{tag}.e2")
        P.op("dve", lambda e: e.tensor_mul(msq, r0, r0),
             reads=[rows_buf], writes=[rows_buf], name=f"{tag}.msq")
        P.op("dve", lambda e: e.tensor_sub(r1, r1, msq),
             reads=[rows_buf], writes=[rows_buf], name=f"{tag}.var")
        P.op("dve", lambda e: e.tensor_scalar_add(r1, r1, float(eps)),
             reads=[rows_buf], writes=[rows_buf], name=f"{tag}.veps")
        P.op("act", lambda e: e.activation(r1, r1, AF.Sqrt, bias=0.0, scale=1.0),
             reads=[rows_buf], writes=[rows_buf], name=f"{tag}.sd")
        P.op("dve", lambda e: e.reciprocal(r1, r1),
             reads=[rows_buf], writes=[rows_buf], name=f"{tag}.a")
        P.op("dve", lambda e: e.scalar_tensor_tensor(r0, r0, -1.0, r1, ALU.mult, ALU.mult),
             reads=[rows_buf], writes=[rows_buf], name=f"{tag}.c")
        sl = (ck % 2) * 2 * CKA
        P.op("dve", lambda e, sl=sl: e.tensor_copy(rbf[0:1, sl:sl + CKA], r1),
             reads=[rows_buf], writes=[rbf_bufs[ck % 2]], name=f"{tag}.abf")
        P.op("dve", lambda e, sl=sl: e.tensor_copy(rbf[0:1, sl + CKA:sl + 2 * CKA], r0),
             reads=[rows_buf], writes=[rbf_bufs[ck % 2]], name=f"{tag}.cbf")

    def apply(ck):
        c0 = ck * CKA
        sl = (ck % 2) * 2 * CKA
        mmg(P, [(ps_a[:, 0:CKA], ones_row[0:1, 0:128], rbf[0:1, sl:sl + CKA],
                 True, True)],
            reads=[rbf_bufs[ck % 2], const_buf], writes=[ps_a_buf], name=f"{tag}.bca")
        mmg(P, [(ps_c[:, 0:CKA], ones_row[0:1, 0:128], rbf[0:1, sl + CKA:sl + 2 * CKA],
                 True, True)],
            reads=[rbf_bufs[ck % 2], const_buf], writes=[ps_c_buf], name=f"{tag}.bcc")
        def o3(t, w):
            return t[:].rearrange("p (a c) -> p a c", c=w)[:, :, c0:c0 + CKA]
        bca_b = ps_a[:, 0:CKA].rearrange("p (a c) -> p a c", a=1).broadcast_to(
            [128, NBLK, CKA])
        bcc_b = ps_c[:, 0:CKA].rearrange("p (a c) -> p a c", a=1).broadcast_to(
            [128, NBLK, CKA])
        P.op("dve", lambda e: e.tensor_mul(
            o3(out.t, out.width), o3(xy.t, xy.width), bca_b),
            reads=list(xy.bufs) + [ps_a_buf], writes=list(out.bufs),
            name=f"{tag}.ap1")
        P.op("dve", lambda e: e.tensor_add(
            o3(out.t, out.width), o3(out.t, out.width), bcc_b),
            reads=list(out.bufs) + [ps_c_buf], writes=list(out.bufs),
            name=f"{tag}.ap2")
        if g_col is not None:
            for blk in range(NBLK):
                gap, gbufs = g_col(blk)
                bap, bbufs = b_col(blk)
                P.op("dve", lambda e, b=blk, c0=c0, gap=gap, bap=bap: e.tensor_scalar(
                    out.ap(b, c0, c0 + CKA), out.ap(b, c0, c0 + CKA),
                    gap, bap, ALU.mult, ALU.add),
                    reads=[out.bufs[blk]] + gbufs + bbufs, writes=[out.bufs[blk]],
                    name=f"{tag}.ap3")
        if outbf is not None:
            P.op("dve", lambda e: e.tensor_copy(
                o3(outbf.t, outbf.width), o3(out.t, out.width)),
                reads=list(out.bufs), writes=list(outbf.bufs), name=f"{tag}.bf")

    if interleave is not None:
        for ck in range(nck_):
            interleave(ck)
            stats(ck)
            if ck >= 1:
                apply(ck - 1)
            if ck >= 2 and chunk_cb is not None:
                chunk_cb(ck - 2)
        apply(nck_ - 1)
        if chunk_cb is not None:
            if nck_ >= 2:
                chunk_cb(nck_ - 2)
            chunk_cb(nck_ - 1)
        return
    stats(0)
    if nck_ > 1:
        stats(1)
    apply(0)
    for ck in range(1, nck_):
        if ck + 1 < nck_:
            stats(ck + 1)
        apply(ck)
        if chunk_cb is not None:
            chunk_cb(ck - 1)
    if chunk_cb is not None:
        chunk_cb(nck_ - 1)


def make_ln_common(P, nt, es, G, pfx=""):
    C = {}
    C["cbf"] = DT(P, pfx + "ln_cbf", NBLK, min(256, nt), BF16, es=es)
    C["rows"] = P.sbuf(pfx + "ln_rows", [1, 3 * min(512, nt)], F32, es=es)
    C["rows_buf"] = Buf("ln_rows")
    C["rows_bf"] = P.sbuf(pfx + "ln_rows_bf", [1, 4 * min(512, nt)], BF16, es=es)
    C["rows_bf_bufs"] = [Buf("ln_rows_bf0"), Buf("ln_rows_bf1")]
    C["ps_st"] = P.psum(pfx + "ps_st", [1, 512], F32, es=es)
    C["ps_st_buf"] = Buf("ps_st")
    C["ps_bca"] = P.psum(pfx + "ps_bca", [128, 512], F32, es=es)
    C["ps_bca_buf"] = Buf("ps_bca")
    C["ps_bcc"] = P.psum(pfx + "ps_bcc", [128, 512], F32, es=es)
    C["ps_bcc_buf"] = Buf("ps_bcc")
    C["ones_col"] = G["ones_col"]
    C["ones_row"] = G["ones_row"]
    C["const_buf"] = G["const_buf"]
    return C



def emit_globals(P, G):
    ones_col = P.sbuf("g_ones_col", [128, 4], BF16)
    ones_row = P.sbuf("g_ones_row", [1, 512], BF16)
    ones_col_f = P.sbuf("g_ones_col_f", [128, 1], F32)
    ones_row_f = P.sbuf("g_ones_row_f", [1, 128], F32)
    eqm = P.sbuf("g_eqm", [128, 128], BF16)
    eps_row = P.sbuf("g_eps_row", [1, 16], BF16)
    es_tmp = P.scope()
    eqf = P.sbuf("g_eqf", [128, 128], F32, es=es_tmp)
    const_buf = Buf("g_const")
    P.op("pool", lambda e: e.memset(ones_col[:], 1.0), writes=[const_buf], name="g.oc")
    P.op("pool", lambda e: e.memset(ones_row[:], 1.0), writes=[const_buf], name="g.or")
    P.op("pool", lambda e: e.memset(eps_row[:], 1e-6), writes=[const_buf], name="g.eps")
    P.op("pool", lambda e: e.memset(ones_col_f[:], 1.0), writes=[const_buf], name="g.ocf")
    P.op("pool", lambda e: e.memset(ones_row_f[:], 1.0), writes=[const_buf], name="g.orf")
    P.op("pool", lambda e: e.iota(eqf[:], [[1, 128]], base=0, channel_multiplier=-1, allow_small_or_imprecise_dtypes=True),
         writes=[const_buf], name="g.iota")
    P.op("dve", lambda e: e.tensor_single_scalar(eqm[:], eqf[:], 0.0, ALU.is_equal),
         reads=[const_buf], writes=[const_buf], name="g.eqm")
    P.close_scope(es_tmp)
    G["ones_col"] = ones_col
    G["ones_row"] = ones_row
    G["ones_col_f"] = ones_col_f
    G["ones_row_f"] = ones_row_f
    G["eqm"] = eqm
    G["eps_row"] = eps_row
    G["const_buf"] = const_buf


class ParamPack:

    def __init__(self, P, name, specs, queue="sp", es=None):
        self.off = {}
        tot = 0
        for k, ap, w in specs:
            self.off[k] = (tot, w)
            tot += w
        self.t = P.sbuf(name, [128, tot], F32, es=es)
        self.buf = Buf(name)
        pairs = []
        for k, ap, w in specs:
            o, _ = self.off[k]
            pairs.append((self.t[:, o:o + w], ap, {}))
        P.dma(queue, pairs, writes=[self.buf], name=f"{name}.load")

    def col(self, key, i):
        o, w = self.off[key]
        assert i < w
        return self.t[:, o + i:o + i + 1]

    def ap(self, key):
        o, w = self.off[key]
        return self.t[:, o:o + w]



def emit_l0_attn(P, G, W, x_in_ap, x_out_ap, nt, upto=None, xio=None, prestage_cb=None):
    xio = xio or {}
    XIN = (lambda b: [xio["in"][b]]) if xio.get("in") else (lambda b: [])
    XOUT = (lambda b: [xio["out"][b]]) if xio.get("out") else (lambda b: [])
    FINAL = xio.get("final", True)
    """L0_attn block (+ module postLN). x_in_ap(b, blk)/x_out_ap(b, blk) -> dram APs."""
    nb = G["nb"]
    CK = min(512, nt)
    nck = nt // CK
    nsub = nt // 128
    es = P.scope()

    hx = DT(P, "a_hx", NBLK, nt, F32, es=es)
    ctmp = Tile(P, "a_ctmp", [128, CK], F32, es=es)
    ps1 = Tile(P, "a_ps1", [128, 512], F32, es=es, psum=True)
    ps2 = Tile(P, "a_ps2", [128, 512], F32, es=es, psum=True)
    ps3 = Tile(P, "a_ps3", [128, 512], F32, es=es, psum=True)
    ps4 = Tile(P, "a_ps4", [128, 512], F32, es=es, psum=True)
    C = make_ln_common(P, nt, es, G, "a_")

    PP = ParamPack(P, "a_pp", [
        ("w1c", W["conv1_w_cols"], 160),
        ("w2c", W["conv2_w_cols"], 160),
        ("b1c", W["conv1_b_cols"], 32),
        ("b2c", W["conv2_b_cols"], 8),
        ("ln0g", W["ln0_g_cols"], 8),
        ("ln0b", W["ln0_b_cols"], 8),
        ("bq", W["bq_cols"], 32),
        ("bk", W["bk_cols"], 32),
        ("lnAg", W["lnA_g_cols"], 32),
        ("lnAb", W["lnA_b_cols"], 32),
        ("b1f", W["ffn_b1_cols"], 32),
        ("b2f", W["ffn_b2_cols"], 8),
        ("ln2g", W["ln2_g_cols"], 8),
        ("ln2b", W["ln2_b_cols"], 8),
    ], es=es)
    bvr = P.sbuf("a_bvr", [1, D], BF16, es=es)
    bvr_buf = Buf("a_bvr")
    S_sw = P.sbuf("a_Ssw", [128, 128], BF16, es=es)
    S_buf = Buf("a_Ssw")
    P.dma("sp", [(S_sw[:], W["S_swap"], {})], writes=[S_buf], name="a.S")

    wkb = P.nc.dram_tensor("a_wkb", [L0L, 128, NBLK, D], BF16, kind="Internal").ap()
    wvb = P.nc.dram_tensor("a_wvb", [L0L, 128, NBLK, D], BF16, kind="Internal").ap()
    wqb = P.nc.dram_tensor("a_wqb", [L0L, 128, NBLK, D], BF16, kind="Internal").ap()
    w1b = P.nc.dram_tensor("a_w1b", [FBLK, 128, NBLK * 128], BF16, kind="Internal").ap()
    w2b = P.nc.dram_tensor("a_w2b", [NBLK, 128, FBLK * 128], BF16, kind="Internal").ap()
    wkb_bufs = [Buf(f"a_wkb{l}", nofence=True) for l in range(L0L)]
    wvb_bufs = [Buf(f"a_wvb{l}", nofence=True) for l in range(L0L)]
    wqb_bufs = [Buf(f"a_wqb{l}", nofence=True) for l in range(L0L)]
    w1b_bufs = [Buf(f"a_w1b{i}", nofence=True) for i in range(4)]
    w2b_bufs = [Buf(f"a_w2b{i}", nofence=True) for i in range(4)]
    for l in range(L0L):
        P.dma("pool", [(wkb[l], W["Wk_r"][l], {})], writes=[wkb_bufs[l]],
              name=f"a.ck{l}")
        P.dma("pool", [(wvb[l], W["Wv_r"][l], {})], writes=[wvb_bufs[l]],
              name=f"a.cv{l}")
        P.dma("pool", [(wqb[l], W["Wq_r"][l], {})], writes=[wqb_bufs[l]],
              name=f"a.cq{l}")
        h = l
        nq1 = FBLK // 4
        P.dma("pool", [(w1b[h * nq1 + j].rearrange("p (a m) -> p a m", m=128),
                        W["W1_r"][:, :, (h * nq1 + j) * 128:(h * nq1 + j + 1) * 128],
                        {}) for j in range(nq1)],
              writes=[w1b_bufs[h]], name=f"a.c1w{h}")
        P.dma("pool", [(w2b[h * 2 + j].rearrange("p (f m) -> p f m", m=128)
                        [:, fh * 16:(fh + 1) * 16],
                        W["W2_r"][:, fh * 16:(fh + 1) * 16,
                                  (h * 2 + j) * 128:(h * 2 + j + 1) * 128], {})
                       for j in range(2) for fh in range(2)],
              writes=[w2b_bufs[h]], name=f"a.c2w{h}")
    bvb = P.nc.dram_tensor("a_bvb", [1, L0L * D], BF16, kind="Internal").ap()
    bvb_buf = Buf("a_bvb", nofence=True)
    P.dma("pool", [(bvb, W["bv_rows"], {})], writes=[bvb_buf], name="a.cbv")
    if prestage_cb is not None:
        prestage_cb()

    for b in range(nb):
        esc = P.scope()
        xpad = P.sbuf(f"a_xpad{b}", [128, NBLK * (nt + 4)], BF16, es=esc)
        xpad_bufs = [Buf(f"a_xpad.{i}") for i in range(NBLK)]
        cg = P.sbuf(f"a_cg{b}", [128, 4 * (nt + 4)], BF16, es=esc)
        cg_bufs = [Buf(f"a_cg.{i}") for i in range(4)]
        diag1 = P.sbuf(f"a_diag1{b}", [128, 20 * 128], BF16, es=esc)
        diag1_buf = Buf("a_diag1")
        diag2 = P.sbuf(f"a_diag2{b}", [128, 20 * 128], BF16, es=esc)
        diag2_buf = Buf("a_diag2")

        def zero_borders(e):
            for blk in range(NBLK):
                o = blk * (nt + 4)
                e.memset(xpad[:, o:o + 2], 0.0)
                e.memset(xpad[:, o + nt + 2:o + nt + 4], 0.0)
            for m in range(4):
                o = m * (nt + 4)
                e.memset(cg[:, o:o + 2], 0.0)
                last = e.memset(cg[:, o + nt + 2:o + nt + 4], 0.0)
            return last
        P.op("pool", zero_borders, writes=xpad_bufs + cg_bufs, name=f"a.zb{b}")

        for blk in range(NBLK):
            P.dma("sp" if blk % 2 == 0 else "act",
                  [(hx.ap(blk), x_in_ap(b, blk), {})],
                  reads=XIN(b), writes=[hx.bufs[blk]], name=f"a.x{b}.{blk}")
            P.op("pool", lambda e, blk=blk: e.tensor_copy(
                xpad[:, blk * (nt + 4) + 2: blk * (nt + 4) + 2 + nt], hx.ap(blk)),
                reads=[hx.bufs[blk]], writes=[xpad_bufs[blk]], name=f"a.xpc{b}.{blk}")

        for blk in range(NBLK):
            def build_diag(e, blk=blk, diag=diag1, key="w1c"):
                for i in range(20):
                    last = e.tensor_scalar_mul(
                        diag[:, i * 128:(i + 1) * 128], G["eqm"][:],
                        PP.col(key, blk * 20 + i))
                return last
            P.op("dve", build_diag, reads=[G["const_buf"], PP.buf],
                 writes=[diag1_buf], name=f"a.d1.{b}.{blk}")
            def build_diag2(e, blk=blk):
                for i in range(20):
                    last = e.tensor_scalar_mul(
                        diag2[:, i * 128:(i + 1) * 128], G["eqm"][:],
                        PP.col("w2c", blk * 20 + i))
                return last
            P.op("dve", build_diag2, reads=[G["const_buf"], PP.buf],
                 writes=[diag2_buf], name=f"a.d2.{b}.{blk}")

            for m in range(4):
                for ck in range(nck):
                    c0 = ck * CK
                    ps = ps1 if (ck % 2 == 0) else ps2
                    mmg(P, [(ps[:, 0:CK], diag1[:, (m * 5 + j) * 128:(m * 5 + j + 1) * 128],
                             xpad[:, blk * (nt + 4) + j + c0: blk * (nt + 4) + j + c0 + CK],
                             st, sp) for j, (st, sp) in zip(range(5), accg(5))],
                        reads=[diag1_buf, xpad_bufs[blk]], writes=[ps.buf],
                        name=f"a.c1.{b}.{blk}.{m}.{ck}")
                    P.op("act", lambda e, m=m, c0=c0, ps=ps, blk=blk: e.activation(
                        cg[:, m * (nt + 4) + 2 + c0: m * (nt + 4) + 2 + c0 + CK],
                        ps[:, 0:CK], AF.Gelu, bias=PP.col("b1c", blk * 4 + m), scale=1.0),
                        reads=[ps.buf, PP.buf], writes=[cg_bufs[m]],
                        name=f"a.c1e.{b}.{blk}.{m}.{ck}")
            for ck in range(nck):
                c0 = ck * CK
                psc = ps3 if ck % 2 == 0 else ps4
                items = []
                flags = accg(20)
                for m in range(4):
                    for j in range(5):
                        i = m * 5 + j
                        st, sp = flags[i]
                        items.append((psc[:, 0:CK],
                                      diag2[:, i * 128:(i + 1) * 128],
                                      cg[:, m * (nt + 4) + j + c0: m * (nt + 4) + j + c0 + CK],
                                      st, sp))
                mmg(P, items, reads=[diag2_buf] + cg_bufs, writes=[psc.buf],
                    name=f"a.c2.{b}.{blk}.{ck}")
                P.op("act", lambda e, c0=c0, blk=blk, psc=psc: e.copy(
                    ctmp[:, 0:CK], psc[:, 0:CK]),
                    reads=[psc.buf], writes=[ctmp.buf], name=f"a.c2e.{b}.{blk}.{ck}")
                P.op("dve", lambda e, c0=c0, blk=blk: e.scalar_tensor_tensor(
                    hx.ap(blk, c0, c0 + CK), ctmp[:, 0:CK], PP.col("b2c", blk),
                    hx.ap(blk, c0, c0 + CK), ALU.add, ALU.add),
                    reads=[ctmp.buf, PP.buf, hx.bufs[blk]], writes=[hx.bufs[blk]],
                    name=f"a.c2a.{b}.{blk}.{ck}")
        P.close_scope(esc)

        esm = P.scope()
        hbf = DT(P, f"a_hbf{b}", NBLK, nt, BF16, es=esm)
        esi = P.scope()
        wA = P.sbuf(f"a_wA{b}", [128, NBLK * D], BF16, es=esi)
        wA_buf = Buf("a_wA")
        wB = P.sbuf(f"a_wB{b}", [128, NBLK * D], BF16, es=esi)
        wB_buf = Buf("a_wB")
        cosck = P.sbuf(f"a_cos{b}", [128, NBLK * CK], BF16, es=esi)
        cos_buf = Buf("a_cos")
        sinck = P.sbuf(f"a_sin{b}", [128, NBLK * CK], BF16, es=esi)
        sin_buf = Buf("a_sin")
        wQ = P.sbuf(f"a_wQ{b}", [128, NBLK * D], BF16, es=esi)
        wQ_buf = Buf("a_wQ")
        ksb = DT(P, f"a_k{b}", NBLK, CK, BF16, es=esi)
        qsb = ksb
        tmp1 = Tile(P, f"a_tmp1{b}", [128, CK], BF16, es=esi)
        krt = P.sbuf(f"a_krt{b}", [128, 2 * D], BF16, es=esi)
        krt_bufs = [Buf("a_krt.0"), Buf("a_krt.1")]
        vtk = P.sbuf(f"a_vtk{b}", [128, (CK // 128) * D], BF16, es=esi)
        vtk_bufs = [Buf(f"a_vtk.{i}") for i in range(CK // 128)]
        kvsb = Tile(P, f"a_kv{b}", [128, 512], BF16, es=esi)
        ps_kv = Tile(P, f"a_pskv{b}", [128, 512], F32, es=esi, psum=True)
        kmacc = Tile(P, f"a_kmacc{b}", [128, NBLK * (nck + 2)], F32, es=esi)
        kmbf = Tile(P, f"a_kmbf{b}", [128, NBLK], BF16, es=esi)
        zbf = Tile(P, f"a_zbf{b}", [16, CK], BF16, es=esi)
        kmsel = Tile(P, f"a_kmsel{b}", [128, 128], BF16, es=esi)
        zsel = Tile(P, f"a_zsel{b}", [16, 1024], BF16, es=esi)
        zbc = tmp1.t
        zbc_bufs = [tmp1.buf]

        P.dma("sp", [(zsel[:, :], W["Z_sel"], {})], writes=[zsel.buf], name=f"a.zsel{b}")
        def rope_inplace(P, dst, blk, psr, tag):
            mmg(P, [(psr[:, 0:CK], S_sw[:], dst.ap(blk), True, True)],
                reads=[S_buf, dst.bufs[blk]], writes=[psr.buf], name=f"{tag}.xr")
            P.op("dve", lambda e, blk=blk, psr=psr: e.tensor_mul(
                tmp1[:, 0:CK], psr[:, 0:CK], sinck[:, blk * CK:(blk + 1) * CK]),
                reads=[psr.buf, sin_buf], writes=[tmp1.buf], name=f"{tag}.s")
            P.op("dve", lambda e, blk=blk: e.tensor_mul(
                dst.ap(blk), dst.ap(blk), cosck[:, blk * CK:(blk + 1) * CK]),
                reads=[dst.bufs[blk], cos_buf], writes=[dst.bufs[blk]], name=f"{tag}.c")
            P.op("dve", lambda e, blk=blk: e.tensor_add(
                dst.ap(blk), dst.ap(blk), tmp1[:, 0:CK]),
                reads=[dst.bufs[blk], tmp1.buf], writes=[dst.bufs[blk]], name=f"{tag}.a")

        def emit_layer_loads(l):
            P.dma("sp", [(wA[:].rearrange("p (a m) -> p a m", m=D), wkb[l], {})],
                  reads=[wkb_bufs[l]], writes=[wA_buf], name=f"a.wk{b}{l}")
            P.dma("act", [(wB[:].rearrange("p (a m) -> p a m", m=D), wvb[l], {})],
                  reads=[wvb_bufs[l]], writes=[wB_buf], name=f"a.wv{b}{l}")
            P.dma("sp", [(wQ[:].rearrange("p (a m) -> p a m", m=D), wqb[l], {})],
                  reads=[wqb_bufs[l]], writes=[wQ_buf], name=f"a.wq{b}{l}")
            P.dma("sp", [(bvr[0:1, :], bvb[0:1, l * D:(l + 1) * D], {})],
                  reads=[bvb_buf], writes=[bvr_buf], name=f"a.bv{b}{l}")

        def k_ck(l, ck):
            c0 = ck * CK
            P.dma("sp", [(cosck[:], W["cos_t"][ck], {})],
                  writes=[cos_buf], name=f"a.cos.{b}.{l}.k{ck}")
            P.dma("act", [(sinck[:], W["sin_t"][ck], {})],
                  writes=[sin_buf], name=f"a.sin.{b}.{l}.k{ck}")
            for blk in range(NBLK):
                psk = (ps1, ps4, ps2)[blk % 3]
                mmg(P, [(psk[:, 0:CK], wA[:, kb * D + blk * 128:kb * D + blk * 128 + 128],
                         hbf.ap(kb, c0, c0 + CK), st, sp)
                        for kb, (st, sp) in zip(range(NBLK), accg(NBLK))],
                    reads=[wA_buf] + [hbf.bufs[kb] for kb in range(NBLK)],
                    writes=[psk.buf], name=f"a.kmm.{b}.{l}.{ck}.{blk}")
                P.op("act", lambda e, blk=blk, l=l, ck=ck, psk=psk: e.activation(
                    ksb.ap(blk), psk[:, 0:CK], AF.Gelu,
                    bias=PP.col("bk", l * 8 + blk), scale=1.0,
                    accum_out=kmacc[:, blk * (nck + 2) + ck: blk * (nck + 2) + ck + 1]),
                    reads=[psk.buf, PP.buf], writes=[ksb.bufs[blk], kmacc.buf],
                    name=f"a.kev.{b}.{l}.{ck}.{blk}")
                if blk == NBLK - 1:
                    P.op("dve", lambda e: e.tensor_scalar_add(
                        ksb.t[:], ksb.t[:], 0.21),
                        reads=ksb.bufs, writes=ksb.bufs,
                        name=f"a.k21.{b}.{l}.{ck}")
            vpairs = [(s, half) for s in range(CK // 128) for half in range(2)]
            for blk in range(NBLK):
                if blk < len(vpairs):
                    s, half = vpairs[blk]
                    t0 = s * 128
                    items = [(ps3[:, 0:512],
                              hbf.ap(kb, c0 + t0, c0 + t0 + 128),
                              wB[:, kb * D + half * 512:kb * D + half * 512 + 512],
                              st, False) for kb, (st, _) in zip(range(NBLK), accg(NBLK))]
                    items.append((ps3[:, 0:512], G["ones_row"][0:1, 0:128],
                                  bvr[0:1, half * 512:half * 512 + 512],
                                  False, True))
                    mmg(P, items,
                        reads=[wB_buf, bvr_buf, G["const_buf"]] +
                              [hbf.bufs[kb] for kb in range(NBLK)],
                        writes=[ps3.buf], name=f"a.vmm.{b}.{l}.{ck}.{s}.{half}")
                    P.op("act", lambda e, s=s, half=half: e.copy(
                        vtk[:, (s * 2 + half) * 512:(s * 2 + half) * 512 + 512],
                        ps3[:, 0:512]),
                        reads=[ps3.buf], writes=[vtk_bufs[s]],
                        name=f"a.vev.{b}.{l}.{ck}.{s}.{half}")
                psr = ps1 if blk % 2 == 0 else ps4
                rope_inplace(P, ksb, blk, psr, f"a.kr.{b}.{l}.{ck}.{blk}")
            def tp_s(s):
                t0 = s * 128
                glob_s = ck * (CK // 128) + s
                ko = (s % 2) * D
                for half in range(2):
                    pst = ps1 if half == 0 else ps4
                    mmg(P, [(pst[:, q * 128:(q + 1) * 128],
                             ksb.ap(half * 4 + q, t0, t0 + 128), G["eqm"][:],
                             q == 0, q == 3) for q in range(4)],
                        reads=[ksb.bufs[half * 4 + q] for q in range(4)] + [G["const_buf"]],
                        writes=[pst.buf], name=f"a.ktp.{b}.{l}.{glob_s}.{half}")
                    P.op("act", lambda e, half=half, pst=pst, ko=ko: e.copy(
                        krt[:, ko + half * 512:ko + (half + 1) * 512], pst[:, 0:512]),
                        reads=[pst.buf], writes=[krt_bufs[s % 2]],
                        name=f"a.ktpe.{b}.{l}.{glob_s}.{half}")

            def kv_s(s):
                glob_s = ck * (CK // 128) + s
                ko = (s % 2) * D
                items = []
                for h in range(NH):
                    items.append((kv_psum_ap(ps_kv, h),
                                  krt[:, ko + h * HD:ko + (h + 1) * HD],
                                  vtk[:, s * D + h * HD:s * D + (h + 1) * HD],
                                  glob_s == 0 and h < 2,
                                  glob_s == nsub - 1 and h >= NH - 2))
                mmg(P, items, reads=[krt_bufs[s % 2], vtk_bufs[s]], writes=[ps_kv.buf],
                    name=f"a.kv.{b}.{l}.{glob_s}", skip_check=True)

            tp_s(0)
            for s in range(1, CK // 128):
                tp_s(s)
                kv_s(s - 1)
            kv_s(CK // 128 - 1)

        def k_fin(l):
            P.op("act", lambda e: e.activation(kvsb[:, :], ps_kv[:, :],
                                               AF.Identity, bias=0.0, scale=1.0 / nt),
                 reads=[ps_kv.buf], writes=[kvsb.buf], name=f"a.kvev.{b}.{l}")

            def km_red(e):
                for blk in range(NBLK):
                    o = blk * (nck + 2)
                    last = e.reduce_sum(kmacc[:, o + nck:o + nck + 1],
                                        kmacc[:, o:o + nck], axis=AX.X)
                return last
            P.op("dve", km_red, reads=[kmacc.buf], writes=[kmacc.buf],
                 name=f"a.kmr.{b}.{l}")

            def km_fin(e):
                for blk in range(NBLK):
                    o = blk * (nck + 2)
                    last = e.tensor_scalar(kmacc[:, o + nck + 1:o + nck + 2],
                                           kmacc[:, o + nck:o + nck + 1],
                                           1.0 / nt, 0.21, ALU.mult, ALU.add)
                return last
            P.op("dve", km_fin, reads=[kmacc.buf], writes=[kmacc.buf],
                 name=f"a.kmf.{b}.{l}")

            def km_cast(e):
                for blk in range(NBLK):
                    o = blk * (nck + 2)
                    last = e.tensor_copy(kmbf[:, blk:blk + 1],
                                         kmacc[:, o + nck + 1:o + nck + 2])
                return last
            P.op("dve", km_cast, reads=[kmacc.buf], writes=[kmbf.buf],
                 name=f"a.kmc.{b}.{l}")
            P.op("pool", lambda e: e.memset(kmsel[:, :], 0.0),
                 reads=[], writes=[kmsel.buf], name=f"a.kms0.{b}.{l}")

            def kmsel_fill(e):
                for blk in range(NBLK):
                    o = blk * 16
                    e.tensor_copy(kmsel[0:64, o + 2 * blk:o + 2 * blk + 1],
                                  kmbf[0:64, blk:blk + 1])
                    last = e.tensor_copy(kmsel[64:128, o + 2 * blk + 1:o + 2 * blk + 2],
                                         kmbf[64:128, blk:blk + 1])
                return last
            P.op("dve", kmsel_fill, reads=[kmbf.buf, kmsel.buf], writes=[kmsel.buf],
                 name=f"a.kms.{b}.{l}")


        def q_ck(l, ck):
            c0 = ck * CK
            P.dma("sp", [(cosck[:], W["cos_t"][ck], {})],
                  writes=[cos_buf], name=f"a.cos.{b}.{l}.q{ck}")
            P.dma("act", [(sinck[:], W["sin_t"][ck], {})],
                  writes=[sin_buf], name=f"a.sin.{b}.{l}.q{ck}")
            for blk in range(NBLK):
                psk = (ps1, ps4, ps2)[blk % 3]
                mmg(P, [(psk[:, 0:CK], wQ[:, kb * D + blk * 128:kb * D + blk * 128 + 128],
                         hbf.ap(kb, c0, c0 + CK), st, sp)
                        for kb, (st, sp) in zip(range(NBLK), accg(NBLK))],
                    reads=[wQ_buf] + [hbf.bufs[kb] for kb in range(NBLK)],
                    writes=[psk.buf], name=f"a.qmm.{b}.{l}.{ck}.{blk}")
                P.op("act", lambda e, blk=blk, l=l, psk=psk: e.activation(
                    qsb.ap(blk), psk[:, 0:CK], AF.Gelu,
                    bias=PP.col("bq", l * 8 + blk), scale=1.0),
                    reads=[psk.buf, PP.buf], writes=[qsb.bufs[blk]],
                    name=f"a.qev.{b}.{l}.{ck}.{blk}")
            P.op("dve", lambda e: e.tensor_scalar_add(qsb.t[:], qsb.t[:], 0.21),
                 reads=qsb.bufs, writes=qsb.bufs, name=f"a.q21.{b}.{l}.{ck}")
            items = [(ps3[0:16, 0:CK], kmsel[:, blk * 16:(blk + 1) * 16],
                      qsb.ap(blk), st, False)
                     for blk, (st, _) in zip(range(NBLK), accg(NBLK))]
            items.append((ps3[0:16, 0:CK], G["eps_row"][0:1, 0:16],
                          G["ones_row"][0:1, 0:CK], False, True))
            mmg(P, items,
                reads=[kmsel.buf, G["const_buf"]] +
                      [qsb.bufs[blk] for blk in range(NBLK)],
                writes=[ps3.buf], name=f"a.zdot.{b}.{l}.{ck}")
            def zrec(e, c0=c0):
                with P.nc.allow_low_precision(reason="z cast to bf16 as before"):
                    return e.reciprocal(zbf[0:16, 0:CK], ps3[0:16, 0:CK])
            P.op("dve", zrec, reads=[ps3.buf], writes=[zbf.buf],
                 name=f"a.zr.{b}.{l}.{ck}")
            for blk in range(NBLK):
                psr = ps1 if blk % 2 == 0 else ps4
                rope_inplace(P, qsb, blk, psr, f"a.qr.{b}.{l}.{ck}.{blk}")
            for blk in range(NBLK):
                psz = ps2 if blk % 2 == 0 else ps3
                mmg(P, [(psz[:, 0:CK], zsel[:, blk * 128:(blk + 1) * 128],
                         zbf[0:16, 0:CK], True, True)],
                    reads=[zsel.buf, zbf.buf], writes=[psz.buf],
                    name=f"a.zbc.{b}.{l}.{ck}.{blk}")
                P.op("act", lambda e, psz=psz: e.copy(
                    zbc[:, 0:CK], psz[:, 0:CK]),
                    reads=[psz.buf], writes=[zbc_bufs[0]],
                    name=f"a.zbce.{b}.{l}.{ck}.{blk}")
                P.op("dve", lambda e, blk=blk: e.tensor_mul(
                    qsb.ap(blk), qsb.ap(blk), zbc[:, 0:CK]),
                    reads=[qsb.bufs[blk], zbc_bufs[0]], writes=[qsb.bufs[blk]],
                    name=f"a.qz.{b}.{l}.{ck}.{blk}")
            for blk in range(NBLK):
                pso = ps2 if blk % 2 == 0 else ps3
                items = []
                for hh in range(2):
                    h = blk * 2 + hh
                    items.append((pso[hh * 64:hh * 64 + 64, 0:CK],
                                  kvsb[64 * (h % 2):64 * (h % 2) + 64,
                                       64 * (h // 2):64 * (h // 2) + 64],
                                  qsb.t[64 * (h % 2):64 * (h % 2) + 64,
                                        (h // 2) * CK + 0:(h // 2) * CK + CK],
                                  True, True))
                mmg(P, items, reads=[kvsb.buf, qsb.bufs[blk]],
                    writes=[pso.buf], name=f"a.o.{b}.{l}.{ck}.{blk}")
                P.op("dve", lambda e, blk=blk, c0=c0, pso=pso: e.tensor_add(
                    hx.ap(blk, c0, c0 + CK), hx.ap(blk, c0, c0 + CK), pso[:, 0:CK]),
                    reads=[hx.bufs[blk], pso.buf], writes=[hx.bufs[blk]],
                    name=f"a.oa.{b}.{l}.{ck}.{blk}")

        def lnA_cols(i):
            if i is None:
                return (lambda blk: (PP.col("ln0g", blk), [PP.buf]),
                        lambda blk: (PP.col("ln0b", blk), [PP.buf]),
                        f"a.ln0.{b}")
            return (lambda blk, i=i: (PP.col("lnAg", i * 8 + blk), [PP.buf]),
                    lambda blk, i=i: (PP.col("lnAb", i * 8 + blk), [PP.buf]),
                    f"a.lnA.{b}.{i}")

        def dump_out():
            P.dma("sp", [(x_out_ap(b, blk), hx.ap(blk), {}) for blk in range(NBLK)],
                  reads=[hx.bufs[blk] for blk in range(NBLK)],
                  writes=XOUT(b), name=f"a.out{b}", is_output=FINAL)

        broke = False
        emit_layer_loads(0)
        gc, bc, tg = lnA_cols(None)
        if upto == "ln0":
            emit_ln(P, C, hx, hx, hbf, gc, bc, EPS_LN0, tg)
            dump_out()
            broke = True
        else:
            emit_ln(P, C, hx, hx, hbf, gc, bc, EPS_LN0, tg,
                    chunk_cb=lambda ck: k_ck(0, ck))
            k_fin(0)
        for l in range(L0L):
            if broke:
                break
            gc, bc, tg = lnA_cols(l)
            if upto is not None:
                for ck in range(nck):
                    q_ck(l, ck)
                if upto == f"lnA{l}":
                    emit_ln(P, C, hx, hx, hbf, gc, bc, EPS_LN0, tg)
                    dump_out()
                    broke = True
                    break
                if l < L0L - 1:
                    emit_layer_loads(l + 1)
                    emit_ln(P, C, hx, hx, hbf, gc, bc, EPS_LN0, tg,
                            chunk_cb=lambda ck, l=l: k_ck(l + 1, ck))
                    k_fin(l + 1)
            elif l < L0L - 1:
                emit_layer_loads(l + 1)
                emit_ln(P, C, hx, hx, hbf, gc, bc, EPS_LN0, tg,
                        chunk_cb=lambda ck, l=l: k_ck(l + 1, ck),
                        interleave=lambda ck, l=l: q_ck(l, ck))
                k_fin(l + 1)
            else:
                for ck in range(nck):
                    q_ck(l, ck)
        if broke:
            P.close_scope(esi)
            P.close_scope(esm)
            continue
        P.close_scope(esi)
        esf = P.scope()
        w1t = P.sbuf(f"a_w1t{b}", [128, 4 * 1024], BF16, es=esf)
        w1t_bufs = [Buf(f"a_w1t.{i}") for i in range(4)]
        w2t = P.sbuf(f"a_w2t{b}", [128, 2 * FBLK * 128], BF16, es=esf)
        w2t_bufs = [Buf("a_w2t.0"), Buf("a_w2t.1")]
        h1bf = P.sbuf(f"a_h1{b}", [128, FBLK * CK], BF16, es=esf)
        h1_bufs = [Buf(f"a_h1.{i}") for i in range(FBLK)]
        xres = P.sbuf(f"a_xres{b}", [128, 2 * CK], F32, es=esf)
        xres_bufs = [Buf("a_xres.0"), Buf("a_xres.1")]

        def ffn_ck(ck):
            c0 = ck * CK
            for mb in range(FBLK):
                slot = mb % 4
                P.dma(("sp", "act", "pool")[mb % 3],
                      [(w1t[:, slot * 1024:(slot + 1) * 1024], w1b[mb], {})],
                      reads=[w1b_bufs[mb // (FBLK // 4)]],
                      writes=[w1t_bufs[slot]], name=f"a.w1.{b}.{ck}.{mb}")
                ps = (ps1, ps2, ps3)[mb % 3]
                mmg(P, [(ps[:, 0:CK],
                         w1t[:, slot * 1024 + kb * 128: slot * 1024 + kb * 128 + 128],
                         hbf.ap(kb, c0, c0 + CK), st, sp)
                        for kb, (st, sp) in zip(range(NBLK), accg(NBLK))],
                    reads=[w1t_bufs[slot]] + [hbf.bufs[kb] for kb in range(NBLK)],
                    writes=[ps.buf], name=f"a.f1.{b}.{ck}.{mb}")
                P.op("act", lambda e, mb=mb, ps=ps: e.activation(
                    h1bf[:, mb * CK:(mb + 1) * CK], ps[:, 0:CK], AF.Gelu,
                    bias=PP.col("b1f", mb), scale=1.0),
                    reads=[ps.buf, PP.buf], writes=[h1_bufs[mb]],
                    name=f"a.f1e.{b}.{ck}.{mb}")
            for blk in range(NBLK):
                slot = blk % 2
                psf = ps3 if slot == 0 else ps4
                P.dma("sp" if blk % 2 == 0 else "act",
                      [(w2t[:, slot * FBLK * 128:(slot + 1) * FBLK * 128], w2b[blk], {})],
                      reads=[w2b_bufs[blk // 2]],
                      writes=[w2t_bufs[slot]], name=f"a.w2.{b}.{ck}.{blk}")
                mmg(P, [(psf[:, 0:CK],
                         w2t[:, slot * FBLK * 128 + hb * 128: slot * FBLK * 128 + hb * 128 + 128],
                         h1bf[:, hb * CK:(hb + 1) * CK], st, sp)
                        for hb, (st, sp) in zip(range(FBLK), accg(FBLK))],
                    reads=[w2t_bufs[slot]] + h1_bufs, writes=[psf.buf],
                    name=f"a.f2.{b}.{ck}.{blk}")
                P.op("act", lambda e, blk=blk, psf=psf: e.copy(ctmp[:, 0:CK], psf[:, 0:CK]),
                     reads=[psf.buf], writes=[ctmp.buf], name=f"a.f2e.{b}.{ck}.{blk}")
                P.op("dve", lambda e, blk=blk, c0=c0: e.scalar_tensor_tensor(
                    hx.ap(blk, c0, c0 + CK), ctmp[:, 0:CK], PP.col("b2f", blk),
                    hx.ap(blk, c0, c0 + CK), ALU.add, ALU.add),
                    reads=[ctmp.buf, PP.buf, hx.bufs[blk]], writes=[hx.bufs[blk]],
                    name=f"a.f2a.{b}.{ck}.{blk}")

        gc, bc, tg = lnA_cols(3)
        emit_ln(P, C, hx, hx, hbf, gc, bc, EPS_LN0, tg, chunk_cb=ffn_ck)

        def xadd_ck(ck):
            c0 = ck * CK
            for blk in range(NBLK):
                slot = blk % 2
                P.dma("sp" if blk % 2 == 0 else "act",
                      [(xres[:, slot * CK:slot * CK + CK],
                        x_in_ap(b, blk)[:, c0:c0 + CK], {})],
                      reads=XIN(b), writes=[xres_bufs[slot]], name=f"a.xr.{b}.{blk}.{ck}")
                P.op("dve", lambda e, blk=blk, c0=c0, slot=slot: e.tensor_add(
                    hx.ap(blk, c0, c0 + CK), hx.ap(blk, c0, c0 + CK),
                    xres[:, slot * CK:slot * CK + CK]),
                    reads=[hx.bufs[blk], xres_bufs[slot]], writes=[hx.bufs[blk]],
                    name=f"a.xra.{b}.{blk}.{ck}")

        emit_ln(P, C, hx, hx, None,
                lambda blk: (PP.col("ln2g", blk), [PP.buf]),
                lambda blk: (PP.col("ln2b", blk), [PP.buf]),
                EPS_LN0, f"a.ln2.{b}")
        if upto == "ln2":
            P.dma("sp", [(x_out_ap(b, blk), hx.ap(blk), {}) for blk in range(NBLK)],
                  reads=[hx.bufs[blk] for blk in range(NBLK)],
                  writes=XOUT(b), name=f"a.out{b}", is_output=FINAL)
            P.close_scope(esf)
            P.close_scope(esm)
            continue

        out_buf_chain = XOUT(b) if XOUT(b) else [Buf(f"a_outc{b}")]

        def out_ck(ck):
            c0 = ck * CK
            P.dma("sp", [(x_out_ap(b, blk)[:, c0:c0 + CK],
                          hx.ap(blk, c0, c0 + CK), {}) for blk in range(NBLK)],
                  reads=[hx.bufs[blk] for blk in range(NBLK)],
                  writes=out_buf_chain, name=f"a.out{b}.{ck}", is_output=FINAL)

        emit_ln(P, C, hx, hx, None, None, None, EPS_MOD, f"a.pln.{b}",
                chunk_cb=out_ck, interleave=xadd_ck)
        P.close_scope(esf)
        P.close_scope(esm)

    P.close_scope(es)


def kv_psum_ap(ps, h):
    r = 64 * (h % 2)
    c = 64 * (h // 2)
    return ps[r:r + 64, c:c + 64]



def _cols(v):
    v = np.asarray(v, np.float32).reshape(-1, 128)
    return np.ascontiguousarray(v.T)


def declare_l0a_weights(nc, nt):
    def t(n, s, dt=F32):
        return nc.dram_tensor(n, s, dt, kind="ExternalInput").ap()
    W = {}
    W["conv1_w_cols"] = t("l0a_w1c", [128, 160])
    W["conv2_w_cols"] = t("l0a_w2c", [128, 160])
    W["conv1_b_cols"] = t("l0a_b1c", [128, 32])
    W["conv2_b_cols"] = t("l0a_b2c", [128, 8])
    W["ln0_g_cols"] = t("l0a_ln0g", [128, 8])
    W["ln0_b_cols"] = t("l0a_ln0b", [128, 8])
    W["Wq_r"] = t("l0a_wq", [4, 128, 8, D])
    W["Wk_r"] = t("l0a_wk", [4, 128, 8, D])
    W["Wv_r"] = t("l0a_wv", [4, 128, 8, D])
    W["bq_cols"] = t("l0a_bq", [128, 32])
    W["bk_cols"] = t("l0a_bk", [128, 32])
    W["bv_rows"] = t("l0a_bv", [1, 4 * D])
    W["lnA_g_cols"] = t("l0a_lnAg", [128, 32])
    W["lnA_b_cols"] = t("l0a_lnAb", [128, 32])
    W["W1_r"] = t("l0a_w1", [128, 8, FF])
    W["ffn_b1_cols"] = t("l0a_fb1", [128, 32])
    W["W2_r"] = t("l0a_w2", [128, FBLK, D])
    W["ffn_b2_cols"] = t("l0a_fb2", [128, 8])
    W["ln2_g_cols"] = t("l0a_ln2g", [128, 8])
    W["ln2_b_cols"] = t("l0a_ln2b", [128, 8])
    ckl = min(512, nt)
    W["cos_t"] = t("l0a_cos", [nt // ckl, 128, 8 * ckl], BF16)
    W["sin_t"] = t("l0a_sin", [nt // ckl, 128, 8 * ckl], BF16)
    W["S_swap"] = t("l0a_S", [128, 128], BF16)
    W["Z_sel"] = t("l0a_zsel", [16, 1024], BF16)
    return W


def host_l0a(inp, nt):
    H = {}
    w1v = np.asarray(inp["L0_attn_conv1_w"], np.float32)[:, 0, :]
    H["l0a_w1c"] = np.ascontiguousarray(
        w1v.reshape(8, 128, 4, 5).transpose(1, 0, 2, 3).reshape(128, 160))
    w2v = np.asarray(inp["L0_attn_conv2_w"], np.float32)
    H["l0a_w2c"] = np.ascontiguousarray(
        w2v.reshape(8, 128, 4, 5).transpose(1, 0, 2, 3).reshape(128, 160))
    b1 = np.asarray(inp["L0_attn_conv1_b"], np.float32)
    H["l0a_b1c"] = np.ascontiguousarray(
        b1.reshape(8, 128, 4).transpose(1, 0, 2).reshape(128, 32))
    H["l0a_b2c"] = _cols(inp["L0_attn_conv2_b"])
    H["l0a_ln0g"] = _cols(inp["L0_attn_ln0_g"])
    H["l0a_ln0b"] = _cols(inp["L0_attn_ln0_b"])
    for nm, key in (("wq", "L0_attn_Wq"), ("wk", "L0_attn_Wk"), ("wv", "L0_attn_Wv")):
        w = np.asarray(inp[key], np.float32)
        H["l0a_" + nm] = np.ascontiguousarray(
            w.reshape(4, 8, 128, D).transpose(0, 2, 1, 3))
    for nm, key in (("bq", "L0_attn_bq"), ("bk", "L0_attn_bk")):
        bv = np.asarray(inp[key], np.float32)
        H["l0a_" + nm] = np.ascontiguousarray(
            bv.reshape(4, 8, 128).transpose(2, 0, 1).reshape(128, 32))
    H["l0a_bv"] = np.asarray(inp["L0_attn_bv"], np.float32).reshape(1, 4 * D).copy()
    for nm, key in (("lnAg", "L0_attn_lnA_g"), ("lnAb", "L0_attn_lnA_b")):
        v = np.asarray(inp[key], np.float32)
        H["l0a_" + nm] = np.ascontiguousarray(
            v.reshape(4, 8, 128).transpose(2, 0, 1).reshape(128, 32))
    w1 = np.asarray(inp["L0_attn_W1"], np.float32)
    H["l0a_w1"] = np.ascontiguousarray(w1.reshape(8, 128, FF).transpose(1, 0, 2))
    H["l0a_fb1"] = np.ascontiguousarray(
        np.asarray(inp["L0_attn_b1"], np.float32).reshape(32, 128).T)
    w2 = np.asarray(inp["L0_attn_W2"], np.float32)
    H["l0a_w2"] = np.ascontiguousarray(w2.reshape(FBLK, 128, D).transpose(1, 0, 2))
    H["l0a_fb2"] = _cols(inp["L0_attn_b2"])
    H["l0a_ln2g"] = _cols(inp["L0_attn_ln2_g"])
    H["l0a_ln2b"] = _cols(inp["L0_attn_ln2_b"])
    inv = (1e4 ** (-np.arange(0, D, 2, dtype=np.float64) / D))
    freqs = np.arange(nt, dtype=np.float64)[:, None] * inv[None, :]
    cosf = np.repeat(np.cos(freqs).T, 2, axis=0)
    sinf = np.repeat(np.sin(freqs).T, 2, axis=0)
    sinf[0::2] *= -1.0
    ckl = min(512, nt)
    H["l0a_cos"] = np.ascontiguousarray(
        cosf.reshape(8, 128, nt // ckl, ckl).transpose(2, 1, 0, 3)
        .reshape(nt // ckl, 128, 8 * ckl)).astype(BF)
    H["l0a_sin"] = np.ascontiguousarray(
        sinf.reshape(8, 128, nt // ckl, ckl).transpose(2, 1, 0, 3)
        .reshape(nt // ckl, 128, 8 * ckl)).astype(BF)
    S = np.zeros((128, 128), np.float32)
    for i in range(64):
        S[2 * i, 2 * i + 1] = 1.0
        S[2 * i + 1, 2 * i] = 1.0
    H["l0a_S"] = S.astype(BF)
    zs = np.zeros((16, 1024), np.float32)
    for blk in range(8):
        zs[2 * blk, blk * 128: blk * 128 + 64] = 1.0
        zs[2 * blk + 1, blk * 128 + 64: blk * 128 + 128] = 1.0
    H["l0a_zsel"] = zs.astype(BF)
    return H



def declare_l1a_weights(nc, nt):
    def t(n, s, dt=F32):
        return nc.dram_tensor(n, s, dt, kind="ExternalInput").ap()
    W = {}
    W["WqT"] = t("l1a_wq", [128, 8, D])
    W["WkT"] = t("l1a_wk", [128, 8, D])
    W["WvT"] = t("l1a_wv", [128, 8, D])
    W["WoT"] = t("l1a_wo", [128, 8, D])
    W["bq_cols"] = t("l1a_bq", [128, 8])
    W["bk_cols"] = t("l1a_bk", [128, 8])
    W["bv_row"] = t("l1a_bv", [1, D])
    W["bo_cols"] = t("l1a_bo", [128, 8])
    W["tril"] = t("l1a_tril", [128, 128], BF16)
    return W


def host_l1a(inp, nt):
    H = {}
    for nm, key in (("wq", "L1_attn_Wq"), ("wk", "L1_attn_Wk"),
                    ("wv", "L1_attn_Wv"), ("wo", "L1_attn_Wo")):
        w = np.ascontiguousarray(np.asarray(inp[key], np.float32).T)
        H["l1a_" + nm] = np.ascontiguousarray(w.reshape(8, 128, D).transpose(1, 0, 2))
    H["l1a_bq"] = _cols(inp["L1_attn_bq"])
    H["l1a_bk"] = _cols(inp["L1_attn_bk"])
    H["l1a_bv"] = np.asarray(inp["L1_attn_bv"], np.float32).reshape(1, D).copy()
    H["l1a_bo"] = _cols(inp["L1_attn_bo"])
    m = np.zeros((128, 128), np.float32)
    for p in range(128):
        m[p, : p + 1] = 1.0
    H["l1a_tril"] = m.astype(BF)
    return H


def emit_l1_attn(P, G, W, x_in_ap, x_out_ap, nt, scratch, xio=None):
    xio = xio or {}
    XIN = (lambda b: [xio["in"][b]]) if xio.get("in") else (lambda b: [])
    XOUT = (lambda b: [xio["out"][b]]) if xio.get("out") else (lambda b: [])
    FINAL = xio.get("final", True)
    """scratch: tks [nb, 8, 128, nt] bf16 (k d-major), tvs [nb, nt//128, 128, D] bf16."""
    nb = G["nb"]
    CK = min(512, nt)
    nck = nt // CK
    nq = nt // 128
    GQ = 2
    es = P.scope()

    hx = DT(P, "t_hx", NBLK, nt, F32, es=es)
    ctmp = Tile(P, "t_ctmp", [128, 256], F32, es=es)
    ps1 = Tile(P, "t_ps1", [128, 512], F32, es=es, psum=True)
    ps_pv = Tile(P, "t_pspv", [128, 2048], F32, es=es, psum=True)
    C = make_ln_common(P, nt, es, G, "t_")
    PP = ParamPack(P, "t_pp", [
        ("bq", W["bq_cols"], 8),
        ("bk", W["bk_cols"], 8),
        ("bo", W["bo_cols"], 8),
    ], es=es)
    bvr = P.sbuf("t_bvr", [1, D], BF16, es=es)
    bvr_buf = Buf("t_bvr")
    P.dma("pool", [(bvr[0:1, :], W["bv_row"], {})], writes=[bvr_buf], name="t.bvr")
    tril = P.sbuf("t_tril", [128, 128], BF16, es=es)
    tril_buf = Buf("t_tril")
    P.dma("sp", [(tril[:], W["tril"], {})], writes=[tril_buf], name="t.mask")

    wbuf = P.sbuf("t_w", [128, NBLK * D], BF16, es=es)
    w_buf = Buf("t_w")
    wo = P.sbuf("t_wo", [128, NBLK * D], BF16, es=es)
    wo_buf = Buf("t_wo")
    xck = P.sbuf("t_xck", [128, 2 * NBLK * CK], BF16, es=es)
    xck_bufs = [Buf("t_xck.0"), Buf("t_xck.1")]
    kch = P.sbuf("t_kch", [128, 2 * NBLK * CK], BF16, es=es)
    kch_bufs = [Buf("t_kch.0"), Buf("t_kch.1")]
    kstg = kch
    kstg_bufs = [kch_bufs[0]] * NBLK
    vch = P.sbuf("t_vch", [128, 2 * D], BF16, es=es)
    vch_bufs = [Buf("t_vch.0"), Buf("t_vch.1")]
    qg = DT(P, "t_qg", NBLK, GQ * 128, BF16, es=es)
    sab = P.sbuf("t_sab", [128, GQ * nt], F32, es=es)
    sab_bufs = [Buf(f"t_sab.{i}") for i in range(GQ)]
    ebf = Tile(P, "t_ebf", [128, nt], BF16, es=es)
    wn = P.sbuf("t_wn", [128, GQ * nt], BF16, es=es)
    wn_bufs = [Buf(f"t_wn.{i}") for i in range(GQ)]
    wT = P.sbuf("t_wT", [128, (nt // 128) * GQ * 128], BF16, es=es)
    wT_buf = Buf("t_wT")
    ao = DT(P, "t_ao", NBLK, GQ * 128, BF16, es=es)
    cols = Tile(P, "t_cols", [128, 8], F32, es=es)
    xres = P.sbuf("t_xres", [128, 2 * GQ * 128], F32, es=es)
    xres_bufs = [Buf("t_xres.0"), Buf("t_xres.1")]
    tks_bufs = [Buf(f"t_tks.{i}", nofence=True) for i in range(nb)]
    tvs_bufs = [Buf(f"t_tvs.{i}", nofence=True) for i in range(nb)]

    for b in range(nb):
        P.dma("pool", [(wbuf[:].rearrange("p (a m) -> p a m", m=D), W["WkT"], {})],
              writes=[w_buf], name=f"t.wk{b}")
        for ck in range(nck):
            c0 = ck * CK
            slot = ck % 2
            P.dma("pool", [(xck[:, slot * NBLK * CK:(slot + 1) * NBLK * CK].rearrange(
                                "p (a c) -> p a c", c=CK),
                            x_in_ap(b, None)[:, :, c0:c0 + CK].rearrange(
                                "a p c -> p a c"), {})],
                  reads=XIN(b), writes=[xck_bufs[slot]], name=f"t.xk{b}.{ck}")
            for blk in range(NBLK):
                mmg(P, [(ps1[:, 0:CK],
                         wbuf[:, kb * D + blk * 128:kb * D + blk * 128 + 128],
                         xck[:, slot * NBLK * CK + kb * CK: slot * NBLK * CK + kb * CK + CK],
                         st, sp) for kb, (st, sp) in zip(range(NBLK), accg(NBLK))],
                    reads=[w_buf, xck_bufs[slot]], writes=[ps1.buf],
                    name=f"t.kmm{b}.{ck}.{blk}")
                P.op("act", lambda e, blk=blk: e.activation(
                    kstg[:, blk * CK:(blk + 1) * CK], ps1[:, 0:CK], AF.Identity,
                    bias=PP.col("bk", blk), scale=1.0),
                    reads=[ps1.buf, PP.buf], writes=[kstg_bufs[blk]],
                    name=f"t.kev{b}.{ck}.{blk}")
            P.dma("sp", [(scratch["tks"][b, blk][:, c0:c0 + CK],
                          kstg[:, blk * CK:(blk + 1) * CK], {}) for blk in range(NBLK)],
                  reads=kstg_bufs, writes=[tks_bufs[b]], name=f"t.kst{b}.{ck}")
        P.dma("pool", [(wbuf[:].rearrange("p (a m) -> p a m", m=D), W["WvT"], {})],
              writes=[w_buf], name=f"t.wv{b}")
        for ck in range(nck):
            c0 = ck * CK
            slot = ck % 2
            P.dma("pool", [(xck[:, slot * NBLK * CK:(slot + 1) * NBLK * CK].rearrange(
                                "p (a c) -> p a c", c=CK),
                            x_in_ap(b, None)[:, :, c0:c0 + CK].rearrange(
                                "a p c -> p a c"), {})],
                  reads=XIN(b), writes=[xck_bufs[slot]], name=f"t.xv{b}.{ck}")
            for s in range(CK // 128):
                sub = ck * (CK // 128) + s
                t0 = s * 128
                vslot = sub % 2
                for half in range(2):
                    items = [(ps1[:, 0:512],
                              xck[:, slot * NBLK * CK + kb * CK + t0:
                                  slot * NBLK * CK + kb * CK + t0 + 128],
                              wbuf[:, kb * D + half * 512:kb * D + half * 512 + 512],
                              st, False) for kb, (st, _) in zip(range(NBLK), accg(NBLK))]
                    items.append((ps1[:, 0:512], G["ones_row"][0:1, 0:128],
                                  bvr[0:1, half * 512:half * 512 + 512], False, True))
                    mmg(P, items, reads=[w_buf, bvr_buf, G["const_buf"], xck_bufs[slot]],
                        writes=[ps1.buf], name=f"t.vmm{b}.{sub}.{half}")
                    P.op("act", lambda e, vslot=vslot, half=half: e.copy(
                        vch[:, vslot * D + half * 512: vslot * D + half * 512 + 512],
                        ps1[:, 0:512]),
                        reads=[ps1.buf], writes=[vch_bufs[vslot]],
                        name=f"t.vev{b}.{sub}.{half}")
                P.dma("sp", [(scratch["tvs"][b, sub], vch[:, vslot * D:(vslot + 1) * D], {})],
                      reads=[vch_bufs[vslot]], writes=[tvs_bufs[b]], name=f"t.vst{b}.{sub}")
        P.dma("pool", [(wbuf[:].rearrange("p (a m) -> p a m", m=D), W["WqT"], {})],
              writes=[w_buf], name=f"t.wq{b}")
        P.dma("pool", [(wo[:].rearrange("p (a m) -> p a m", m=D), W["WoT"], {})],
              writes=[wo_buf], name=f"t.wo{b}")

        for gq in range(nq // GQ):
            q0 = gq * GQ * 128
            slot = gq % 2
            P.dma("pool", [(xck[:, slot * NBLK * CK: slot * NBLK * CK + NBLK * GQ * 128]
                            .rearrange("p (a c) -> p a c", c=GQ * 128),
                            x_in_ap(b, None)[:, :, q0:q0 + GQ * 128].rearrange(
                                "a p c -> p a c"), {})],
                  reads=XIN(b), writes=[xck_bufs[slot]], name=f"t.xq{b}.{gq}")
            for blk in range(NBLK):
                mmg(P, [(ps1[:, 0:GQ * 128],
                         wbuf[:, kb * D + blk * 128:kb * D + blk * 128 + 128],
                         xck[:, slot * NBLK * CK + kb * GQ * 128:
                             slot * NBLK * CK + kb * GQ * 128 + GQ * 128],
                         st, sp) for kb, (st, sp) in zip(range(NBLK), accg(NBLK))],
                    reads=[w_buf, xck_bufs[slot]], writes=[ps1.buf],
                    name=f"t.qmm{b}.{gq}.{blk}")
                P.op("act", lambda e, blk=blk: e.activation(
                    qg.ap(blk), ps1[:, 0:GQ * 128], AF.Identity,
                    bias=PP.col("bq", blk), scale=1.0),
                    reads=[ps1.buf, PP.buf], writes=[qg.bufs[blk]],
                    name=f"t.qev{b}.{gq}.{blk}")
            nvalid = q0 + GQ * 128
            nkc_s = (nvalid + CK - 1) // CK
            for kc in range(nkc_s):
                k0 = kc * CK
                kslot = kc % 2
                P.dma("sp", [(kch[:, kslot * NBLK * CK + blk * CK:
                                  kslot * NBLK * CK + (blk + 1) * CK],
                              scratch["tks"][b, blk][:, k0:k0 + CK], {})
                             for blk in range(NBLK)],
                      reads=[tks_bufs[b]], writes=[kch_bufs[kslot]],
                      name=f"t.kld{b}.{gq}.{kc}")
                for qi in range(GQ):
                    mmg(P, [(ps1[:, 0:CK],
                             qg.ap(kb, qi * 128, qi * 128 + 128),
                             kch[:, kslot * NBLK * CK + kb * CK:
                                 kslot * NBLK * CK + kb * CK + CK], st, sp)
                            for kb, (st, sp) in zip(range(NBLK), accg(NBLK))],
                        reads=[qg.bufs[kb] for kb in range(NBLK)] + [kch_bufs[kslot]],
                        writes=[ps1.buf], name=f"t.smm{b}.{gq}.{qi}.{kc}")
                    P.op("act", lambda e, qi=qi, k0=k0: e.activation(
                        sab[:, qi * nt + k0: qi * nt + k0 + CK], ps1[:, 0:CK], AF.Abs,
                        bias=0.0, scale=1.0 / 32.0),
                        reads=[ps1.buf], writes=[sab_bufs[qi]],
                        name=f"t.sev{b}.{gq}.{qi}.{kc}")
            for qi in range(GQ):
                qtile = gq * GQ + qi
                vq = (qtile + 1) * 128
                P.op("dve", lambda e, qi=qi, vq=vq: e.reduce_max(
                    cols[:, 0:1], sab[:, qi * nt: qi * nt + vq], axis=AX.X, negate=True),
                    reads=[sab_bufs[qi]], writes=[cols.buf], name=f"t.mx{b}.{qtile}")
                P.op("act", lambda e, qi=qi, vq=vq: e.activation(
                    ebf[:, 0:vq], sab[:, qi * nt: qi * nt + vq],
                    AF.Exp, bias=cols[:, 0:1], scale=1.0),
                    reads=[sab_bufs[qi], cols.buf], writes=[ebf.buf],
                    name=f"t.exp{b}.{qtile}")
                if qtile > 0:
                    P.op("dve", lambda e, qi=qi, vq=vq: e.tensor_copy(
                        wn[:, qi * nt: qi * nt + vq - 128], ebf[:, 0:vq - 128]),
                        reads=[ebf.buf], writes=[wn_bufs[qi]], name=f"t.cpe{b}.{qtile}")
                P.op("dve", lambda e, qi=qi, vq=vq: e.tensor_mul(
                    wn[:, qi * nt + vq - 128: qi * nt + vq],
                    ebf[:, vq - 128:vq], tril[:, :]),
                    reads=[ebf.buf, tril_buf], writes=[wn_bufs[qi]],
                    name=f"t.msk{b}.{qtile}")
                if qi == 0 and GQ == 2:
                    P.op("dve", lambda e, qi=qi, vq=vq: e.memset(
                        wn[:, qi * nt + vq: qi * nt + vq + 128], 0.0),
                        reads=[], writes=[wn_bufs[qi]], name=f"t.z{b}.{qtile}")
                P.op("dve", lambda e, qi=qi, vq=vq: e.reduce_sum(
                    cols[:, 1:2], wn[:, qi * nt: qi * nt + vq], axis=AX.X),
                    reads=[wn_bufs[qi]], writes=[cols.buf], name=f"t.sum{b}.{qtile}")
                P.op("dve", lambda e: e.reciprocal(cols[:, 2:3], cols[:, 1:2]),
                     reads=[cols.buf], writes=[cols.buf], name=f"t.rc{b}.{qtile}")
                P.op("dve", lambda e, qi=qi, vq=vq: e.tensor_scalar_mul(
                    wn[:, qi * nt: qi * nt + vq], wn[:, qi * nt: qi * nt + vq],
                    cols[:, 2:3]),
                    reads=[wn_bufs[qi], cols.buf], writes=[wn_bufs[qi]],
                    name=f"t.wsc{b}.{qtile}")
            nkc128 = nvalid // 128
            for kc in range(nkc128):
                mmg(P, [(ps1[:, qi * 128:(qi + 1) * 128],
                         wn[:, qi * nt + kc * 128: qi * nt + kc * 128 + 128],
                         G["eqm"][:], qi == 0, qi == GQ - 1) for qi in range(GQ)],
                    reads=[wn_bufs[qi] for qi in range(GQ)] + [G["const_buf"]],
                    writes=[ps1.buf], name=f"t.tp{b}.{gq}.{kc}")
                P.op("act", lambda e, kc=kc: e.copy(
                    wT[:, kc * GQ * 128:(kc + 1) * GQ * 128], ps1[:, 0:GQ * 128]),
                    reads=[ps1.buf], writes=[wT_buf], name=f"t.tpe{b}.{gq}.{kc}")
            for kc in range(nkc128):
                vslot = kc % 2
                P.dma("sp", [(vch[:, vslot * D:(vslot + 1) * D],
                              scratch["tvs"][b, kc], {})],
                      reads=[tvs_bufs[b]], writes=[vch_bufs[vslot]],
                      name=f"t.vld{b}.{gq}.{kc}")
                items = []
                for vd in range(NBLK):
                    items.append((ps_pv[:, vd * 256: vd * 256 + GQ * 128],
                                  vch[:, vslot * D + vd * 128: vslot * D + vd * 128 + 128],
                                  wT[:, kc * GQ * 128:(kc + 1) * GQ * 128],
                                  kc == 0 and vd % 2 == 0,
                                  kc == nkc128 - 1 and vd % 2 == 1))
                mmg(P, items, reads=[vch_bufs[vslot], wT_buf], writes=[ps_pv.buf],
                    name=f"t.pv{b}.{gq}.{kc}")
            for vd in range(NBLK):
                P.op("dve", lambda e, vd=vd: e.tensor_copy(
                    ao.ap(vd), ps_pv[:, vd * 256: vd * 256 + GQ * 128]),
                    reads=[ps_pv.buf], writes=[ao.bufs[vd]], name=f"t.pve{b}.{gq}.{vd}")
            for blk in range(NBLK):
                mmg(P, [(ps1[:, 0:GQ * 128],
                         wo[:, kb * D + blk * 128:kb * D + blk * 128 + 128],
                         ao.ap(kb), st, sp)
                        for kb, (st, sp) in zip(range(NBLK), accg(NBLK))],
                    reads=[wo_buf] + [ao.bufs[kb] for kb in range(NBLK)],
                    writes=[ps1.buf], name=f"t.omm{b}.{gq}.{blk}")
                P.op("act", lambda e, blk=blk: e.activation(
                    ctmp[:, 0:GQ * 128], ps1[:, 0:GQ * 128], AF.Identity,
                    bias=PP.col("bo", blk), scale=1.0),
                    reads=[ps1.buf, PP.buf], writes=[ctmp.buf],
                    name=f"t.oev{b}.{gq}.{blk}")
                xslot = blk % 2
                P.dma("sp", [(xres[:, xslot * GQ * 128:(xslot + 1) * GQ * 128],
                              x_in_ap(b, blk)[:, q0:q0 + GQ * 128], {})],
                      reads=XIN(b), writes=[xres_bufs[xslot]], name=f"t.xr{b}.{gq}.{blk}")
                P.op("dve", lambda e, blk=blk, q0=q0, xslot=xslot: e.tensor_add(
                    hx.ap(blk, q0, q0 + GQ * 128), ctmp[:, 0:GQ * 128],
                    xres[:, xslot * GQ * 128:(xslot + 1) * GQ * 128]),
                    reads=[ctmp.buf, xres_bufs[xslot]], writes=[hx.bufs[blk]],
                    name=f"t.oadd{b}.{gq}.{blk}")
        emit_ln(P, C, hx, hx, None, None, None, EPS_MOD, f"t.pln.{b}")
        P.dma("sp", [(x_out_ap(b, blk), hx.ap(blk), {}) for blk in range(NBLK)],
              reads=[hx.bufs[blk] for blk in range(NBLK)],
              writes=XOUT(b), name=f"t.out{b}", is_output=FINAL)

    P.close_scope(es)



def declare_l0f_weights(nc, nt):
    def t(n, s, dt=F32):
        return nc.dram_tensor(n, s, dt, kind="ExternalInput").ap()
    W = {}
    W["WQT"] = t("l0f_wq", [128, 8, D])
    W["WKT"] = t("l0f_wk", [128, 8, D])
    W["WOT"] = t("l0f_wo", [128, 8, D])
    W["WinvT"] = t("l0f_winv", [128, 8, D])
    W["Vf"] = t("l0f_vf", [128, 8, 512])
    W["Wf"] = t("l0f_wf", [128, 8, 512])
    W["UTf"] = t("l0f_uf", [4, 128, D])
    W["bf"] = t("l0f_bf", [8, D])
    W["Vi"] = t("l0f_vi", [128, 8, 512])
    W["Wie"] = t("l0f_wie", [128, 8, 512])
    W["UTi"] = t("l0f_ui", [4, 128, D])
    W["bi"] = t("l0f_bi", [8, D])
    W["rw1T"] = t("l0f_rw1", [128, 8, D])
    W["rb1_cols"] = t("l0f_rb1", [128, 8])
    W["rw2T"] = t("l0f_rw2", [128, 8, 8])
    W["rb2"] = t("l0f_rb2", [1, 8])
    W["ebias"] = t("l0f_eb", [1, 8])
    W["alpha"] = t("l0f_alpha", [1, 1])
    W["nv"] = t("l0f_nv", [128, nt])
    W["S_all"] = t("l0f_sall", [8, 512], BF16)
    return W


def _kr_layout(v):
    v = np.asarray(v, np.float32)
    return np.ascontiguousarray(v.reshape(KEXP, 8, 128, RR).transpose(2, 1, 0, 3)
                                .reshape(128, 8, KEXP * RR))


def host_l0f(inp, nt):
    H = {}
    for nm, key in (("wq", "L0_ffn_W_Q"), ("wk", "L0_ffn_W_K"),
                    ("wo", "L0_ffn_W_O"), ("winv", "L0_ffn_W_inv")):
        w = np.ascontiguousarray(np.asarray(inp[key], np.float32).T)
        H["l0f_" + nm] = np.ascontiguousarray(w.reshape(8, 128, D).transpose(1, 0, 2))
    H["l0f_vf"] = _kr_layout(inp["L0_ffn_V_fwd"])
    H["l0f_wf"] = _kr_layout(inp["L0_ffn_W_fwd"])
    H["l0f_vi"] = _kr_layout(inp["L0_ffn_V_inv"])
    H["l0f_wie"] = _kr_layout(inp["L0_ffn_W_inv_exp"])
    for nm, key in (("uf", "L0_ffn_U_fwd"), ("ui", "L0_ffn_U_inv")):
        u = np.asarray(inp[key], np.float32)
        H["l0f_" + nm] = np.ascontiguousarray(
            u.transpose(0, 2, 1).reshape(4, 128, D))
    H["l0f_bf"] = np.asarray(inp["L0_ffn_b_fwd"], np.float32).copy()
    H["l0f_bi"] = np.asarray(inp["L0_ffn_b_inv"], np.float32).copy()
    rw1 = np.ascontiguousarray(np.asarray(inp["L0_ffn_router_w1"], np.float32).T)
    H["l0f_rw1"] = np.ascontiguousarray(rw1.reshape(8, 128, D).transpose(1, 0, 2))
    H["l0f_rb1"] = _cols(inp["L0_ffn_router_b1"])
    rw2 = np.ascontiguousarray(np.asarray(inp["L0_ffn_router_w2"], np.float32).T)
    H["l0f_rw2"] = np.ascontiguousarray(rw2.reshape(8, 128, 8).transpose(1, 0, 2))
    H["l0f_rb2"] = np.asarray(inp["L0_ffn_router_b2"], np.float32).reshape(1, 8).copy()
    H["l0f_eb"] = np.asarray(inp["L0_ffn_expert_bias"], np.float32).reshape(1, 8).copy()
    H["l0f_alpha"] = np.asarray(inp["L0_ffn_alpha_bi"], np.float32).reshape(1, 1).copy()
    H["l0f_nv"] = np.broadcast_to(
        1.0 / np.arange(1, nt + 1, dtype=np.float32), (128, nt)).copy()
    sall = np.zeros((8, 512), np.float32)
    for mp in range(4):
        for p in range(128):
            sall[2 * mp + p // 64, mp * 128 + p] = 1.0
    H["l0f_sall"] = sall.astype(BF)
    return H


def emit_l0_ffn(P, G, W, x_in_ap, x_out_ap, nt, scratch, xio=None):
    xio = xio or {}
    XIN = (lambda b: [xio["in"][b]]) if xio.get("in") else (lambda b: [])
    XOUT = (lambda b: [xio["out"][b]]) if xio.get("out") else (lambda b: [])
    FINAL = xio.get("final", True)
    """scratch: dict with dram APs qs, ks, fs: [nb, 8, 128, nt] bf16,
    pcs: [128, 4*D] bf16 (precombined inv weights: VVi | WWi | RW1i x2)."""
    nb = G["nb"]
    CK = min(512, nt)
    nck = nt // CK
    es = P.scope()

    ps1 = Tile(P, "f_ps1", [128, 512], F32, es=es, psum=True)
    ps2 = Tile(P, "f_ps2", [128, 512], F32, es=es, psum=True)
    ps3 = Tile(P, "f_ps3", [128, 512], F32, es=es, psum=True)
    ps_o = Tile(P, "f_pso", [128, 512], F32, es=es, psum=True)
    C = make_ln_common(P, nt, es, G, "f_")
    PP = ParamPack(P, "f_pp", [("rb1", W["rb1_cols"], 8)], es=es)

    nv = P.sbuf("f_nv", [128, 512], F32, es=es)
    nv_buf = Buf("f_nv")
    sall = P.sbuf("f_sall", [8, 512], BF16, es=es)
    sall_buf = Buf("f_sall")
    P.dma("sp", [(sall[:], W["S_all"], {})], writes=[sall_buf], name="f.sall")
    rw2sb = P.sbuf("f_rw2", [128, 64], BF16, es=es)
    rw2_buf = Buf("f_rw2")
    P.dma("pool", [(rw2sb[:].rearrange("p (a e) -> p a e", e=8), W["rw2T"], {})],
          writes=[rw2_buf], name="f.rw2")
    srow = Tile(P, "f_srow", [1, 24], F32, es=es)
    cebbf = Tile(P, "f_ceb", [1, 8], BF16, es=es)
    P.dma("sp", [(srow[0:1, 0:8], W["rb2"], {}), (srow[0:1, 8:16], W["ebias"], {}),
                 (srow[0:1, 16:17], W["alpha"], {})],
          writes=[srow.buf], name="f.rows")
    P.op("dve", lambda e: e.tensor_add(srow[0:1, 0:8], srow[0:1, 0:8], srow[0:1, 8:16]),
         reads=[srow.buf], writes=[srow.buf], name="f.ceb")
    P.op("dve", lambda e: e.tensor_copy(cebbf[0:1, :], srow[0:1, 0:8]),
         reads=[srow.buf], writes=[cebbf.buf], name="f.cebc")
    acol = Tile(P, "f_acol", [128, 1], F32, es=es)
    mmg(P, [(ps1[:, 0:1], G["ones_row_f"][0:1, :], srow[0:1, 16:17], True, True)],
        reads=[srow.buf, G["const_buf"]], writes=[ps1.buf], name="f.abc")
    P.op("act", lambda e: e.copy(acol[:, 0:1], ps1[:, 0:1]),
         reads=[ps1.buf], writes=[acol.buf], name="f.abe")

    vX = P.sbuf("f_vX", [128, 8 * 512], BF16, es=es)
    vX_buf = Buf("f_vX")
    wX = P.sbuf("f_wX", [128, 8 * 512], BF16, es=es)
    wX_buf = Buf("f_wX")
    uX = P.sbuf("f_uX", [128, 4 * D], BF16, es=es)
    uX_buf = Buf("f_uX")
    bX = P.sbuf("f_bX", [8, D], BF16, es=es)
    bX_buf = Buf("f_bX")
    rX = P.sbuf("f_rX", [128, NBLK * D], BF16, es=es)
    rX_buf = Buf("f_rX")
    woX = P.sbuf("f_woX", [128, NBLK * D], BF16, es=es)
    wo_buf = Buf("f_woX")
    qs_bufs = [Buf(f"f_qs.{i}", nofence=True) for i in range(nb)]
    ks_bufs = [Buf(f"f_ks.{i}", nofence=True) for i in range(nb)]
    fs_bufs = [Buf(f"f_fs.{i}", nofence=True) for i in range(nb)]
    pcs_buf = Buf("f_pcs", nofence=True)
    pcs2_buf = Buf("f_pcs2", nofence=True)

    es0 = P.scope()
    winv = P.sbuf("f_winv", [128, NBLK * D], BF16, es=es0)
    winv_buf = Buf("f_winv")
    P.dma("pool", [(winv[:].rearrange("p (a m) -> p a m", m=D), W["WinvT"], {})],
          writes=[winv_buf], name="f.winv")
    for dst_off, srcW in ((0, "Wie"), (1, "Vi")):
        P.dma("pool", [(vX[:].rearrange("p (a c) -> p a c", c=512), W[srcW], {})],
              writes=[vX_buf], name=f"f.p0l{dst_off}")
        for dblk in range(NBLK):
            for k in range(KEXP):
                mmg(P, [(ps1[:, 0:64],
                         winv[:, kb * D + dblk * 128: kb * D + dblk * 128 + 128],
                         vX[:, kb * 512 + k * 64: kb * 512 + k * 64 + 64], st, sp)
                        for kb, (st, sp) in zip(range(NBLK), accg(NBLK))],
                    reads=[winv_buf, vX_buf], writes=[ps1.buf],
                    name=f"f.p0m{dst_off}.{dblk}.{k}")
                P.op("act", lambda e, dblk=dblk, k=k: e.copy(
                    wX[:, dblk * 512 + k * 64: dblk * 512 + k * 64 + 64], ps1[:, 0:64]),
                    reads=[ps1.buf], writes=[wX_buf], name=f"f.p0e{dst_off}.{dblk}.{k}")
        P.dma("sp", [(scratch["pcs"][:, dst_off * 4096:(dst_off + 1) * 4096], wX[:], {})],
              reads=[wX_buf], writes=[pcs_buf], name=f"f.p0s{dst_off}")
    rwi = P.sbuf("f_rwi", [128, NBLK * D], BF16, es=es0)
    rwi_buf = Buf("f_rwi")
    P.dma("pool", [(rX[:].rearrange("p (a m) -> p a m", m=D), W["rw1T"], {})],
          writes=[rX_buf], name="f.p0r")
    for dblk in range(NBLK):
        for hc in range(2):
            mmg(P, [(ps1[:, 0:512],
                     winv[:, kb * D + dblk * 128: kb * D + dblk * 128 + 128],
                     rX[:, kb * D + hc * 512: kb * D + hc * 512 + 512], st, sp)
                    for kb, (st, sp) in zip(range(NBLK), accg(NBLK))],
                reads=[winv_buf, rX_buf], writes=[ps1.buf], name=f"f.p0rm.{dblk}.{hc}")
            P.op("act", lambda e, dblk=dblk, hc=hc: e.copy(
                rwi[:, dblk * 1024 + hc * 512: dblk * 1024 + hc * 512 + 512],
                ps1[:, 0:512]),
                reads=[ps1.buf], writes=[rwi_buf], name=f"f.p0re.{dblk}.{hc}")
    P.dma("sp", [(scratch["pcs2"][:, :], rwi[:], {})], reads=[rwi_buf],
          writes=[pcs2_buf], name="f.p0rs")
    P.close_scope(es0)

    qck = P.sbuf("f_qck", [128, NBLK * CK], BF16, es=es)
    qck_bufs = [Buf(f"f_qck.{i}") for i in range(NBLK)]
    kck = P.sbuf("f_kck", [128, NBLK * CK], BF16, es=es)
    kck_bufs = [Buf(f"f_kck.{i}") for i in range(NBLK)]
    esqk = P.scope()
    xck = P.sbuf("f_xck", [128, 2 * NBLK * CK], BF16, es=esqk)
    xck_bufs = [Buf("f_xck.0"), Buf("f_xck.1")]
    for tgt, wkey, snm in (("q", "WQT", "qs"), ("k", "WKT", "ks")):
        P.dma("pool", [(rX[:].rearrange("p (a m) -> p a m", m=D), W[wkey], {})],
              writes=[rX_buf], name=f"f.w{tgt}")
        for b in range(nb):
            for ck in range(nck):
                c0 = ck * CK
                slot = ck % 2
                P.dma("pool", [(xck[:, slot * NBLK * CK:(slot + 1) * NBLK * CK]
                                .rearrange("p (a c) -> p a c", c=CK),
                                x_in_ap(b, None)[:, :, c0:c0 + CK].rearrange(
                                    "a p c -> p a c"), {})],
                      reads=XIN(b), writes=[xck_bufs[slot]], name=f"f.x{tgt}{b}.{ck}")
                for blk in range(NBLK):
                    mmg(P, [(ps1[:, 0:CK],
                             rX[:, kb * D + blk * 128:kb * D + blk * 128 + 128],
                             xck[:, slot * NBLK * CK + kb * CK:
                                 slot * NBLK * CK + kb * CK + CK], st, sp)
                            for kb, (st, sp) in zip(range(NBLK), accg(NBLK))],
                        reads=[rX_buf, xck_bufs[slot]], writes=[ps1.buf],
                        name=f"f.{tgt}mm{b}.{ck}.{blk}")
                    P.op("act", lambda e, blk=blk: e.copy(
                        qck[:, blk * CK:(blk + 1) * CK], ps1[:, 0:CK]),
                        reads=[ps1.buf], writes=[qck_bufs[blk]],
                        name=f"f.{tgt}ev{b}.{ck}.{blk}")
                P.dma("sp", [(scratch[snm][b, blk][:, c0:c0 + CK],
                              qck[:, blk * CK:(blk + 1) * CK], {})
                             for blk in range(NBLK)],
                      reads=qck_bufs,
                      writes=[(qs_bufs if tgt == "q" else ks_bufs)[b]],
                      name=f"f.{tgt}st{b}.{ck}")

    P.close_scope(esqk)
    hx = DT(P, "f_hx", NBLK, nt, F32, es=es)
    hrbf = P.sbuf("f_hr", [128, NBLK * CK], BF16, es=es)
    hr_bufs = [Buf(f"f_hr.{i}") for i in range(NBLK)]
    lgt = Tile(P, "f_lgt", [128, 8], F32, es=es)
    prb = Tile(P, "f_prb", [128, 8], F32, es=es)
    pcol = Tile(P, "f_pcol", [128, 4], F32, es=es)
    wbf = Tile(P, "f_wbf", [128, 8], BF16, es=es)
    wTb = Tile(P, "f_wT", [8, CK], BF16, es=es)
    xv = P.sbuf("f_xv", [128, 4 * CK], BF16, es=es)
    xv_bufs = [Buf(f"f_xv.{i}") for i in range(4)]
    cum = P.sbuf("f_cum", [128, 4 * CK], BF16, es=es)
    cum_bufs = [Buf(f"f_cum.{i}") for i in range(4)]
    pw = P.sbuf("f_pw", [128, 4 * CK], BF16, es=es)
    pw_bufs = [Buf(f"f_pw.{i}") for i in range(4)]
    carry = Tile(P, "f_carry", [128, 4], F32, es=es)
    fck = P.sbuf("f_fck", [128, NBLK * CK], BF16, es=es)
    fck_bufs = [Buf(f"f_fck.{i}") for i in range(NBLK)]
    cbk = fck
    cbk_bufs = fck_bufs
    tmpb = Tile(P, "f_tmpb", [128, CK], BF16, es=es)
    ctmp = Tile(P, "f_ctmp", [128, CK], F32, es=es)
    xres = P.sbuf("f_xres", [128, 2 * CK], F32, es=es)
    xres_bufs = [Buf("f_xres.0"), Buf("f_xres.1")]

    P.dma("pool", [(woX[:].rearrange("p (a m) -> p a m", m=D), W["WOT"], {})],
          writes=[wo_buf], name="f.wo")

    for br in range(2):
        if br == 0:
            P.dma("pool", [(vX[:].rearrange("p (a c) -> p a c", c=512), W["Vf"], {})],
                  writes=[vX_buf], name="f.vf")
            P.dma("pool", [(wX[:].rearrange("p (a c) -> p a c", c=512), W["Wf"], {})],
                  writes=[wX_buf], name="f.wf")
            P.dma("pool", [(uX[:].rearrange("p (a m) -> p a m", m=D),
                            W["UTf"].rearrange("a p m -> p a m"), {})],
                  writes=[uX_buf], name="f.uf")
            P.dma("pool", [(bX[:], W["bf"], {})], writes=[bX_buf], name="f.bf")
            P.dma("pool", [(rX[:].rearrange("p (a m) -> p a m", m=D), W["rw1T"], {})],
                  writes=[rX_buf], name="f.r1f")
        else:
            P.dma("sp", [(vX[:], scratch["pcs"][:, 0:4096], {})],
                  reads=[pcs_buf], writes=[vX_buf], name="f.vi")
            P.dma("sp", [(wX[:], scratch["pcs"][:, 4096:8192], {})],
                  reads=[pcs_buf], writes=[wX_buf], name="f.wi")
            P.dma("pool", [(uX[:].rearrange("p (a m) -> p a m", m=D),
                            W["UTi"].rearrange("a p m -> p a m"), {})],
                  writes=[uX_buf], name="f.ui")
            P.dma("pool", [(bX[:], W["bi"], {})], writes=[bX_buf], name="f.bi")
            P.dma("sp", [(rX[:], scratch["pcs2"][:, :], {})],
                  reads=[pcs2_buf], writes=[rX_buf], name="f.r1i")
        for b in range(nb):
            P.op("pool", lambda e: e.memset(carry[:, :], 0.0),
                 writes=[carry.buf], name=f"f.cz{br}.{b}")
            for ck in range(nck):
                c0 = ck * CK
                P.dma("sp", [(qck[:, blk * CK:(blk + 1) * CK],
                              scratch["qs"][b, blk][:, c0:c0 + CK], {})
                             for blk in range(NBLK)],
                      reads=[qs_bufs[b]], writes=qck_bufs, name=f"f.lq{br}.{b}.{ck}")
                P.dma("sp", [(kck[:, blk * CK:(blk + 1) * CK],
                              scratch["ks"][b, blk][:, c0:c0 + CK], {})
                             for blk in range(NBLK)],
                      reads=[ks_bufs[b]], writes=kck_bufs, name=f"f.lk{br}.{b}.{ck}")
                if br == 1:
                    P.dma("sp", [(fck[:, blk * CK:(blk + 1) * CK],
                                  scratch["fs"][b, blk][:, c0:c0 + CK], {})
                                 for blk in range(NBLK)],
                          reads=[fs_bufs[b]], writes=fck_bufs, name=f"f.lf{b}.{ck}")
                for blk in range(NBLK):
                    mmg(P, [(ps1[:, 0:CK],
                             rX[:, kb * D + blk * 128:kb * D + blk * 128 + 128],
                             qck[:, kb * CK:(kb + 1) * CK], st, sp)
                            for kb, (st, sp) in zip(range(NBLK), accg(NBLK))],
                        reads=[rX_buf] + qck_bufs, writes=[ps1.buf],
                        name=f"f.rm{br}.{b}.{ck}.{blk}")
                    P.op("act", lambda e, blk=blk: e.activation(
                        hrbf[:, blk * CK:(blk + 1) * CK], ps1[:, 0:CK], AF.Gelu,
                        bias=PP.col("rb1", blk), scale=1.0),
                        reads=[ps1.buf, PP.buf], writes=[hr_bufs[blk]],
                        name=f"f.re{br}.{b}.{ck}.{blk}")
                for tch in range(CK // 128):
                    t0 = tch * 128
                    items = [(ps_o[:, 0:8],
                              hrbf[:, kb * CK + t0: kb * CK + t0 + 128],
                              rw2sb[:, kb * 8:(kb + 1) * 8], st, False)
                             for kb, (st, _) in zip(range(NBLK), accg(NBLK))]
                    items.append((ps_o[:, 0:8], G["ones_row"][0:1, 0:128],
                                  cebbf[0:1, :], False, True))
                    mmg(P, items, reads=hr_bufs + [rw2_buf, cebbf.buf, G["const_buf"]],
                        writes=[ps_o.buf], name=f"f.lg{br}.{b}.{ck}.{tch}")
                    P.op("dve", lambda e: e.tensor_copy(lgt[:, :], ps_o[:, 0:8]),
                         reads=[ps_o.buf], writes=[lgt.buf], name=f"f.lge{br}.{b}.{ck}.{tch}")
                    P.op("dve", lambda e: e.reduce_max(pcol[:, 0:1], lgt[:, :],
                                                       axis=AX.X, negate=True),
                         reads=[lgt.buf], writes=[pcol.buf], name=f"f.mx{br}.{b}.{ck}.{tch}")
                    P.op("act", lambda e: e.activation(prb[:, :], lgt[:, :], AF.Exp,
                                                       bias=pcol[:, 0:1], scale=1.0,
                                                       accum_out=pcol[:, 1:2]),
                         reads=[lgt.buf, pcol.buf], writes=[prb.buf, pcol.buf],
                         name=f"f.ex{br}.{b}.{ck}.{tch}")
                    P.op("dve", lambda e: e.reciprocal(pcol[:, 2:3], pcol[:, 1:2]),
                         reads=[pcol.buf], writes=[pcol.buf], name=f"f.rc{br}.{b}.{ck}.{tch}")
                    P.op("dve", lambda e: e.tensor_scalar_mul(wbf[:, :], prb[:, :],
                                                              pcol[:, 2:3]),
                         reads=[prb.buf, pcol.buf], writes=[wbf.buf],
                         name=f"f.wn{br}.{b}.{ck}.{tch}")
                    mmg(P, [(ps3[0:8, t0:t0 + 128], wbf[:, :], G["eqm"][:],
                             tch == 0, tch == CK // 128 - 1)],
                        reads=[wbf.buf, G["const_buf"]], writes=[ps3.buf],
                        name=f"f.wt{br}.{b}.{ck}.{tch}")
                P.op("act", lambda e: e.copy(wTb[0:8, 0:CK], ps3[0:8, 0:CK]),
                     reads=[ps3.buf], writes=[wTb.buf], name=f"f.wte{br}.{b}.{ck}")
                for mp in range(4):
                    mmg(P, [(ps2[:, 0:CK],
                             vX[:, kb * 512 + mp * 128: kb * 512 + mp * 128 + 128],
                             qck[:, kb * CK:(kb + 1) * CK], st, sp)
                            for kb, (st, sp) in zip(range(NBLK), accg(NBLK))],
                        reads=[vX_buf] + qck_bufs, writes=[ps2.buf],
                        name=f"f.xv{br}.{b}.{ck}.{mp}")
                    P.op("act", lambda e, mp=mp: e.copy(
                        xv[:, mp * CK:(mp + 1) * CK], ps2[:, 0:CK]),
                        reads=[ps2.buf], writes=[xv_bufs[mp]],
                        name=f"f.xve{br}.{b}.{ck}.{mp}")
                    mmg(P, [(ps2[:, 0:CK],
                             wX[:, kb * 512 + mp * 128: kb * 512 + mp * 128 + 128],
                             kck[:, kb * CK:(kb + 1) * CK], st, sp)
                            for kb, (st, sp) in zip(range(NBLK), accg(NBLK))],
                        reads=[wX_buf] + kck_bufs, writes=[ps2.buf],
                        name=f"f.yw{br}.{b}.{ck}.{mp}")
                    P.op("dve", lambda e, mp=mp: e.tensor_tensor_scan(
                        cum[:, mp * CK:(mp + 1) * CK],
                        G["ones_col_f"][:, 0:1].broadcast_to([128, CK]),
                        ps2[:, 0:CK], carry[:, mp:mp + 1], ALU.mult, ALU.add),
                        reads=[ps2.buf, carry.buf, G["const_buf"]],
                        writes=[cum_bufs[mp]], name=f"f.sc{br}.{b}.{ck}.{mp}")
                    P.op("dve", lambda e, mp=mp: e.tensor_copy(
                        carry[:, mp:mp + 1], cum[:, mp * CK + CK - 1: mp * CK + CK]),
                        reads=[cum_bufs[mp]], writes=[carry.buf],
                        name=f"f.cu{br}.{b}.{ck}.{mp}")
                    mmg(P, [(ps3[:, 0:CK], sall[:, mp * 128:(mp + 1) * 128],
                             wTb[0:8, 0:CK], True, True)],
                        reads=[sall_buf, wTb.buf], writes=[ps3.buf],
                        name=f"f.wb{br}.{b}.{ck}.{mp}")
                    P.op("dve", lambda e, mp=mp: e.tensor_mul(
                        tmpb[:, 0:CK], xv[:, mp * CK:(mp + 1) * CK],
                        cum[:, mp * CK:(mp + 1) * CK]),
                        reads=[xv_bufs[mp], cum_bufs[mp]], writes=[tmpb.buf],
                        name=f"f.pr{br}.{b}.{ck}.{mp}")
                    P.op("dve", lambda e, mp=mp: e.tensor_mul(
                        pw[:, mp * CK:(mp + 1) * CK], tmpb[:, 0:CK], ps3[:, 0:CK]),
                        reads=[tmpb.buf, ps3.buf], writes=[pw_bufs[mp]],
                        name=f"f.pw{br}.{b}.{ck}.{mp}")
                for dblk in range(NBLK):
                    items = [(ps_o[:, 0:CK],
                              uX[:, mp * D + dblk * 128: mp * D + dblk * 128 + 128],
                              pw[:, mp * CK:(mp + 1) * CK], st, False)
                             for mp, (st, _) in zip(range(4), accg(4))]
                    items.append((ps_o[:, 0:CK], bX[0:8, dblk * 128:dblk * 128 + 128],
                                  wTb[0:8, 0:CK], False, True))
                    mmg(P, items, reads=pw_bufs + [uX_buf, bX_buf, wTb.buf],
                        writes=[ps_o.buf], name=f"f.om{br}.{b}.{ck}.{dblk}")
                    if br == 0:
                        P.op("act", lambda e, dblk=dblk: e.copy(
                            fck[:, dblk * CK:(dblk + 1) * CK], ps_o[:, 0:CK]),
                            reads=[ps_o.buf], writes=[fck_bufs[dblk]],
                            name=f"f.oe{br}.{b}.{ck}.{dblk}")
                    else:
                        P.op("dve", lambda e, dblk=dblk: e.scalar_tensor_tensor(
                            cbk[:, dblk * CK:(dblk + 1) * CK], ps_o[:, 0:CK],
                            acol[:, 0:1], fck[:, dblk * CK:(dblk + 1) * CK],
                            ALU.mult, ALU.add),
                            reads=[ps_o.buf, acol.buf, fck_bufs[dblk]],
                            writes=[cbk_bufs[dblk]], name=f"f.cb{br}.{b}.{ck}.{dblk}")
                if br == 0:
                    P.dma("sp", [(scratch["fs"][b, blk][:, c0:c0 + CK],
                                  fck[:, blk * CK:(blk + 1) * CK], {})
                                 for blk in range(NBLK)],
                          reads=fck_bufs, writes=[fs_bufs[b]], name=f"f.fst{b}.{ck}")
                if br == 1:
                    P.dma("sp", [(nv[:, 0:CK], W["nv"][:, c0:c0 + CK], {})],
                          writes=[nv_buf], name=f"f.nvl{b}.{ck}")
                    for blk in range(NBLK):
                        mmg(P, [(ps1[:, 0:CK],
                                 woX[:, kb * D + blk * 128:kb * D + blk * 128 + 128],
                                 cbk[:, kb * CK:(kb + 1) * CK], st, sp)
                                for kb, (st, sp) in zip(range(NBLK), accg(NBLK))],
                            reads=[wo_buf] + cbk_bufs, writes=[ps1.buf],
                            name=f"f.wom{b}.{ck}.{blk}")
                        P.op("dve", lambda e, blk=blk: e.tensor_mul(
                            ctmp[:, 0:CK], ps1[:, 0:CK], nv[:, 0:CK]),
                            reads=[ps1.buf, nv_buf], writes=[ctmp.buf],
                            name=f"f.nv{b}.{ck}.{blk}")
                        xslot = blk % 2
                        P.dma("sp", [(xres[:, xslot * CK:(xslot + 1) * CK],
                                      x_in_ap(b, blk)[:, c0:c0 + CK], {})],
                              reads=XIN(b), writes=[xres_bufs[xslot]],
                              name=f"f.xr{b}.{ck}.{blk}")
                        P.op("dve", lambda e, blk=blk, c0=c0, xslot=xslot: e.tensor_add(
                            hx.ap(blk, c0, c0 + CK), ctmp[:, 0:CK],
                            xres[:, xslot * CK:(xslot + 1) * CK]),
                            reads=[ctmp.buf, xres_bufs[xslot]], writes=[hx.bufs[blk]],
                            name=f"f.hx{b}.{ck}.{blk}")
            if br == 1:
                emit_ln(P, C, hx, hx, None, None, None, EPS_MOD, f"f.pln.{b}")
                P.dma("sp", [(x_out_ap(b, blk), hx.ap(blk), {}) for blk in range(NBLK)],
                      reads=[hx.bufs[blk] for blk in range(NBLK)],
                      writes=XOUT(b), name=f"f.out{b}", is_output=FINAL)

    P.close_scope(es)



def declare_l1f_weights(nc, nt):
    def t(n, s, dt=F32):
        return nc.dram_tensor(n, s, dt, kind="ExternalInput").ap()
    W = {}
    W["gwT"] = t("l1f_gw", [128, 8, 8])
    W["gpT"] = t("l1f_gp", [KEXP, 128, 8, MH])
    W["upT"] = t("l1f_up", [KEXP, 128, 8, MH])
    W["dpT"] = t("l1f_dp", [KEXP, 128, MHB, D])
    W["TRI"] = t("l1f_tri", [128, 128], BF16)
    W["SEL"] = t("l1f_sel", [16, 2])
    W["ESEL"] = t("l1f_esel", [8, 1024], BF16)
    return W


def host_l1f(inp, nt):
    H = {}
    gw = np.ascontiguousarray(np.asarray(inp["L1_ffn_gate_w"], np.float32).T)
    H["l1f_gw"] = np.ascontiguousarray(gw.reshape(8, 128, 8).transpose(1, 0, 2))
    gp = np.asarray(inp["L1_ffn_gate_proj_w"], np.float32)
    H["l1f_gp"] = np.ascontiguousarray(
        gp.transpose(0, 2, 1).reshape(KEXP, 8, 128, MH).transpose(0, 2, 1, 3))
    up = np.asarray(inp["L1_ffn_up_proj_w"], np.float32)
    H["l1f_up"] = np.ascontiguousarray(
        up.transpose(0, 2, 1).reshape(KEXP, 8, 128, MH).transpose(0, 2, 1, 3))
    dp = np.asarray(inp["L1_ffn_down_proj_w"], np.float32)
    H["l1f_dp"] = np.ascontiguousarray(
        dp.transpose(0, 2, 1).reshape(KEXP, MHB, 128, D).transpose(0, 2, 1, 3))
    tri = np.triu(np.ones((128, 128), np.float32))
    H["l1f_tri"] = tri.astype(BF)
    esel = np.zeros((8, 1024), np.float32)
    for ei in range(8):
        esel[ei, ei * 128:(ei + 1) * 128] = 1.0
    H["l1f_esel"] = esel.astype(BF)
    return H


def emit_l1f_prestage(P, W):
    gpb = P.nc.dram_tensor("m_gpb", [KEXP, 2, 128, NBLK * (MH // 2)], BF16,
                           kind="Internal").ap()
    upb = P.nc.dram_tensor("m_upb", [KEXP, 2, 128, NBLK * (MH // 2)], BF16,
                           kind="Internal").ap()
    dpb = P.nc.dram_tensor("m_dpb", [KEXP, 128, MHB, D], BF16, kind="Internal").ap()
    wbf_bufs = {"gp": [Buf(f"m_gpb{i}", nofence=True) for i in range(KEXP)],
                "up": [Buf(f"m_upb{i}", nofence=True) for i in range(KEXP)],
                "dp": [Buf(f"m_dpb{i}", nofence=True) for i in range(KEXP)]}
    for e_ in range(KEXP):
        P.dma("pool", [(gpb[e_, half].rearrange("p (a m) -> p a m", m=MH // 2),
                        W["gpT"][e_][:, :, half * (MH // 2):(half + 1) * (MH // 2)],
                        {}) for half in range(2)],
              writes=[wbf_bufs["gp"][e_]], name=f"m.cgp{e_}")
        P.dma("pool", [(upb[e_, half].rearrange("p (a m) -> p a m", m=MH // 2),
                        W["upT"][e_][:, :, half * (MH // 2):(half + 1) * (MH // 2)],
                        {}) for half in range(2)],
              writes=[wbf_bufs["up"][e_]], name=f"m.cup{e_}")
        P.dma("pool", [(dpb[e_], W["dpT"][e_], {})], writes=[wbf_bufs["dp"][e_]],
              name=f"m.cdp{e_}")
    return {"gpb": gpb, "upb": upb, "dpb": dpb, "bufs": wbf_bufs}


def emit_l1_ffn(P, G, W, x_in_ap, x_out_ap, nt, capacity, do_collective, cc, xio=None,
                prestage=None):
    xio = xio or {}
    XIN = (lambda b: [xio["in"][b]]) if xio.get("in") else (lambda b: [])
    XOUT = (lambda b: [xio["out"][b]]) if xio.get("out") else (lambda b: [])
    FINAL = xio.get("final", True)
    """cc: dict with Internal dram APs cc_in (2,8), cc_out (16,8) (if do_collective)."""
    nb = G["nb"]
    CK = min(512, nt)
    nck = nt // CK
    ntch = nt // 128
    HHB = MHB // 2
    es = P.scope()

    ps_g = Tile(P, "m_psg", [128, 512], F32, es=es, psum=True)
    ps_g2 = Tile(P, "m_psg2", [128, 512], F32, es=es, psum=True)
    ps_a = Tile(P, "m_psa", [128, 512], F32, es=es, psum=True)
    ps_u = Tile(P, "m_psu", [128, 512], F32, es=es, psum=True)
    ps_y = Tile(P, "m_psy", [128, 512], F32, es=es, psum=True)
    C = make_ln_common(P, nt, es, G, "m_")

    gwsb = P.sbuf("m_gw", [128, 64], BF16, es=es)
    gw_buf = Buf("m_gw")
    P.dma("pool", [(gwsb[:].rearrange("p (a e) -> p a e", e=8), W["gwT"], {})],
          writes=[gw_buf], name="m.gw")
    tri = P.sbuf("m_tri", [128, 128], BF16, es=es)
    tri_buf = Buf("m_tri")
    P.dma("sp", [(tri[:], W["TRI"], {})], writes=[tri_buf], name="m.tri")
    selsb = P.sbuf("m_sel", [16, 2], F32, es=es)
    sel_buf = Buf("m_sel")
    P.dma("sp", [(selsb[:], W["SEL"], {})], writes=[sel_buf], name="m.sel")
    esel = P.sbuf("m_esel", [8, 1024], BF16, es=es)
    esel_buf = Buf("m_esel")
    P.dma("sp", [(esel[:], W["ESEL"], {})], writes=[esel_buf], name="m.esel")

    ohsb = P.sbuf("m_oh", [128, nb * 2 * ntch * 8], BF16, es=es)
    oh_buf = Buf("m_oh")
    ohfb = P.sbuf("m_ohf", [128, nb * 2 * ntch * 8], F32, es=es)
    ohf_buf = Buf("m_ohf")
    wcol = P.sbuf("m_wc", [128, nb * 2 * ntch], F32, es=es)
    wc_buf = Buf("m_wc")
    mrc = P.sbuf("m_mr", [128, nb * 2 * ntch], F32, es=es)
    mr_buf = Buf("m_mr")
    prev = Tile(P, "m_prev", [1, 16], F32, es=es)
    offs = Tile(P, "m_offs", [1, 16], F32, es=es)
    ccsb = Tile(P, "m_cc", [16, 8], F32, es=es)
    lgt = Tile(P, "m_lgt", [128, 8], F32, es=es)
    prb = Tile(P, "m_prb", [128, 8], F32, es=es)
    p2 = Tile(P, "m_p2", [128, 8], F32, es=es)
    pcol = Tile(P, "m_pcol", [128, 8], F32, es=es)
    t8 = Tile(P, "m_t8", [128, 8], F32, es=es)
    xall = P.sbuf("m_xall", [128, NBLK * nt], BF16, es=es)
    xall_bufs = [Buf(f"m_xall.{i}") for i in range(NBLK)]

    P.op("pool", lambda e: e.memset(prev[:, :], 0.0), writes=[prev.buf], name="m.pz")

    if prestage is None:
        prestage = emit_l1f_prestage(P, W)
    gpb, upb, dpb = prestage["gpb"], prestage["upb"], prestage["dpb"]
    wbf_bufs = prestage["bufs"]

    def oh_ap(b, k, tch, t=None):
        o = (((b * 2 + k) * ntch) + tch) * 8
        return o

    for b in range(nb):
        P.dma("pool", [(xall[:, kb * nt:(kb + 1) * nt], x_in_ap(b, kb), {})
                       for kb in range(NBLK)],
              reads=XIN(b), writes=xall_bufs, name=f"m.x{b}")
        for tch in range(ntch):
            t0 = tch * 128
            mmg(P, [(ps_g[:, 0:8],
                     xall[:, kb * nt + t0: kb * nt + t0 + 128],
                     gwsb[:, kb * 8:(kb + 1) * 8], st, sp)
                    for kb, (st, sp) in zip(range(NBLK), accg(NBLK))],
                reads=[gw_buf] + xall_bufs, writes=[ps_g.buf],
                name=f"m.lg{b}.{tch}")
            P.op("dve", lambda e: e.tensor_copy(lgt[:, :], ps_g[:, 0:8]),
                 reads=[ps_g.buf], writes=[lgt.buf], name=f"m.lge{b}.{tch}")
            P.op("dve", lambda e: e.reduce_max(pcol[:, 0:1], lgt[:, :],
                                               axis=AX.X, negate=True),
                 reads=[lgt.buf], writes=[pcol.buf], name=f"m.mx{b}.{tch}")
            P.op("act", lambda e: e.activation(prb[:, :], lgt[:, :], AF.Exp,
                                               bias=pcol[:, 0:1], scale=1.0,
                                               accum_out=pcol[:, 1:2]),
                 reads=[lgt.buf, pcol.buf], writes=[prb.buf, pcol.buf],
                 name=f"m.ex{b}.{tch}")
            P.op("dve", lambda e: e.reciprocal(pcol[:, 2:3], pcol[:, 1:2]),
                 reads=[pcol.buf], writes=[pcol.buf], name=f"m.rc{b}.{tch}")
            P.op("dve", lambda e: e.tensor_scalar_mul(prb[:, :], prb[:, :], pcol[:, 2:3]),
                 reads=[prb.buf, pcol.buf], writes=[prb.buf], name=f"m.pn{b}.{tch}")
            P.op("dve", lambda e: e.reduce_max(pcol[:, 3:4], prb[:, :], axis=AX.X),
                 reads=[prb.buf], writes=[pcol.buf], name=f"m.v1{b}.{tch}")
            o = oh_ap(b, 0, tch)
            P.op("dve", lambda e, o=o: e.tensor_single_scalar(
                ohfb[:, o:o + 8], prb[:, :], pcol[:, 3:4], ALU.is_equal),
                reads=[prb.buf, pcol.buf], writes=[ohf_buf], name=f"m.oh1{b}.{tch}")
            P.op("dve", lambda e, o=o: e.tensor_copy(ohsb[:, o:o + 8], ohfb[:, o:o + 8]),
                 reads=[ohf_buf], writes=[oh_buf], name=f"m.oh1c{b}.{tch}")
            P.op("dve", lambda e, o=o: e.scalar_tensor_tensor(
                p2[:, :], ohfb[:, o:o + 8], -1e30, prb[:, :], ALU.mult, ALU.add),
                reads=[ohf_buf, prb.buf], writes=[p2.buf], name=f"m.p2{b}.{tch}")
            P.op("dve", lambda e: e.reduce_max(pcol[:, 4:5], p2[:, :], axis=AX.X),
                 reads=[p2.buf], writes=[pcol.buf], name=f"m.v2{b}.{tch}")
            o2 = oh_ap(b, 1, tch)
            P.op("dve", lambda e, o2=o2: e.tensor_single_scalar(
                ohfb[:, o2:o2 + 8], p2[:, :], pcol[:, 4:5], ALU.is_equal),
                reads=[p2.buf, pcol.buf], writes=[ohf_buf], name=f"m.oh2{b}.{tch}")
            P.op("dve", lambda e, o2=o2: e.tensor_copy(ohsb[:, o2:o2 + 8], ohfb[:, o2:o2 + 8]),
                 reads=[ohf_buf], writes=[oh_buf], name=f"m.oh2c{b}.{tch}")
            P.op("dve", lambda e: e.tensor_add(pcol[:, 5:6], pcol[:, 3:4], pcol[:, 4:5]),
                 reads=[pcol.buf], writes=[pcol.buf], name=f"m.s12{b}.{tch}")
            P.op("dve", lambda e: e.tensor_scalar_add(pcol[:, 5:6], pcol[:, 5:6], 1e-9),
                 reads=[pcol.buf], writes=[pcol.buf], name=f"m.se{b}.{tch}")
            P.op("dve", lambda e: e.reciprocal(pcol[:, 6:7], pcol[:, 5:6]),
                 reads=[pcol.buf], writes=[pcol.buf], name=f"m.sr{b}.{tch}")
            wo1 = (b * 2 + 0) * ntch + tch
            wo2 = (b * 2 + 1) * ntch + tch
            P.op("dve", lambda e, wo1=wo1: e.tensor_mul(
                wcol[:, wo1:wo1 + 1], pcol[:, 3:4], pcol[:, 6:7]),
                reads=[pcol.buf], writes=[wc_buf], name=f"m.w1{b}.{tch}")
            P.op("dve", lambda e, wo2=wo2: e.tensor_mul(
                wcol[:, wo2:wo2 + 1], pcol[:, 4:5], pcol[:, 6:7]),
                reads=[pcol.buf], writes=[wc_buf], name=f"m.w2{b}.{tch}")
            items = []
            for k, o_ in ((0, o), (1, o2)):
                items.append((ps_g[:, 16 + k * 8:16 + k * 8 + 8], tri[:],
                              ohsb[:, o_:o_ + 8], k == 0, False))
            for k, o_ in ((0, o), (1, o2)):
                items.append((ps_g[0:1, 32 + k * 8:32 + k * 8 + 8],
                              G["ones_col"][:, 0:1], ohsb[:, o_:o_ + 8], False, False))

            def prev_items(eng, items=items):
                last = None
                for (oap, l, r, st, sp) in items:
                    last = eng.matmul(oap, l, r, start=st, stop=sp,
                                      skip_group_check=True)
                for k in range(2):
                    last = eng.matmul(ps_g[:, 16 + k * 8:16 + k * 8 + 8],
                                      G["ones_row_f"][0:1, :],
                                      prev[0:1, k * 8:(k + 1) * 8],
                                      start=False, stop=(k == 1),
                                      skip_group_check=True)
                return last
            P.op("pe", prev_items,
                 reads=[tri_buf, oh_buf, prev.buf, G["const_buf"]],
                 writes=[ps_g.buf], name=f"m.rk{b}.{tch}")
            P.op("dve", lambda e: e.tensor_add(prev[0:1, :], prev[0:1, :],
                                               ps_g[0:1, 32:48]),
                 reads=[prev.buf, ps_g.buf], writes=[prev.buf], name=f"m.pu{b}.{tch}")
            for k, o_ in ((0, o), (1, o2)):
                wo = (b * 2 + k) * ntch + tch
                P.op("dve", lambda e, k=k, o_=o_: e.tensor_mul(
                    t8[:, :], ps_g[:, 16 + k * 8:16 + k * 8 + 8], ohfb[:, o_:o_ + 8]),
                    reads=[ps_g.buf, ohf_buf], writes=[t8.buf], name=f"m.mr1{b}.{tch}.{k}")
                P.op("dve", lambda e, wo=wo: e.reduce_sum(
                    mrc[:, wo:wo + 1], t8[:, :], axis=AX.X),
                    reads=[t8.buf], writes=[mr_buf], name=f"m.mr2{b}.{tch}.{k}")

    if do_collective:
        ccin_buf = Buf("m_ccin")
        ccbuf = Buf("m_ccdma")
        zpad = Tile(P, "m_zpad", [2, 128], F32, es=es)
        P.op("pool", lambda e: e.memset(zpad[:, :], 0.0), writes=[zpad.buf], name="m.zp")
        P.dma("sp", [(cc["cc_in"], zpad[:, :], {})],
              reads=[zpad.buf], writes=[ccin_buf], name="m.ccz")
        P.dma("sp", [(cc["cc_in"][0:1, 0:8], prev[0:1, 0:8], {}),
                     (cc["cc_in"][1:2, 0:8], prev[0:1, 8:16], {})],
              reads=[prev.buf], writes=[ccin_buf], name="m.ccout")

        def collfn(e):
            return e.collective_compute(
                "AllGather", ALU.bypass, replica_groups=[list(range(8))],
                ins=[cc["cc_in"].opt()], outs=[cc["cc_out"].opt()])
        P.op("pool", collfn, reads=[ccin_buf], writes=[ccbuf], name="m.cc")
        P.dma("sp", [(ccsb[:, :], cc["cc_out"][:, 0:8], {})], reads=[ccbuf],
              writes=[ccsb.buf], name="m.ccin")
        mmg(P, [(ps_g[0:1, k * 8:(k + 1) * 8], selsb[:, k:k + 1], ccsb[:, :],
                 k == 0, k == 1) for k in range(2)],
            reads=[sel_buf, ccsb.buf], writes=[ps_g.buf], name="m.off")
        P.op("act", lambda e: e.copy(offs[0:1, :], ps_g[0:1, 0:16]),
             reads=[ps_g.buf], writes=[offs.buf], name="m.offe")
    else:
        P.op("pool", lambda e: e.memset(offs[:, :], 0.0), writes=[offs.buf], name="m.off0")

    hx = DT(P, "m_hx", NBLK, nt, F32, es=es)
    cmb = Tile(P, "m_cmb", [128, 8], F32, es=es)
    cmbbf = Tile(P, "m_cmbbf", [128, 8], BF16, es=es)
    combT = Tile(P, "m_combT", [8, nt], BF16, es=es)
    gph = P.sbuf("m_gph", [128, NBLK * MH // 2], BF16, es=es)
    gph_buf = Buf("m_gph")
    uph = P.sbuf("m_uph", [128, NBLK * MH // 2], BF16, es=es)
    uph_buf = Buf("m_uph")
    dph = P.sbuf("m_dph", [128, HHB * D], BF16, es=es)
    dph_buf = Buf("m_dph")
    cck = P.sbuf("m_cck", [128, HHB * CK], BF16, es=es)
    cck_bufs = [Buf(f"m_cck.{i}") for i in range(HHB)]
    xek = P.sbuf("m_xek", [128, NBLK * CK], BF16, es=es)
    xek_bufs = [Buf(f"m_xek.{i}") for i in range(NBLK)]
    cbs = Tile(P, "m_cbs", [128, CK], BF16, es=es)
    gsb = P.sbuf("m_gsb", [128, 2 * CK], BF16, es=es)
    gsb_bufs = [Buf("m_gsb.0"), Buf("m_gsb.1")]

    for b in range(nb):
        for tch in range(ntch):
            t0 = tch * 128
            o = oh_ap(b, 0, tch)
            o2 = oh_ap(b, 1, tch)
            for k, o_ in ((0, o), (1, o2)):
                wo = (b * 2 + k) * ntch + tch
                mmg(P, [(ps_a[:, 0:8], G["ones_row_f"][0:1, :],
                         offs[0:1, k * 8:(k + 1) * 8], True, True)],
                    reads=[offs.buf, G["const_buf"]], writes=[ps_a.buf],
                    name=f"m.ob{b}.{tch}.{k}")
                P.op("dve", lambda e, o_=o_: e.tensor_mul(
                    t8[:, :], ps_a[:, 0:8], ohfb[:, o_:o_ + 8]),
                    reads=[ps_a.buf, ohf_buf], writes=[t8.buf], name=f"m.os{b}.{tch}.{k}")
                P.op("dve", lambda e: e.reduce_sum(pcol[:, 0:1], t8[:, :], axis=AX.X),
                     reads=[t8.buf], writes=[pcol.buf], name=f"m.or{b}.{tch}.{k}")
                P.op("dve", lambda e, wo=wo: e.tensor_add(
                    pcol[:, 1:2], pcol[:, 0:1], mrc[:, wo:wo + 1]),
                    reads=[pcol.buf, mr_buf], writes=[pcol.buf], name=f"m.rt{b}.{tch}.{k}")
                P.op("dve", lambda e: e.tensor_single_scalar(
                    pcol[:, 2:3], pcol[:, 1:2], capacity + 0.5, ALU.is_lt),
                    reads=[pcol.buf], writes=[pcol.buf], name=f"m.kp{b}.{tch}.{k}")
                P.op("dve", lambda e, wo=wo, k=k: e.tensor_mul(
                    pcol[:, 3 + k:4 + k], pcol[:, 2:3], wcol[:, wo:wo + 1]),
                    reads=[pcol.buf, wc_buf], writes=[pcol.buf], name=f"m.wk{b}.{tch}.{k}")
            P.op("dve", lambda e, o=o: e.tensor_scalar_mul(
                cmb[:, :], ohfb[:, o:o + 8], pcol[:, 3:4]),
                reads=[ohf_buf, pcol.buf], writes=[cmb.buf], name=f"m.cb1{b}.{tch}")
            P.op("dve", lambda e, o2=o2: e.scalar_tensor_tensor(
                cmb[:, :], ohfb[:, o2:o2 + 8], pcol[:, 4:5], cmb[:, :],
                ALU.mult, ALU.add),
                reads=[ohf_buf, pcol.buf, cmb.buf], writes=[cmb.buf],
                name=f"m.cb2{b}.{tch}")
            P.op("dve", lambda e: e.tensor_copy(cmbbf[:, :], cmb[:, :]),
                 reads=[cmb.buf], writes=[cmbbf.buf], name=f"m.cbc{b}.{tch}")
            mmg(P, [(ps_y[0:8, (t0 % 512):(t0 % 512) + 128], cmbbf[:, :], G["eqm"][:],
                     t0 % 512 == 0, t0 % 512 == 384 or tch == ntch - 1)],
                reads=[cmbbf.buf, G["const_buf"]], writes=[ps_y.buf],
                name=f"m.ct{b}.{tch}")
            if t0 % 512 == 384 or tch == ntch - 1:
                c0 = (t0 // 512) * 512
                P.op("act", lambda e, c0=c0, t0=t0: e.copy(
                    combT[0:8, c0:c0 + (t0 % 512) + 128], ps_y[0:8, 0:(t0 % 512) + 128]),
                    reads=[ps_y.buf], writes=[combT.buf], name=f"m.cte{b}.{tch}")
        P.dma("sp", [(hx.ap(blk), x_in_ap(b, blk), {}) for blk in range(NBLK)],
              reads=XIN(b), writes=[hx.bufs[i] for i in range(NBLK)], name=f"m.hx{b}")
        P.dma("pool", [(xall[:, kb * nt:(kb + 1) * nt], x_in_ap(b, kb), {})
                       for kb in range(NBLK)],
              reads=XIN(b), writes=xall_bufs, name=f"m.xc{b}")
        for e_ in range(KEXP):
            for half in range(2):
                h0 = half * HHB
                P.dma("pool", [(gph[:], gpb[e_, half], {})],
                      reads=[wbf_bufs["gp"][e_]], writes=[gph_buf], name=f"m.gp{b}.{e_}.{half}")
                P.dma("sp", [(uph[:], upb[e_, half], {})],
                      reads=[wbf_bufs["up"][e_]], writes=[uph_buf], name=f"m.up{b}.{e_}.{half}")
                P.dma("act", [(dph[:].rearrange("p (a m) -> p a m", m=D),
                               dpb[e_][:, h0:h0 + HHB, :], {})],
                      reads=[wbf_bufs["dp"][e_]], writes=[dph_buf], name=f"m.dp{b}.{e_}.{half}")
                for ck in range(nck):
                    c0 = ck * CK
                    mmg(P, [(ps_a[:, 0:CK], esel[:, e_ * 128:(e_ + 1) * 128],
                             combT[0:8, c0:c0 + CK], True, True)],
                        reads=[combT.buf, esel_buf], writes=[ps_a.buf],
                        name=f"m.cbb{b}.{e_}.{half}.{ck}")
                    P.op("act", lambda e: e.copy(cbs[:, 0:CK], ps_a[:, 0:CK]),
                         reads=[ps_a.buf], writes=[cbs.buf],
                         name=f"m.cbe{b}.{e_}.{half}.{ck}")

                    def mk_xe(e, c0=c0):
                        for kb in range(NBLK):
                            last = e.tensor_mul(
                                xek[:, kb * CK:(kb + 1) * CK],
                                xall[:, kb * nt + c0: kb * nt + c0 + CK], cbs[:, 0:CK])
                        return last
                    P.op("dve", mk_xe, reads=xall_bufs + [cbs.buf],
                         writes=xek_bufs, name=f"m.xe{b}.{e_}.{half}.{ck}")
                    for hb in range(HHB):
                        psg = ps_g if hb % 2 == 0 else ps_g2
                        mmg(P, [(psg[:, 0:CK],
                                 gph[:, kb * (MH // 2) + hb * 128:
                                     kb * (MH // 2) + hb * 128 + 128],
                                 xall[:, kb * nt + c0: kb * nt + c0 + CK], st, sp)
                                for kb, (st, sp) in zip(range(NBLK), accg(NBLK))],
                            reads=[gph_buf] + xall_bufs, writes=[psg.buf],
                            name=f"m.g{b}.{e_}.{half}.{ck}.{hb}")
                        gslot = hb % 2
                        P.op("act", lambda e, psg=psg, gslot=gslot: e.activation(
                            gsb[:, gslot * CK:(gslot + 1) * CK], psg[:, 0:CK],
                            AF.Silu, bias=0.0, scale=1.0),
                            reads=[psg.buf], writes=[gsb_bufs[gslot]],
                            name=f"m.si{b}.{e_}.{half}.{ck}.{hb}")
                        mmg(P, [(ps_u[:, 0:CK],
                                 uph[:, kb * (MH // 2) + hb * 128:
                                     kb * (MH // 2) + hb * 128 + 128],
                                 xek[:, kb * CK:(kb + 1) * CK], st, sp)
                                for kb, (st, sp) in zip(range(NBLK), accg(NBLK))],
                            reads=[uph_buf] + xek_bufs, writes=[ps_u.buf],
                            name=f"m.u{b}.{e_}.{half}.{ck}.{hb}")
                        P.op("dve", lambda e, hb=hb, gslot=gslot: e.tensor_mul(
                            cck[:, hb * CK:(hb + 1) * CK],
                            gsb[:, gslot * CK:(gslot + 1) * CK], ps_u[:, 0:CK]),
                            reads=[gsb_bufs[gslot], ps_u.buf], writes=[cck_bufs[hb]],
                            name=f"m.c{b}.{e_}.{half}.{ck}.{hb}")
                    for dblk in range(NBLK):
                        psy = ps_y if dblk % 2 == 0 else ps_a
                        mmg(P, [(psy[:, 0:CK],
                                 dph[:, hb * D + dblk * 128: hb * D + dblk * 128 + 128],
                                 cck[:, hb * CK:(hb + 1) * CK], st, sp)
                                for hb, (st, sp) in zip(range(HHB), accg(HHB))],
                            reads=[dph_buf] + cck_bufs, writes=[psy.buf],
                            name=f"m.y{b}.{e_}.{half}.{ck}.{dblk}")
                        P.op("dve", lambda e, dblk=dblk, c0=c0, psy=psy: e.tensor_add(
                            hx.ap(dblk, c0, c0 + CK), hx.ap(dblk, c0, c0 + CK),
                            psy[:, 0:CK]),
                            reads=[hx.bufs[dblk], psy.buf], writes=[hx.bufs[dblk]],
                            name=f"m.ya{b}.{e_}.{half}.{ck}.{dblk}")
        emit_ln(P, C, hx, hx, None, None, None, EPS_MOD, f"m.pln.{b}")
        P.dma("sp", [(x_out_ap(b, blk), hx.ap(blk), {}) for blk in range(NBLK)],
              reads=[hx.bufs[blk] for blk in range(NBLK)],
              writes=XOUT(b), name=f"m.out{b}", is_output=FINAL)

    P.close_scope(es)



NB_CORE = 4
NT_FULL = 2048
NCORES = 8
CAPACITY = int(1.25 * 32 * 2048 / KEXP)

_prog_cache = {}


def _build_stage(stage):
    if stage in _prog_cache:
        return _prog_cache[stage]
    nt, nb = NT_FULL, NB_CORE
    nc = bass.Bass("TRN2", target_bir_lowering=False)
    x_in = nc.dram_tensor("x_in", [nb, 8, 128, nt], F32, kind="ExternalInput").ap()
    x_out = nc.dram_tensor("x_out", [nb, 8, 128, nt], F32, kind="ExternalOutput").ap()
    P = Prog(nc)
    G = {"nb": nb}
    emit_globals(P, G)
    in_ap = lambda b, blk: x_in[b] if blk is None else x_in[b, blk]
    out_ap = lambda b, blk: x_out[b, blk]
    if stage == "l0a":
        W = declare_l0a_weights(nc, nt)
        emit_l0_attn(P, G, W, in_ap, out_ap, nt)
    elif stage == "l0f":
        W = declare_l0f_weights(nc, nt)
        scratch = {
            "qs": nc.dram_tensor("f_qs", [nb, 8, 128, nt], BF16, kind="Internal").ap(),
            "ks": nc.dram_tensor("f_ks", [nb, 8, 128, nt], BF16, kind="Internal").ap(),
            "fs": nc.dram_tensor("f_fs", [nb, 8, 128, nt], BF16, kind="Internal").ap(),
            "pcs": nc.dram_tensor("f_pcs", [128, 8192], BF16, kind="Internal").ap(),
            "pcs2": nc.dram_tensor("f_pcs2", [128, 8192], BF16, kind="Internal").ap(),
        }
        emit_l0_ffn(P, G, W, in_ap, out_ap, nt, scratch)
    elif stage == "l1a":
        W = declare_l1a_weights(nc, nt)
        tsc = {
            "tks": nc.dram_tensor("t_tks", [nb, 8, 128, nt], BF16, kind="Internal").ap(),
            "tvs": nc.dram_tensor("t_tvs", [nb, nt // 128, 128, D], BF16,
                                  kind="Internal").ap(),
        }
        emit_l1_attn(P, G, W, in_ap, out_ap, nt, tsc)
    elif stage == "l1f":
        W = declare_l1f_weights(nc, nt)
        cc = {
            "cc_in": nc.dram_tensor("m_ccin", [2, 128], F32, kind="Internal").ap(),
            "cc_out": nc.dram_tensor("m_ccout", [16, 128], F32, kind="Internal").ap(),
        }
        emit_l1_ffn(P, G, W, in_ap, out_ap, nt, CAPACITY, True, cc)
    P.finish_and_build()
    nc.finalize()
    names = {"x_in"} | {ap.tensor.name for ap in W.values()}
    _prog_cache[stage] = (nc, names)
    return _prog_cache[stage]


def _sel_core(c):
    sel = np.zeros((16, 2), np.float32)
    for r in range(16):
        if r // 2 < c:
            sel[r, r % 2] = 1.0
    return sel


def kernel_unfused(**inputs):
    nt, nb = NT_FULL, NB_CORE
    inputs = {k: np.asarray(v) for k, v in inputs.items()}
    Hw = {}
    Hw.update(host_l0a(inputs, nt))
    Hw.update(host_l0f(inputs, nt))
    Hw.update(host_l1a(inputs, nt))
    Hw.update(host_l1f(inputs, nt))
    x = np.asarray(inputs["L0_attn_x"], np.float32)
    xd = np.ascontiguousarray(x.transpose(0, 2, 1)).reshape(NCORES, nb, 8, 128, nt)

    cur = [xd[c] for c in range(NCORES)]
    for stage in ["l0a", "l0f", "l1a", "l1f"]:
        nc, names = _build_stage(stage)
        in_maps = []
        for c in range(NCORES):
            m = {k: v for k, v in Hw.items() if k in names}
            m["x_in"] = cur[c]
            if "l1f_sel" in names:
                m["l1f_sel"] = _sel_core(c)
            in_maps.append(m)
        res = run_bass_kernel_spmd(nc, in_maps, core_ids=list(range(NCORES)))
        cur = [np.asarray(res.results[c]["x_out"]) for c in range(NCORES)]

    y = np.stack(cur).reshape(NCORES * nb, 1024, nt)
    return np.ascontiguousarray(y.transpose(0, 2, 1)).astype(np.float32)



def _build_fused():
    if "fused" in _prog_cache:
        return _prog_cache["fused"]
    nt, nb = NT_FULL, NB_CORE
    nc = bass.Bass("TRN2", target_bir_lowering=False)
    x_in = nc.dram_tensor("x_in", [nb, 8, 128, nt], F32, kind="ExternalInput").ap()
    x_out = nc.dram_tensor("x_out", [nb, 8, 128, nt], F32, kind="ExternalOutput").ap()
    xs1 = nc.dram_tensor("xs1", [nb, 8, 128, nt], F32, kind="Internal").ap()
    xs2 = nc.dram_tensor("xs2", [nb, 8, 128, nt], F32, kind="Internal").ap()
    xs3 = nc.dram_tensor("xs3", [nb, 8, 128, nt], F32, kind="Internal").ap()

    Wa = declare_l0a_weights(nc, nt)
    Wf = declare_l0f_weights(nc, nt)
    Wt = declare_l1a_weights(nc, nt)
    Wm = declare_l1f_weights(nc, nt)
    fscr = {
        "qs": nc.dram_tensor("f_qs", [nb, 8, 128, nt], BF16, kind="Internal").ap(),
        "ks": nc.dram_tensor("f_ks", [nb, 8, 128, nt], BF16, kind="Internal").ap(),
        "fs": nc.dram_tensor("f_fs", [nb, 8, 128, nt], BF16, kind="Internal").ap(),
        "pcs": nc.dram_tensor("f_pcs", [128, 8192], BF16, kind="Internal").ap(),
        "pcs2": nc.dram_tensor("f_pcs2", [128, 8192], BF16, kind="Internal").ap(),
    }
    tscr = {
        "tks": nc.dram_tensor("t_tks", [nb, 8, 128, nt], BF16, kind="Internal").ap(),
        "tvs": nc.dram_tensor("t_tvs", [nb, nt // 128, 128, D], BF16,
                              kind="Internal").ap(),
    }
    cc = {
        "cc_in": nc.dram_tensor("m_ccin", [2, 128], F32, kind="Internal").ap(),
        "cc_out": nc.dram_tensor("m_ccout", [16, 128], F32, kind="Internal").ap(),
    }

    P = Prog(nc)
    G = {"nb": nb}
    emit_globals(P, G)
    b1 = [Buf(f"xs1.{i}") for i in range(nb)]
    b2 = [Buf(f"xs2.{i}") for i in range(nb)]
    b3 = [Buf(f"xs3.{i}") for i in range(nb)]

    def mk(ap):
        return lambda b, blk: ap[b] if blk is None else ap[b, blk]

    ps_holder = {}
    emit_l0_attn(P, G, Wa, mk(x_in), mk(xs1), nt,
                 xio={"in": None, "out": b1, "final": False},
                 prestage_cb=lambda: ps_holder.update(emit_l1f_prestage(P, Wm)))
    emit_l0_ffn(P, G, Wf, mk(xs1), mk(xs2), nt, fscr,
                xio={"in": b1, "out": b2, "final": False})
    emit_l1_attn(P, G, Wt, mk(xs2), mk(xs3), nt, tscr,
                 xio={"in": b2, "out": b3, "final": False})
    emit_l1_ffn(P, G, Wm, mk(xs3), mk(x_out), nt, CAPACITY, True, cc,
                xio={"in": b3, "out": None, "final": True}, prestage=ps_holder)
    P.finish_and_build()
    nc.finalize()
    names = {"x_in"}
    for W in (Wa, Wf, Wt, Wm):
        names |= {ap.tensor.name for ap in W.values()}
    _prog_cache["fused"] = (nc, names)
    _prog_cache["fused_P"] = P
    return _prog_cache["fused"]


def kernel(**inputs):
    nt, nb = NT_FULL, NB_CORE
    inputs = {k: np.asarray(v) for k, v in inputs.items()}
    Hw = {}
    Hw.update(host_l0a(inputs, nt))
    Hw.update(host_l0f(inputs, nt))
    Hw.update(host_l1a(inputs, nt))
    Hw.update(host_l1f(inputs, nt))
    x = np.asarray(inputs["L0_attn_x"], np.float32)
    xd = np.ascontiguousarray(x.transpose(0, 2, 1)).reshape(NCORES, nb, 8, 128, nt)
    nc, names = _build_fused()
    in_maps = []
    for c in range(NCORES):
        m = {k: v for k, v in Hw.items() if k in names}
        m["x_in"] = xd[c]
        m["l1f_sel"] = _sel_core(c)
        in_maps.append(m)
    res = run_bass_kernel_spmd(nc, in_maps, core_ids=list(range(NCORES)))
    cur = [np.asarray(res.results[c]["x_out"]) for c in range(NCORES)]
    y = np.stack(cur).reshape(NCORES * nb, 1024, nt)
    return np.ascontiguousarray(y.transpose(0, 2, 1)).astype(np.float32)


# version 20
# speedup vs baseline: 1.1333x; 1.0145x over previous
import contextlib
import numpy as np
import ml_dtypes

import concourse.bass as bass
import concourse.mybir as mybir
from concourse.bass_utils import run_bass_kernel_spmd

F32 = mybir.dt.float32
BF16 = mybir.dt.bfloat16
AF = mybir.ActivationFunctionType
ALU = mybir.AluOpType
AX = mybir.AxisListType
BF = ml_dtypes.bfloat16

D = 1024
NBLK = 8
NH = 16
HD = 64
L0L = 4
FF = 4096
FBLK = 32
KEXP = 8
RR = 64
MH = 2048
MHB = 16
EPS_LN0 = 1e-12
EPS_MOD = 1e-5


class Buf:
    __slots__ = ("name", "writer", "readers", "fenced")

    def __init__(self, name="", nofence=False):
        self.name = name
        self.writer = None
        self.readers = []
        self.fenced = nofence


SAME_ENGINE_WAIT = {"pe": False, "act": True, "dve": True, "pool": True, "sp": True}


class Prog:
    ENGINES = ("sp", "pe", "act", "dve", "pool")

    def __init__(self, nc, n_dma_sems=24):
        self.nc = nc
        self.ops = {e: [] for e in self.ENGINES}
        self.ctr = {e: 0 for e in self.ENGINES}
        self.hw = {e: {} for e in self.ENGINES}
        self.dma_pools = {
            "sp": [f"dma_s{i}" for i in range(10)],
            "pool": [f"dma_p{i}" for i in range(10)],
            "act": [f"dma_a{i}" for i in range(4)],
        }
        self.dma_sems = [s for v in self.dma_pools.values() for s in v]
        self.dma_base = {s: 0 for s in self.dma_sems}
        self.dma_rr = {q: 0 for q in self.dma_pools}
        self.sem_handles = {}
        self.out_groups = []
        self.fence = {}
        self._es = contextlib.ExitStack()

    def scope(self):
        return contextlib.ExitStack()

    def close_scope(self, es):
        es.close()
        f = dict(self.fence)
        for e in self.ENGINES:
            if self.ctr[e] > 0:
                f["s_" + e] = max(f.get("s_" + e, 0), self.ctr[e])
        for s, v in self.dma_base.items():
            if v > 0:
                f[s] = max(f.get(s, 0), v)
        self.fence = f

    def sbuf(self, name, shape, dtype, es=None):
        return (es or self._es).enter_context(self.nc.sbuf_tensor(name, shape, dtype))

    def psum(self, name, shape, dtype=F32, es=None):
        return (es or self._es).enter_context(self.nc.psum_tensor(name, shape, dtype))

    def _want(self, waits, engine, sem, val):
        if sem == "s_" + engine and not SAME_ENGINE_WAIT[engine]:
            return
        if self.hw[engine].get(sem, 0) >= val:
            return
        if waits.get(sem, 0) < val:
            waits[sem] = val

    def _dep_waits(self, engine, reads, writes):
        waits = {}
        for b in reads:
            if b.writer:
                self._want(waits, engine, b.writer[0], b.writer[1])
        for b in writes:
            if b.writer is None and not b.fenced:
                b.fenced = True
                for s, v in self.fence.items():
                    self._want(waits, engine, s, v)
            if b.writer:
                self._want(waits, engine, b.writer[0], b.writer[1])
            for r in b.readers:
                self._want(waits, engine, r[0], r[1])
        return waits

    def _commit(self, token, reads, writes):
        for b in reads:
            b.readers.append(token)
        for b in writes:
            b.writer = token
            b.readers = []

    def op(self, engine, fn, reads=(), writes=(), name=""):
        waits = self._dep_waits(engine, reads, writes)
        self.ctr[engine] += 1
        val = self.ctr[engine]
        sem = "s_" + engine
        self.ops[engine].append((waits, fn, sem, name))
        for s, v in waits.items():
            self.hw[engine][s] = max(self.hw[engine].get(s, 0), v)
        if not SAME_ENGINE_WAIT[engine]:
            self.hw[engine][sem] = val
        self._commit((sem, val), reads, writes)
        return (sem, val)

    def dma(self, queue, pairs, reads=(), writes=(), name="", is_output=False):
        pool = self.dma_pools[queue]
        sem = pool[self.dma_rr[queue]]
        self.dma_rr[queue] = (self.dma_rr[queue] + 1) % len(pool)
        base = self.dma_base[sem]
        waits = self._dep_waits(queue, reads, writes)
        if base > 0:
            self._want(waits, queue, sem, base)
        fin = base + 16 * len(pairs)
        self.dma_base[sem] = fin

        def fn(eng, pairs=pairs, sem_name=sem):
            last = None
            for (o, i, kw) in pairs:
                last = eng.dma_start(o, i, **kw).then_inc(self.sem_handles[sem_name], 16)
            return last

        self.ops[queue].append((waits, fn, None, name))
        for s, v in waits.items():
            self.hw[queue][s] = max(self.hw[queue].get(s, 0), v)
        self._commit((sem, fin), reads, writes)
        if is_output:
            self.out_groups.append((sem, fin))
        return (sem, fin)

    def finish_and_build(self):
        waits = {}
        for s, v in self.out_groups:
            waits[s] = max(waits.get(s, 0), v)
        for s, v in self.dma_base.items():
            if v > 0 and self.hw["sp"].get(s, 0) < v:
                waits[s] = max(waits.get(s, 0), v)
        self.ops["sp"].append((waits, lambda eng: eng.nop(), None, "final"))
        nc = self.nc
        for s in ["s_" + e for e in self.ENGINES] + self.dma_sems:
            self.sem_handles[s] = self._es.enter_context(nc.semaphore(s))
        attr = {"sp": "sync", "pe": "tensor", "act": "scalar",
                "dve": "vector", "pool": "gpsimd"}
        self.inst_map = []
        with nc.Block() as blk:
            for key in self.ENGINES:
                def body(eng, key=key):
                    for waits_, fn, incsem, name_ in self.ops[key]:
                        w0 = nc.n_instructions()
                        for s, v in waits_.items():
                            eng.wait_ge(self.sem_handles[s], v)
                        s0 = nc.n_instructions()
                        last = fn(eng)
                        if incsem is not None:
                            last.then_inc(self.sem_handles[incsem], 1)
                        self.inst_map.append((w0, s0, nc.n_instructions(), key, name_))
                getattr(blk, attr[key])(body)
        self._es.close()


class DT:

    def __init__(self, P, name, nblk, width, dtype, es=None):
        self.t = P.sbuf(name, [128, nblk * width], dtype, es=es)
        self.nblk = nblk
        self.width = width
        self.bufs = [Buf(f"{name}.{i}") for i in range(nblk)]

    def ap(self, blk, c0=0, c1=None):
        if c1 is None:
            c1 = self.width
        return self.t[:, blk * self.width + c0: blk * self.width + c1]


def mmg(P, items, reads, writes, name="", skip_check=False):
    def fn(eng):
        last = None
        for (o, l, r, st, sp) in items:
            last = eng.matmul(o, l, r, start=st, stop=sp, skip_group_check=skip_check)
        return last
    return P.op("pe", fn, reads=reads, writes=writes, name=name)


def accg(n):
    return [(i == 0, i == n - 1) for i in range(n)]


class Tile:

    def __init__(self, P, name, shape, dtype, es=None, psum=False):
        self.t = (P.psum(name, shape, dtype, es=es) if psum
                  else P.sbuf(name, shape, dtype, es=es))
        self.buf = Buf(name)

    def __getitem__(self, sl):
        return self.t[sl]



def emit_ln(P, C, xy, out, outbf, g_col, b_col, eps, tag, chunk_cb=None,
            interleave=None):
    nt = xy.width
    CKS = min(256, nt)
    CKA = min(512, nt)
    nck_ = nt // CKA
    cbf = C["cbf"]
    rows, rows_buf = C["rows"], C["rows_buf"]
    rbf, rbf_bufs = C["rows_bf"], C["rows_bf_bufs"]
    ps_st, ps_st_buf = C["ps_st"], C["ps_st_buf"]
    ps_a, ps_a_buf = C["ps_bca"], C["ps_bca_buf"]
    ps_c, ps_c_buf = C["ps_bcc"], C["ps_bcc_buf"]
    ones_col, ones_row, const_buf = C["ones_col"], C["ones_row"], C["const_buf"]
    inv_d = 1.0 / D

    def stats(ck):
        c0 = ck * CKA
        for sub in range(CKA // CKS):
            s0 = c0 + sub * CKS
            r = sub * CKS
            P.op("act", lambda e, s0=s0: e.copy(
                cbf.t[:].rearrange("p (a c) -> p a c", c=CKS),
                xy.t[:].rearrange("p (a c) -> p a c", c=xy.width)[:, :, s0:s0 + CKS]),
                reads=xy.bufs, writes=cbf.bufs, name=f"{tag}.cast")
            mmg(P, [(ps_st[0:1, 0:CKS], ones_col[:, 0:1], cbf.ap(b), st, sp)
                    for b, (st, sp) in zip(range(NBLK), accg(NBLK))],
                reads=[cbf.bufs[b] for b in range(NBLK)] + [const_buf],
                writes=[ps_st_buf], name=f"{tag}.s1")
            P.op("dve", lambda e: e.tensor_mul(cbf.t[:], cbf.t[:], cbf.t[:]),
                 reads=cbf.bufs, writes=cbf.bufs, name=f"{tag}.sq")
            P.op("act", lambda e, r=r: e.copy(rows[0:1, r:r + CKS], ps_st[0:1, 0:CKS]),
                 reads=[ps_st_buf], writes=[rows_buf], name=f"{tag}.s1ev")
            mmg(P, [(ps_a[0:1, 0:CKS], ones_col[:, 0:1], cbf.ap(b), st, sp)
                    for b, (st, sp) in zip(range(NBLK), accg(NBLK))],
                reads=[cbf.bufs[b] for b in range(NBLK)] + [const_buf],
                writes=[ps_a_buf], name=f"{tag}.s2")
            P.op("act", lambda e, r=r: e.copy(rows[0:1, CKA + r:CKA + r + CKS],
                                              ps_a[0:1, 0:CKS]),
                 reads=[ps_a_buf], writes=[rows_buf], name=f"{tag}.s2ev")

        r0 = rows[0:1, 0:CKA]
        r1 = rows[0:1, CKA:2 * CKA]
        msq = rows[0:1, 2 * CKA:3 * CKA]
        P.op("dve", lambda e: e.tensor_scalar_mul(r0, r0, inv_d),
             reads=[rows_buf], writes=[rows_buf], name=f"{tag}.m")
        P.op("dve", lambda e: e.tensor_scalar_mul(r1, r1, inv_d),
             reads=[rows_buf], writes=[rows_buf], name=f"{tag}.e2")
        P.op("dve", lambda e: e.tensor_mul(msq, r0, r0),
             reads=[rows_buf], writes=[rows_buf], name=f"{tag}.msq")
        P.op("dve", lambda e: e.tensor_sub(r1, r1, msq),
             reads=[rows_buf], writes=[rows_buf], name=f"{tag}.var")
        P.op("dve", lambda e: e.tensor_scalar_add(r1, r1, float(eps)),
             reads=[rows_buf], writes=[rows_buf], name=f"{tag}.veps")
        P.op("act", lambda e: e.activation(r1, r1, AF.Sqrt, bias=0.0, scale=1.0),
             reads=[rows_buf], writes=[rows_buf], name=f"{tag}.sd")
        P.op("dve", lambda e: e.reciprocal(r1, r1),
             reads=[rows_buf], writes=[rows_buf], name=f"{tag}.a")
        P.op("dve", lambda e: e.scalar_tensor_tensor(r0, r0, -1.0, r1, ALU.mult, ALU.mult),
             reads=[rows_buf], writes=[rows_buf], name=f"{tag}.c")
        sl = (ck % 2) * 2 * CKA
        P.op("dve", lambda e, sl=sl: e.tensor_copy(rbf[0:1, sl:sl + CKA], r1),
             reads=[rows_buf], writes=[rbf_bufs[ck % 2]], name=f"{tag}.abf")
        P.op("dve", lambda e, sl=sl: e.tensor_copy(rbf[0:1, sl + CKA:sl + 2 * CKA], r0),
             reads=[rows_buf], writes=[rbf_bufs[ck % 2]], name=f"{tag}.cbf")

    def apply(ck):
        c0 = ck * CKA
        sl = (ck % 2) * 2 * CKA
        mmg(P, [(ps_a[:, 0:CKA], ones_row[0:1, 0:128], rbf[0:1, sl:sl + CKA],
                 True, True)],
            reads=[rbf_bufs[ck % 2], const_buf], writes=[ps_a_buf], name=f"{tag}.bca")
        mmg(P, [(ps_c[:, 0:CKA], ones_row[0:1, 0:128], rbf[0:1, sl + CKA:sl + 2 * CKA],
                 True, True)],
            reads=[rbf_bufs[ck % 2], const_buf], writes=[ps_c_buf], name=f"{tag}.bcc")
        def o3(t, w):
            return t[:].rearrange("p (a c) -> p a c", c=w)[:, :, c0:c0 + CKA]
        bca_b = ps_a[:, 0:CKA].rearrange("p (a c) -> p a c", a=1).broadcast_to(
            [128, NBLK, CKA])
        bcc_b = ps_c[:, 0:CKA].rearrange("p (a c) -> p a c", a=1).broadcast_to(
            [128, NBLK, CKA])
        P.op("dve", lambda e: e.tensor_mul(
            o3(out.t, out.width), o3(xy.t, xy.width), bca_b),
            reads=list(xy.bufs) + [ps_a_buf], writes=list(out.bufs),
            name=f"{tag}.ap1")
        P.op("dve", lambda e: e.tensor_add(
            o3(out.t, out.width), o3(out.t, out.width), bcc_b),
            reads=list(out.bufs) + [ps_c_buf], writes=list(out.bufs),
            name=f"{tag}.ap2")
        if g_col is not None:
            for blk in range(NBLK):
                gap, gbufs = g_col(blk)
                bap, bbufs = b_col(blk)
                P.op("dve", lambda e, b=blk, c0=c0, gap=gap, bap=bap: e.tensor_scalar(
                    out.ap(b, c0, c0 + CKA), out.ap(b, c0, c0 + CKA),
                    gap, bap, ALU.mult, ALU.add),
                    reads=[out.bufs[blk]] + gbufs + bbufs, writes=[out.bufs[blk]],
                    name=f"{tag}.ap3")
        if outbf is not None:
            P.op("dve", lambda e: e.tensor_copy(
                o3(outbf.t, outbf.width), o3(out.t, out.width)),
                reads=list(out.bufs), writes=list(outbf.bufs), name=f"{tag}.bf")

    if interleave is not None:
        for ck in range(nck_):
            interleave(ck)
            stats(ck)
            if ck >= 1:
                apply(ck - 1)
            if ck >= 2 and chunk_cb is not None:
                chunk_cb(ck - 2)
        apply(nck_ - 1)
        if chunk_cb is not None:
            if nck_ >= 2:
                chunk_cb(nck_ - 2)
            chunk_cb(nck_ - 1)
        return
    stats(0)
    if nck_ > 1:
        stats(1)
    apply(0)
    for ck in range(1, nck_):
        if ck + 1 < nck_:
            stats(ck + 1)
        apply(ck)
        if chunk_cb is not None:
            chunk_cb(ck - 1)
    if chunk_cb is not None:
        chunk_cb(nck_ - 1)


def make_ln_common(P, nt, es, G, pfx=""):
    C = {}
    C["cbf"] = DT(P, pfx + "ln_cbf", NBLK, min(256, nt), BF16, es=es)
    C["rows"] = P.sbuf(pfx + "ln_rows", [1, 3 * min(512, nt)], F32, es=es)
    C["rows_buf"] = Buf("ln_rows")
    C["rows_bf"] = P.sbuf(pfx + "ln_rows_bf", [1, 4 * min(512, nt)], BF16, es=es)
    C["rows_bf_bufs"] = [Buf("ln_rows_bf0"), Buf("ln_rows_bf1")]
    C["ps_st"] = P.psum(pfx + "ps_st", [1, 512], F32, es=es)
    C["ps_st_buf"] = Buf("ps_st")
    C["ps_bca"] = P.psum(pfx + "ps_bca", [128, 512], F32, es=es)
    C["ps_bca_buf"] = Buf("ps_bca")
    C["ps_bcc"] = P.psum(pfx + "ps_bcc", [128, 512], F32, es=es)
    C["ps_bcc_buf"] = Buf("ps_bcc")
    C["ones_col"] = G["ones_col"]
    C["ones_row"] = G["ones_row"]
    C["const_buf"] = G["const_buf"]
    return C



def emit_globals(P, G):
    ones_col = P.sbuf("g_ones_col", [128, 4], BF16)
    ones_row = P.sbuf("g_ones_row", [1, 512], BF16)
    ones_col_f = P.sbuf("g_ones_col_f", [128, 1], F32)
    ones_row_f = P.sbuf("g_ones_row_f", [1, 128], F32)
    eqm = P.sbuf("g_eqm", [128, 128], BF16)
    eps_row = P.sbuf("g_eps_row", [1, 16], BF16)
    es_tmp = P.scope()
    eqf = P.sbuf("g_eqf", [128, 128], F32, es=es_tmp)
    const_buf = Buf("g_const")
    P.op("pool", lambda e: e.memset(ones_col[:], 1.0), writes=[const_buf], name="g.oc")
    P.op("pool", lambda e: e.memset(ones_row[:], 1.0), writes=[const_buf], name="g.or")
    P.op("pool", lambda e: e.memset(eps_row[:], 1e-6), writes=[const_buf], name="g.eps")
    P.op("pool", lambda e: e.memset(ones_col_f[:], 1.0), writes=[const_buf], name="g.ocf")
    P.op("pool", lambda e: e.memset(ones_row_f[:], 1.0), writes=[const_buf], name="g.orf")
    P.op("pool", lambda e: e.iota(eqf[:], [[1, 128]], base=0, channel_multiplier=-1, allow_small_or_imprecise_dtypes=True),
         writes=[const_buf], name="g.iota")
    P.op("dve", lambda e: e.tensor_single_scalar(eqm[:], eqf[:], 0.0, ALU.is_equal),
         reads=[const_buf], writes=[const_buf], name="g.eqm")
    P.close_scope(es_tmp)
    G["ones_col"] = ones_col
    G["ones_row"] = ones_row
    G["ones_col_f"] = ones_col_f
    G["ones_row_f"] = ones_row_f
    G["eqm"] = eqm
    G["eps_row"] = eps_row
    G["const_buf"] = const_buf


class ParamPack:

    def __init__(self, P, name, specs, queue="sp", es=None):
        self.off = {}
        tot = 0
        for k, ap, w in specs:
            self.off[k] = (tot, w)
            tot += w
        self.t = P.sbuf(name, [128, tot], F32, es=es)
        self.buf = Buf(name)
        pairs = []
        for k, ap, w in specs:
            o, _ = self.off[k]
            pairs.append((self.t[:, o:o + w], ap, {}))
        P.dma(queue, pairs, writes=[self.buf], name=f"{name}.load")

    def col(self, key, i):
        o, w = self.off[key]
        assert i < w
        return self.t[:, o + i:o + i + 1]

    def ap(self, key):
        o, w = self.off[key]
        return self.t[:, o:o + w]



def emit_l0_attn(P, G, W, x_in_ap, x_out_ap, nt, upto=None, xio=None, prestage_cb=None):
    xio = xio or {}
    XIN = (lambda b: [xio["in"][b]]) if xio.get("in") else (lambda b: [])
    XOUT = (lambda b: [xio["out"][b]]) if xio.get("out") else (lambda b: [])
    FINAL = xio.get("final", True)
    """L0_attn block (+ module postLN). x_in_ap(b, blk)/x_out_ap(b, blk) -> dram APs."""
    nb = G["nb"]
    CK = min(512, nt)
    nck = nt // CK
    nsub = nt // 128
    es = P.scope()

    hx = DT(P, "a_hx", NBLK, nt, F32, es=es)
    ctmp = Tile(P, "a_ctmp", [128, CK], F32, es=es)
    ps1 = Tile(P, "a_ps1", [128, 512], F32, es=es, psum=True)
    ps2 = Tile(P, "a_ps2", [128, 512], F32, es=es, psum=True)
    ps3 = Tile(P, "a_ps3", [128, 512], F32, es=es, psum=True)
    ps4 = Tile(P, "a_ps4", [128, 512], F32, es=es, psum=True)
    C = make_ln_common(P, nt, es, G, "a_")

    PP = ParamPack(P, "a_pp", [
        ("w1c", W["conv1_w_cols"], 160),
        ("w2c", W["conv2_w_cols"], 160),
        ("b1c", W["conv1_b_cols"], 32),
        ("b2c", W["conv2_b_cols"], 8),
        ("ln0g", W["ln0_g_cols"], 8),
        ("ln0b", W["ln0_b_cols"], 8),
        ("bq", W["bq_cols"], 32),
        ("bk", W["bk_cols"], 32),
        ("lnAg", W["lnA_g_cols"], 32),
        ("lnAb", W["lnA_b_cols"], 32),
        ("b1f", W["ffn_b1_cols"], 32),
        ("b2f", W["ffn_b2_cols"], 8),
        ("ln2g", W["ln2_g_cols"], 8),
        ("ln2b", W["ln2_b_cols"], 8),
    ], es=es)
    bvr = P.sbuf("a_bvr", [1, D], BF16, es=es)
    bvr_buf = Buf("a_bvr")
    S_sw = P.sbuf("a_Ssw", [128, 128], BF16, es=es)
    S_buf = Buf("a_Ssw")
    P.dma("sp", [(S_sw[:], W["S_swap"], {})], writes=[S_buf], name="a.S")

    wkb = P.nc.dram_tensor("a_wkb", [L0L, 128, NBLK, D], BF16, kind="Internal").ap()
    wvb = P.nc.dram_tensor("a_wvb", [L0L, 128, NBLK, D], BF16, kind="Internal").ap()
    wqb = P.nc.dram_tensor("a_wqb", [L0L, 128, NBLK, D], BF16, kind="Internal").ap()
    w1b = P.nc.dram_tensor("a_w1b", [FBLK, 128, NBLK * 128], BF16, kind="Internal").ap()
    w2b = P.nc.dram_tensor("a_w2b", [NBLK, 128, FBLK * 128], BF16, kind="Internal").ap()
    wkb_bufs = [Buf(f"a_wkb{l}", nofence=True) for l in range(L0L)]
    wvb_bufs = [Buf(f"a_wvb{l}", nofence=True) for l in range(L0L)]
    wqb_bufs = [Buf(f"a_wqb{l}", nofence=True) for l in range(L0L)]
    w1b_bufs = [Buf(f"a_w1b{i}", nofence=True) for i in range(4)]
    w2b_bufs = [Buf(f"a_w2b{i}", nofence=True) for i in range(4)]
    for l in range(L0L):
        P.dma("pool", [(wkb[l], W["Wk_r"][l], {})], writes=[wkb_bufs[l]],
              name=f"a.ck{l}")
        P.dma("pool", [(wvb[l], W["Wv_r"][l], {})], writes=[wvb_bufs[l]],
              name=f"a.cv{l}")
        P.dma("pool", [(wqb[l], W["Wq_r"][l], {})], writes=[wqb_bufs[l]],
              name=f"a.cq{l}")
        h = l
        nq1 = FBLK // 4
        P.dma("pool", [(w1b[h * nq1 + j].rearrange("p (a m) -> p a m", m=128),
                        W["W1_r"][:, :, (h * nq1 + j) * 128:(h * nq1 + j + 1) * 128],
                        {}) for j in range(nq1)],
              writes=[w1b_bufs[h]], name=f"a.c1w{h}")
        P.dma("pool", [(w2b[h * 2 + j].rearrange("p (f m) -> p f m", m=128)
                        [:, fh * 16:(fh + 1) * 16],
                        W["W2_r"][:, fh * 16:(fh + 1) * 16,
                                  (h * 2 + j) * 128:(h * 2 + j + 1) * 128], {})
                       for j in range(2) for fh in range(2)],
              writes=[w2b_bufs[h]], name=f"a.c2w{h}")
    bvb = P.nc.dram_tensor("a_bvb", [1, L0L * D], BF16, kind="Internal").ap()
    bvb_buf = Buf("a_bvb", nofence=True)
    P.dma("pool", [(bvb, W["bv_rows"], {})], writes=[bvb_buf], name="a.cbv")
    if prestage_cb is not None:
        prestage_cb()

    for b in range(nb):
        esc = P.scope()
        xpad = P.sbuf(f"a_xpad{b}", [128, NBLK * (nt + 4)], BF16, es=esc)
        xpad_bufs = [Buf(f"a_xpad.{i}") for i in range(NBLK)]
        cg = P.sbuf(f"a_cg{b}", [128, 4 * (nt + 4)], BF16, es=esc)
        cg_bufs = [Buf(f"a_cg.{i}") for i in range(4)]
        diag1 = P.sbuf(f"a_diag1{b}", [128, 2 * 20 * 128], BF16, es=esc)
        diag1_bufs = [Buf("a_diag1.0"), Buf("a_diag1.1")]
        diag2 = P.sbuf(f"a_diag2{b}", [128, 2 * 20 * 128], BF16, es=esc)
        diag2_bufs = [Buf("a_diag2.0"), Buf("a_diag2.1")]

        def zero_borders(e):
            for blk in range(NBLK):
                o = blk * (nt + 4)
                e.memset(xpad[:, o:o + 2], 0.0)
                e.memset(xpad[:, o + nt + 2:o + nt + 4], 0.0)
            for m in range(4):
                o = m * (nt + 4)
                e.memset(cg[:, o:o + 2], 0.0)
                last = e.memset(cg[:, o + nt + 2:o + nt + 4], 0.0)
            return last
        P.op("pool", zero_borders, writes=xpad_bufs + cg_bufs, name=f"a.zb{b}")

        for blk in range(NBLK):
            P.dma("sp" if blk % 2 == 0 else "act",
                  [(hx.ap(blk), x_in_ap(b, blk), {})],
                  reads=XIN(b), writes=[hx.bufs[blk]], name=f"a.x{b}.{blk}")
            P.op("pool", lambda e, blk=blk: e.tensor_copy(
                xpad[:, blk * (nt + 4) + 2: blk * (nt + 4) + 2 + nt], hx.ap(blk)),
                reads=[hx.bufs[blk]], writes=[xpad_bufs[blk]], name=f"a.xpc{b}.{blk}")

        for blk in range(NBLK):
            ds = blk % 2
            do = ds * 20 * 128
            def build_diag(e, blk=blk, do=do):
                for i in range(20):
                    last = e.tensor_scalar_mul(
                        diag1[:, do + i * 128:do + (i + 1) * 128], G["eqm"][:],
                        PP.col("w1c", blk * 20 + i))
                return last
            P.op("dve", build_diag, reads=[G["const_buf"], PP.buf],
                 writes=[diag1_bufs[ds]], name=f"a.d1.{b}.{blk}")
            def build_diag2(e, blk=blk, do=do):
                for i in range(20):
                    last = e.tensor_scalar_mul(
                        diag2[:, do + i * 128:do + (i + 1) * 128], G["eqm"][:],
                        PP.col("w2c", blk * 20 + i))
                return last
            P.op("dve", build_diag2, reads=[G["const_buf"], PP.buf],
                 writes=[diag2_bufs[ds]], name=f"a.d2.{b}.{blk}")

            for m in range(4):
                for ck in range(nck):
                    c0 = ck * CK
                    ps = ps1 if (ck % 2 == 0) else ps2
                    mmg(P, [(ps[:, 0:CK],
                             diag1[:, do + (m * 5 + j) * 128:do + (m * 5 + j + 1) * 128],
                             xpad[:, blk * (nt + 4) + j + c0: blk * (nt + 4) + j + c0 + CK],
                             st, sp) for j, (st, sp) in zip(range(5), accg(5))],
                        reads=[diag1_bufs[ds], xpad_bufs[blk]], writes=[ps.buf],
                        name=f"a.c1.{b}.{blk}.{m}.{ck}")
                    P.op("act", lambda e, m=m, c0=c0, ps=ps, blk=blk: e.activation(
                        cg[:, m * (nt + 4) + 2 + c0: m * (nt + 4) + 2 + c0 + CK],
                        ps[:, 0:CK], AF.Gelu, bias=PP.col("b1c", blk * 4 + m), scale=1.0),
                        reads=[ps.buf, PP.buf], writes=[cg_bufs[m]],
                        name=f"a.c1e.{b}.{blk}.{m}.{ck}")
            for ck in range(nck):
                c0 = ck * CK
                psc = ps3 if ck % 2 == 0 else ps4
                items = []
                flags = accg(20)
                for m in range(4):
                    for j in range(5):
                        i = m * 5 + j
                        st, sp = flags[i]
                        items.append((psc[:, 0:CK],
                                      diag2[:, do + i * 128:do + (i + 1) * 128],
                                      cg[:, m * (nt + 4) + j + c0: m * (nt + 4) + j + c0 + CK],
                                      st, sp))
                mmg(P, items, reads=[diag2_bufs[ds]] + cg_bufs, writes=[psc.buf],
                    name=f"a.c2.{b}.{blk}.{ck}")
                P.op("act", lambda e, c0=c0, blk=blk, psc=psc: e.copy(
                    ctmp[:, 0:CK], psc[:, 0:CK]),
                    reads=[psc.buf], writes=[ctmp.buf], name=f"a.c2e.{b}.{blk}.{ck}")
                P.op("dve", lambda e, c0=c0, blk=blk: e.scalar_tensor_tensor(
                    hx.ap(blk, c0, c0 + CK), ctmp[:, 0:CK], PP.col("b2c", blk),
                    hx.ap(blk, c0, c0 + CK), ALU.add, ALU.add),
                    reads=[ctmp.buf, PP.buf, hx.bufs[blk]], writes=[hx.bufs[blk]],
                    name=f"a.c2a.{b}.{blk}.{ck}")
        P.close_scope(esc)

        esm = P.scope()
        hbf = DT(P, f"a_hbf{b}", NBLK, nt, BF16, es=esm)
        esi = P.scope()
        wA = P.sbuf(f"a_wA{b}", [128, NBLK * D], BF16, es=esi)
        wA_buf = Buf("a_wA")
        wB = P.sbuf(f"a_wB{b}", [128, NBLK * D], BF16, es=esi)
        wB_buf = Buf("a_wB")
        cosck = P.sbuf(f"a_cos{b}", [128, NBLK * CK], BF16, es=esi)
        cos_buf = Buf("a_cos")
        sinck = P.sbuf(f"a_sin{b}", [128, NBLK * CK], BF16, es=esi)
        sin_buf = Buf("a_sin")
        wQ = P.sbuf(f"a_wQ{b}", [128, NBLK * D], BF16, es=esi)
        wQ_buf = Buf("a_wQ")
        ksb = DT(P, f"a_k{b}", NBLK, CK, BF16, es=esi)
        qsb = ksb
        tmp1 = Tile(P, f"a_tmp1{b}", [128, CK], BF16, es=esi)
        krt = P.sbuf(f"a_krt{b}", [128, 2 * D], BF16, es=esi)
        krt_bufs = [Buf("a_krt.0"), Buf("a_krt.1")]
        vtk = P.sbuf(f"a_vtk{b}", [128, (CK // 128) * D], BF16, es=esi)
        vtk_bufs = [Buf(f"a_vtk.{i}") for i in range(CK // 128)]
        kvsb = Tile(P, f"a_kv{b}", [128, 512], BF16, es=esi)
        ps_kv = Tile(P, f"a_pskv{b}", [128, 512], F32, es=esi, psum=True)
        kmacc = Tile(P, f"a_kmacc{b}", [128, NBLK * (nck + 2)], F32, es=esi)
        kmbf = Tile(P, f"a_kmbf{b}", [128, NBLK], BF16, es=esi)
        zbf = Tile(P, f"a_zbf{b}", [16, CK], BF16, es=esi)
        kmsel = Tile(P, f"a_kmsel{b}", [128, 128], BF16, es=esi)
        zsel = Tile(P, f"a_zsel{b}", [16, 1024], BF16, es=esi)
        zbc = tmp1.t
        zbc_bufs = [tmp1.buf]

        P.dma("sp", [(zsel[:, :], W["Z_sel"], {})], writes=[zsel.buf], name=f"a.zsel{b}")
        def rope_inplace(P, dst, blk, psr, tag):
            mmg(P, [(psr[:, 0:CK], S_sw[:], dst.ap(blk), True, True)],
                reads=[S_buf, dst.bufs[blk]], writes=[psr.buf], name=f"{tag}.xr")
            P.op("dve", lambda e, blk=blk, psr=psr: e.tensor_mul(
                tmp1[:, 0:CK], psr[:, 0:CK], sinck[:, blk * CK:(blk + 1) * CK]),
                reads=[psr.buf, sin_buf], writes=[tmp1.buf], name=f"{tag}.s")
            P.op("dve", lambda e, blk=blk: e.tensor_mul(
                dst.ap(blk), dst.ap(blk), cosck[:, blk * CK:(blk + 1) * CK]),
                reads=[dst.bufs[blk], cos_buf], writes=[dst.bufs[blk]], name=f"{tag}.c")
            P.op("dve", lambda e, blk=blk: e.tensor_add(
                dst.ap(blk), dst.ap(blk), tmp1[:, 0:CK]),
                reads=[dst.bufs[blk], tmp1.buf], writes=[dst.bufs[blk]], name=f"{tag}.a")

        def emit_layer_loads(l):
            P.dma("sp", [(wA[:].rearrange("p (a m) -> p a m", m=D), wkb[l], {})],
                  reads=[wkb_bufs[l]], writes=[wA_buf], name=f"a.wk{b}{l}")
            P.dma("act", [(wB[:].rearrange("p (a m) -> p a m", m=D), wvb[l], {})],
                  reads=[wvb_bufs[l]], writes=[wB_buf], name=f"a.wv{b}{l}")
            P.dma("sp", [(wQ[:].rearrange("p (a m) -> p a m", m=D), wqb[l], {})],
                  reads=[wqb_bufs[l]], writes=[wQ_buf], name=f"a.wq{b}{l}")
            P.dma("sp", [(bvr[0:1, :], bvb[0:1, l * D:(l + 1) * D], {})],
                  reads=[bvb_buf], writes=[bvr_buf], name=f"a.bv{b}{l}")

        def k_ck(l, ck):
            c0 = ck * CK
            P.dma("sp", [(cosck[:], W["cos_t"][ck], {})],
                  writes=[cos_buf], name=f"a.cos.{b}.{l}.k{ck}")
            P.dma("act", [(sinck[:], W["sin_t"][ck], {})],
                  writes=[sin_buf], name=f"a.sin.{b}.{l}.k{ck}")
            for blk in range(NBLK):
                psk = (ps1, ps4, ps2)[blk % 3]
                mmg(P, [(psk[:, 0:CK], wA[:, kb * D + blk * 128:kb * D + blk * 128 + 128],
                         hbf.ap(kb, c0, c0 + CK), st, sp)
                        for kb, (st, sp) in zip(range(NBLK), accg(NBLK))],
                    reads=[wA_buf] + [hbf.bufs[kb] for kb in range(NBLK)],
                    writes=[psk.buf], name=f"a.kmm.{b}.{l}.{ck}.{blk}")
                P.op("act", lambda e, blk=blk, l=l, ck=ck, psk=psk: e.activation(
                    ksb.ap(blk), psk[:, 0:CK], AF.Gelu,
                    bias=PP.col("bk", l * 8 + blk), scale=1.0,
                    accum_out=kmacc[:, blk * (nck + 2) + ck: blk * (nck + 2) + ck + 1]),
                    reads=[psk.buf, PP.buf], writes=[ksb.bufs[blk], kmacc.buf],
                    name=f"a.kev.{b}.{l}.{ck}.{blk}")
                if blk == NBLK - 1:
                    P.op("dve", lambda e: e.tensor_scalar_add(
                        ksb.t[:], ksb.t[:], 0.21),
                        reads=ksb.bufs, writes=ksb.bufs,
                        name=f"a.k21.{b}.{l}.{ck}")
            vpairs = [(s, half) for s in range(CK // 128) for half in range(2)]
            for blk in range(NBLK):
                if blk < len(vpairs):
                    s, half = vpairs[blk]
                    t0 = s * 128
                    items = [(ps3[:, 0:512],
                              hbf.ap(kb, c0 + t0, c0 + t0 + 128),
                              wB[:, kb * D + half * 512:kb * D + half * 512 + 512],
                              st, False) for kb, (st, _) in zip(range(NBLK), accg(NBLK))]
                    items.append((ps3[:, 0:512], G["ones_row"][0:1, 0:128],
                                  bvr[0:1, half * 512:half * 512 + 512],
                                  False, True))
                    mmg(P, items,
                        reads=[wB_buf, bvr_buf, G["const_buf"]] +
                              [hbf.bufs[kb] for kb in range(NBLK)],
                        writes=[ps3.buf], name=f"a.vmm.{b}.{l}.{ck}.{s}.{half}")
                    P.op("act", lambda e, s=s, half=half: e.copy(
                        vtk[:, (s * 2 + half) * 512:(s * 2 + half) * 512 + 512],
                        ps3[:, 0:512]),
                        reads=[ps3.buf], writes=[vtk_bufs[s]],
                        name=f"a.vev.{b}.{l}.{ck}.{s}.{half}")
                psr = ps1 if blk % 2 == 0 else ps4
                rope_inplace(P, ksb, blk, psr, f"a.kr.{b}.{l}.{ck}.{blk}")
            def tp_s(s):
                t0 = s * 128
                glob_s = ck * (CK // 128) + s
                ko = (s % 2) * D
                for half in range(2):
                    pst = ps1 if half == 0 else ps4
                    mmg(P, [(pst[:, q * 128:(q + 1) * 128],
                             ksb.ap(half * 4 + q, t0, t0 + 128), G["eqm"][:],
                             q == 0, q == 3) for q in range(4)],
                        reads=[ksb.bufs[half * 4 + q] for q in range(4)] + [G["const_buf"]],
                        writes=[pst.buf], name=f"a.ktp.{b}.{l}.{glob_s}.{half}")
                    P.op("act", lambda e, half=half, pst=pst, ko=ko: e.copy(
                        krt[:, ko + half * 512:ko + (half + 1) * 512], pst[:, 0:512]),
                        reads=[pst.buf], writes=[krt_bufs[s % 2]],
                        name=f"a.ktpe.{b}.{l}.{glob_s}.{half}")

            def kv_s(s):
                glob_s = ck * (CK // 128) + s
                ko = (s % 2) * D
                items = []
                for h in range(NH):
                    items.append((kv_psum_ap(ps_kv, h),
                                  krt[:, ko + h * HD:ko + (h + 1) * HD],
                                  vtk[:, s * D + h * HD:s * D + (h + 1) * HD],
                                  glob_s == 0 and h < 2,
                                  glob_s == nsub - 1 and h >= NH - 2))
                mmg(P, items, reads=[krt_bufs[s % 2], vtk_bufs[s]], writes=[ps_kv.buf],
                    name=f"a.kv.{b}.{l}.{glob_s}", skip_check=True)

            tp_s(0)
            for s in range(1, CK // 128):
                tp_s(s)
                kv_s(s - 1)
            kv_s(CK // 128 - 1)

        def k_fin(l):
            P.op("act", lambda e: e.activation(kvsb[:, :], ps_kv[:, :],
                                               AF.Identity, bias=0.0, scale=1.0 / nt),
                 reads=[ps_kv.buf], writes=[kvsb.buf], name=f"a.kvev.{b}.{l}")

            def km_red(e):
                for blk in range(NBLK):
                    o = blk * (nck + 2)
                    last = e.reduce_sum(kmacc[:, o + nck:o + nck + 1],
                                        kmacc[:, o:o + nck], axis=AX.X)
                return last
            P.op("dve", km_red, reads=[kmacc.buf], writes=[kmacc.buf],
                 name=f"a.kmr.{b}.{l}")

            def km_fin(e):
                for blk in range(NBLK):
                    o = blk * (nck + 2)
                    last = e.tensor_scalar(kmacc[:, o + nck + 1:o + nck + 2],
                                           kmacc[:, o + nck:o + nck + 1],
                                           1.0 / nt, 0.21, ALU.mult, ALU.add)
                return last
            P.op("dve", km_fin, reads=[kmacc.buf], writes=[kmacc.buf],
                 name=f"a.kmf.{b}.{l}")

            def km_cast(e):
                for blk in range(NBLK):
                    o = blk * (nck + 2)
                    last = e.tensor_copy(kmbf[:, blk:blk + 1],
                                         kmacc[:, o + nck + 1:o + nck + 2])
                return last
            P.op("dve", km_cast, reads=[kmacc.buf], writes=[kmbf.buf],
                 name=f"a.kmc.{b}.{l}")
            P.op("pool", lambda e: e.memset(kmsel[:, :], 0.0),
                 reads=[], writes=[kmsel.buf], name=f"a.kms0.{b}.{l}")

            def kmsel_fill(e):
                for blk in range(NBLK):
                    o = blk * 16
                    e.tensor_copy(kmsel[0:64, o + 2 * blk:o + 2 * blk + 1],
                                  kmbf[0:64, blk:blk + 1])
                    last = e.tensor_copy(kmsel[64:128, o + 2 * blk + 1:o + 2 * blk + 2],
                                         kmbf[64:128, blk:blk + 1])
                return last
            P.op("dve", kmsel_fill, reads=[kmbf.buf, kmsel.buf], writes=[kmsel.buf],
                 name=f"a.kms.{b}.{l}")


        def q_ck(l, ck):
            c0 = ck * CK
            P.dma("sp", [(cosck[:], W["cos_t"][ck], {})],
                  writes=[cos_buf], name=f"a.cos.{b}.{l}.q{ck}")
            P.dma("act", [(sinck[:], W["sin_t"][ck], {})],
                  writes=[sin_buf], name=f"a.sin.{b}.{l}.q{ck}")
            for blk in range(NBLK):
                psk = (ps1, ps4, ps2)[blk % 3]
                mmg(P, [(psk[:, 0:CK], wQ[:, kb * D + blk * 128:kb * D + blk * 128 + 128],
                         hbf.ap(kb, c0, c0 + CK), st, sp)
                        for kb, (st, sp) in zip(range(NBLK), accg(NBLK))],
                    reads=[wQ_buf] + [hbf.bufs[kb] for kb in range(NBLK)],
                    writes=[psk.buf], name=f"a.qmm.{b}.{l}.{ck}.{blk}")
                P.op("act", lambda e, blk=blk, l=l, psk=psk: e.activation(
                    qsb.ap(blk), psk[:, 0:CK], AF.Gelu,
                    bias=PP.col("bq", l * 8 + blk), scale=1.0),
                    reads=[psk.buf, PP.buf], writes=[qsb.bufs[blk]],
                    name=f"a.qev.{b}.{l}.{ck}.{blk}")
            P.op("dve", lambda e: e.tensor_scalar_add(qsb.t[:], qsb.t[:], 0.21),
                 reads=qsb.bufs, writes=qsb.bufs, name=f"a.q21.{b}.{l}.{ck}")
            items = [(ps3[0:16, 0:CK], kmsel[:, blk * 16:(blk + 1) * 16],
                      qsb.ap(blk), st, False)
                     for blk, (st, _) in zip(range(NBLK), accg(NBLK))]
            items.append((ps3[0:16, 0:CK], G["eps_row"][0:1, 0:16],
                          G["ones_row"][0:1, 0:CK], False, True))
            mmg(P, items,
                reads=[kmsel.buf, G["const_buf"]] +
                      [qsb.bufs[blk] for blk in range(NBLK)],
                writes=[ps3.buf], name=f"a.zdot.{b}.{l}.{ck}")
            def zrec(e, c0=c0):
                with P.nc.allow_low_precision(reason="z cast to bf16 as before"):
                    return e.reciprocal(zbf[0:16, 0:CK], ps3[0:16, 0:CK])
            P.op("dve", zrec, reads=[ps3.buf], writes=[zbf.buf],
                 name=f"a.zr.{b}.{l}.{ck}")
            for blk in range(NBLK):
                psr = ps1 if blk % 2 == 0 else ps4
                rope_inplace(P, qsb, blk, psr, f"a.qr.{b}.{l}.{ck}.{blk}")
            def zchain(blk):
                psz = ps2 if blk % 2 == 0 else ps3
                mmg(P, [(psz[:, 0:CK], zsel[:, blk * 128:(blk + 1) * 128],
                         zbf[0:16, 0:CK], True, True)],
                    reads=[zsel.buf, zbf.buf], writes=[psz.buf],
                    name=f"a.zbc.{b}.{l}.{ck}.{blk}")
                P.op("act", lambda e, psz=psz: e.copy(
                    zbc[:, 0:CK], psz[:, 0:CK]),
                    reads=[psz.buf], writes=[zbc_bufs[0]],
                    name=f"a.zbce.{b}.{l}.{ck}.{blk}")
                P.op("dve", lambda e, blk=blk: e.tensor_mul(
                    qsb.ap(blk), qsb.ap(blk), zbc[:, 0:CK]),
                    reads=[qsb.bufs[blk], zbc_bufs[0]], writes=[qsb.bufs[blk]],
                    name=f"a.qz.{b}.{l}.{ck}.{blk}")

            def outmm(blk):
                pso = ps1 if blk % 2 == 0 else ps4
                items = []
                for hh in range(2):
                    h = blk * 2 + hh
                    items.append((pso[hh * 64:hh * 64 + 64, 0:CK],
                                  kvsb[64 * (h % 2):64 * (h % 2) + 64,
                                       64 * (h // 2):64 * (h // 2) + 64],
                                  qsb.t[64 * (h % 2):64 * (h % 2) + 64,
                                        (h // 2) * CK + 0:(h // 2) * CK + CK],
                                  True, True))
                mmg(P, items, reads=[kvsb.buf, qsb.bufs[blk]],
                    writes=[pso.buf], name=f"a.o.{b}.{l}.{ck}.{blk}")
                P.op("dve", lambda e, blk=blk, c0=c0, pso=pso: e.tensor_add(
                    hx.ap(blk, c0, c0 + CK), hx.ap(blk, c0, c0 + CK), pso[:, 0:CK]),
                    reads=[hx.bufs[blk], pso.buf], writes=[hx.bufs[blk]],
                    name=f"a.oa.{b}.{l}.{ck}.{blk}")

            for blk in range(NBLK):
                zchain(blk)
                if blk >= 2:
                    outmm(blk - 2)
            outmm(NBLK - 2)
            outmm(NBLK - 1)

        def lnA_cols(i):
            if i is None:
                return (lambda blk: (PP.col("ln0g", blk), [PP.buf]),
                        lambda blk: (PP.col("ln0b", blk), [PP.buf]),
                        f"a.ln0.{b}")
            return (lambda blk, i=i: (PP.col("lnAg", i * 8 + blk), [PP.buf]),
                    lambda blk, i=i: (PP.col("lnAb", i * 8 + blk), [PP.buf]),
                    f"a.lnA.{b}.{i}")

        def dump_out():
            P.dma("sp", [(x_out_ap(b, blk), hx.ap(blk), {}) for blk in range(NBLK)],
                  reads=[hx.bufs[blk] for blk in range(NBLK)],
                  writes=XOUT(b), name=f"a.out{b}", is_output=FINAL)

        broke = False
        emit_layer_loads(0)
        gc, bc, tg = lnA_cols(None)
        if upto == "ln0":
            emit_ln(P, C, hx, hx, hbf, gc, bc, EPS_LN0, tg)
            dump_out()
            broke = True
        else:
            emit_ln(P, C, hx, hx, hbf, gc, bc, EPS_LN0, tg,
                    chunk_cb=lambda ck: k_ck(0, ck))
            k_fin(0)
        for l in range(L0L):
            if broke:
                break
            gc, bc, tg = lnA_cols(l)
            if upto is not None:
                for ck in range(nck):
                    q_ck(l, ck)
                if upto == f"lnA{l}":
                    emit_ln(P, C, hx, hx, hbf, gc, bc, EPS_LN0, tg)
                    dump_out()
                    broke = True
                    break
                if l < L0L - 1:
                    emit_layer_loads(l + 1)
                    emit_ln(P, C, hx, hx, hbf, gc, bc, EPS_LN0, tg,
                            chunk_cb=lambda ck, l=l: k_ck(l + 1, ck))
                    k_fin(l + 1)
            elif l < L0L - 1:
                emit_layer_loads(l + 1)
                emit_ln(P, C, hx, hx, hbf, gc, bc, EPS_LN0, tg,
                        chunk_cb=lambda ck, l=l: k_ck(l + 1, ck),
                        interleave=lambda ck, l=l: q_ck(l, ck))
                k_fin(l + 1)
            else:
                for ck in range(nck):
                    q_ck(l, ck)
        if broke:
            P.close_scope(esi)
            P.close_scope(esm)
            continue
        P.close_scope(esi)
        esf = P.scope()
        w1t = P.sbuf(f"a_w1t{b}", [128, 4 * 1024], BF16, es=esf)
        w1t_bufs = [Buf(f"a_w1t.{i}") for i in range(4)]
        w2t = P.sbuf(f"a_w2t{b}", [128, 2 * FBLK * 128], BF16, es=esf)
        w2t_bufs = [Buf("a_w2t.0"), Buf("a_w2t.1")]
        h1bf = P.sbuf(f"a_h1{b}", [128, FBLK * CK], BF16, es=esf)
        h1_bufs = [Buf(f"a_h1.{i}") for i in range(FBLK)]
        xres = P.sbuf(f"a_xres{b}", [128, 2 * CK], F32, es=esf)
        xres_bufs = [Buf("a_xres.0"), Buf("a_xres.1")]

        def ffn_ck(ck):
            c0 = ck * CK
            for mb in range(FBLK):
                slot = mb % 4
                P.dma(("sp", "act", "pool")[mb % 3],
                      [(w1t[:, slot * 1024:(slot + 1) * 1024], w1b[mb], {})],
                      reads=[w1b_bufs[mb // (FBLK // 4)]],
                      writes=[w1t_bufs[slot]], name=f"a.w1.{b}.{ck}.{mb}")
                ps = (ps1, ps2, ps3)[mb % 3]
                mmg(P, [(ps[:, 0:CK],
                         w1t[:, slot * 1024 + kb * 128: slot * 1024 + kb * 128 + 128],
                         hbf.ap(kb, c0, c0 + CK), st, sp)
                        for kb, (st, sp) in zip(range(NBLK), accg(NBLK))],
                    reads=[w1t_bufs[slot]] + [hbf.bufs[kb] for kb in range(NBLK)],
                    writes=[ps.buf], name=f"a.f1.{b}.{ck}.{mb}")
                P.op("act", lambda e, mb=mb, ps=ps: e.activation(
                    h1bf[:, mb * CK:(mb + 1) * CK], ps[:, 0:CK], AF.Gelu,
                    bias=PP.col("b1f", mb), scale=1.0),
                    reads=[ps.buf, PP.buf], writes=[h1_bufs[mb]],
                    name=f"a.f1e.{b}.{ck}.{mb}")
            for blk in range(NBLK):
                slot = blk % 2
                psf = ps3 if slot == 0 else ps4
                P.dma("sp" if blk % 2 == 0 else "act",
                      [(w2t[:, slot * FBLK * 128:(slot + 1) * FBLK * 128], w2b[blk], {})],
                      reads=[w2b_bufs[blk // 2]],
                      writes=[w2t_bufs[slot]], name=f"a.w2.{b}.{ck}.{blk}")
                mmg(P, [(psf[:, 0:CK],
                         w2t[:, slot * FBLK * 128 + hb * 128: slot * FBLK * 128 + hb * 128 + 128],
                         h1bf[:, hb * CK:(hb + 1) * CK], st, sp)
                        for hb, (st, sp) in zip(range(FBLK), accg(FBLK))],
                    reads=[w2t_bufs[slot]] + h1_bufs, writes=[psf.buf],
                    name=f"a.f2.{b}.{ck}.{blk}")
                P.op("act", lambda e, blk=blk, psf=psf: e.copy(ctmp[:, 0:CK], psf[:, 0:CK]),
                     reads=[psf.buf], writes=[ctmp.buf], name=f"a.f2e.{b}.{ck}.{blk}")
                P.op("dve", lambda e, blk=blk, c0=c0: e.scalar_tensor_tensor(
                    hx.ap(blk, c0, c0 + CK), ctmp[:, 0:CK], PP.col("b2f", blk),
                    hx.ap(blk, c0, c0 + CK), ALU.add, ALU.add),
                    reads=[ctmp.buf, PP.buf, hx.bufs[blk]], writes=[hx.bufs[blk]],
                    name=f"a.f2a.{b}.{ck}.{blk}")

        gc, bc, tg = lnA_cols(3)
        emit_ln(P, C, hx, hx, hbf, gc, bc, EPS_LN0, tg, chunk_cb=ffn_ck)

        def xadd_ck(ck):
            c0 = ck * CK
            for blk in range(NBLK):
                slot = blk % 2
                P.dma("sp" if blk % 2 == 0 else "act",
                      [(xres[:, slot * CK:slot * CK + CK],
                        x_in_ap(b, blk)[:, c0:c0 + CK], {})],
                      reads=XIN(b), writes=[xres_bufs[slot]], name=f"a.xr.{b}.{blk}.{ck}")
                P.op("dve", lambda e, blk=blk, c0=c0, slot=slot: e.tensor_add(
                    hx.ap(blk, c0, c0 + CK), hx.ap(blk, c0, c0 + CK),
                    xres[:, slot * CK:slot * CK + CK]),
                    reads=[hx.bufs[blk], xres_bufs[slot]], writes=[hx.bufs[blk]],
                    name=f"a.xra.{b}.{blk}.{ck}")

        emit_ln(P, C, hx, hx, None,
                lambda blk: (PP.col("ln2g", blk), [PP.buf]),
                lambda blk: (PP.col("ln2b", blk), [PP.buf]),
                EPS_LN0, f"a.ln2.{b}")
        if upto == "ln2":
            P.dma("sp", [(x_out_ap(b, blk), hx.ap(blk), {}) for blk in range(NBLK)],
                  reads=[hx.bufs[blk] for blk in range(NBLK)],
                  writes=XOUT(b), name=f"a.out{b}", is_output=FINAL)
            P.close_scope(esf)
            P.close_scope(esm)
            continue

        out_buf_chain = XOUT(b) if XOUT(b) else [Buf(f"a_outc{b}")]

        def out_ck(ck):
            c0 = ck * CK
            P.dma("sp", [(x_out_ap(b, blk)[:, c0:c0 + CK],
                          hx.ap(blk, c0, c0 + CK), {}) for blk in range(NBLK)],
                  reads=[hx.bufs[blk] for blk in range(NBLK)],
                  writes=out_buf_chain, name=f"a.out{b}.{ck}", is_output=FINAL)

        emit_ln(P, C, hx, hx, None, None, None, EPS_MOD, f"a.pln.{b}",
                chunk_cb=out_ck, interleave=xadd_ck)
        P.close_scope(esf)
        P.close_scope(esm)

    P.close_scope(es)


def kv_psum_ap(ps, h):
    r = 64 * (h % 2)
    c = 64 * (h // 2)
    return ps[r:r + 64, c:c + 64]



def _cols(v):
    v = np.asarray(v, np.float32).reshape(-1, 128)
    return np.ascontiguousarray(v.T)


def declare_l0a_weights(nc, nt):
    def t(n, s, dt=F32):
        return nc.dram_tensor(n, s, dt, kind="ExternalInput").ap()
    W = {}
    W["conv1_w_cols"] = t("l0a_w1c", [128, 160])
    W["conv2_w_cols"] = t("l0a_w2c", [128, 160])
    W["conv1_b_cols"] = t("l0a_b1c", [128, 32])
    W["conv2_b_cols"] = t("l0a_b2c", [128, 8])
    W["ln0_g_cols"] = t("l0a_ln0g", [128, 8])
    W["ln0_b_cols"] = t("l0a_ln0b", [128, 8])
    W["Wq_r"] = t("l0a_wq", [4, 128, 8, D])
    W["Wk_r"] = t("l0a_wk", [4, 128, 8, D])
    W["Wv_r"] = t("l0a_wv", [4, 128, 8, D])
    W["bq_cols"] = t("l0a_bq", [128, 32])
    W["bk_cols"] = t("l0a_bk", [128, 32])
    W["bv_rows"] = t("l0a_bv", [1, 4 * D])
    W["lnA_g_cols"] = t("l0a_lnAg", [128, 32])
    W["lnA_b_cols"] = t("l0a_lnAb", [128, 32])
    W["W1_r"] = t("l0a_w1", [128, 8, FF])
    W["ffn_b1_cols"] = t("l0a_fb1", [128, 32])
    W["W2_r"] = t("l0a_w2", [128, FBLK, D])
    W["ffn_b2_cols"] = t("l0a_fb2", [128, 8])
    W["ln2_g_cols"] = t("l0a_ln2g", [128, 8])
    W["ln2_b_cols"] = t("l0a_ln2b", [128, 8])
    ckl = min(512, nt)
    W["cos_t"] = t("l0a_cos", [nt // ckl, 128, 8 * ckl], BF16)
    W["sin_t"] = t("l0a_sin", [nt // ckl, 128, 8 * ckl], BF16)
    W["S_swap"] = t("l0a_S", [128, 128], BF16)
    W["Z_sel"] = t("l0a_zsel", [16, 1024], BF16)
    return W


def host_l0a(inp, nt):
    H = {}
    w1v = np.asarray(inp["L0_attn_conv1_w"], np.float32)[:, 0, :]
    H["l0a_w1c"] = np.ascontiguousarray(
        w1v.reshape(8, 128, 4, 5).transpose(1, 0, 2, 3).reshape(128, 160))
    w2v = np.asarray(inp["L0_attn_conv2_w"], np.float32)
    H["l0a_w2c"] = np.ascontiguousarray(
        w2v.reshape(8, 128, 4, 5).transpose(1, 0, 2, 3).reshape(128, 160))
    b1 = np.asarray(inp["L0_attn_conv1_b"], np.float32)
    H["l0a_b1c"] = np.ascontiguousarray(
        b1.reshape(8, 128, 4).transpose(1, 0, 2).reshape(128, 32))
    H["l0a_b2c"] = _cols(inp["L0_attn_conv2_b"])
    H["l0a_ln0g"] = _cols(inp["L0_attn_ln0_g"])
    H["l0a_ln0b"] = _cols(inp["L0_attn_ln0_b"])
    for nm, key in (("wq", "L0_attn_Wq"), ("wk", "L0_attn_Wk"), ("wv", "L0_attn_Wv")):
        w = np.asarray(inp[key], np.float32)
        H["l0a_" + nm] = np.ascontiguousarray(
            w.reshape(4, 8, 128, D).transpose(0, 2, 1, 3))
    for nm, key in (("bq", "L0_attn_bq"), ("bk", "L0_attn_bk")):
        bv = np.asarray(inp[key], np.float32)
        H["l0a_" + nm] = np.ascontiguousarray(
            bv.reshape(4, 8, 128).transpose(2, 0, 1).reshape(128, 32))
    H["l0a_bv"] = np.asarray(inp["L0_attn_bv"], np.float32).reshape(1, 4 * D).copy()
    for nm, key in (("lnAg", "L0_attn_lnA_g"), ("lnAb", "L0_attn_lnA_b")):
        v = np.asarray(inp[key], np.float32)
        H["l0a_" + nm] = np.ascontiguousarray(
            v.reshape(4, 8, 128).transpose(2, 0, 1).reshape(128, 32))
    w1 = np.asarray(inp["L0_attn_W1"], np.float32)
    H["l0a_w1"] = np.ascontiguousarray(w1.reshape(8, 128, FF).transpose(1, 0, 2))
    H["l0a_fb1"] = np.ascontiguousarray(
        np.asarray(inp["L0_attn_b1"], np.float32).reshape(32, 128).T)
    w2 = np.asarray(inp["L0_attn_W2"], np.float32)
    H["l0a_w2"] = np.ascontiguousarray(w2.reshape(FBLK, 128, D).transpose(1, 0, 2))
    H["l0a_fb2"] = _cols(inp["L0_attn_b2"])
    H["l0a_ln2g"] = _cols(inp["L0_attn_ln2_g"])
    H["l0a_ln2b"] = _cols(inp["L0_attn_ln2_b"])
    inv = (1e4 ** (-np.arange(0, D, 2, dtype=np.float64) / D))
    freqs = np.arange(nt, dtype=np.float64)[:, None] * inv[None, :]
    cosf = np.repeat(np.cos(freqs).T, 2, axis=0)
    sinf = np.repeat(np.sin(freqs).T, 2, axis=0)
    sinf[0::2] *= -1.0
    ckl = min(512, nt)
    H["l0a_cos"] = np.ascontiguousarray(
        cosf.reshape(8, 128, nt // ckl, ckl).transpose(2, 1, 0, 3)
        .reshape(nt // ckl, 128, 8 * ckl)).astype(BF)
    H["l0a_sin"] = np.ascontiguousarray(
        sinf.reshape(8, 128, nt // ckl, ckl).transpose(2, 1, 0, 3)
        .reshape(nt // ckl, 128, 8 * ckl)).astype(BF)
    S = np.zeros((128, 128), np.float32)
    for i in range(64):
        S[2 * i, 2 * i + 1] = 1.0
        S[2 * i + 1, 2 * i] = 1.0
    H["l0a_S"] = S.astype(BF)
    zs = np.zeros((16, 1024), np.float32)
    for blk in range(8):
        zs[2 * blk, blk * 128: blk * 128 + 64] = 1.0
        zs[2 * blk + 1, blk * 128 + 64: blk * 128 + 128] = 1.0
    H["l0a_zsel"] = zs.astype(BF)
    return H



def declare_l1a_weights(nc, nt):
    def t(n, s, dt=F32):
        return nc.dram_tensor(n, s, dt, kind="ExternalInput").ap()
    W = {}
    W["WqT"] = t("l1a_wq", [128, 8, D])
    W["WkT"] = t("l1a_wk", [128, 8, D])
    W["WvT"] = t("l1a_wv", [128, 8, D])
    W["WoT"] = t("l1a_wo", [128, 8, D])
    W["bq_cols"] = t("l1a_bq", [128, 8])
    W["bk_cols"] = t("l1a_bk", [128, 8])
    W["bv_row"] = t("l1a_bv", [1, D])
    W["bo_cols"] = t("l1a_bo", [128, 8])
    W["tril"] = t("l1a_tril", [128, 128], BF16)
    return W


def host_l1a(inp, nt):
    H = {}
    for nm, key in (("wq", "L1_attn_Wq"), ("wk", "L1_attn_Wk"),
                    ("wv", "L1_attn_Wv"), ("wo", "L1_attn_Wo")):
        w = np.ascontiguousarray(np.asarray(inp[key], np.float32).T)
        H["l1a_" + nm] = np.ascontiguousarray(w.reshape(8, 128, D).transpose(1, 0, 2))
    H["l1a_bq"] = _cols(inp["L1_attn_bq"])
    H["l1a_bk"] = _cols(inp["L1_attn_bk"])
    H["l1a_bv"] = np.asarray(inp["L1_attn_bv"], np.float32).reshape(1, D).copy()
    H["l1a_bo"] = _cols(inp["L1_attn_bo"])
    m = np.zeros((128, 128), np.float32)
    for p in range(128):
        m[p, : p + 1] = 1.0
    H["l1a_tril"] = m.astype(BF)
    return H


def emit_l1_attn(P, G, W, x_in_ap, x_out_ap, nt, scratch, xio=None):
    xio = xio or {}
    XIN = (lambda b: [xio["in"][b]]) if xio.get("in") else (lambda b: [])
    XOUT = (lambda b: [xio["out"][b]]) if xio.get("out") else (lambda b: [])
    FINAL = xio.get("final", True)
    """scratch: tks [nb, 8, 128, nt] bf16 (k d-major), tvs [nb, nt//128, 128, D] bf16."""
    nb = G["nb"]
    CK = min(512, nt)
    nck = nt // CK
    nq = nt // 128
    GQ = 2
    es = P.scope()

    hx = DT(P, "t_hx", NBLK, nt, F32, es=es)
    ctmp = Tile(P, "t_ctmp", [128, 256], F32, es=es)
    ps1 = Tile(P, "t_ps1", [128, 512], F32, es=es, psum=True)
    ps_pv = Tile(P, "t_pspv", [128, 2048], F32, es=es, psum=True)
    C = make_ln_common(P, nt, es, G, "t_")
    PP = ParamPack(P, "t_pp", [
        ("bq", W["bq_cols"], 8),
        ("bk", W["bk_cols"], 8),
        ("bo", W["bo_cols"], 8),
    ], es=es)
    bvr = P.sbuf("t_bvr", [1, D], BF16, es=es)
    bvr_buf = Buf("t_bvr")
    P.dma("pool", [(bvr[0:1, :], W["bv_row"], {})], writes=[bvr_buf], name="t.bvr")
    tril = P.sbuf("t_tril", [128, 128], BF16, es=es)
    tril_buf = Buf("t_tril")
    P.dma("sp", [(tril[:], W["tril"], {})], writes=[tril_buf], name="t.mask")

    wbuf = P.sbuf("t_w", [128, NBLK * D], BF16, es=es)
    w_buf = Buf("t_w")
    wo = P.sbuf("t_wo", [128, NBLK * D], BF16, es=es)
    wo_buf = Buf("t_wo")
    xck = P.sbuf("t_xck", [128, 2 * NBLK * CK], BF16, es=es)
    xck_bufs = [Buf("t_xck.0"), Buf("t_xck.1")]
    kch = P.sbuf("t_kch", [128, 2 * NBLK * CK], BF16, es=es)
    kch_bufs = [Buf("t_kch.0"), Buf("t_kch.1")]
    kstg = kch
    kstg_bufs = [kch_bufs[0]] * NBLK
    vch = P.sbuf("t_vch", [128, 2 * D], BF16, es=es)
    vch_bufs = [Buf("t_vch.0"), Buf("t_vch.1")]
    qg = DT(P, "t_qg", NBLK, GQ * 128, BF16, es=es)
    sab = P.sbuf("t_sab", [128, GQ * nt], F32, es=es)
    sab_bufs = [Buf(f"t_sab.{i}") for i in range(GQ)]
    ebf = Tile(P, "t_ebf", [128, nt], BF16, es=es)
    wn = P.sbuf("t_wn", [128, GQ * nt], BF16, es=es)
    wn_bufs = [Buf(f"t_wn.{i}") for i in range(GQ)]
    wT = P.sbuf("t_wT", [128, (nt // 128) * GQ * 128], BF16, es=es)
    wT_buf = Buf("t_wT")
    ao = DT(P, "t_ao", NBLK, GQ * 128, BF16, es=es)
    cols = Tile(P, "t_cols", [128, 8], F32, es=es)
    xres = P.sbuf("t_xres", [128, 2 * GQ * 128], F32, es=es)
    xres_bufs = [Buf("t_xres.0"), Buf("t_xres.1")]
    tks_bufs = [Buf(f"t_tks.{i}", nofence=True) for i in range(nb)]
    tvs_bufs = [Buf(f"t_tvs.{i}", nofence=True) for i in range(nb)]

    for b in range(nb):
        P.dma("pool", [(wbuf[:].rearrange("p (a m) -> p a m", m=D), W["WkT"], {})],
              writes=[w_buf], name=f"t.wk{b}")
        for ck in range(nck):
            c0 = ck * CK
            slot = ck % 2
            P.dma("pool", [(xck[:, slot * NBLK * CK:(slot + 1) * NBLK * CK].rearrange(
                                "p (a c) -> p a c", c=CK),
                            x_in_ap(b, None)[:, :, c0:c0 + CK].rearrange(
                                "a p c -> p a c"), {})],
                  reads=XIN(b), writes=[xck_bufs[slot]], name=f"t.xk{b}.{ck}")
            for blk in range(NBLK):
                mmg(P, [(ps1[:, 0:CK],
                         wbuf[:, kb * D + blk * 128:kb * D + blk * 128 + 128],
                         xck[:, slot * NBLK * CK + kb * CK: slot * NBLK * CK + kb * CK + CK],
                         st, sp) for kb, (st, sp) in zip(range(NBLK), accg(NBLK))],
                    reads=[w_buf, xck_bufs[slot]], writes=[ps1.buf],
                    name=f"t.kmm{b}.{ck}.{blk}")
                P.op("act", lambda e, blk=blk: e.activation(
                    kstg[:, blk * CK:(blk + 1) * CK], ps1[:, 0:CK], AF.Identity,
                    bias=PP.col("bk", blk), scale=1.0),
                    reads=[ps1.buf, PP.buf], writes=[kstg_bufs[blk]],
                    name=f"t.kev{b}.{ck}.{blk}")
            P.dma("sp", [(scratch["tks"][b, blk][:, c0:c0 + CK],
                          kstg[:, blk * CK:(blk + 1) * CK], {}) for blk in range(NBLK)],
                  reads=kstg_bufs, writes=[tks_bufs[b]], name=f"t.kst{b}.{ck}")
        P.dma("pool", [(wbuf[:].rearrange("p (a m) -> p a m", m=D), W["WvT"], {})],
              writes=[w_buf], name=f"t.wv{b}")
        for ck in range(nck):
            c0 = ck * CK
            slot = ck % 2
            P.dma("pool", [(xck[:, slot * NBLK * CK:(slot + 1) * NBLK * CK].rearrange(
                                "p (a c) -> p a c", c=CK),
                            x_in_ap(b, None)[:, :, c0:c0 + CK].rearrange(
                                "a p c -> p a c"), {})],
                  reads=XIN(b), writes=[xck_bufs[slot]], name=f"t.xv{b}.{ck}")
            for s in range(CK // 128):
                sub = ck * (CK // 128) + s
                t0 = s * 128
                vslot = sub % 2
                for half in range(2):
                    items = [(ps1[:, 0:512],
                              xck[:, slot * NBLK * CK + kb * CK + t0:
                                  slot * NBLK * CK + kb * CK + t0 + 128],
                              wbuf[:, kb * D + half * 512:kb * D + half * 512 + 512],
                              st, False) for kb, (st, _) in zip(range(NBLK), accg(NBLK))]
                    items.append((ps1[:, 0:512], G["ones_row"][0:1, 0:128],
                                  bvr[0:1, half * 512:half * 512 + 512], False, True))
                    mmg(P, items, reads=[w_buf, bvr_buf, G["const_buf"], xck_bufs[slot]],
                        writes=[ps1.buf], name=f"t.vmm{b}.{sub}.{half}")
                    P.op("act", lambda e, vslot=vslot, half=half: e.copy(
                        vch[:, vslot * D + half * 512: vslot * D + half * 512 + 512],
                        ps1[:, 0:512]),
                        reads=[ps1.buf], writes=[vch_bufs[vslot]],
                        name=f"t.vev{b}.{sub}.{half}")
                P.dma("sp", [(scratch["tvs"][b, sub], vch[:, vslot * D:(vslot + 1) * D], {})],
                      reads=[vch_bufs[vslot]], writes=[tvs_bufs[b]], name=f"t.vst{b}.{sub}")
        P.dma("pool", [(wbuf[:].rearrange("p (a m) -> p a m", m=D), W["WqT"], {})],
              writes=[w_buf], name=f"t.wq{b}")
        P.dma("pool", [(wo[:].rearrange("p (a m) -> p a m", m=D), W["WoT"], {})],
              writes=[wo_buf], name=f"t.wo{b}")

        for gq in range(nq // GQ):
            q0 = gq * GQ * 128
            slot = gq % 2
            P.dma("pool", [(xck[:, slot * NBLK * CK: slot * NBLK * CK + NBLK * GQ * 128]
                            .rearrange("p (a c) -> p a c", c=GQ * 128),
                            x_in_ap(b, None)[:, :, q0:q0 + GQ * 128].rearrange(
                                "a p c -> p a c"), {})],
                  reads=XIN(b), writes=[xck_bufs[slot]], name=f"t.xq{b}.{gq}")
            for blk in range(NBLK):
                mmg(P, [(ps1[:, 0:GQ * 128],
                         wbuf[:, kb * D + blk * 128:kb * D + blk * 128 + 128],
                         xck[:, slot * NBLK * CK + kb * GQ * 128:
                             slot * NBLK * CK + kb * GQ * 128 + GQ * 128],
                         st, sp) for kb, (st, sp) in zip(range(NBLK), accg(NBLK))],
                    reads=[w_buf, xck_bufs[slot]], writes=[ps1.buf],
                    name=f"t.qmm{b}.{gq}.{blk}")
                P.op("act", lambda e, blk=blk: e.activation(
                    qg.ap(blk), ps1[:, 0:GQ * 128], AF.Identity,
                    bias=PP.col("bq", blk), scale=1.0),
                    reads=[ps1.buf, PP.buf], writes=[qg.bufs[blk]],
                    name=f"t.qev{b}.{gq}.{blk}")
            nvalid = q0 + GQ * 128
            nkc_s = (nvalid + CK - 1) // CK
            for kc in range(nkc_s):
                k0 = kc * CK
                kslot = kc % 2
                P.dma("sp", [(kch[:, kslot * NBLK * CK + blk * CK:
                                  kslot * NBLK * CK + (blk + 1) * CK],
                              scratch["tks"][b, blk][:, k0:k0 + CK], {})
                             for blk in range(NBLK)],
                      reads=[tks_bufs[b]], writes=[kch_bufs[kslot]],
                      name=f"t.kld{b}.{gq}.{kc}")
                for qi in range(GQ):
                    mmg(P, [(ps1[:, 0:CK],
                             qg.ap(kb, qi * 128, qi * 128 + 128),
                             kch[:, kslot * NBLK * CK + kb * CK:
                                 kslot * NBLK * CK + kb * CK + CK], st, sp)
                            for kb, (st, sp) in zip(range(NBLK), accg(NBLK))],
                        reads=[qg.bufs[kb] for kb in range(NBLK)] + [kch_bufs[kslot]],
                        writes=[ps1.buf], name=f"t.smm{b}.{gq}.{qi}.{kc}")
                    P.op("act", lambda e, qi=qi, k0=k0: e.activation(
                        sab[:, qi * nt + k0: qi * nt + k0 + CK], ps1[:, 0:CK], AF.Abs,
                        bias=0.0, scale=1.0 / 32.0),
                        reads=[ps1.buf], writes=[sab_bufs[qi]],
                        name=f"t.sev{b}.{gq}.{qi}.{kc}")
            for qi in range(GQ):
                qtile = gq * GQ + qi
                vq = (qtile + 1) * 128
                P.op("dve", lambda e, qi=qi, vq=vq: e.reduce_max(
                    cols[:, 0:1], sab[:, qi * nt: qi * nt + vq], axis=AX.X, negate=True),
                    reads=[sab_bufs[qi]], writes=[cols.buf], name=f"t.mx{b}.{qtile}")
                P.op("act", lambda e, qi=qi, vq=vq: e.activation(
                    ebf[:, 0:vq], sab[:, qi * nt: qi * nt + vq],
                    AF.Exp, bias=cols[:, 0:1], scale=1.0),
                    reads=[sab_bufs[qi], cols.buf], writes=[ebf.buf],
                    name=f"t.exp{b}.{qtile}")
                if qtile > 0:
                    P.op("dve", lambda e, qi=qi, vq=vq: e.tensor_copy(
                        wn[:, qi * nt: qi * nt + vq - 128], ebf[:, 0:vq - 128]),
                        reads=[ebf.buf], writes=[wn_bufs[qi]], name=f"t.cpe{b}.{qtile}")
                P.op("dve", lambda e, qi=qi, vq=vq: e.tensor_mul(
                    wn[:, qi * nt + vq - 128: qi * nt + vq],
                    ebf[:, vq - 128:vq], tril[:, :]),
                    reads=[ebf.buf, tril_buf], writes=[wn_bufs[qi]],
                    name=f"t.msk{b}.{qtile}")
                if qi == 0 and GQ == 2:
                    P.op("dve", lambda e, qi=qi, vq=vq: e.memset(
                        wn[:, qi * nt + vq: qi * nt + vq + 128], 0.0),
                        reads=[], writes=[wn_bufs[qi]], name=f"t.z{b}.{qtile}")
                P.op("dve", lambda e, qi=qi, vq=vq: e.reduce_sum(
                    cols[:, 1:2], wn[:, qi * nt: qi * nt + vq], axis=AX.X),
                    reads=[wn_bufs[qi]], writes=[cols.buf], name=f"t.sum{b}.{qtile}")
                P.op("dve", lambda e: e.reciprocal(cols[:, 2:3], cols[:, 1:2]),
                     reads=[cols.buf], writes=[cols.buf], name=f"t.rc{b}.{qtile}")
                P.op("dve", lambda e, qi=qi, vq=vq: e.tensor_scalar_mul(
                    wn[:, qi * nt: qi * nt + vq], wn[:, qi * nt: qi * nt + vq],
                    cols[:, 2:3]),
                    reads=[wn_bufs[qi], cols.buf], writes=[wn_bufs[qi]],
                    name=f"t.wsc{b}.{qtile}")
            nkc128 = nvalid // 128
            for kc in range(nkc128):
                mmg(P, [(ps1[:, qi * 128:(qi + 1) * 128],
                         wn[:, qi * nt + kc * 128: qi * nt + kc * 128 + 128],
                         G["eqm"][:], qi == 0, qi == GQ - 1) for qi in range(GQ)],
                    reads=[wn_bufs[qi] for qi in range(GQ)] + [G["const_buf"]],
                    writes=[ps1.buf], name=f"t.tp{b}.{gq}.{kc}")
                P.op("act", lambda e, kc=kc: e.copy(
                    wT[:, kc * GQ * 128:(kc + 1) * GQ * 128], ps1[:, 0:GQ * 128]),
                    reads=[ps1.buf], writes=[wT_buf], name=f"t.tpe{b}.{gq}.{kc}")
            for kc in range(nkc128):
                vslot = kc % 2
                P.dma("sp", [(vch[:, vslot * D:(vslot + 1) * D],
                              scratch["tvs"][b, kc], {})],
                      reads=[tvs_bufs[b]], writes=[vch_bufs[vslot]],
                      name=f"t.vld{b}.{gq}.{kc}")
                items = []
                for vd in range(NBLK):
                    items.append((ps_pv[:, vd * 256: vd * 256 + GQ * 128],
                                  vch[:, vslot * D + vd * 128: vslot * D + vd * 128 + 128],
                                  wT[:, kc * GQ * 128:(kc + 1) * GQ * 128],
                                  kc == 0 and vd % 2 == 0,
                                  kc == nkc128 - 1 and vd % 2 == 1))
                mmg(P, items, reads=[vch_bufs[vslot], wT_buf], writes=[ps_pv.buf],
                    name=f"t.pv{b}.{gq}.{kc}")
            for vd in range(NBLK):
                P.op("dve", lambda e, vd=vd: e.tensor_copy(
                    ao.ap(vd), ps_pv[:, vd * 256: vd * 256 + GQ * 128]),
                    reads=[ps_pv.buf], writes=[ao.bufs[vd]], name=f"t.pve{b}.{gq}.{vd}")
            for blk in range(NBLK):
                mmg(P, [(ps1[:, 0:GQ * 128],
                         wo[:, kb * D + blk * 128:kb * D + blk * 128 + 128],
                         ao.ap(kb), st, sp)
                        for kb, (st, sp) in zip(range(NBLK), accg(NBLK))],
                    reads=[wo_buf] + [ao.bufs[kb] for kb in range(NBLK)],
                    writes=[ps1.buf], name=f"t.omm{b}.{gq}.{blk}")
                P.op("act", lambda e, blk=blk: e.activation(
                    ctmp[:, 0:GQ * 128], ps1[:, 0:GQ * 128], AF.Identity,
                    bias=PP.col("bo", blk), scale=1.0),
                    reads=[ps1.buf, PP.buf], writes=[ctmp.buf],
                    name=f"t.oev{b}.{gq}.{blk}")
                xslot = blk % 2
                P.dma("sp", [(xres[:, xslot * GQ * 128:(xslot + 1) * GQ * 128],
                              x_in_ap(b, blk)[:, q0:q0 + GQ * 128], {})],
                      reads=XIN(b), writes=[xres_bufs[xslot]], name=f"t.xr{b}.{gq}.{blk}")
                P.op("dve", lambda e, blk=blk, q0=q0, xslot=xslot: e.tensor_add(
                    hx.ap(blk, q0, q0 + GQ * 128), ctmp[:, 0:GQ * 128],
                    xres[:, xslot * GQ * 128:(xslot + 1) * GQ * 128]),
                    reads=[ctmp.buf, xres_bufs[xslot]], writes=[hx.bufs[blk]],
                    name=f"t.oadd{b}.{gq}.{blk}")
        emit_ln(P, C, hx, hx, None, None, None, EPS_MOD, f"t.pln.{b}")
        P.dma("sp", [(x_out_ap(b, blk), hx.ap(blk), {}) for blk in range(NBLK)],
              reads=[hx.bufs[blk] for blk in range(NBLK)],
              writes=XOUT(b), name=f"t.out{b}", is_output=FINAL)

    P.close_scope(es)



def declare_l0f_weights(nc, nt):
    def t(n, s, dt=F32):
        return nc.dram_tensor(n, s, dt, kind="ExternalInput").ap()
    W = {}
    W["WQT"] = t("l0f_wq", [128, 8, D])
    W["WKT"] = t("l0f_wk", [128, 8, D])
    W["WOT"] = t("l0f_wo", [128, 8, D])
    W["WinvT"] = t("l0f_winv", [128, 8, D])
    W["Vf"] = t("l0f_vf", [128, 8, 512])
    W["Wf"] = t("l0f_wf", [128, 8, 512])
    W["UTf"] = t("l0f_uf", [4, 128, D])
    W["bf"] = t("l0f_bf", [8, D])
    W["Vi"] = t("l0f_vi", [128, 8, 512])
    W["Wie"] = t("l0f_wie", [128, 8, 512])
    W["UTi"] = t("l0f_ui", [4, 128, D])
    W["bi"] = t("l0f_bi", [8, D])
    W["rw1T"] = t("l0f_rw1", [128, 8, D])
    W["rb1_cols"] = t("l0f_rb1", [128, 8])
    W["rw2T"] = t("l0f_rw2", [128, 8, 8])
    W["rb2"] = t("l0f_rb2", [1, 8])
    W["ebias"] = t("l0f_eb", [1, 8])
    W["alpha"] = t("l0f_alpha", [1, 1])
    W["nv"] = t("l0f_nv", [128, nt])
    W["S_all"] = t("l0f_sall", [8, 512], BF16)
    return W


def _kr_layout(v):
    v = np.asarray(v, np.float32)
    return np.ascontiguousarray(v.reshape(KEXP, 8, 128, RR).transpose(2, 1, 0, 3)
                                .reshape(128, 8, KEXP * RR))


def host_l0f(inp, nt):
    H = {}
    for nm, key in (("wq", "L0_ffn_W_Q"), ("wk", "L0_ffn_W_K"),
                    ("wo", "L0_ffn_W_O"), ("winv", "L0_ffn_W_inv")):
        w = np.ascontiguousarray(np.asarray(inp[key], np.float32).T)
        H["l0f_" + nm] = np.ascontiguousarray(w.reshape(8, 128, D).transpose(1, 0, 2))
    H["l0f_vf"] = _kr_layout(inp["L0_ffn_V_fwd"])
    H["l0f_wf"] = _kr_layout(inp["L0_ffn_W_fwd"])
    H["l0f_vi"] = _kr_layout(inp["L0_ffn_V_inv"])
    H["l0f_wie"] = _kr_layout(inp["L0_ffn_W_inv_exp"])
    for nm, key in (("uf", "L0_ffn_U_fwd"), ("ui", "L0_ffn_U_inv")):
        u = np.asarray(inp[key], np.float32)
        H["l0f_" + nm] = np.ascontiguousarray(
            u.transpose(0, 2, 1).reshape(4, 128, D))
    H["l0f_bf"] = np.asarray(inp["L0_ffn_b_fwd"], np.float32).copy()
    H["l0f_bi"] = np.asarray(inp["L0_ffn_b_inv"], np.float32).copy()
    rw1 = np.ascontiguousarray(np.asarray(inp["L0_ffn_router_w1"], np.float32).T)
    H["l0f_rw1"] = np.ascontiguousarray(rw1.reshape(8, 128, D).transpose(1, 0, 2))
    H["l0f_rb1"] = _cols(inp["L0_ffn_router_b1"])
    rw2 = np.ascontiguousarray(np.asarray(inp["L0_ffn_router_w2"], np.float32).T)
    H["l0f_rw2"] = np.ascontiguousarray(rw2.reshape(8, 128, 8).transpose(1, 0, 2))
    H["l0f_rb2"] = np.asarray(inp["L0_ffn_router_b2"], np.float32).reshape(1, 8).copy()
    H["l0f_eb"] = np.asarray(inp["L0_ffn_expert_bias"], np.float32).reshape(1, 8).copy()
    H["l0f_alpha"] = np.asarray(inp["L0_ffn_alpha_bi"], np.float32).reshape(1, 1).copy()
    H["l0f_nv"] = np.broadcast_to(
        1.0 / np.arange(1, nt + 1, dtype=np.float32), (128, nt)).copy()
    sall = np.zeros((8, 512), np.float32)
    for mp in range(4):
        for p in range(128):
            sall[2 * mp + p // 64, mp * 128 + p] = 1.0
    H["l0f_sall"] = sall.astype(BF)
    return H


def emit_l0_ffn(P, G, W, x_in_ap, x_out_ap, nt, scratch, xio=None):
    xio = xio or {}
    XIN = (lambda b: [xio["in"][b]]) if xio.get("in") else (lambda b: [])
    XOUT = (lambda b: [xio["out"][b]]) if xio.get("out") else (lambda b: [])
    FINAL = xio.get("final", True)
    """scratch: dict with dram APs qs, ks, fs: [nb, 8, 128, nt] bf16,
    pcs: [128, 4*D] bf16 (precombined inv weights: VVi | WWi | RW1i x2)."""
    nb = G["nb"]
    CK = min(512, nt)
    nck = nt // CK
    es = P.scope()

    ps1 = Tile(P, "f_ps1", [128, 512], F32, es=es, psum=True)
    ps2 = Tile(P, "f_ps2", [128, 512], F32, es=es, psum=True)
    ps3 = Tile(P, "f_ps3", [128, 512], F32, es=es, psum=True)
    ps_o = Tile(P, "f_pso", [128, 512], F32, es=es, psum=True)
    C = make_ln_common(P, nt, es, G, "f_")
    PP = ParamPack(P, "f_pp", [("rb1", W["rb1_cols"], 8)], es=es)

    nv = P.sbuf("f_nv", [128, 512], F32, es=es)
    nv_buf = Buf("f_nv")
    sall = P.sbuf("f_sall", [8, 512], BF16, es=es)
    sall_buf = Buf("f_sall")
    P.dma("sp", [(sall[:], W["S_all"], {})], writes=[sall_buf], name="f.sall")
    rw2sb = P.sbuf("f_rw2", [128, 64], BF16, es=es)
    rw2_buf = Buf("f_rw2")
    P.dma("pool", [(rw2sb[:].rearrange("p (a e) -> p a e", e=8), W["rw2T"], {})],
          writes=[rw2_buf], name="f.rw2")
    srow = Tile(P, "f_srow", [1, 24], F32, es=es)
    cebbf = Tile(P, "f_ceb", [1, 8], BF16, es=es)
    P.dma("sp", [(srow[0:1, 0:8], W["rb2"], {}), (srow[0:1, 8:16], W["ebias"], {}),
                 (srow[0:1, 16:17], W["alpha"], {})],
          writes=[srow.buf], name="f.rows")
    P.op("dve", lambda e: e.tensor_add(srow[0:1, 0:8], srow[0:1, 0:8], srow[0:1, 8:16]),
         reads=[srow.buf], writes=[srow.buf], name="f.ceb")
    P.op("dve", lambda e: e.tensor_copy(cebbf[0:1, :], srow[0:1, 0:8]),
         reads=[srow.buf], writes=[cebbf.buf], name="f.cebc")
    acol = Tile(P, "f_acol", [128, 1], F32, es=es)
    mmg(P, [(ps1[:, 0:1], G["ones_row_f"][0:1, :], srow[0:1, 16:17], True, True)],
        reads=[srow.buf, G["const_buf"]], writes=[ps1.buf], name="f.abc")
    P.op("act", lambda e: e.copy(acol[:, 0:1], ps1[:, 0:1]),
         reads=[ps1.buf], writes=[acol.buf], name="f.abe")

    vX = P.sbuf("f_vX", [128, 8 * 512], BF16, es=es)
    vX_buf = Buf("f_vX")
    wX = P.sbuf("f_wX", [128, 8 * 512], BF16, es=es)
    wX_buf = Buf("f_wX")
    uX = P.sbuf("f_uX", [128, 4 * D], BF16, es=es)
    uX_buf = Buf("f_uX")
    bX = P.sbuf("f_bX", [8, D], BF16, es=es)
    bX_buf = Buf("f_bX")
    rX = P.sbuf("f_rX", [128, NBLK * D], BF16, es=es)
    rX_buf = Buf("f_rX")
    woX = P.sbuf("f_woX", [128, NBLK * D], BF16, es=es)
    wo_buf = Buf("f_woX")
    qs_bufs = [Buf(f"f_qs.{i}", nofence=True) for i in range(nb)]
    ks_bufs = [Buf(f"f_ks.{i}", nofence=True) for i in range(nb)]
    fs_bufs = [Buf(f"f_fs.{i}", nofence=True) for i in range(nb)]
    pcs_buf = Buf("f_pcs", nofence=True)
    pcs2_buf = Buf("f_pcs2", nofence=True)

    es0 = P.scope()
    winv = P.sbuf("f_winv", [128, NBLK * D], BF16, es=es0)
    winv_buf = Buf("f_winv")
    P.dma("pool", [(winv[:].rearrange("p (a m) -> p a m", m=D), W["WinvT"], {})],
          writes=[winv_buf], name="f.winv")
    for dst_off, srcW in ((0, "Wie"), (1, "Vi")):
        P.dma("pool", [(vX[:].rearrange("p (a c) -> p a c", c=512), W[srcW], {})],
              writes=[vX_buf], name=f"f.p0l{dst_off}")
        for dblk in range(NBLK):
            for k in range(KEXP):
                mmg(P, [(ps1[:, 0:64],
                         winv[:, kb * D + dblk * 128: kb * D + dblk * 128 + 128],
                         vX[:, kb * 512 + k * 64: kb * 512 + k * 64 + 64], st, sp)
                        for kb, (st, sp) in zip(range(NBLK), accg(NBLK))],
                    reads=[winv_buf, vX_buf], writes=[ps1.buf],
                    name=f"f.p0m{dst_off}.{dblk}.{k}")
                P.op("act", lambda e, dblk=dblk, k=k: e.copy(
                    wX[:, dblk * 512 + k * 64: dblk * 512 + k * 64 + 64], ps1[:, 0:64]),
                    reads=[ps1.buf], writes=[wX_buf], name=f"f.p0e{dst_off}.{dblk}.{k}")
        P.dma("sp", [(scratch["pcs"][:, dst_off * 4096:(dst_off + 1) * 4096], wX[:], {})],
              reads=[wX_buf], writes=[pcs_buf], name=f"f.p0s{dst_off}")
    rwi = P.sbuf("f_rwi", [128, NBLK * D], BF16, es=es0)
    rwi_buf = Buf("f_rwi")
    P.dma("pool", [(rX[:].rearrange("p (a m) -> p a m", m=D), W["rw1T"], {})],
          writes=[rX_buf], name="f.p0r")
    for dblk in range(NBLK):
        for hc in range(2):
            mmg(P, [(ps1[:, 0:512],
                     winv[:, kb * D + dblk * 128: kb * D + dblk * 128 + 128],
                     rX[:, kb * D + hc * 512: kb * D + hc * 512 + 512], st, sp)
                    for kb, (st, sp) in zip(range(NBLK), accg(NBLK))],
                reads=[winv_buf, rX_buf], writes=[ps1.buf], name=f"f.p0rm.{dblk}.{hc}")
            P.op("act", lambda e, dblk=dblk, hc=hc: e.copy(
                rwi[:, dblk * 1024 + hc * 512: dblk * 1024 + hc * 512 + 512],
                ps1[:, 0:512]),
                reads=[ps1.buf], writes=[rwi_buf], name=f"f.p0re.{dblk}.{hc}")
    P.dma("sp", [(scratch["pcs2"][:, :], rwi[:], {})], reads=[rwi_buf],
          writes=[pcs2_buf], name="f.p0rs")
    P.close_scope(es0)

    qck = P.sbuf("f_qck", [128, NBLK * CK], BF16, es=es)
    qck_bufs = [Buf(f"f_qck.{i}") for i in range(NBLK)]
    kck = P.sbuf("f_kck", [128, NBLK * CK], BF16, es=es)
    kck_bufs = [Buf(f"f_kck.{i}") for i in range(NBLK)]
    esqk = P.scope()
    xck = P.sbuf("f_xck", [128, 2 * NBLK * CK], BF16, es=esqk)
    xck_bufs = [Buf("f_xck.0"), Buf("f_xck.1")]
    for tgt, wkey, snm in (("q", "WQT", "qs"), ("k", "WKT", "ks")):
        P.dma("pool", [(rX[:].rearrange("p (a m) -> p a m", m=D), W[wkey], {})],
              writes=[rX_buf], name=f"f.w{tgt}")
        for b in range(nb):
            for ck in range(nck):
                c0 = ck * CK
                slot = ck % 2
                P.dma("pool", [(xck[:, slot * NBLK * CK:(slot + 1) * NBLK * CK]
                                .rearrange("p (a c) -> p a c", c=CK),
                                x_in_ap(b, None)[:, :, c0:c0 + CK].rearrange(
                                    "a p c -> p a c"), {})],
                      reads=XIN(b), writes=[xck_bufs[slot]], name=f"f.x{tgt}{b}.{ck}")
                for blk in range(NBLK):
                    mmg(P, [(ps1[:, 0:CK],
                             rX[:, kb * D + blk * 128:kb * D + blk * 128 + 128],
                             xck[:, slot * NBLK * CK + kb * CK:
                                 slot * NBLK * CK + kb * CK + CK], st, sp)
                            for kb, (st, sp) in zip(range(NBLK), accg(NBLK))],
                        reads=[rX_buf, xck_bufs[slot]], writes=[ps1.buf],
                        name=f"f.{tgt}mm{b}.{ck}.{blk}")
                    P.op("act", lambda e, blk=blk: e.copy(
                        qck[:, blk * CK:(blk + 1) * CK], ps1[:, 0:CK]),
                        reads=[ps1.buf], writes=[qck_bufs[blk]],
                        name=f"f.{tgt}ev{b}.{ck}.{blk}")
                P.dma("sp", [(scratch[snm][b, blk][:, c0:c0 + CK],
                              qck[:, blk * CK:(blk + 1) * CK], {})
                             for blk in range(NBLK)],
                      reads=qck_bufs,
                      writes=[(qs_bufs if tgt == "q" else ks_bufs)[b]],
                      name=f"f.{tgt}st{b}.{ck}")

    P.close_scope(esqk)
    hx = DT(P, "f_hx", NBLK, nt, F32, es=es)
    hrbf = P.sbuf("f_hr", [128, NBLK * CK], BF16, es=es)
    hr_bufs = [Buf(f"f_hr.{i}") for i in range(NBLK)]
    lgt = Tile(P, "f_lgt", [128, 8], F32, es=es)
    prb = Tile(P, "f_prb", [128, 8], F32, es=es)
    pcol = Tile(P, "f_pcol", [128, 4], F32, es=es)
    wbf = Tile(P, "f_wbf", [128, 8], BF16, es=es)
    wTb = Tile(P, "f_wT", [8, CK], BF16, es=es)
    xv = P.sbuf("f_xv", [128, 4 * CK], BF16, es=es)
    xv_bufs = [Buf(f"f_xv.{i}") for i in range(4)]
    cum = P.sbuf("f_cum", [128, 4 * CK], BF16, es=es)
    cum_bufs = [Buf(f"f_cum.{i}") for i in range(4)]
    pw = P.sbuf("f_pw", [128, 4 * CK], BF16, es=es)
    pw_bufs = [Buf(f"f_pw.{i}") for i in range(4)]
    carry = Tile(P, "f_carry", [128, 4], F32, es=es)
    fck = P.sbuf("f_fck", [128, NBLK * CK], BF16, es=es)
    fck_bufs = [Buf(f"f_fck.{i}") for i in range(NBLK)]
    cbk = fck
    cbk_bufs = fck_bufs
    tmpb = Tile(P, "f_tmpb", [128, CK], BF16, es=es)
    ctmp = Tile(P, "f_ctmp", [128, CK], F32, es=es)
    xres = P.sbuf("f_xres", [128, 2 * CK], F32, es=es)
    xres_bufs = [Buf("f_xres.0"), Buf("f_xres.1")]

    P.dma("pool", [(woX[:].rearrange("p (a m) -> p a m", m=D), W["WOT"], {})],
          writes=[wo_buf], name="f.wo")

    for br in range(2):
        if br == 0:
            P.dma("pool", [(vX[:].rearrange("p (a c) -> p a c", c=512), W["Vf"], {})],
                  writes=[vX_buf], name="f.vf")
            P.dma("pool", [(wX[:].rearrange("p (a c) -> p a c", c=512), W["Wf"], {})],
                  writes=[wX_buf], name="f.wf")
            P.dma("pool", [(uX[:].rearrange("p (a m) -> p a m", m=D),
                            W["UTf"].rearrange("a p m -> p a m"), {})],
                  writes=[uX_buf], name="f.uf")
            P.dma("pool", [(bX[:], W["bf"], {})], writes=[bX_buf], name="f.bf")
            P.dma("pool", [(rX[:].rearrange("p (a m) -> p a m", m=D), W["rw1T"], {})],
                  writes=[rX_buf], name="f.r1f")
        else:
            P.dma("sp", [(vX[:], scratch["pcs"][:, 0:4096], {})],
                  reads=[pcs_buf], writes=[vX_buf], name="f.vi")
            P.dma("sp", [(wX[:], scratch["pcs"][:, 4096:8192], {})],
                  reads=[pcs_buf], writes=[wX_buf], name="f.wi")
            P.dma("pool", [(uX[:].rearrange("p (a m) -> p a m", m=D),
                            W["UTi"].rearrange("a p m -> p a m"), {})],
                  writes=[uX_buf], name="f.ui")
            P.dma("pool", [(bX[:], W["bi"], {})], writes=[bX_buf], name="f.bi")
            P.dma("sp", [(rX[:], scratch["pcs2"][:, :], {})],
                  reads=[pcs2_buf], writes=[rX_buf], name="f.r1i")
        for b in range(nb):
            P.op("pool", lambda e: e.memset(carry[:, :], 0.0),
                 writes=[carry.buf], name=f"f.cz{br}.{b}")
            for ck in range(nck):
                c0 = ck * CK
                P.dma("sp", [(qck[:, blk * CK:(blk + 1) * CK],
                              scratch["qs"][b, blk][:, c0:c0 + CK], {})
                             for blk in range(NBLK)],
                      reads=[qs_bufs[b]], writes=qck_bufs, name=f"f.lq{br}.{b}.{ck}")
                P.dma("sp", [(kck[:, blk * CK:(blk + 1) * CK],
                              scratch["ks"][b, blk][:, c0:c0 + CK], {})
                             for blk in range(NBLK)],
                      reads=[ks_bufs[b]], writes=kck_bufs, name=f"f.lk{br}.{b}.{ck}")
                if br == 1:
                    P.dma("sp", [(fck[:, blk * CK:(blk + 1) * CK],
                                  scratch["fs"][b, blk][:, c0:c0 + CK], {})
                                 for blk in range(NBLK)],
                          reads=[fs_bufs[b]], writes=fck_bufs, name=f"f.lf{b}.{ck}")
                for blk in range(NBLK):
                    mmg(P, [(ps1[:, 0:CK],
                             rX[:, kb * D + blk * 128:kb * D + blk * 128 + 128],
                             qck[:, kb * CK:(kb + 1) * CK], st, sp)
                            for kb, (st, sp) in zip(range(NBLK), accg(NBLK))],
                        reads=[rX_buf] + qck_bufs, writes=[ps1.buf],
                        name=f"f.rm{br}.{b}.{ck}.{blk}")
                    P.op("act", lambda e, blk=blk: e.activation(
                        hrbf[:, blk * CK:(blk + 1) * CK], ps1[:, 0:CK], AF.Gelu,
                        bias=PP.col("rb1", blk), scale=1.0),
                        reads=[ps1.buf, PP.buf], writes=[hr_bufs[blk]],
                        name=f"f.re{br}.{b}.{ck}.{blk}")
                for tch in range(CK // 128):
                    t0 = tch * 128
                    items = [(ps_o[:, 0:8],
                              hrbf[:, kb * CK + t0: kb * CK + t0 + 128],
                              rw2sb[:, kb * 8:(kb + 1) * 8], st, False)
                             for kb, (st, _) in zip(range(NBLK), accg(NBLK))]
                    items.append((ps_o[:, 0:8], G["ones_row"][0:1, 0:128],
                                  cebbf[0:1, :], False, True))
                    mmg(P, items, reads=hr_bufs + [rw2_buf, cebbf.buf, G["const_buf"]],
                        writes=[ps_o.buf], name=f"f.lg{br}.{b}.{ck}.{tch}")
                    P.op("dve", lambda e: e.tensor_copy(lgt[:, :], ps_o[:, 0:8]),
                         reads=[ps_o.buf], writes=[lgt.buf], name=f"f.lge{br}.{b}.{ck}.{tch}")
                    P.op("dve", lambda e: e.reduce_max(pcol[:, 0:1], lgt[:, :],
                                                       axis=AX.X, negate=True),
                         reads=[lgt.buf], writes=[pcol.buf], name=f"f.mx{br}.{b}.{ck}.{tch}")
                    P.op("act", lambda e: e.activation(prb[:, :], lgt[:, :], AF.Exp,
                                                       bias=pcol[:, 0:1], scale=1.0,
                                                       accum_out=pcol[:, 1:2]),
                         reads=[lgt.buf, pcol.buf], writes=[prb.buf, pcol.buf],
                         name=f"f.ex{br}.{b}.{ck}.{tch}")
                    P.op("dve", lambda e: e.reciprocal(pcol[:, 2:3], pcol[:, 1:2]),
                         reads=[pcol.buf], writes=[pcol.buf], name=f"f.rc{br}.{b}.{ck}.{tch}")
                    P.op("dve", lambda e: e.tensor_scalar_mul(wbf[:, :], prb[:, :],
                                                              pcol[:, 2:3]),
                         reads=[prb.buf, pcol.buf], writes=[wbf.buf],
                         name=f"f.wn{br}.{b}.{ck}.{tch}")
                    mmg(P, [(ps3[0:8, t0:t0 + 128], wbf[:, :], G["eqm"][:],
                             tch == 0, tch == CK // 128 - 1)],
                        reads=[wbf.buf, G["const_buf"]], writes=[ps3.buf],
                        name=f"f.wt{br}.{b}.{ck}.{tch}")
                P.op("act", lambda e: e.copy(wTb[0:8, 0:CK], ps3[0:8, 0:CK]),
                     reads=[ps3.buf], writes=[wTb.buf], name=f"f.wte{br}.{b}.{ck}")
                for mp in range(4):
                    mmg(P, [(ps2[:, 0:CK],
                             vX[:, kb * 512 + mp * 128: kb * 512 + mp * 128 + 128],
                             qck[:, kb * CK:(kb + 1) * CK], st, sp)
                            for kb, (st, sp) in zip(range(NBLK), accg(NBLK))],
                        reads=[vX_buf] + qck_bufs, writes=[ps2.buf],
                        name=f"f.xv{br}.{b}.{ck}.{mp}")
                    P.op("act", lambda e, mp=mp: e.copy(
                        xv[:, mp * CK:(mp + 1) * CK], ps2[:, 0:CK]),
                        reads=[ps2.buf], writes=[xv_bufs[mp]],
                        name=f"f.xve{br}.{b}.{ck}.{mp}")
                    mmg(P, [(ps2[:, 0:CK],
                             wX[:, kb * 512 + mp * 128: kb * 512 + mp * 128 + 128],
                             kck[:, kb * CK:(kb + 1) * CK], st, sp)
                            for kb, (st, sp) in zip(range(NBLK), accg(NBLK))],
                        reads=[wX_buf] + kck_bufs, writes=[ps2.buf],
                        name=f"f.yw{br}.{b}.{ck}.{mp}")
                    P.op("dve", lambda e, mp=mp: e.tensor_tensor_scan(
                        cum[:, mp * CK:(mp + 1) * CK],
                        G["ones_col_f"][:, 0:1].broadcast_to([128, CK]),
                        ps2[:, 0:CK], carry[:, mp:mp + 1], ALU.mult, ALU.add),
                        reads=[ps2.buf, carry.buf, G["const_buf"]],
                        writes=[cum_bufs[mp]], name=f"f.sc{br}.{b}.{ck}.{mp}")
                    P.op("dve", lambda e, mp=mp: e.tensor_copy(
                        carry[:, mp:mp + 1], cum[:, mp * CK + CK - 1: mp * CK + CK]),
                        reads=[cum_bufs[mp]], writes=[carry.buf],
                        name=f"f.cu{br}.{b}.{ck}.{mp}")
                    mmg(P, [(ps3[:, 0:CK], sall[:, mp * 128:(mp + 1) * 128],
                             wTb[0:8, 0:CK], True, True)],
                        reads=[sall_buf, wTb.buf], writes=[ps3.buf],
                        name=f"f.wb{br}.{b}.{ck}.{mp}")
                    P.op("dve", lambda e, mp=mp: e.tensor_mul(
                        tmpb[:, 0:CK], xv[:, mp * CK:(mp + 1) * CK],
                        cum[:, mp * CK:(mp + 1) * CK]),
                        reads=[xv_bufs[mp], cum_bufs[mp]], writes=[tmpb.buf],
                        name=f"f.pr{br}.{b}.{ck}.{mp}")
                    P.op("dve", lambda e, mp=mp: e.tensor_mul(
                        pw[:, mp * CK:(mp + 1) * CK], tmpb[:, 0:CK], ps3[:, 0:CK]),
                        reads=[tmpb.buf, ps3.buf], writes=[pw_bufs[mp]],
                        name=f"f.pw{br}.{b}.{ck}.{mp}")
                for dblk in range(NBLK):
                    items = [(ps_o[:, 0:CK],
                              uX[:, mp * D + dblk * 128: mp * D + dblk * 128 + 128],
                              pw[:, mp * CK:(mp + 1) * CK], st, False)
                             for mp, (st, _) in zip(range(4), accg(4))]
                    items.append((ps_o[:, 0:CK], bX[0:8, dblk * 128:dblk * 128 + 128],
                                  wTb[0:8, 0:CK], False, True))
                    mmg(P, items, reads=pw_bufs + [uX_buf, bX_buf, wTb.buf],
                        writes=[ps_o.buf], name=f"f.om{br}.{b}.{ck}.{dblk}")
                    if br == 0:
                        P.op("act", lambda e, dblk=dblk: e.copy(
                            fck[:, dblk * CK:(dblk + 1) * CK], ps_o[:, 0:CK]),
                            reads=[ps_o.buf], writes=[fck_bufs[dblk]],
                            name=f"f.oe{br}.{b}.{ck}.{dblk}")
                    else:
                        P.op("dve", lambda e, dblk=dblk: e.scalar_tensor_tensor(
                            cbk[:, dblk * CK:(dblk + 1) * CK], ps_o[:, 0:CK],
                            acol[:, 0:1], fck[:, dblk * CK:(dblk + 1) * CK],
                            ALU.mult, ALU.add),
                            reads=[ps_o.buf, acol.buf, fck_bufs[dblk]],
                            writes=[cbk_bufs[dblk]], name=f"f.cb{br}.{b}.{ck}.{dblk}")
                if br == 0:
                    P.dma("sp", [(scratch["fs"][b, blk][:, c0:c0 + CK],
                                  fck[:, blk * CK:(blk + 1) * CK], {})
                                 for blk in range(NBLK)],
                          reads=fck_bufs, writes=[fs_bufs[b]], name=f"f.fst{b}.{ck}")
                if br == 1:
                    P.dma("sp", [(nv[:, 0:CK], W["nv"][:, c0:c0 + CK], {})],
                          writes=[nv_buf], name=f"f.nvl{b}.{ck}")
                    for blk in range(NBLK):
                        mmg(P, [(ps1[:, 0:CK],
                                 woX[:, kb * D + blk * 128:kb * D + blk * 128 + 128],
                                 cbk[:, kb * CK:(kb + 1) * CK], st, sp)
                                for kb, (st, sp) in zip(range(NBLK), accg(NBLK))],
                            reads=[wo_buf] + cbk_bufs, writes=[ps1.buf],
                            name=f"f.wom{b}.{ck}.{blk}")
                        P.op("dve", lambda e, blk=blk: e.tensor_mul(
                            ctmp[:, 0:CK], ps1[:, 0:CK], nv[:, 0:CK]),
                            reads=[ps1.buf, nv_buf], writes=[ctmp.buf],
                            name=f"f.nv{b}.{ck}.{blk}")
                        xslot = blk % 2
                        P.dma("sp", [(xres[:, xslot * CK:(xslot + 1) * CK],
                                      x_in_ap(b, blk)[:, c0:c0 + CK], {})],
                              reads=XIN(b), writes=[xres_bufs[xslot]],
                              name=f"f.xr{b}.{ck}.{blk}")
                        P.op("dve", lambda e, blk=blk, c0=c0, xslot=xslot: e.tensor_add(
                            hx.ap(blk, c0, c0 + CK), ctmp[:, 0:CK],
                            xres[:, xslot * CK:(xslot + 1) * CK]),
                            reads=[ctmp.buf, xres_bufs[xslot]], writes=[hx.bufs[blk]],
                            name=f"f.hx{b}.{ck}.{blk}")
            if br == 1:
                emit_ln(P, C, hx, hx, None, None, None, EPS_MOD, f"f.pln.{b}")
                P.dma("sp", [(x_out_ap(b, blk), hx.ap(blk), {}) for blk in range(NBLK)],
                      reads=[hx.bufs[blk] for blk in range(NBLK)],
                      writes=XOUT(b), name=f"f.out{b}", is_output=FINAL)

    P.close_scope(es)



def declare_l1f_weights(nc, nt):
    def t(n, s, dt=F32):
        return nc.dram_tensor(n, s, dt, kind="ExternalInput").ap()
    W = {}
    W["gwT"] = t("l1f_gw", [128, 8, 8])
    W["gpT"] = t("l1f_gp", [KEXP, 128, 8, MH])
    W["upT"] = t("l1f_up", [KEXP, 128, 8, MH])
    W["dpT"] = t("l1f_dp", [KEXP, 128, MHB, D])
    W["TRI"] = t("l1f_tri", [128, 128], BF16)
    W["SEL"] = t("l1f_sel", [16, 2])
    W["ESEL"] = t("l1f_esel", [8, 1024], BF16)
    return W


def host_l1f(inp, nt):
    H = {}
    gw = np.ascontiguousarray(np.asarray(inp["L1_ffn_gate_w"], np.float32).T)
    H["l1f_gw"] = np.ascontiguousarray(gw.reshape(8, 128, 8).transpose(1, 0, 2))
    gp = np.asarray(inp["L1_ffn_gate_proj_w"], np.float32)
    H["l1f_gp"] = np.ascontiguousarray(
        gp.transpose(0, 2, 1).reshape(KEXP, 8, 128, MH).transpose(0, 2, 1, 3))
    up = np.asarray(inp["L1_ffn_up_proj_w"], np.float32)
    H["l1f_up"] = np.ascontiguousarray(
        up.transpose(0, 2, 1).reshape(KEXP, 8, 128, MH).transpose(0, 2, 1, 3))
    dp = np.asarray(inp["L1_ffn_down_proj_w"], np.float32)
    H["l1f_dp"] = np.ascontiguousarray(
        dp.transpose(0, 2, 1).reshape(KEXP, MHB, 128, D).transpose(0, 2, 1, 3))
    tri = np.triu(np.ones((128, 128), np.float32))
    H["l1f_tri"] = tri.astype(BF)
    esel = np.zeros((8, 1024), np.float32)
    for ei in range(8):
        esel[ei, ei * 128:(ei + 1) * 128] = 1.0
    H["l1f_esel"] = esel.astype(BF)
    return H


def emit_l1f_prestage(P, W):
    gpb = P.nc.dram_tensor("m_gpb", [KEXP, 2, 128, NBLK * (MH // 2)], BF16,
                           kind="Internal").ap()
    upb = P.nc.dram_tensor("m_upb", [KEXP, 2, 128, NBLK * (MH // 2)], BF16,
                           kind="Internal").ap()
    dpb = P.nc.dram_tensor("m_dpb", [KEXP, 128, MHB, D], BF16, kind="Internal").ap()
    wbf_bufs = {"gp": [Buf(f"m_gpb{i}", nofence=True) for i in range(KEXP)],
                "up": [Buf(f"m_upb{i}", nofence=True) for i in range(KEXP)],
                "dp": [Buf(f"m_dpb{i}", nofence=True) for i in range(KEXP)]}
    for e_ in range(KEXP):
        P.dma("pool", [(gpb[e_, half].rearrange("p (a m) -> p a m", m=MH // 2),
                        W["gpT"][e_][:, :, half * (MH // 2):(half + 1) * (MH // 2)],
                        {}) for half in range(2)],
              writes=[wbf_bufs["gp"][e_]], name=f"m.cgp{e_}")
        P.dma("pool", [(upb[e_, half].rearrange("p (a m) -> p a m", m=MH // 2),
                        W["upT"][e_][:, :, half * (MH // 2):(half + 1) * (MH // 2)],
                        {}) for half in range(2)],
              writes=[wbf_bufs["up"][e_]], name=f"m.cup{e_}")
        P.dma("pool", [(dpb[e_], W["dpT"][e_], {})], writes=[wbf_bufs["dp"][e_]],
              name=f"m.cdp{e_}")
    return {"gpb": gpb, "upb": upb, "dpb": dpb, "bufs": wbf_bufs}


def emit_l1_ffn(P, G, W, x_in_ap, x_out_ap, nt, capacity, do_collective, cc, xio=None,
                prestage=None):
    xio = xio or {}
    XIN = (lambda b: [xio["in"][b]]) if xio.get("in") else (lambda b: [])
    XOUT = (lambda b: [xio["out"][b]]) if xio.get("out") else (lambda b: [])
    FINAL = xio.get("final", True)
    """cc: dict with Internal dram APs cc_in (2,8), cc_out (16,8) (if do_collective)."""
    nb = G["nb"]
    CK = min(512, nt)
    nck = nt // CK
    ntch = nt // 128
    HHB = MHB // 2
    es = P.scope()

    ps_g = Tile(P, "m_psg", [128, 512], F32, es=es, psum=True)
    ps_g2 = Tile(P, "m_psg2", [128, 512], F32, es=es, psum=True)
    ps_a = Tile(P, "m_psa", [128, 512], F32, es=es, psum=True)
    ps_u = Tile(P, "m_psu", [128, 512], F32, es=es, psum=True)
    ps_y = Tile(P, "m_psy", [128, 512], F32, es=es, psum=True)
    C = make_ln_common(P, nt, es, G, "m_")

    gwsb = P.sbuf("m_gw", [128, 64], BF16, es=es)
    gw_buf = Buf("m_gw")
    P.dma("pool", [(gwsb[:].rearrange("p (a e) -> p a e", e=8), W["gwT"], {})],
          writes=[gw_buf], name="m.gw")
    tri = P.sbuf("m_tri", [128, 128], BF16, es=es)
    tri_buf = Buf("m_tri")
    P.dma("sp", [(tri[:], W["TRI"], {})], writes=[tri_buf], name="m.tri")
    selsb = P.sbuf("m_sel", [16, 2], F32, es=es)
    sel_buf = Buf("m_sel")
    P.dma("sp", [(selsb[:], W["SEL"], {})], writes=[sel_buf], name="m.sel")
    esel = P.sbuf("m_esel", [8, 1024], BF16, es=es)
    esel_buf = Buf("m_esel")
    P.dma("sp", [(esel[:], W["ESEL"], {})], writes=[esel_buf], name="m.esel")

    ohsb = P.sbuf("m_oh", [128, nb * 2 * ntch * 8], BF16, es=es)
    oh_buf = Buf("m_oh")
    ohfb = P.sbuf("m_ohf", [128, nb * 2 * ntch * 8], F32, es=es)
    ohf_buf = Buf("m_ohf")
    wcol = P.sbuf("m_wc", [128, nb * 2 * ntch], F32, es=es)
    wc_buf = Buf("m_wc")
    mrc = P.sbuf("m_mr", [128, nb * 2 * ntch], F32, es=es)
    mr_buf = Buf("m_mr")
    prev = Tile(P, "m_prev", [1, 16], F32, es=es)
    offs = Tile(P, "m_offs", [1, 16], F32, es=es)
    ccsb = Tile(P, "m_cc", [16, 8], F32, es=es)
    lgt = Tile(P, "m_lgt", [128, 8], F32, es=es)
    prb = Tile(P, "m_prb", [128, 8], F32, es=es)
    p2 = Tile(P, "m_p2", [128, 8], F32, es=es)
    pcol = Tile(P, "m_pcol", [128, 8], F32, es=es)
    t8 = Tile(P, "m_t8", [128, 8], F32, es=es)
    xall = P.sbuf("m_xall", [128, NBLK * nt], BF16, es=es)
    xall_bufs = [Buf(f"m_xall.{i}") for i in range(NBLK)]

    P.op("pool", lambda e: e.memset(prev[:, :], 0.0), writes=[prev.buf], name="m.pz")

    if prestage is None:
        prestage = emit_l1f_prestage(P, W)
    gpb, upb, dpb = prestage["gpb"], prestage["upb"], prestage["dpb"]
    wbf_bufs = prestage["bufs"]

    def oh_ap(b, k, tch, t=None):
        o = (((b * 2 + k) * ntch) + tch) * 8
        return o

    for b in range(nb):
        P.dma("pool", [(xall[:, kb * nt:(kb + 1) * nt], x_in_ap(b, kb), {})
                       for kb in range(NBLK)],
              reads=XIN(b), writes=xall_bufs, name=f"m.x{b}")
        for tch in range(ntch):
            t0 = tch * 128
            mmg(P, [(ps_g[:, 0:8],
                     xall[:, kb * nt + t0: kb * nt + t0 + 128],
                     gwsb[:, kb * 8:(kb + 1) * 8], st, sp)
                    for kb, (st, sp) in zip(range(NBLK), accg(NBLK))],
                reads=[gw_buf] + xall_bufs, writes=[ps_g.buf],
                name=f"m.lg{b}.{tch}")
            P.op("dve", lambda e: e.tensor_copy(lgt[:, :], ps_g[:, 0:8]),
                 reads=[ps_g.buf], writes=[lgt.buf], name=f"m.lge{b}.{tch}")
            P.op("dve", lambda e: e.reduce_max(pcol[:, 0:1], lgt[:, :],
                                               axis=AX.X, negate=True),
                 reads=[lgt.buf], writes=[pcol.buf], name=f"m.mx{b}.{tch}")
            P.op("act", lambda e: e.activation(prb[:, :], lgt[:, :], AF.Exp,
                                               bias=pcol[:, 0:1], scale=1.0,
                                               accum_out=pcol[:, 1:2]),
                 reads=[lgt.buf, pcol.buf], writes=[prb.buf, pcol.buf],
                 name=f"m.ex{b}.{tch}")
            P.op("dve", lambda e: e.reciprocal(pcol[:, 2:3], pcol[:, 1:2]),
                 reads=[pcol.buf], writes=[pcol.buf], name=f"m.rc{b}.{tch}")
            P.op("dve", lambda e: e.tensor_scalar_mul(prb[:, :], prb[:, :], pcol[:, 2:3]),
                 reads=[prb.buf, pcol.buf], writes=[prb.buf], name=f"m.pn{b}.{tch}")
            P.op("dve", lambda e: e.reduce_max(pcol[:, 3:4], prb[:, :], axis=AX.X),
                 reads=[prb.buf], writes=[pcol.buf], name=f"m.v1{b}.{tch}")
            o = oh_ap(b, 0, tch)
            P.op("dve", lambda e, o=o: e.tensor_single_scalar(
                ohfb[:, o:o + 8], prb[:, :], pcol[:, 3:4], ALU.is_equal),
                reads=[prb.buf, pcol.buf], writes=[ohf_buf], name=f"m.oh1{b}.{tch}")
            P.op("dve", lambda e, o=o: e.tensor_copy(ohsb[:, o:o + 8], ohfb[:, o:o + 8]),
                 reads=[ohf_buf], writes=[oh_buf], name=f"m.oh1c{b}.{tch}")
            P.op("dve", lambda e, o=o: e.scalar_tensor_tensor(
                p2[:, :], ohfb[:, o:o + 8], -1e30, prb[:, :], ALU.mult, ALU.add),
                reads=[ohf_buf, prb.buf], writes=[p2.buf], name=f"m.p2{b}.{tch}")
            P.op("dve", lambda e: e.reduce_max(pcol[:, 4:5], p2[:, :], axis=AX.X),
                 reads=[p2.buf], writes=[pcol.buf], name=f"m.v2{b}.{tch}")
            o2 = oh_ap(b, 1, tch)
            P.op("dve", lambda e, o2=o2: e.tensor_single_scalar(
                ohfb[:, o2:o2 + 8], p2[:, :], pcol[:, 4:5], ALU.is_equal),
                reads=[p2.buf, pcol.buf], writes=[ohf_buf], name=f"m.oh2{b}.{tch}")
            P.op("dve", lambda e, o2=o2: e.tensor_copy(ohsb[:, o2:o2 + 8], ohfb[:, o2:o2 + 8]),
                 reads=[ohf_buf], writes=[oh_buf], name=f"m.oh2c{b}.{tch}")
            P.op("dve", lambda e: e.tensor_add(pcol[:, 5:6], pcol[:, 3:4], pcol[:, 4:5]),
                 reads=[pcol.buf], writes=[pcol.buf], name=f"m.s12{b}.{tch}")
            P.op("dve", lambda e: e.tensor_scalar_add(pcol[:, 5:6], pcol[:, 5:6], 1e-9),
                 reads=[pcol.buf], writes=[pcol.buf], name=f"m.se{b}.{tch}")
            P.op("dve", lambda e: e.reciprocal(pcol[:, 6:7], pcol[:, 5:6]),
                 reads=[pcol.buf], writes=[pcol.buf], name=f"m.sr{b}.{tch}")
            wo1 = (b * 2 + 0) * ntch + tch
            wo2 = (b * 2 + 1) * ntch + tch
            P.op("dve", lambda e, wo1=wo1: e.tensor_mul(
                wcol[:, wo1:wo1 + 1], pcol[:, 3:4], pcol[:, 6:7]),
                reads=[pcol.buf], writes=[wc_buf], name=f"m.w1{b}.{tch}")
            P.op("dve", lambda e, wo2=wo2: e.tensor_mul(
                wcol[:, wo2:wo2 + 1], pcol[:, 4:5], pcol[:, 6:7]),
                reads=[pcol.buf], writes=[wc_buf], name=f"m.w2{b}.{tch}")
            items = []
            for k, o_ in ((0, o), (1, o2)):
                items.append((ps_g[:, 16 + k * 8:16 + k * 8 + 8], tri[:],
                              ohsb[:, o_:o_ + 8], k == 0, False))
            for k, o_ in ((0, o), (1, o2)):
                items.append((ps_g[0:1, 32 + k * 8:32 + k * 8 + 8],
                              G["ones_col"][:, 0:1], ohsb[:, o_:o_ + 8], False, False))

            def prev_items(eng, items=items):
                last = None
                for (oap, l, r, st, sp) in items:
                    last = eng.matmul(oap, l, r, start=st, stop=sp,
                                      skip_group_check=True)
                for k in range(2):
                    last = eng.matmul(ps_g[:, 16 + k * 8:16 + k * 8 + 8],
                                      G["ones_row_f"][0:1, :],
                                      prev[0:1, k * 8:(k + 1) * 8],
                                      start=False, stop=(k == 1),
                                      skip_group_check=True)
                return last
            P.op("pe", prev_items,
                 reads=[tri_buf, oh_buf, prev.buf, G["const_buf"]],
                 writes=[ps_g.buf], name=f"m.rk{b}.{tch}")
            P.op("dve", lambda e: e.tensor_add(prev[0:1, :], prev[0:1, :],
                                               ps_g[0:1, 32:48]),
                 reads=[prev.buf, ps_g.buf], writes=[prev.buf], name=f"m.pu{b}.{tch}")
            for k, o_ in ((0, o), (1, o2)):
                wo = (b * 2 + k) * ntch + tch
                P.op("dve", lambda e, k=k, o_=o_: e.tensor_mul(
                    t8[:, :], ps_g[:, 16 + k * 8:16 + k * 8 + 8], ohfb[:, o_:o_ + 8]),
                    reads=[ps_g.buf, ohf_buf], writes=[t8.buf], name=f"m.mr1{b}.{tch}.{k}")
                P.op("dve", lambda e, wo=wo: e.reduce_sum(
                    mrc[:, wo:wo + 1], t8[:, :], axis=AX.X),
                    reads=[t8.buf], writes=[mr_buf], name=f"m.mr2{b}.{tch}.{k}")

    if do_collective:
        ccin_buf = Buf("m_ccin")
        ccbuf = Buf("m_ccdma")
        zpad = Tile(P, "m_zpad", [2, 128], F32, es=es)
        P.op("pool", lambda e: e.memset(zpad[:, :], 0.0), writes=[zpad.buf], name="m.zp")
        P.dma("sp", [(cc["cc_in"], zpad[:, :], {})],
              reads=[zpad.buf], writes=[ccin_buf], name="m.ccz")
        P.dma("sp", [(cc["cc_in"][0:1, 0:8], prev[0:1, 0:8], {}),
                     (cc["cc_in"][1:2, 0:8], prev[0:1, 8:16], {})],
              reads=[prev.buf], writes=[ccin_buf], name="m.ccout")

        def collfn(e):
            return e.collective_compute(
                "AllGather", ALU.bypass, replica_groups=[list(range(8))],
                ins=[cc["cc_in"].opt()], outs=[cc["cc_out"].opt()])
        P.op("pool", collfn, reads=[ccin_buf], writes=[ccbuf], name="m.cc")
        P.dma("sp", [(ccsb[:, :], cc["cc_out"][:, 0:8], {})], reads=[ccbuf],
              writes=[ccsb.buf], name="m.ccin")
        mmg(P, [(ps_g[0:1, k * 8:(k + 1) * 8], selsb[:, k:k + 1], ccsb[:, :],
                 k == 0, k == 1) for k in range(2)],
            reads=[sel_buf, ccsb.buf], writes=[ps_g.buf], name="m.off")
        P.op("act", lambda e: e.copy(offs[0:1, :], ps_g[0:1, 0:16]),
             reads=[ps_g.buf], writes=[offs.buf], name="m.offe")
    else:
        P.op("pool", lambda e: e.memset(offs[:, :], 0.0), writes=[offs.buf], name="m.off0")

    hx = DT(P, "m_hx", NBLK, nt, F32, es=es)
    cmb = Tile(P, "m_cmb", [128, 8], F32, es=es)
    cmbbf = Tile(P, "m_cmbbf", [128, 8], BF16, es=es)
    combT = Tile(P, "m_combT", [8, nt], BF16, es=es)
    gph = P.sbuf("m_gph", [128, NBLK * MH // 2], BF16, es=es)
    gph_buf = Buf("m_gph")
    uph = P.sbuf("m_uph", [128, NBLK * MH // 2], BF16, es=es)
    uph_buf = Buf("m_uph")
    dph = P.sbuf("m_dph", [128, HHB * D], BF16, es=es)
    dph_buf = Buf("m_dph")
    cck = P.sbuf("m_cck", [128, HHB * CK], BF16, es=es)
    cck_bufs = [Buf(f"m_cck.{i}") for i in range(HHB)]
    xek = P.sbuf("m_xek", [128, NBLK * CK], BF16, es=es)
    xek_bufs = [Buf(f"m_xek.{i}") for i in range(NBLK)]
    cbs = Tile(P, "m_cbs", [128, CK], BF16, es=es)
    gsb = P.sbuf("m_gsb", [128, 2 * CK], BF16, es=es)
    gsb_bufs = [Buf("m_gsb.0"), Buf("m_gsb.1")]

    for b in range(nb):
        for tch in range(ntch):
            t0 = tch * 128
            o = oh_ap(b, 0, tch)
            o2 = oh_ap(b, 1, tch)
            for k, o_ in ((0, o), (1, o2)):
                wo = (b * 2 + k) * ntch + tch
                mmg(P, [(ps_a[:, 0:8], G["ones_row_f"][0:1, :],
                         offs[0:1, k * 8:(k + 1) * 8], True, True)],
                    reads=[offs.buf, G["const_buf"]], writes=[ps_a.buf],
                    name=f"m.ob{b}.{tch}.{k}")
                P.op("dve", lambda e, o_=o_: e.tensor_mul(
                    t8[:, :], ps_a[:, 0:8], ohfb[:, o_:o_ + 8]),
                    reads=[ps_a.buf, ohf_buf], writes=[t8.buf], name=f"m.os{b}.{tch}.{k}")
                P.op("dve", lambda e: e.reduce_sum(pcol[:, 0:1], t8[:, :], axis=AX.X),
                     reads=[t8.buf], writes=[pcol.buf], name=f"m.or{b}.{tch}.{k}")
                P.op("dve", lambda e, wo=wo: e.tensor_add(
                    pcol[:, 1:2], pcol[:, 0:1], mrc[:, wo:wo + 1]),
                    reads=[pcol.buf, mr_buf], writes=[pcol.buf], name=f"m.rt{b}.{tch}.{k}")
                P.op("dve", lambda e: e.tensor_single_scalar(
                    pcol[:, 2:3], pcol[:, 1:2], capacity + 0.5, ALU.is_lt),
                    reads=[pcol.buf], writes=[pcol.buf], name=f"m.kp{b}.{tch}.{k}")
                P.op("dve", lambda e, wo=wo, k=k: e.tensor_mul(
                    pcol[:, 3 + k:4 + k], pcol[:, 2:3], wcol[:, wo:wo + 1]),
                    reads=[pcol.buf, wc_buf], writes=[pcol.buf], name=f"m.wk{b}.{tch}.{k}")
            P.op("dve", lambda e, o=o: e.tensor_scalar_mul(
                cmb[:, :], ohfb[:, o:o + 8], pcol[:, 3:4]),
                reads=[ohf_buf, pcol.buf], writes=[cmb.buf], name=f"m.cb1{b}.{tch}")
            P.op("dve", lambda e, o2=o2: e.scalar_tensor_tensor(
                cmb[:, :], ohfb[:, o2:o2 + 8], pcol[:, 4:5], cmb[:, :],
                ALU.mult, ALU.add),
                reads=[ohf_buf, pcol.buf, cmb.buf], writes=[cmb.buf],
                name=f"m.cb2{b}.{tch}")
            P.op("dve", lambda e: e.tensor_copy(cmbbf[:, :], cmb[:, :]),
                 reads=[cmb.buf], writes=[cmbbf.buf], name=f"m.cbc{b}.{tch}")
            mmg(P, [(ps_y[0:8, (t0 % 512):(t0 % 512) + 128], cmbbf[:, :], G["eqm"][:],
                     t0 % 512 == 0, t0 % 512 == 384 or tch == ntch - 1)],
                reads=[cmbbf.buf, G["const_buf"]], writes=[ps_y.buf],
                name=f"m.ct{b}.{tch}")
            if t0 % 512 == 384 or tch == ntch - 1:
                c0 = (t0 // 512) * 512
                P.op("act", lambda e, c0=c0, t0=t0: e.copy(
                    combT[0:8, c0:c0 + (t0 % 512) + 128], ps_y[0:8, 0:(t0 % 512) + 128]),
                    reads=[ps_y.buf], writes=[combT.buf], name=f"m.cte{b}.{tch}")
        P.dma("sp", [(hx.ap(blk), x_in_ap(b, blk), {}) for blk in range(NBLK)],
              reads=XIN(b), writes=[hx.bufs[i] for i in range(NBLK)], name=f"m.hx{b}")
        P.dma("pool", [(xall[:, kb * nt:(kb + 1) * nt], x_in_ap(b, kb), {})
                       for kb in range(NBLK)],
              reads=XIN(b), writes=xall_bufs, name=f"m.xc{b}")
        for e_ in range(KEXP):
            for half in range(2):
                h0 = half * HHB
                P.dma("pool", [(gph[:], gpb[e_, half], {})],
                      reads=[wbf_bufs["gp"][e_]], writes=[gph_buf], name=f"m.gp{b}.{e_}.{half}")
                P.dma("sp", [(uph[:], upb[e_, half], {})],
                      reads=[wbf_bufs["up"][e_]], writes=[uph_buf], name=f"m.up{b}.{e_}.{half}")
                P.dma("act", [(dph[:].rearrange("p (a m) -> p a m", m=D),
                               dpb[e_][:, h0:h0 + HHB, :], {})],
                      reads=[wbf_bufs["dp"][e_]], writes=[dph_buf], name=f"m.dp{b}.{e_}.{half}")
                for ck in range(nck):
                    c0 = ck * CK
                    mmg(P, [(ps_a[:, 0:CK], esel[:, e_ * 128:(e_ + 1) * 128],
                             combT[0:8, c0:c0 + CK], True, True)],
                        reads=[combT.buf, esel_buf], writes=[ps_a.buf],
                        name=f"m.cbb{b}.{e_}.{half}.{ck}")
                    P.op("act", lambda e: e.copy(cbs[:, 0:CK], ps_a[:, 0:CK]),
                         reads=[ps_a.buf], writes=[cbs.buf],
                         name=f"m.cbe{b}.{e_}.{half}.{ck}")

                    def mk_xe(e, c0=c0):
                        for kb in range(NBLK):
                            last = e.tensor_mul(
                                xek[:, kb * CK:(kb + 1) * CK],
                                xall[:, kb * nt + c0: kb * nt + c0 + CK], cbs[:, 0:CK])
                        return last
                    P.op("dve", mk_xe, reads=xall_bufs + [cbs.buf],
                         writes=xek_bufs, name=f"m.xe{b}.{e_}.{half}.{ck}")
                    for hb in range(HHB):
                        psg = ps_g if hb % 2 == 0 else ps_g2
                        mmg(P, [(psg[:, 0:CK],
                                 gph[:, kb * (MH // 2) + hb * 128:
                                     kb * (MH // 2) + hb * 128 + 128],
                                 xall[:, kb * nt + c0: kb * nt + c0 + CK], st, sp)
                                for kb, (st, sp) in zip(range(NBLK), accg(NBLK))],
                            reads=[gph_buf] + xall_bufs, writes=[psg.buf],
                            name=f"m.g{b}.{e_}.{half}.{ck}.{hb}")
                        gslot = hb % 2
                        P.op("act", lambda e, psg=psg, gslot=gslot: e.activation(
                            gsb[:, gslot * CK:(gslot + 1) * CK], psg[:, 0:CK],
                            AF.Silu, bias=0.0, scale=1.0),
                            reads=[psg.buf], writes=[gsb_bufs[gslot]],
                            name=f"m.si{b}.{e_}.{half}.{ck}.{hb}")
                        mmg(P, [(ps_u[:, 0:CK],
                                 uph[:, kb * (MH // 2) + hb * 128:
                                     kb * (MH // 2) + hb * 128 + 128],
                                 xek[:, kb * CK:(kb + 1) * CK], st, sp)
                                for kb, (st, sp) in zip(range(NBLK), accg(NBLK))],
                            reads=[uph_buf] + xek_bufs, writes=[ps_u.buf],
                            name=f"m.u{b}.{e_}.{half}.{ck}.{hb}")
                        P.op("dve", lambda e, hb=hb, gslot=gslot: e.tensor_mul(
                            cck[:, hb * CK:(hb + 1) * CK],
                            gsb[:, gslot * CK:(gslot + 1) * CK], ps_u[:, 0:CK]),
                            reads=[gsb_bufs[gslot], ps_u.buf], writes=[cck_bufs[hb]],
                            name=f"m.c{b}.{e_}.{half}.{ck}.{hb}")
                    for dblk in range(NBLK):
                        psy = ps_y if dblk % 2 == 0 else ps_a
                        mmg(P, [(psy[:, 0:CK],
                                 dph[:, hb * D + dblk * 128: hb * D + dblk * 128 + 128],
                                 cck[:, hb * CK:(hb + 1) * CK], st, sp)
                                for hb, (st, sp) in zip(range(HHB), accg(HHB))],
                            reads=[dph_buf] + cck_bufs, writes=[psy.buf],
                            name=f"m.y{b}.{e_}.{half}.{ck}.{dblk}")
                        P.op("dve", lambda e, dblk=dblk, c0=c0, psy=psy: e.tensor_add(
                            hx.ap(dblk, c0, c0 + CK), hx.ap(dblk, c0, c0 + CK),
                            psy[:, 0:CK]),
                            reads=[hx.bufs[dblk], psy.buf], writes=[hx.bufs[dblk]],
                            name=f"m.ya{b}.{e_}.{half}.{ck}.{dblk}")
        emit_ln(P, C, hx, hx, None, None, None, EPS_MOD, f"m.pln.{b}")
        P.dma("sp", [(x_out_ap(b, blk), hx.ap(blk), {}) for blk in range(NBLK)],
              reads=[hx.bufs[blk] for blk in range(NBLK)],
              writes=XOUT(b), name=f"m.out{b}", is_output=FINAL)

    P.close_scope(es)



NB_CORE = 4
NT_FULL = 2048
NCORES = 8
CAPACITY = int(1.25 * 32 * 2048 / KEXP)

_prog_cache = {}


def _build_stage(stage):
    if stage in _prog_cache:
        return _prog_cache[stage]
    nt, nb = NT_FULL, NB_CORE
    nc = bass.Bass("TRN2", target_bir_lowering=False)
    x_in = nc.dram_tensor("x_in", [nb, 8, 128, nt], F32, kind="ExternalInput").ap()
    x_out = nc.dram_tensor("x_out", [nb, 8, 128, nt], F32, kind="ExternalOutput").ap()
    P = Prog(nc)
    G = {"nb": nb}
    emit_globals(P, G)
    in_ap = lambda b, blk: x_in[b] if blk is None else x_in[b, blk]
    out_ap = lambda b, blk: x_out[b, blk]
    if stage == "l0a":
        W = declare_l0a_weights(nc, nt)
        emit_l0_attn(P, G, W, in_ap, out_ap, nt)
    elif stage == "l0f":
        W = declare_l0f_weights(nc, nt)
        scratch = {
            "qs": nc.dram_tensor("f_qs", [nb, 8, 128, nt], BF16, kind="Internal").ap(),
            "ks": nc.dram_tensor("f_ks", [nb, 8, 128, nt], BF16, kind="Internal").ap(),
            "fs": nc.dram_tensor("f_fs", [nb, 8, 128, nt], BF16, kind="Internal").ap(),
            "pcs": nc.dram_tensor("f_pcs", [128, 8192], BF16, kind="Internal").ap(),
            "pcs2": nc.dram_tensor("f_pcs2", [128, 8192], BF16, kind="Internal").ap(),
        }
        emit_l0_ffn(P, G, W, in_ap, out_ap, nt, scratch)
    elif stage == "l1a":
        W = declare_l1a_weights(nc, nt)
        tsc = {
            "tks": nc.dram_tensor("t_tks", [nb, 8, 128, nt], BF16, kind="Internal").ap(),
            "tvs": nc.dram_tensor("t_tvs", [nb, nt // 128, 128, D], BF16,
                                  kind="Internal").ap(),
        }
        emit_l1_attn(P, G, W, in_ap, out_ap, nt, tsc)
    elif stage == "l1f":
        W = declare_l1f_weights(nc, nt)
        cc = {
            "cc_in": nc.dram_tensor("m_ccin", [2, 128], F32, kind="Internal").ap(),
            "cc_out": nc.dram_tensor("m_ccout", [16, 128], F32, kind="Internal").ap(),
        }
        emit_l1_ffn(P, G, W, in_ap, out_ap, nt, CAPACITY, True, cc)
    P.finish_and_build()
    nc.finalize()
    names = {"x_in"} | {ap.tensor.name for ap in W.values()}
    _prog_cache[stage] = (nc, names)
    return _prog_cache[stage]


def _sel_core(c):
    sel = np.zeros((16, 2), np.float32)
    for r in range(16):
        if r // 2 < c:
            sel[r, r % 2] = 1.0
    return sel


def kernel_unfused(**inputs):
    nt, nb = NT_FULL, NB_CORE
    inputs = {k: np.asarray(v) for k, v in inputs.items()}
    Hw = {}
    Hw.update(host_l0a(inputs, nt))
    Hw.update(host_l0f(inputs, nt))
    Hw.update(host_l1a(inputs, nt))
    Hw.update(host_l1f(inputs, nt))
    x = np.asarray(inputs["L0_attn_x"], np.float32)
    xd = np.ascontiguousarray(x.transpose(0, 2, 1)).reshape(NCORES, nb, 8, 128, nt)

    cur = [xd[c] for c in range(NCORES)]
    for stage in ["l0a", "l0f", "l1a", "l1f"]:
        nc, names = _build_stage(stage)
        in_maps = []
        for c in range(NCORES):
            m = {k: v for k, v in Hw.items() if k in names}
            m["x_in"] = cur[c]
            if "l1f_sel" in names:
                m["l1f_sel"] = _sel_core(c)
            in_maps.append(m)
        res = run_bass_kernel_spmd(nc, in_maps, core_ids=list(range(NCORES)))
        cur = [np.asarray(res.results[c]["x_out"]) for c in range(NCORES)]

    y = np.stack(cur).reshape(NCORES * nb, 1024, nt)
    return np.ascontiguousarray(y.transpose(0, 2, 1)).astype(np.float32)



def _build_fused():
    if "fused" in _prog_cache:
        return _prog_cache["fused"]
    nt, nb = NT_FULL, NB_CORE
    nc = bass.Bass("TRN2", target_bir_lowering=False)
    x_in = nc.dram_tensor("x_in", [nb, 8, 128, nt], F32, kind="ExternalInput").ap()
    x_out = nc.dram_tensor("x_out", [nb, 8, 128, nt], F32, kind="ExternalOutput").ap()
    xs1 = nc.dram_tensor("xs1", [nb, 8, 128, nt], F32, kind="Internal").ap()
    xs2 = nc.dram_tensor("xs2", [nb, 8, 128, nt], F32, kind="Internal").ap()
    xs3 = nc.dram_tensor("xs3", [nb, 8, 128, nt], F32, kind="Internal").ap()

    Wa = declare_l0a_weights(nc, nt)
    Wf = declare_l0f_weights(nc, nt)
    Wt = declare_l1a_weights(nc, nt)
    Wm = declare_l1f_weights(nc, nt)
    fscr = {
        "qs": nc.dram_tensor("f_qs", [nb, 8, 128, nt], BF16, kind="Internal").ap(),
        "ks": nc.dram_tensor("f_ks", [nb, 8, 128, nt], BF16, kind="Internal").ap(),
        "fs": nc.dram_tensor("f_fs", [nb, 8, 128, nt], BF16, kind="Internal").ap(),
        "pcs": nc.dram_tensor("f_pcs", [128, 8192], BF16, kind="Internal").ap(),
        "pcs2": nc.dram_tensor("f_pcs2", [128, 8192], BF16, kind="Internal").ap(),
    }
    tscr = {
        "tks": nc.dram_tensor("t_tks", [nb, 8, 128, nt], BF16, kind="Internal").ap(),
        "tvs": nc.dram_tensor("t_tvs", [nb, nt // 128, 128, D], BF16,
                              kind="Internal").ap(),
    }
    cc = {
        "cc_in": nc.dram_tensor("m_ccin", [2, 128], F32, kind="Internal").ap(),
        "cc_out": nc.dram_tensor("m_ccout", [16, 128], F32, kind="Internal").ap(),
    }

    P = Prog(nc)
    G = {"nb": nb}
    emit_globals(P, G)
    b1 = [Buf(f"xs1.{i}") for i in range(nb)]
    b2 = [Buf(f"xs2.{i}") for i in range(nb)]
    b3 = [Buf(f"xs3.{i}") for i in range(nb)]

    def mk(ap):
        return lambda b, blk: ap[b] if blk is None else ap[b, blk]

    ps_holder = {}
    emit_l0_attn(P, G, Wa, mk(x_in), mk(xs1), nt,
                 xio={"in": None, "out": b1, "final": False},
                 prestage_cb=lambda: ps_holder.update(emit_l1f_prestage(P, Wm)))
    emit_l0_ffn(P, G, Wf, mk(xs1), mk(xs2), nt, fscr,
                xio={"in": b1, "out": b2, "final": False})
    emit_l1_attn(P, G, Wt, mk(xs2), mk(xs3), nt, tscr,
                 xio={"in": b2, "out": b3, "final": False})
    emit_l1_ffn(P, G, Wm, mk(xs3), mk(x_out), nt, CAPACITY, True, cc,
                xio={"in": b3, "out": None, "final": True}, prestage=ps_holder)
    P.finish_and_build()
    nc.finalize()
    names = {"x_in"}
    for W in (Wa, Wf, Wt, Wm):
        names |= {ap.tensor.name for ap in W.values()}
    _prog_cache["fused"] = (nc, names)
    _prog_cache["fused_P"] = P
    return _prog_cache["fused"]


def kernel(**inputs):
    nt, nb = NT_FULL, NB_CORE
    inputs = {k: np.asarray(v) for k, v in inputs.items()}
    Hw = {}
    Hw.update(host_l0a(inputs, nt))
    Hw.update(host_l0f(inputs, nt))
    Hw.update(host_l1a(inputs, nt))
    Hw.update(host_l1f(inputs, nt))
    x = np.asarray(inputs["L0_attn_x"], np.float32)
    xd = np.ascontiguousarray(x.transpose(0, 2, 1)).reshape(NCORES, nb, 8, 128, nt)
    nc, names = _build_fused()
    in_maps = []
    for c in range(NCORES):
        m = {k: v for k, v in Hw.items() if k in names}
        m["x_in"] = xd[c]
        m["l1f_sel"] = _sel_core(c)
        in_maps.append(m)
    res = run_bass_kernel_spmd(nc, in_maps, core_ids=list(range(NCORES)))
    cur = [np.asarray(res.results[c]["x_out"]) for c in range(NCORES)]
    y = np.stack(cur).reshape(NCORES * nb, 1024, nt)
    return np.ascontiguousarray(y.transpose(0, 2, 1)).astype(np.float32)
